# version 05
# baseline (speedup 1.0000x reference)
import contextlib
import numpy as np
import ml_dtypes
import concourse.bass as bass
import concourse.mybir as mybir
from concourse.bass_utils import run_bass_kernel_spmd

F32 = mybir.dt.float32
BF16 = mybir.dt.bfloat16
AF = mybir.ActivationFunctionType
ALU = mybir.AluOpType
AX = mybir.AxisListType

BF = ml_dtypes.bfloat16

D = 2048
T = 1024
HALO = 4
TF = T + HALO
TP = 1152
DIN = 4096
NH = 64
HD = 64
NST = 128
CH = 256
NCH = T // CH
CONVD = DIN + 2 * NST
EIN = 2 * DIN + 2 * NST + NH
XBCD = CONVD + NH
FF0 = 8192
FF1 = 5632
EPS = 1e-5

PAIRS = [[0, 1], [2, 3], [4, 5], [6, 7]]
ENGS = ["tensor", "vector", "scalar", "gpsimd", "sync"]


def cdiv(a, b):
    return (a + b - 1) // b


class Ctx:
    def __init__(self, nc, stack):
        self.nc = nc
        self.t = {}
        self.sems = {}
        self.stack = stack
        self.phase_idx = -1
        self.pool = []
        self.phase_map = {}
        self.consts = {}
        self.cpool = None

    def init_pool(self, n=88):
        for i in range(n):
            self.pool.append(self.stack.enter_context(self.nc.semaphore(f"pool{i}")))

    def constf(self, value):
        if self.cpool is None:
            self.cpool = self.sb("cpool", [128, 32], F32)
        v = float(value)
        if v not in self.consts:
            self.consts[v] = len(self.consts)
            assert len(self.consts) <= 32
        return self.cpool[:, self.consts[v]:self.consts[v] + 1]

    def constf_r(self, value, rows):
        ap = self.constf(value)
        return self.cpool[0:rows, self.consts[float(value)]:self.consts[float(value)] + 1]

    def sem(self, name):
        if self.phase_idx < 0 or name.startswith("g_"):
            if name not in self.sems:
                self.sems[name] = self.stack.enter_context(self.nc.semaphore(name))
            return self.sems[name]
        key = (self.phase_idx, name)
        if key not in self.phase_map:
            used = len([1 for (p, _) in self.phase_map if p == self.phase_idx])
            assert used < len(self.pool), "phase sem pool exhausted"
            self.phase_map[key] = used
        return self.pool[self.phase_map[key]]

    def sb(self, name, shape, dtype, **kw):
        h = self.stack.enter_context(self.nc.sbuf_tensor(name, shape, dtype, **kw))
        self.t[name] = h
        return h

    def ps(self, name, shape, dtype=F32):
        h = self.stack.enter_context(self.nc.psum_tensor(name, shape, dtype))
        self.t[name] = h
        return h

    def dram(self, name, shape, dtype, kind=None, **kw):
        if kind:
            h = self.nc.dram_tensor(name, shape, dtype, kind=kind, **kw)
        else:
            h = self.nc.dram_tensor(name, shape, dtype, **kw)
        self.t[name] = h
        return h


def inc_last(eng, inst, sem):
    eng.drain()
    eng.sem_inc(sem, 1)


def dinc(eng, sem, n=1):
    eng.drain()
    eng.sem_inc(sem, n)


def emit_phases(nc, ctx, phases):
    ctx.init_pool()
    pool_nums = sorted(sm.num for sm in ctx.pool)
    lo, hi = pool_nums[0], pool_nums[-1] + 1
    assert hi - lo == len(pool_nums), "pool sems not contiguous"
    emitted = 0
    for p in phases:
        if isinstance(p, dict):
            ctx.t.update(p)
            continue
        i = emitted
        emitted += 1
        ctx.phase_idx = i
        if i > 0:
            with nc.Block() as bc:
                def clr(g):
                    g.dma_reset(range(lo, hi))
                    g.sem_clear(range(lo, hi))
                bc.gpsimd(clr)
            nc.all_engine_barrier()
        with nc.Block() as block:
            for eng_name in ENGS:
                fn = getattr(p, eng_name, None)
                if fn is None:
                    continue
                def make_body(fn=fn):
                    def body(eng):
                        fn(ctx, eng)
                    return body
                getattr(block, eng_name)(make_body())
        nc.all_engine_barrier()
    ctx.phase_idx = -1


class PDebugStore:

    def __init__(self, names):
        self.names = names

    def sync(self, ctx, sy):
        s_o = ctx.sem("dbg_out")
        k = 0
        for n in self.names:
            sy.dma_start(out=ctx.t["dbg_" + n].ap(),
                         in_=as_ap(ctx.t[n])[:, :]).then_inc(s_o, 16)
            k += 1
        sy.wait_ge(s_o, 16 * k)
        return None


def as_ap(obj):
    return obj.ap() if callable(getattr(obj, "ap", None)) else obj


def add_debug(ctx, names):
    for n in names:
        ap = as_ap(ctx.t[n])
        ctx.dram("dbg_" + n, list(ap.shape), ap.dtype, kind="ExternalOutput")
    return PDebugStore(names)


class P0Consts:
    def __init__(self, loads):
        self.loads = loads

    def gpsimd(self, ctx, g):
        ident = ctx.t["ident_bf"]
        g.memset(ident[:, :], 0.0)
        g.drain()
        g.affine_select(out=ident[:, :], in_=ident[:, :],
                        compare_op=ALU.not_equal, fill=1.0,
                        base=0, pattern=[[-1, 128]], channel_multiplier=1)
        ident32 = ctx.t["ident_f32"]
        g.memset(ident32[:, :], 0.0)
        g.drain()
        g.affine_select(out=ident32[:, :], in_=ident32[:, :],
                        compare_op=ALU.not_equal, fill=1.0,
                        base=0, pattern=[[-1, 128]], channel_multiplier=1)
        ma = ctx.t["gmaskA"]
        g.memset(ma[:, :], 1.0)
        g.drain()
        g.affine_select(out=ma[:, :], in_=ma[:, :], compare_op=ALU.is_ge,
                        fill=0.0, base=0, pattern=[[1, CH]], channel_multiplier=-1)
        mb = ctx.t["gmaskB"]
        g.memset(mb[:, :], 1.0)
        g.drain()
        last = g.affine_select(out=mb[:, :], in_=mb[:, :], compare_op=ALU.is_ge,
                               fill=0.0, base=-128, pattern=[[1, CH]],
                               channel_multiplier=-1)
        return last

    def vector(self, ctx, v):
        last = v.memset(ctx.t["ones_bf"][:, :], 1.0)
        for val, idx in ctx.consts.items():
            last = v.memset(ctx.cpool[:, idx:idx + 1], val)
        return last

    def sync(self, ctx, sy):
        sem = ctx.sem("p0_ld")
        n = 0
        for sb_name, dram_name in self.loads:
            sb = ctx.t[sb_name]
            dr = ctx.t[dram_name]
            sy.dma_start(out=sb[:, :], in_=dr.ap()).then_inc(sem, 16)
            n += 1
        if n:
            sy.wait_ge(sem, 16 * n)
        return None


class P1LN:

    def sync(self, ctx, sy):
        u = ctx.t["u_dram"]
        u_tok = ctx.t["u_tok"]
        s_u = ctx.sem("p1_u_ld")
        src = u.ap().rearrange("(tt p) d -> p tt d", p=128)
        dst = u_tok.rearrange("p (tt d) -> p tt d", tt=9)
        sy.dma_start(out=dst, in_=src).then_inc(s_u, 16)
        return None

    def vector(self, ctx, v):
        u_tok = ctx.t["u_tok"]
        u_bf = ctx.t["u_bf"]
        aggr = ctx.t["ln_aggr"]
        rows = ctx.t["ln_rows"]
        sq = ctx.t["ln_tmp"]
        s_u = ctx.sem("p1_u_ld")
        s_ready = ctx.sem("p1_tok_ready")
        s_rstd = ctx.sem("p1_rstd")
        s_sq = ctx.sem("p1_sq")
        for tt in range(9):
            if tt == 0:
                v.wait_ge(s_u, 16)
            ut = u_tok[:, tt * D:(tt + 1) * D]
            a = aggr[:, 2 * tt:2 * tt + 2]
            r = rows[:, 4 * tt:4 * tt + 4]
            v.reduce_sum(r[:, 2:3], ut, axis=AX.X)
            v.wait_ge(s_sq, tt + 1)
            v.reduce_sum(r[:, 3:4], sq[:, :], axis=AX.X)
            dinc(v, ctx.sem("p1_sq_free"))
            v.drain()
            v.tensor_scalar(a[:, 0:1], r[:, 2:3], 1.0 / D, None, op0=ALU.mult)
            v.tensor_scalar(r[:, 3:4], r[:, 3:4], 1.0 / D, None, op0=ALU.mult)
            v.drain()
            v.tensor_tensor(a[:, 1:2], a[:, 0:1], a[:, 0:1], op=ALU.mult)
            v.drain()
            v.tensor_tensor(a[:, 1:2], r[:, 3:4], a[:, 1:2], op=ALU.subtract)
            dinc(v, ctx.sem("p1_var_ready"))
        tmp2 = ctx.t["ln_tmp2"]
        for tt in range(9):
            ut = u_tok[:, tt * D:(tt + 1) * D]
            a = aggr[:, 2 * tt:2 * tt + 2]
            r = rows[:, 4 * tt:4 * tt + 4]
            v.wait_ge(s_rstd, tt + 1)
            v.tensor_scalar(tmp2[:, :], ut, a[:, 0:1], r[:, 0:1],
                            op0=ALU.subtract, op1=ALU.mult)
            v.drain()
            v.tensor_copy(u_bf[:, tt * D:(tt + 1) * D], tmp2[:, :])
            dinc(v, s_ready)
        return None

    def scalar(self, ctx, s):
        aggr = ctx.t["ln_aggr"]
        rows = ctx.t["ln_rows"]
        s_var = ctx.sem("p1_var_ready")
        s_rstd = ctx.sem("p1_rstd")
        xn = ctx.t["xn"]
        s_tp = ctx.sem("p1_tp")
        s_ev = ctx.sem("p1_ev")
        sq = ctx.t["ln_tmp"]
        u_tok = ctx.t["u_tok"]
        s_sq = ctx.sem("p1_sq")
        s_sq_free = ctx.sem("p1_sq_free")
        last = None
        for tt in range(9):
            if tt == 0:
                s.wait_ge(ctx.sem("p1_u_ld"), 16)
            if tt > 0:
                s.wait_ge(s_sq_free, tt)
            s.activation(sq[:, :], u_tok[:, tt * D:(tt + 1) * D], AF.Square)
            dinc(s, s_sq)
        for tt in range(9):
            a = aggr[:, 2 * tt:2 * tt + 2]
            r = rows[:, 4 * tt:4 * tt + 4]
            s.wait_ge(s_var, tt + 1)
            s.activation(r[:, 1:2], a[:, 1:2], AF.Ln, bias=ctx.constf(EPS))
            s.drain()
            s.activation(r[:, 0:1], r[:, 1:2], AF.Exp, scale=-0.5)
            dinc(s, s_rstd)
        for tt in range(9):
            w = 128 if tt < 8 else HALO
            for dt in range(16):
                k = tt * 16 + dt
                s.wait_ge(s_tp, k + 1)
                pt = ctx.t["pt0"] if k % 2 == 0 else ctx.t["pt1"]
                last = s.copy(xn[:, dt * TF + tt * 128: dt * TF + tt * 128 + w],
                              pt[:, 0:w])
                last.then_inc(s_ev, 1)
        return None

    def tensor(self, ctx, t):
        u_bf = ctx.t["u_bf"]
        ident = ctx.t["ident_bf"]
        s_ready = ctx.sem("p1_tok_ready")
        s_tp = ctx.sem("p1_tp")
        s_ev = ctx.sem("p1_ev")
        last = None
        for tt in range(9):
            t.wait_ge(s_ready, tt + 1)
            for dt in range(16):
                k = tt * 16 + dt
                if k >= 2:
                    t.wait_ge(s_ev, k - 1)
                pt = ctx.t["pt0"] if k % 2 == 0 else ctx.t["pt1"]
                last = t.transpose(pt[:, :],
                                   u_bf[:, tt * D + dt * 128: tt * D + dt * 128 + 128],
                                   ident[:, :])
                last.then_inc(s_tp, 1)
        return None


def prep_u_core(u, core):
    b, half = core // 2, core % 2
    t0 = half * 1024
    out = np.zeros((TP, D), dtype=np.float32)
    lo = t0 - HALO
    src_lo = max(lo, 0)
    off = src_lo - lo
    out[off:HALO + 1024, :] = u[b, src_lo:t0 + 1024, :]
    return np.ascontiguousarray(out)


P2_CHUNKS = [(DIN + 512 * i, 512) for i in range(8)] + [(DIN + 4096, 320)]
P2_TCS = [(0, 512), (512, 512), (1024, HALO)]
CONV_ORDER = [32, 33] + list(range(32))


def etiles(w):
    out = []
    o = 0
    while o < w:
        out.append((o, min(128, w - o)))
        o += 128
    return out


class P2InProjA:

    def groups(self):
        idx = 0
        for ci, (e0, ew) in enumerate(P2_CHUNKS):
            for j, (jo, jw) in enumerate(etiles(ew)):
                jglob = (e0 - DIN + jo) // 128
                for tc0, tw in P2_TCS:
                    yield idx, ci, ew, jo, jw, jglob, tc0, tw
                    idx += 1

    def chunk_last(self):
        last = {}
        for idx, ci, *_ in self.groups():
            last[ci] = idx
        return last

    def gpsimd(self, ctx, g):
        w = ctx.t["W_in"]
        s_mm = ctx.sem("p2_mm")
        last = self.chunk_last()
        for ci, (e0, ew) in enumerate(P2_CHUNKS):
            if ci >= 2:
                g.wait_ge(s_mm, last[ci - 2] + 1)
            slab = ctx.t["slabA"] if ci % 2 == 0 else ctx.t["slabB"]
            s_wl = ctx.sem("p2_wlA" if ci % 2 == 0 else "p2_wlB")
            src = w.ap()[:, e0:e0 + ew].rearrange("(k p) e -> p k e", p=128)
            dst = slab[:, 0:16 * ew].rearrange("p (k e) -> p k e", k=16)
            g.dma_start(out=dst, in_=src).then_inc(s_wl, 16)
        return None

    def tensor(self, ctx, t):
        xn = ctx.t["xn"]
        s_mm = ctx.sem("p2_mm")
        s_evV = ctx.sem("p2_evV")
        s_evS = ctx.sem("p2_evS")
        last_ci = -1
        for idx, ci, ew, jo, jw, jglob, tc0, tw in self.groups():
            if ci != last_ci:
                t.wait_ge(ctx.sem("p2_wlA" if ci % 2 == 0 else "p2_wlB"),
                          16 * (ci // 2 + 1))
                last_ci = ci
            slab = ctx.t["slabA"] if ci % 2 == 0 else ctx.t["slabB"]
            p = ctx.t["pA"] if idx % 2 == 0 else ctx.t["pB"]
            sev = s_evV if idx % 2 == 0 else s_evS
            if idx >= 2:
                t.wait_ge(sev, idx // 2)
            for k in range(16):
                mm = t.matmul(p[0:jw, 0:tw],
                              slab[:, k * ew + jo:k * ew + jo + jw],
                              xn[:, k * TF + tc0:k * TF + tc0 + tw],
                              start=(k == 0), stop=(k == 15))
            mm.then_inc(s_mm, 1)
        return None

    def _evac(self, ctx, eng, parity):
        xBC = ctx.t["xBC"]
        s_mm = ctx.sem("p2_mm")
        sev = ctx.sem("p2_evV" if parity == 0 else "p2_evS")
        for idx, ci, ew, jo, jw, jglob, tc0, tw in self.groups():
            if idx % 2 != parity:
                continue
            p = ctx.t["pA"] if parity == 0 else ctx.t["pB"]
            eng.wait_ge(s_mm, idx + 1)
            eng.tensor_copy(xBC[0:jw, jglob * TF + tc0:jglob * TF + tc0 + tw],
                            p[0:jw, 0:tw])
            dinc(eng, sev)
        return None

    def vector(self, ctx, v):
        return self._evac(ctx, v, 0)

    def scalar(self, ctx, s):
        xBC = ctx.t["xBC"]
        s_mm = ctx.sem("p2_mm")
        sev = ctx.sem("p2_evS")
        for idx, ci, ew, jo, jw, jglob, tc0, tw in self.groups():
            if idx % 2 != 1:
                continue
            p = ctx.t["pB"]
            s.wait_ge(s_mm, idx + 1)
            s.copy(xBC[0:jw, jglob * TF + tc0:jglob * TF + tc0 + tw], p[0:jw, 0:tw])
            dinc(s, sev)
        return None


class P3Conv:

    def vector(self, ctx, v):
        xBC = ctx.t["xBC"]
        dtbuf, acsA, acsB = ctx.t["dtbuf"], ctx.t["acsA"], ctx.t["acsB"]
        dt34 = xBC[0:64, 34 * TF + HALO:34 * TF + HALO + T]
        v.tensor_scalar(dtbuf[0:64, :], dt34, ctx.t["dt_bias"][0:64, 0:1], None,
                        op0=ALU.add)
        dinc(v, ctx.sem("p3_a"))
        v.wait_ge(ctx.sem("p3_sp"), 1)
        v.tensor_tensor(dtbuf[0:64, :], acsA[0:64, :], acsB[0:64, :], op=ALU.add)
        v.drain()
        v.tensor_copy(ctx.t["tmp_bf"][0:64, :], dtbuf[0:64, :])
        v.wait_ge(ctx.sem("p3_expA"), 1)
        v.tensor_scalar(acsA[0:64, :], dtbuf[0:64, :], ctx.t["expA"][0:64, 0:1],
                        None, op0=ALU.mult)
        dinc(v, ctx.sem("p3_dtbf"))
        dtT = ctx.t["dtT"]
        for tb in range(8):
            pt = ctx.t["pt0"] if tb % 2 == 0 else ctx.t["pt1"]
            v.wait_ge(ctx.sem("p3_dtT"), tb + 1)
            v.tensor_copy(dtT[:, tb * 64:(tb + 1) * 64], pt[:, 0:64])
            dinc(v, ctx.sem("p3_evdt"))
        for lv, sh in enumerate([1, 2, 4, 8, 16, 32, 64, 128]):
            src = acsA if lv % 2 == 0 else acsB
            dst = acsB if lv % 2 == 0 else acsA
            v.drain()
            for c in range(NCH):
                c0 = c * CH
                v.tensor_tensor(dst[0:64, c0 + sh:c0 + CH],
                                src[0:64, c0 + sh:c0 + CH],
                                src[0:64, c0:c0 + CH - sh], op=ALU.add)
                v.tensor_copy(dst[0:64, c0:c0 + sh], src[0:64, c0:c0 + sh])
        dinc(v, ctx.sem("p3_acsdone"))
        for c in range(NCH):
            c0 = c * CH
            v.tensor_scalar(acsB[0:64, c0:c0 + CH], acsA[0:64, c0:c0 + CH],
                            acsA[0:64, c0 + CH - 1:c0 + CH], None, op0=ALU.subtract)
        dinc(v, ctx.sem("p3_dexp"))
        acsT = ctx.t["acsT"]
        for tb in range(8):
            pf = ctx.t["pf0"] if tb % 2 == 0 else ctx.t["pf1"]
            v.wait_ge(ctx.sem("p3_acsTtp"), tb + 1)
            v.tensor_copy(acsT[:, tb * 64:(tb + 1) * 64], pf[:, 0:64])
            dinc(v, ctx.sem("p3_evacs"))
        for c in range(NCH):
            v.wait_ge(ctx.sem("p3_alT"), c + 1)
            v.tensor_copy(ctx.t["alast_rows"][0:1, c * 64:(c + 1) * 64],
                          ctx.t["pAl"][0:1, 0:64])
            dinc(v, ctx.sem("p3_eval"))
        decayT = ctx.t["decayT"]
        for tb in range(8):
            pt = ctx.t["pt0"] if tb % 2 == 0 else ctx.t["pt1"]
            v.wait_ge(ctx.sem("p3_decT"), tb + 1)
            v.tensor_copy(decayT[:, tb * 64:(tb + 1) * 64], pt[:, 0:64])
            dinc(v, ctx.sem("p3_evdec"))
        v.reciprocal(ctx.t["invdt"][:, :], dtT[:, :])
        v.drain()
        for tb in range(8):
            v.tensor_tensor(ctx.t["Dinvdt"][:, tb * 64:tb * 64 + 64],
                            ctx.t["invdt"][:, tb * 64:tb * 64 + 64],
                            ctx.t["D_tile"][:, 0:64], op=ALU.mult)
        dinc(v, ctx.sem("p3_dinv"))
        wcol = ctx.t["convw"]
        bcol = ctx.t["convb"]
        s_cd = ctx.sem("p3_convV")
        s_cfree = ctx.sem("p3_freeV")
        xdtT = ctx.t["xdtT"]
        BT = ctx.t["BT"]
        s_tp3 = ctx.sem("p3_tp")
        s_ev3 = ctx.sem("p3_ev")

        def evac_B(kb):
            for tb in range(8):
                pt = ctx.t["pt0"] if kb % 2 == 0 else ctx.t["pt1"]
                v.wait_ge(s_tp3, kb + 1)
                v.tensor_copy(BT[:, tb * 128:(tb + 1) * 128], pt[:, :])
                dinc(v, s_ev3)
                kb += 1
            return kb

        def evac_x(m, kb):
            for tb in range(8):
                pt = ctx.t["pt0"] if kb % 2 == 0 else ctx.t["pt1"]
                v.wait_ge(s_tp3, kb + 1)
                v.tensor_scalar(xdtT[:, tb * DIN + (2 * m) * 64:tb * DIN + (2 * m) * 64 + 64],
                                pt[:, 0:64], dtT[:, tb * 64 + 2 * m:tb * 64 + 2 * m + 1],
                                None, op0=ALU.mult)
                v.tensor_scalar(xdtT[:, tb * DIN + (2 * m + 1) * 64:tb * DIN + (2 * m + 1) * 64 + 64],
                                pt[:, 64:128], dtT[:, tb * 64 + 2 * m + 1:tb * 64 + 2 * m + 2],
                                None, op0=ALU.mult)
                dinc(v, s_ev3)
                kb += 1
            return kb

        kb = 0
        for i_o, j in enumerate(CONV_ORDER):
            cf = ctx.t["convfV"] if i_o % 2 == 0 else ctx.t["convfG"]
            if i_o >= 2:
                v.wait_ge(s_cfree, i_o - 1)
            xj = xBC[:, j * TF:(j + 1) * TF]
            v.drain()
            v.tensor_scalar(cf[:, :], xj[:, 1:1 + T], wcol[:, 4 * j:4 * j + 1],
                            bcol[:, j:j + 1], op0=ALU.mult, op1=ALU.add)
            for i in range(1, 4):
                v.drain()
                v.scalar_tensor_tensor(cf[:, :], xj[:, 1 + i:1 + i + T],
                                       wcol[:, 4 * j + i:4 * j + i + 1], cf[:, :],
                                       op0=ALU.mult, op1=ALU.add)
            dinc(v, s_cd)
            if i_o == 2:
                kb = evac_B(kb)
            if i_o >= 3:
                kb = evac_x(i_o - 3, kb)
        kb = evac_x(31, kb)
        return None

    def scalar(self, ctx, s):
        dtbuf, acsA, acsB = ctx.t["dtbuf"], ctx.t["acsA"], ctx.t["acsB"]
        s.activation(ctx.t["expA"][0:64, 0:1], ctx.t["A_log"][0:64, 0:1], AF.Exp)
        dinc(s, ctx.sem("p3_expA"))
        s.wait_ge(ctx.sem("p3_a"), 1)
        s.activation(acsA[0:64, :], dtbuf[0:64, :], AF.Abs)
        s.drain()
        s.activation(acsB[0:64, :], acsA[0:64, :], AF.Exp, scale=-1.0)
        s.drain()
        s.activation(acsA[0:64, :], acsB[0:64, :], AF.Ln, bias=ctx.constf_r(1.0, 64))
        s.drain()
        s.activation(acsB[0:64, :], dtbuf[0:64, :], AF.Relu)
        dinc(s, ctx.sem("p3_sp"))
        s.wait_ge(ctx.sem("p3_dexp"), 1)
        s.activation(ctx.t["decay_bf"][0:64, :], acsB[0:64, :], AF.Exp)
        dinc(s, ctx.sem("p3_dbf"))
        s.wait_ge(ctx.sem("p3_eval"), NCH)
        s.activation(ctx.t["alast_erow"][0:1, 0:NCH * 64],
                     ctx.t["alast_rows"][0:1, 0:NCH * 64], AF.Exp, scale=-1.0)
        dinc(s, ctx.sem("p3_alerow"))
        s_cdV = ctx.sem("p3_convV")
        s_silu = ctx.sem("p3_silu")
        sfree = ctx.sem("p3_freeV")
        for i_o, j in enumerate(CONV_ORDER):
            s.wait_ge(s_cdV, i_o + 1)
            cf = ctx.t["convfV"] if i_o % 2 == 0 else ctx.t["convfG"]
            if j == 32:
                dst = ctx.t["Bfeat"][:, :]
            elif j == 33:
                dst = ctx.t["Cfeat"][:, :]
            else:
                if j >= 2:
                    s.wait_ge(ctx.sem("p3_tp"), 8 * j)
                dst = ctx.t["conv_tmp"][:, (j % 2) * T:(j % 2) * T + T]
            s.activation(dst, cf[:, :], AF.Silu)
            dinc(s, s_silu)
            dinc(s, sfree)
        return None

    def tensor(self, ctx, t):
        ident = ctx.t["ident_bf"]
        ident32 = ctx.t["ident_f32"]
        tmp_bf = ctx.t["tmp_bf"]
        acsA = ctx.t["acsA"]
        t.wait_ge(ctx.sem("p3_dtbf"), 1)
        for tb in range(8):
            pt = ctx.t["pt0"] if tb % 2 == 0 else ctx.t["pt1"]
            if tb >= 2:
                t.wait_ge(ctx.sem("p3_evdt"), tb - 1)
            t.transpose(pt[:, 0:64], tmp_bf[0:64, tb * 128:(tb + 1) * 128],
                        ident[0:64, 0:64]).then_inc(ctx.sem("p3_dtT"), 1)
        t.wait_ge(ctx.sem("p3_acsdone"), 1)
        for tb in range(8):
            pf = ctx.t["pf0"] if tb % 2 == 0 else ctx.t["pf1"]
            if tb >= 2:
                t.wait_ge(ctx.sem("p3_evacs"), tb - 1)
            t.transpose(pf[:, 0:64], acsA[0:64, tb * 128:(tb + 1) * 128],
                        ident32[0:64, 0:64]).then_inc(ctx.sem("p3_acsTtp"), 1)
        for c in range(NCH):
            if c >= 1:
                t.wait_ge(ctx.sem("p3_eval"), c)
            t.transpose(ctx.t["pAl"][0:1, 0:64],
                        acsA[0:64, c * CH + CH - 1:c * CH + CH],
                        ident32[0:64, 0:64]).then_inc(ctx.sem("p3_alT"), 1)
        t.wait_ge(ctx.sem("p3_evdt"), 8)
        t.wait_ge(ctx.sem("p3_dbf"), 1)
        for tb in range(8):
            pt = ctx.t["pt0"] if tb % 2 == 0 else ctx.t["pt1"]
            if tb >= 2:
                t.wait_ge(ctx.sem("p3_evdec"), tb - 1)
            t.transpose(pt[:, 0:64], ctx.t["decay_bf"][0:64, tb * 128:(tb + 1) * 128],
                        ident[0:64, 0:64]).then_inc(ctx.sem("p3_decT"), 1)
        s_silu = ctx.sem("p3_silu")
        s_tp3 = ctx.sem("p3_tp")
        s_ev3 = ctx.sem("p3_ev")
        t.wait_ge(ctx.sem("p3_evdec"), 8)
        kb = 0
        t.wait_ge(s_silu, 1)
        for tb in range(8):
            pt = ctx.t["pt0"] if kb % 2 == 0 else ctx.t["pt1"]
            if kb >= 2:
                t.wait_ge(s_ev3, kb - 1)
            t.transpose(pt[:, :], ctx.t["Bfeat"][:, tb * 128:tb * 128 + 128],
                        ident[:, :]).then_inc(s_tp3, 1)
            kb += 1
        for j in range(32):
            t.wait_ge(s_silu, j + 3)
            for tb in range(8):
                pt = ctx.t["pt0"] if kb % 2 == 0 else ctx.t["pt1"]
                if kb >= 2:
                    t.wait_ge(s_ev3, kb - 1)
                t.transpose(pt[:, :],
                            ctx.t["conv_tmp"][:, (j % 2) * T + tb * 128:(j % 2) * T + tb * 128 + 128],
                            ident[:, :]).then_inc(s_tp3, 1)
                kb += 1
        return None


class P4aState:
    def tensor(self, ctx, t):
        s_bd = ctx.sem("p4a_bd")
        s_mm = ctx.sem("p4a_mm")
        s_rec = ctx.sem("p4a_rec")
        s_alb = ctx.sem("p4a_alb")
        for c in range(NCH):
            p = ctx.t["pAcsB0"] if c % 2 == 0 else ctx.t["pAcsB1"]
            if c >= 2:
                t.wait_ge(s_alb, c - 1)
            t.matmul(p[:, 0:64], ctx.t["ones_bf"][0:1, 0:128],
                     ctx.t["alast_erow"][0:1, c * 64:(c + 1) * 64],
                     start=True, stop=True).then_inc(ctx.sem("p4a_albtp"), 1)
        xdtT = ctx.t["xdtT"]
        bdec = ctx.t["bdec"]
        for idx in range(NCH * NH):
            c, h = idx // NH, idx % NH
            slot = idx % 2
            pS = ctx.t["pS0"] if slot == 0 else ctx.t["pS1"]
            t.wait_ge(s_bd, idx + 1)
            if idx >= 2:
                t.wait_ge(s_rec, idx - 1)
            for tb in (2 * c, 2 * c + 1):
                mm = t.matmul(pS, bdec[:, slot * 256 + (tb % 2) * 128:slot * 256 + (tb % 2) * 128 + 128],
                              xdtT[:, tb * DIN + h * 64:tb * DIN + h * 64 + 64],
                              start=(tb % 2 == 0), stop=(tb % 2 == 1))
            mm.then_inc(s_mm, 1)
        return None

    def vector(self, ctx, v):
        s_bd = ctx.sem("p4a_bd")
        s_mm = ctx.sem("p4a_mm")
        s_rec = ctx.sem("p4a_rec")
        s_alb = ctx.sem("p4a_alb")
        BT = ctx.t["BT"]
        decayT = ctx.t["decayT"]
        bdec = ctx.t["bdec"]
        S_run = ctx.t["S_run"]
        for c in range(NCH):
            p = ctx.t["pAcsB0"] if c % 2 == 0 else ctx.t["pAcsB1"]
            v.wait_ge(ctx.sem("p4a_albtp"), c + 1)
            v.tensor_copy(ctx.t["expAlB"][:, c * 64:(c + 1) * 64], p[:, 0:64])
            dinc(v, s_alb)
        for idx in range(NCH * NH):
            c, h = idx // NH, idx % NH
            slot = idx % 2
            pS = ctx.t["pS0"] if slot == 0 else ctx.t["pS1"]
            if idx >= 2:
                v.wait_ge(s_mm, idx - 1)
            for tb in (2 * c, 2 * c + 1):
                v.tensor_scalar(bdec[:, slot * 256 + (tb % 2) * 128:slot * 256 + (tb % 2) * 128 + 128],
                                BT[:, tb * 128:(tb + 1) * 128],
                                decayT[:, tb * 64 + h:tb * 64 + h + 1], None, op0=ALU.mult)
            dinc(v, s_bd)
            v.wait_ge(s_mm, idx + 1)
            sh = S_run[:, h * 32:(h + 1) * 32].bitcast(BF16) if False else None
            srun_h = ctx.t["S_run"][:, h * 64:(h + 1) * 64]
            if c == 0:
                v.tensor_copy(srun_h, pS)
            else:
                stmp = ctx.t["stmpA"]
                v.tensor_scalar(stmp[:, :], srun_h, ctx.t["expAlB"][:, c * 64 + h:c * 64 + h + 1],
                                None, op0=ALU.mult)
                v.drain()
                v.tensor_tensor(srun_h, stmp[:, :], pS, op=ALU.add)
            dinc(v, s_rec)
        return None


class P4xExchange:
    def gpsimd(self, ctx, g):
        sem = ctx.sem("p4x_dma")
        sem2 = ctx.sem("p4x_dma2")
        cc = ctx.sem("p4x_cc")
        g.dma_start(out=ctx.t["stout"].ap(), in_=ctx.t["S_run"][:, :]).then_inc(sem, 16)
        g.wait_ge(sem, 16)
        g.collective_compute(
            "AllGather", ALU.bypass,
            replica_groups=PAIRS,
            ins=[ctx.t["stout"].ap().opt()],
            outs=[ctx.t["stgath"].ap().opt()],
        ).then_inc(cc, 1)
        g.wait_ge(cc, 1)
        g.dma_start(out=ctx.t["S_run"][:, :], in_=ctx.t["stgath"].ap()[0:128, :]).then_inc(sem2, 16)
        g.wait_ge(sem2, 16)
        return None


class P4bSSD:
    def tensor(self, ctx, t):
        ident32 = ctx.t["ident_f32"]
        ones = ctx.t["ones_bf"]
        acsT = ctx.t["acsT"]
        xdtT = ctx.t["xdtT"]
        s_gm = ctx.sem("p4b_gm")
        s_row = ctx.sem("p4b_rowev")
        s_mt = ctx.sem("p4b_mt")
        s_yev = ctx.sem("p4b_yev")
        s_acsb = ctx.sem("p4b_acsb")
        s_rowtp = ctx.sem("p4b_rowtp")
        s_gtp = ctx.sem("p4b_gtp")
        s_smm = ctx.sem("p4b_smm")
        s_rec = ctx.sem("p4b_rec")
        bdec = ctx.t["bdec"]
        for c in range(NCH):
            if c > 0:
                t.wait_ge(s_row, c * NH)
            for st in range(2):
                pG = ctx.t["pG0"] if st == 0 else ctx.t["pG1"]
                t.matmul(pG, ctx.t["Bfeat"][:, c * CH + st * 128:c * CH + st * 128 + 128],
                         ctx.t["Cfeat"][:, c * CH:(c + 1) * CH],
                         start=True, stop=True).then_inc(s_gtp, 1)
            for h in range(NH):
                idx = c * NH + h
                slot = idx % 2
                if h == 0:
                    t.wait_ge(s_gm, 2 * (c + 1))
                if idx >= 1:
                    t.wait_ge(s_row, idx)
                for tb in range(2):
                    pR = ctx.t["pRow0"] if tb == 0 else ctx.t["pRow1"]
                    t.transpose(pR, acsT[:, (2 * c + tb) * 64 + h:(2 * c + tb) * 64 + h + 1],
                                ident32[:, :]).then_inc(s_rowtp, 1)
                t.wait_ge(s_row, idx + 1)
                if idx >= 2:
                    t.wait_ge(s_acsb, idx - 1)
                    t.wait_ge(ctx.sem("p4b_dcl"), idx - 1)
                pB_ = ctx.t["pAcsB0"] if slot == 0 else ctx.t["pAcsB1"]
                t.matmul(pB_, ones[0:1, 0:128],
                         ctx.t["acsrow_hi"][0:1, slot * 256:slot * 256 + 256],
                         start=True, stop=False)
                t.matmul(pB_, ones[0:1, 0:128],
                         ctx.t["acsrow_lo"][0:1, slot * 256:slot * 256 + 256],
                         start=False, stop=True).then_inc(ctx.sem("p4b_acsbtp"), 1)
                t.wait_ge(s_mt, idx + 1)
                if idx >= 2:
                    t.wait_ge(s_yev, idx - 1)
                pY = ctx.t["pY0"] if slot == 0 else ctx.t["pY1"]
                MT = ctx.t["MT"]
                t.matmul(pY, xdtT[:, (2 * c) * DIN + h * 64:(2 * c) * DIN + h * 64 + 64],
                         MT[:, slot * 512:slot * 512 + 256], start=True, stop=False)
                t.matmul(pY, xdtT[:, (2 * c + 1) * DIN + h * 64:(2 * c + 1) * DIN + h * 64 + 64],
                         MT[:, slot * 512 + 256:slot * 512 + 512], start=False, stop=False)
                ysrc = ctx.t["sbf"][:, slot * 64:slot * 64 + 64] if c == 0 \
                    else ctx.t["S_run"][:, h * 64:(h + 1) * 64]
                t.matmul(pY, ysrc,
                         ctx.t["csdo"][:, slot * 256:slot * 256 + 256], start=False, stop=False)
                t.matmul(pY[0:64, 0:128], ctx.t["xdtmp"][:, slot * 128:slot * 128 + 64],
                         ctx.t["ident_bf"][:, :], start=False, stop=False)
                t.matmul(pY[0:64, 128:256], ctx.t["xdtmp"][:, slot * 128 + 64:slot * 128 + 128],
                         ctx.t["ident_bf"][:, :], start=False, stop=True).then_inc(ctx.sem("p4b_ymm"), 1)
                pS = ctx.t["pS0"] if slot == 0 else ctx.t["pS1"]
                if idx >= 2:
                    t.wait_ge(s_rec, idx - 1)
                for tb in (2 * c, 2 * c + 1):
                    mm = t.matmul(pS, bdec[:, slot * 256 + (tb % 2) * 128:slot * 256 + (tb % 2) * 128 + 128],
                                  xdtT[:, tb * DIN + h * 64:tb * DIN + h * 64 + 64],
                                  start=(tb % 2 == 0), stop=(tb % 2 == 1))
                mm.then_inc(s_smm, 1)
        return None

    def scalar(self, ctx, s):
        acsT = ctx.t["acsT"]
        for idx in range(NCH * NH):
            c, h = idx // NH, idx % NH
            slot = idx % 2
            pB_ = ctx.t["pAcsB0"] if slot == 0 else ctx.t["pAcsB1"]
            s.wait_ge(ctx.sem("p4b_dcl"), idx + 1)
            expL = ctx.t["expL"]
            s.activation(expL[:, slot * 512:slot * 512 + 256],
                         ctx.t["dclamp"][:, 0:256], AF.Exp, scale=-1.0)
            s.activation(expL[:, slot * 512 + 256:slot * 512 + 512],
                         ctx.t["dclamp"][:, 256:512], AF.Exp, scale=-1.0)
            s.activation(ctx.t["sdo_bf"][:, slot * 256:slot * 256 + 256], pB_, AF.Exp,
                         scale=-1.0)
            dinc(s, ctx.sem("p4b_expl"))
            dinc(s, ctx.sem("p4b_acsb"))
            s.wait_ge(ctx.sem("p4b_ymm"), idx + 1)
            pY = ctx.t["pY0"] if slot == 0 else ctx.t["pY1"]
            y = ctx.t["y_feat"]
            r0 = (h % 2) * 64
            s.copy(y[r0:r0 + 64, (h // 2) * T + c * CH:(h // 2) * T + (c + 1) * CH], pY)
            dinc(s, ctx.sem("p4b_yev"))
        return None

    def vector(self, ctx, v):
        acsT = ctx.t["acsT"]
        s_gtp = ctx.sem("p4b_gtp")
        s_gm = ctx.sem("p4b_gm")
        s_rowtp = ctx.sem("p4b_rowtp")
        s_row = ctx.sem("p4b_rowev")
        s_expl = ctx.sem("p4b_expl")
        s_mt = ctx.sem("p4b_mt")
        s_smm = ctx.sem("p4b_smm")
        s_rec = ctx.sem("p4b_rec")
        BT = ctx.t["BT"]
        decayT = ctx.t["decayT"]
        bdec = ctx.t["bdec"]
        S_run = ctx.t["S_run"]
        role = ctx.t["roleflag"]
        for c in range(NCH):
            for st in range(2):
                pG = ctx.t["pG0"] if st == 0 else ctx.t["pG1"]
                v.wait_ge(s_gtp, 2 * c + st + 1)
                v.tensor_tensor(ctx.t["Gm"][:, st * CH:(st + 1) * CH], pG,
                                ctx.t["gmaskA" if st == 0 else "gmaskB"][:, :], op=ALU.mult)
                dinc(v, s_gm)
            for h in range(NH):
                idx = c * NH + h
                slot = idx % 2
                v.wait_ge(s_rowtp, 2 * idx + 2)
                v.tensor_copy(ctx.t["acsrow"][0:1, slot * 256:slot * 256 + 128], ctx.t["pRow0"])
                v.tensor_copy(ctx.t["acsrow"][0:1, slot * 256 + 128:slot * 256 + 256], ctx.t["pRow1"])
                v.drain()
                row = ctx.t["acsrow"][0:1, slot * 256:slot * 256 + 256]
                v.tensor_copy(ctx.t["acsrow_hi"][0:1, slot * 256:slot * 256 + 256], row)
                v.drain()
                v.tensor_tensor(ctx.t["acsrow_lo"][0:1, slot * 256:slot * 256 + 256], row,
                                ctx.t["acsrow_hi"][0:1, slot * 256:slot * 256 + 256], op=ALU.subtract)
                dinc(v, s_row)
                v.wait_ge(ctx.sem("p4b_acsbtp"), idx + 1)
                if idx >= 1:
                    v.wait_ge(s_expl, idx)
                pB_ = ctx.t["pAcsB0"] if slot == 0 else ctx.t["pAcsB1"]
                v.tensor_scalar(ctx.t["dclamp"][:, 0:256], pB_,
                                acsT[:, (2 * c) * 64 + h:(2 * c) * 64 + h + 1], 0.0,
                                op0=ALU.subtract, op1=ALU.max)
                v.tensor_scalar(ctx.t["dclamp"][:, 256:512], pB_,
                                acsT[:, (2 * c + 1) * 64 + h:(2 * c + 1) * 64 + h + 1], 0.0,
                                op0=ALU.subtract, op1=ALU.max)
                dinc(v, ctx.sem("p4b_dcl"))
                v.wait_ge(s_expl, idx + 1)
                MT = ctx.t["MT"]
                for st in range(2):
                    v.tensor_tensor(MT[:, slot * 512 + st * 256:slot * 512 + (st + 1) * 256],
                                    ctx.t["expL"][:, slot * 512 + st * 256:slot * 512 + (st + 1) * 256],
                                    ctx.t["Gm"][:, st * CH:(st + 1) * CH], op=ALU.mult)
                v.tensor_tensor(ctx.t["csdo"][:, slot * 256:slot * 256 + 256],
                                ctx.t["Cfeat"][:, c * CH:(c + 1) * CH],
                                ctx.t["sdo_bf"][:, slot * 256:slot * 256 + 256], op=ALU.mult)
                for tb in range(2):
                    v.tensor_scalar(ctx.t["xdtmp"][:, slot * 128 + tb * 64:slot * 128 + tb * 64 + 64],
                                    ctx.t["xdtT"][:, (2 * c + tb) * DIN + h * 64:(2 * c + tb) * DIN + h * 64 + 64],
                                    ctx.t["Dinvdt"][:, (2 * c + tb) * 64 + h:(2 * c + tb) * 64 + h + 1],
                                    None, op0=ALU.mult)
                sbf = ctx.t["sbf"]
                if c == 0:
                    v.tensor_scalar(sbf[:, slot * 64:slot * 64 + 64],
                                    S_run[:, h * 64:(h + 1) * 64], role[:, 0:1], None,
                                    op0=ALU.mult)
                if idx >= 2:
                    v.wait_ge(s_smm, idx - 1)
                for tb in (2 * c, 2 * c + 1):
                    v.tensor_scalar(bdec[:, slot * 256 + (tb % 2) * 128:slot * 256 + (tb % 2) * 128 + 128],
                                    BT[:, tb * 128:(tb + 1) * 128],
                                    decayT[:, tb * 64 + h:tb * 64 + h + 1], None, op0=ALU.mult)
                dinc(v, s_mt)
                v.wait_ge(s_smm, idx + 1)
                if c < NCH - 1:
                    pS = ctx.t["pS0"] if slot == 0 else ctx.t["pS1"]
                    stmp = ctx.t["stmpA"] if slot == 0 else ctx.t["stmpB"]
                    scur = ctx.t["sbf"][:, slot * 64:slot * 64 + 64] if c == 0 \
                        else S_run[:, h * 64:(h + 1) * 64]
                    v.tensor_scalar(stmp[:, :], scur,
                                    ctx.t["expAlB"][:, c * 64 + h:c * 64 + h + 1], None,
                                    op0=ALU.mult)
                    v.drain()
                    v.tensor_tensor(S_run[:, h * 64:(h + 1) * 64], stmp[:, :], pS,
                                    op=ALU.add)
                dinc(v, s_rec)
        return None


P5_CHUNKS = [(512 * i, 512) for i in range(8)]


class P5Gate:

    def groups(self):
        idx = 0
        for ci, (e0, ew) in enumerate(P5_CHUNKS):
            for j in range(4):
                for tc in range(2):
                    yield idx, ci, ew, j, ci * 4 + j, tc
                    idx += 1

    def gpsimd(self, ctx, g):
        w = ctx.t["W_in"]
        s_mm = ctx.sem("p5_mm")
        for ci, (e0, ew) in enumerate(P5_CHUNKS):
            if ci >= 2:
                g.wait_ge(s_mm, 8 * (ci - 1))
            slab = ctx.t["slabA"] if ci % 2 == 0 else ctx.t["slabB"]
            s_wl = ctx.sem("p5_wlA" if ci % 2 == 0 else "p5_wlB")
            src = w.ap()[:, e0:e0 + ew].rearrange("(k p) e -> p k e", p=128)
            dst = slab[:, 0:16 * ew].rearrange("p (k e) -> p k e", k=16)
            g.dma_start(out=dst, in_=src).then_inc(s_wl, 16)
        g.wait_ge(s_mm, 64)
        wo = ctx.t["W_out_d"]
        qi = 0
        for th in range(2):
            for oh in range(2):
                for et in range(32):
                    if qi >= 2:
                        g.wait_ge(ctx.sem("p5_wou"), qi - 1)
                    slab = ctx.t["slabC"] if qi % 2 == 0 else ctx.t["slabD"]
                    s_wl = ctx.sem("p5_woA" if qi % 2 == 0 else "p5_woB")
                    src = wo.ap()[et * 128:(et + 1) * 128, oh * 1024:(oh + 1) * 1024]
                    g.dma_start(out=slab[:, 0:1024], in_=src).then_inc(s_wl, 16)
                    qi += 1
        return None

    def tensor(self, ctx, t):
        xn = ctx.t["xn"]
        s_silu = ctx.sem("p5_silu")
        s_mm = ctx.sem("p5_mm")
        s_sq = ctx.sem("p5_sq")
        last_ci = -1
        for idx, ci, ew, j, jglob, tc in self.groups():
            if ci != last_ci:
                t.wait_ge(ctx.sem("p5_wlA" if ci % 2 == 0 else "p5_wlB"),
                          16 * (ci // 2 + 1))
                last_ci = ci
            slab = ctx.t["slabA"] if ci % 2 == 0 else ctx.t["slabB"]
            p = ctx.t["pA"] if idx % 2 == 0 else ctx.t["pB"]
            if idx >= 2:
                t.wait_ge(s_silu, idx - 1)
            for k in range(16):
                mm = t.matmul(p, slab[:, k * ew + j * 128:k * ew + j * 128 + 128],
                              xn[:, k * TF + HALO + tc * 512:k * TF + HALO + tc * 512 + 512],
                              start=(k == 0), stop=(k == 15))
            mm.then_inc(s_mm, 1)
            if idx >= 2:
                self._sq_mm(ctx, t, idx - 2)
        self._sq_mm(ctx, t, 62)
        self._sq_mm(ctx, t, 63)
        t.wait_ge(ctx.sem("p5_rrow"), 1)
        for tc in range(2):
            t.matmul(ctx.t["bank0" if tc == 0 else "bank1"].ap()[:, 0:512],
                     ctx.t["ones_bf"][0:1, 0:128],
                     ctx.t["rrow_bf"][0:1, tc * 512:tc * 512 + 512],
                     start=True, stop=True).then_inc(ctx.sem("p5_rbtp"), 1)
        t.wait_ge(ctx.sem("p5_fin"), 32)
        banks = [ctx.t[f"bank{i}"] for i in range(8)]
        yfin = ctx.t["y_feat"]
        qi = 0
        for th in range(2):
            for oh in range(2):
                q = th * 2 + oh
                if q >= 1:
                    t.wait_ge(ctx.sem("p5_wev"), q)
                for et in range(32):
                    t.wait_ge(ctx.sem("p5_woA" if qi % 2 == 0 else "p5_woB"),
                              16 * (qi // 2 + 1))
                    slab = ctx.t["slabC"] if qi % 2 == 0 else ctx.t["slabD"]
                    for tt in range(4):
                        for oc in range(2):
                            mm = t.matmul(banks[tt * 2 + oc].ap(),
                                          yfin[:, et * T + th * 512 + tt * 128:et * T + th * 512 + tt * 128 + 128],
                                          slab[:, oc * 512:oc * 512 + 512],
                                          start=(et == 0), stop=(et == 31))
                    mm.then_inc(ctx.sem("p5_wou"), 1)
                    qi += 1
        return None

    def _sq_mm(self, ctx, t, idx):
        jglob, tc = idx // 2, idx % 2
        t.wait_ge(ctx.sem("p5_sq"), idx + 1)
        psq = ctx.t["bank2"].ap() if tc == 0 else ctx.t["bank3"].ap()
        t.matmul(psq[0:1, 0:512], ctx.t["ones_bf"][:, 0:1],
                 ctx.t["sqbuf"][:, (idx % 2) * 512:(idx % 2) * 512 + 512],
                 start=(jglob == 0), stop=(jglob == 31)).then_inc(ctx.sem("p5_sqmm"), 1)

    def scalar(self, ctx, s):
        s_mm = ctx.sem("p5_mm")
        for idx, ci, ew, j, jglob, tc in self.groups():
            p = ctx.t["pA"] if idx % 2 == 0 else ctx.t["pB"]
            s.wait_ge(s_mm, idx + 1)
            if idx >= 2:
                s.wait_ge(ctx.sem("p5_zt"), idx - 1)
            s.activation(ctx.t["ztmp"][:, (idx % 2) * 512:(idx % 2) * 512 + 512],
                         p, AF.Silu)
            dinc(s, ctx.sem("p5_silu"))
        s.wait_ge(ctx.sem("p5_sqmm"), 64)
        s.activation(ctx.t["rr_ln0"][0:1, 0:512], ctx.t["bank2"].ap()[0:1, 0:512], AF.Ln,
                     scale=1.0 / DIN, bias=ctx.constf_r(EPS, 1))
        s.activation(ctx.t["rr_ln1"][0:1, 0:512], ctx.t["bank3"].ap()[0:1, 0:512], AF.Ln,
                     scale=1.0 / DIN, bias=ctx.constf_r(EPS, 1))
        s.drain()
        s.activation(ctx.t["rrow_bf"][0:1, 0:512], ctx.t["rr_ln0"][0:1, 0:512],
                     AF.Exp, scale=-0.5)
        s.activation(ctx.t["rrow_bf"][0:1, 512:1024], ctx.t["rr_ln1"][0:1, 0:512],
                     AF.Exp, scale=-0.5)
        dinc(s, ctx.sem("p5_rrow"))
        return None

    def vector(self, ctx, v):
        y = ctx.t["y_feat"]
        for idx, ci, ew, j, jglob, tc in self.groups():
            v.wait_ge(ctx.sem("p5_silu"), idx + 1)
            ysl = y[:, jglob * T + tc * 512:jglob * T + tc * 512 + 512]
            zt = ctx.t["ztmp"][:, (idx % 2) * 512:(idx % 2) * 512 + 512]
            bounce = ctx.t["ztmp2"][:, (idx % 2) * 512:(idx % 2) * 512 + 512]
            v.tensor_tensor(bounce, ysl, zt, op=ALU.mult)
            dinc(v, ctx.sem("p5_zt"))
            if idx >= 2:
                v.wait_ge(ctx.sem("p5_sqmm"), idx - 1)
            v.tensor_tensor(ctx.t["sqbuf"][:, (idx % 2) * 512:(idx % 2) * 512 + 512],
                            bounce, bounce, op=ALU.mult)
            dinc(v, ctx.sem("p5_sq"))
            v.tensor_copy(ysl, bounce)
            dinc(v, ctx.sem("p5_yg"))
            v.drain()
        for tc in range(2):
            v.wait_ge(ctx.sem("p5_rbtp"), tc + 1)
            pb = ctx.t["bank0" if tc == 0 else "bank1"].ap()
            v.tensor_copy(ctx.t["rstd_b"][:, tc * 512:tc * 512 + 512], pb[:, 0:512])
        v.drain()
        for jglob in range(32):
            gs = y[:, jglob * T:(jglob + 1) * T]
            v.tensor_tensor(ctx.t["ztmp2"][:, 0:1024], gs, ctx.t["rstd_b"][:, 0:1024],
                            op=ALU.mult)
            v.drain()
            v.tensor_scalar(gs, ctx.t["ztmp2"][:, 0:1024],
                            ctx.t["norm_scale"][:, jglob:jglob + 1], None, op0=ALU.mult)
            dinc(v, ctx.sem("p5_fin"))
        banks = [ctx.t[f"bank{i}"] for i in range(8)]
        for q in range(4):
            v.wait_ge(ctx.sem("p5_wou"), 32 * (q + 1))
            for tt in range(4):
                k = q * 4 + tt
                sl = k % 2
                if k >= 2:
                    v.wait_ge(ctx.sem("p5_oA" if sl == 0 else "p5_oB"), 16 * (k // 2))
                for oc in range(2):
                    v.tensor_copy(ctx.t["stage2"][:, sl * 1024 + oc * 512:sl * 1024 + oc * 512 + 512],
                                  banks[tt * 2 + oc].ap())
                dinc(v, ctx.sem("p5_wst"))
            dinc(v, ctx.sem("p5_wev"))
        return None

    def sync(self, ctx, sy):
        ms = ctx.t["mamba_s"]
        for q in range(4):
            th, oh = q // 2, q % 2
            for tt in range(4):
                k = q * 4 + tt
                sl = k % 2
                sy.wait_ge(ctx.sem("p5_wst"), k + 1)
                dst = ms.ap()[th * 512 + tt * 128:th * 512 + tt * 128 + 128,
                              oh * 1024:(oh + 1) * 1024]
                sy.dma_start(out=dst, in_=ctx.t["stage2"][:, sl * 1024:sl * 1024 + 1024]) \
                    .then_inc(ctx.sem("p5_oA" if sl == 0 else "p5_oB"), 16)
        sy.wait_ge(ctx.sem("p5_oA"), 16 * 8)
        sy.wait_ge(ctx.sem("p5_oB"), 16 * 8)
        return None


class P6FFN0:

    def __init__(self, xn_name="xn", xn_off=HALO, out_name="ffn0_s",
                 wg="WgT_d", wu="WuT_d", wd="WdT_d", pfx="p6",
                 ne=FF0, gate_scale=None, up_scale=None, down_scale=None):
        self.xn_name, self.xn_off, self.out_name = xn_name, xn_off, out_name
        self.wg, self.wu, self.wd, self.pfx = wg, wu, wd, pfx
        self.ne = ne
        self.n_ch = ne // 512
        self.n_et = ne // 128
        self.gate_scale = gate_scale
        self.up_scale = up_scale
        self.down_scale = down_scale

    def gpsimd(self, ctx, g):
        x = self.pfx
        for tc in range(2):
            for ci in range(self.n_ch):
                qi = tc * self.n_ch + ci
                if qi >= 2:
                    g.wait_ge(ctx.sem(x + "_mm"), 4 * (qi - 1))
                sg = ctx.sem(x + ("_wgA" if qi % 2 == 0 else "_wgB"))
                su = ctx.sem(x + ("_wuA" if qi % 2 == 0 else "_wuB"))
                slg = ctx.t["slabA"] if qi % 2 == 0 else ctx.t["slabB"]
                slu = ctx.t["slabC2"] if qi % 2 == 0 else ctx.t["slabD2"]
                for wname, sl, sem in ((self.wg, slg, sg), (self.wu, slu, su)):
                    src = ctx.t[wname].ap()[:, ci * 512:(ci + 1) * 512] \
                        .rearrange("(k p) e -> p k e", p=128)
                    dst = sl[:, 0:8192].rearrange("p (k e) -> p k e", k=16)
                    g.dma_start(out=dst, in_=src).then_inc(sem, 16)
            g.wait_ge(ctx.sem(x + "_mm"), 4 * (tc + 1) * self.n_ch)
            di = 0
            for dh in range(2):
                for et in range(self.n_et):
                    qd = tc * 2 * self.n_et + dh * self.n_et + et
                    if di >= 2:
                        g.wait_ge(ctx.sem(x + "_du"), qd - 1)
                    sl = ctx.t["slabC"] if qd % 2 == 0 else ctx.t["slabD"]
                    sem = ctx.sem(x + ("_wdA" if qd % 2 == 0 else "_wdB"))
                    src = ctx.t[self.wd].ap()[et * 128:(et + 1) * 128,
                                              dh * 1024:(dh + 1) * 1024]
                    g.dma_start(out=sl[:, 0:1024], in_=src).then_inc(sem, 16)
                    di += 1
            if tc == 0:
                g.wait_ge(ctx.sem(x + "_du"), 2 * self.n_et)
        return None

    def tensor(self, ctx, t):
        x = self.pfx
        xn = ctx.t[self.xn_name]
        xw = TF if self.xn_name == "xn" else T
        banks = [ctx.t[f"bank{i}"] for i in range(8)]
        h1 = ctx.t["h1"]
        for tc in range(2):
            if tc > 0:
                t.wait_ge(ctx.sem(x + "_wev"), 2 * tc)
            for ci in range(self.n_ch):
                qi = tc * self.n_ch + ci
                t.wait_ge(ctx.sem(x + ("_wgA" if qi % 2 == 0 else "_wgB")),
                          16 * (qi // 2 + 1))
                t.wait_ge(ctx.sem(x + ("_wuA" if qi % 2 == 0 else "_wuB")),
                          16 * (qi // 2 + 1))
                slg = ctx.t["slabA"] if qi % 2 == 0 else ctx.t["slabB"]
                slu = ctx.t["slabC2"] if qi % 2 == 0 else ctx.t["slabD2"]
                for j in range(4):
                    gi = qi * 4 + j
                    pg = banks[gi % 2].ap()
                    pu = banks[2 + gi % 2].ap()
                    if gi >= 2:
                        t.wait_ge(ctx.sem(x + "_h1"), gi - 1)
                    for k in range(16):
                        t.matmul(pg, slg[:, k * 512 + j * 128:k * 512 + j * 128 + 128],
                                 xn[:, k * xw + self.xn_off + tc * 512:k * xw + self.xn_off + tc * 512 + 512],
                                 start=(k == 0), stop=(k == 15))
                        mm = t.matmul(pu, slu[:, k * 512 + j * 128:k * 512 + j * 128 + 128],
                                      xn[:, k * xw + self.xn_off + tc * 512:k * xw + self.xn_off + tc * 512 + 512],
                                      start=(k == 0), stop=(k == 15))
                    mm.then_inc(ctx.sem(x + "_mm"), 1)
            t.wait_ge(ctx.sem(x + "_h1"), (tc + 1) * self.n_ch * 4)
            for dh in range(2):
                dq = tc * 2 + dh
                if dq >= 1:
                    t.wait_ge(ctx.sem(x + "_wev"), dq)
                for et in range(self.n_et):
                    qd = tc * 2 * self.n_et + dh * self.n_et + et
                    t.wait_ge(ctx.sem(x + ("_wdA" if qd % 2 == 0 else "_wdB")),
                              16 * (qd // 2 + 1))
                    sl = ctx.t["slabC"] if qd % 2 == 0 else ctx.t["slabD"]
                    for tt in range(4):
                        for dc in range(2):
                            mm = t.matmul(banks[tt * 2 + dc].ap(),
                                          h1[:, et * 512 + tt * 128:et * 512 + tt * 128 + 128],
                                          sl[:, dc * 512:dc * 512 + 512],
                                          start=(et == 0), stop=(et == self.n_et - 1))
                    mm.then_inc(ctx.sem(x + "_du"), 1)
        return None

    def scalar(self, ctx, s):
        x = self.pfx
        banks = [ctx.t[f"bank{i}"] for i in range(8)]
        for tc in range(2):
            for gl in range(self.n_ch * 4):
                gi = tc * self.n_ch * 4 + gl
                s.wait_ge(ctx.sem(x + "_mm"), gi + 1)
                if gi >= 2:
                    s.wait_ge(ctx.sem(x + "_zt"), gi - 1)
                if self.gate_scale:
                    et_local = gi % self.n_et
                    s.activation(ctx.t["ztmp"][:, (gi % 2) * 512:(gi % 2) * 512 + 512],
                                 banks[gi % 2].ap(), AF.Silu,
                                 scale=ctx.t[self.gate_scale][:, et_local:et_local + 1])
                else:
                    s.activation(ctx.t["ztmp"][:, (gi % 2) * 512:(gi % 2) * 512 + 512],
                                 banks[gi % 2].ap(), AF.Silu)
                dinc(s, ctx.sem(x + "_silu"))
        return None

    def vector(self, ctx, v):
        x = self.pfx
        banks = [ctx.t[f"bank{i}"] for i in range(8)]
        h1 = ctx.t["h1"]
        for tc in range(2):
            for gl in range(self.n_ch * 4):
                gi = tc * self.n_ch * 4 + gl
                et_local = gi % (self.n_et)
                v.wait_ge(ctx.sem(x + "_silu"), gi + 1)
                dst = h1[:, et_local * 512:(et_local + 1) * 512]
                if self.up_scale:
                    v.scalar_tensor_tensor(dst, banks[2 + gi % 2].ap(),
                                           ctx.t[self.up_scale][:, et_local:et_local + 1],
                                           ctx.t["ztmp"][:, (gi % 2) * 512:(gi % 2) * 512 + 512],
                                           op0=ALU.mult, op1=ALU.mult)
                else:
                    v.tensor_tensor(dst, banks[2 + gi % 2].ap(),
                                    ctx.t["ztmp"][:, (gi % 2) * 512:(gi % 2) * 512 + 512],
                                    op=ALU.mult)
                dinc(v, ctx.sem(x + "_zt"))
                dinc(v, ctx.sem(x + "_h1"))
            for dh in range(2):
                dq = tc * 2 + dh
                v.wait_ge(ctx.sem(x + "_du"), self.n_et * (dq + 1))
                for tt in range(4):
                    k = dq * 4 + tt
                    sl = k % 2
                    if k >= 2:
                        v.wait_ge(ctx.sem(x + ("_oA" if sl == 0 else "_oB")), 16 * (k // 2))
                    for dc in range(2):
                        v.tensor_copy(ctx.t["stage2"][:, sl * 1024 + dc * 512:sl * 1024 + dc * 512 + 512],
                                      banks[tt * 2 + dc].ap())
                    dinc(v, ctx.sem(x + "_wst"))
                dinc(v, ctx.sem(x + "_wev"))
        return None

    def sync(self, ctx, sy):
        x = self.pfx
        out = ctx.t[self.out_name]
        for dq in range(4):
            tc, dh = dq // 2, dq % 2
            for tt in range(4):
                k = dq * 4 + tt
                sl = k % 2
                sy.wait_ge(ctx.sem(x + "_wst"), k + 1)
                dst = out.ap()[tc * 512 + tt * 128:tc * 512 + tt * 128 + 128,
                               dh * 1024:(dh + 1) * 1024]
                sy.dma_start(out=dst, in_=ctx.t["stage2"][:, sl * 1024:sl * 1024 + 1024]) \
                    .then_inc(ctx.sem(x + ("_oA" if sl == 0 else "_oB")), 16)
        sy.wait_ge(ctx.sem(x + "_oA"), 16 * 8)
        sy.wait_ge(ctx.sem(x + "_oB"), 16 * 8)
        return None


class P7Combine:
    def sync(self, ctx, sy):
        u = ctx.t["u_dram"]
        for tt in range(8):
            sl = tt % 2
            if tt >= 2:
                sy.wait_ge(ctx.sem("p7_used"), tt - 1)
            sy.dma_start(out=ctx.t["uR"][:, sl * D:(sl + 1) * D],
                         in_=u.ap()[HALO + tt * 128:HALO + tt * 128 + 128, :]) \
                .then_inc(ctx.sem("p7_uA" if sl == 0 else "p7_uB"), 16)
            sy.dma_start(out=ctx.t["mR"][:, sl * D:(sl + 1) * D],
                         in_=ctx.t["mamba_s"].ap()[tt * 128:(tt + 1) * 128, :]) \
                .then_inc(ctx.sem("p7_mA" if sl == 0 else "p7_mB"), 16)
            sy.dma_start(out=ctx.t["fR"][:, sl * D:(sl + 1) * D],
                         in_=ctx.t["ffn0_s"].ap()[tt * 128:(tt + 1) * 128, :]) \
                .then_inc(ctx.sem("p7_fA" if sl == 0 else "p7_fB"), 16)
            sy.wait_ge(ctx.sem("p7_x1"), tt + 1)
            sy.dma_start(out=ctx.t["x1_out"].ap()[tt * 128:(tt + 1) * 128, :],
                         in_=ctx.t["x1R"][:, sl * D:(sl + 1) * D]) \
                .then_inc(ctx.sem("p7_oA" if sl == 0 else "p7_oB"), 16)
        sy.wait_ge(ctx.sem("p7_oA"), 16 * 4)
        sy.wait_ge(ctx.sem("p7_oB"), 16 * 4)
        return None

    def vector(self, ctx, v):
        for tt in range(8):
            sl = tt % 2
            v.wait_ge(ctx.sem("p7_uA" if sl == 0 else "p7_uB"), 16 * (tt // 2 + 1))
            v.wait_ge(ctx.sem("p7_mA" if sl == 0 else "p7_mB"), 16 * (tt // 2 + 1))
            v.wait_ge(ctx.sem("p7_fA" if sl == 0 else "p7_fB"), 16 * (tt // 2 + 1))
            if tt >= 2:
                v.wait_ge(ctx.sem("p7_oA" if sl == 0 else "p7_oB"), 16 * (tt // 2))
            x1 = ctx.t["x1R"][:, sl * D:(sl + 1) * D]
            x2 = ctx.t["x2R"][:, sl * D:(sl + 1) * D]
            v.tensor_tensor(x2, ctx.t["uR"][:, sl * D:(sl + 1) * D],
                            ctx.t["mR"][:, sl * D:(sl + 1) * D], op=ALU.add)
            v.drain()
            v.tensor_tensor(x1, x2, ctx.t["fR"][:, sl * D:(sl + 1) * D], op=ALU.add)
            dinc(v, ctx.sem("p7_x1"))
            dinc(v, ctx.sem("p7_used"))
        return None


NQH = 16
NKH = 4
TT2 = 2048


class Q1LN:

    def sync(self, ctx, sy):
        x1 = ctx.t["x1_in"]
        s_u = ctx.sem("q1_ld")
        src = x1.ap().rearrange("(tt p) d -> p tt d", p=128)
        dst = ctx.t["u_tok8"].rearrange("p (tt d) -> p tt d", tt=8)
        sy.dma_start(out=dst, in_=src).then_inc(s_u, 16)
        return None

    def vector(self, ctx, v):
        u_tok = ctx.t["u_tok8"]
        xn_tok = ctx.t["xn1_tok"]
        aggr = ctx.t["ln_aggr"]
        rows = ctx.t["ln_rows"]
        sq = ctx.t["ln_sq"]
        v.wait_ge(ctx.sem("q1_ld"), 16)
        for tt in range(8):
            ut = u_tok[:, tt * D:(tt + 1) * D]
            a = aggr[:, 2 * tt:2 * tt + 2]
            r = rows[:, 4 * tt:4 * tt + 4]
            v.reduce_sum(r[:, 2:3], ut, axis=AX.X)
            v.wait_ge(ctx.sem("q1_sq"), tt + 1)
            v.reduce_sum(r[:, 3:4], sq[:, :], axis=AX.X)
            dinc(v, ctx.sem("q1_sqfree"))
            v.drain()
            v.tensor_scalar(a[:, 0:1], r[:, 2:3], 1.0 / D, None, op0=ALU.mult)
            v.tensor_scalar(r[:, 3:4], r[:, 3:4], 1.0 / D, None, op0=ALU.mult)
            v.drain()
            v.tensor_tensor(a[:, 1:2], a[:, 0:1], a[:, 0:1], op=ALU.mult)
            v.drain()
            v.tensor_tensor(a[:, 1:2], r[:, 3:4], a[:, 1:2], op=ALU.subtract)
            dinc(v, ctx.sem("q1_var"))
        tmp2 = ctx.t["ln_tmp2b"]
        for tt in range(8):
            ut = u_tok[:, tt * D:(tt + 1) * D]
            a = aggr[:, 2 * tt:2 * tt + 2]
            r = rows[:, 4 * tt:4 * tt + 4]
            v.wait_ge(ctx.sem("q1_rstd"), tt + 1)
            v.tensor_scalar(tmp2[:, :], ut, a[:, 0:1], r[:, 0:1],
                            op0=ALU.subtract, op1=ALU.mult)
            v.drain()
            v.tensor_copy(xn_tok[:, tt * D:(tt + 1) * D], tmp2[:, :])
            dinc(v, ctx.sem("q1_xn"))
        return None

    def scalar(self, ctx, s):
        u_tok = ctx.t["u_tok8"]
        sq = ctx.t["ln_sq"]
        aggr = ctx.t["ln_aggr"]
        rows = ctx.t["ln_rows"]
        s.wait_ge(ctx.sem("q1_ld"), 16)
        for tt in range(8):
            if tt > 0:
                s.wait_ge(ctx.sem("q1_sqfree"), tt)
            s.activation(sq[:, :], u_tok[:, tt * D:(tt + 1) * D], AF.Square)
            dinc(s, ctx.sem("q1_sq"))
        for tt in range(8):
            a = aggr[:, 2 * tt:2 * tt + 2]
            r = rows[:, 4 * tt:4 * tt + 4]
            s.wait_ge(ctx.sem("q1_var"), tt + 1)
            s.activation(r[:, 1:2], a[:, 1:2], AF.Ln, bias=ctx.constf(EPS))
            s.drain()
            s.activation(r[:, 0:1], r[:, 1:2], AF.Exp, scale=-0.5)
            dinc(s, ctx.sem("q1_rstd"))
        return None

    def gpsimd(self, ctx, g):
        g.wait_ge(ctx.sem("q1_xn"), 8)
        src = ctx.t["xn1_tok"].rearrange("p (tt d) -> p tt d", tt=8)
        dst = ctx.t["xn1b"].ap().rearrange("(tt p) d -> p tt d", p=128)
        g.dma_start(out=dst, in_=src).then_inc(ctx.sem("q1_bd"), 16)
        g.wait_ge(ctx.sem("q1_bd"), 16)
        g.collective_compute("AllGather", ALU.bypass, replica_groups=PAIRS,
                             ins=[ctx.t["xn1b"].ap().opt()],
                             outs=[ctx.t["xn1g"].ap().opt()]).then_inc(ctx.sem("q1_cc"), 1)
        g.wait_ge(ctx.sem("q1_cc"), 1)
        return None


class Q2QKV:

    def sync(self, ctx, sy):
        xg = ctx.t["xn1g"]
        for k in range(16):
            sy.dma_start(out=ctx.t["xnF"][:, k * TT2:(k + 1) * TT2],
                         in_=xg.ap()[:, k * 128:(k + 1) * 128],
                         transpose=True).then_inc(ctx.sem("q2_xf"), 16)
        return None

    def gpsimd(self, ctx, g):
        s_w = ctx.sem("q2_w")
        offs = [0, 8192, 16384, 20480]
        for i, (wname, w) in enumerate([("WqT_d", 512), ("WqT2", 512),
                                        ("WkT_d", 256), ("WvT_d", 256)]):
            src_t = ctx.t["WqT_d"] if wname in ("WqT_d", "WqT2") else ctx.t[wname]
            off = 512 if wname == "WqT2" else 0
            src = src_t.ap()[:, off:off + w].rearrange("(k p) e -> p k e", p=128)
            dst = ctx.t["qkvslab"][:, offs[i]:offs[i] + 16 * w] \
                .rearrange("p (k e) -> p k e", k=16)
            g.dma_start(out=dst, in_=src).then_inc(s_w, 16)
        return None

    def tensor(self, ctx, t):
        xnF = ctx.t["xnF"]
        t.wait_ge(ctx.sem("q2_xf"), 16 * 16)
        t.wait_ge(ctx.sem("q2_w"), 16 * 4)
        mmidx = 0
        offs = [0, 8192, 16384, 20480]
        for ci, w in [(0, 512), (1, 512), (2, 256), (3, 256)]:
            for tt in range(16):
                p = ctx.t["pA"] if mmidx % 2 == 0 else ctx.t["pB"]
                if mmidx >= 2:
                    t.wait_ge(ctx.sem("q2_ev"), mmidx - 1)
                for k in range(16):
                    mm = t.matmul(p[:, 0:w],
                                  xnF[:, k * TT2 + tt * 128:k * TT2 + tt * 128 + 128],
                                  ctx.t["qkvslab"][:, offs[ci] + k * w:offs[ci] + k * w + w],
                                  start=(k == 0), stop=(k == 15))
                mm.then_inc(ctx.sem("q2_mm"), 1)
                mmidx += 1
        t.wait_ge(ctx.sem("q2_rope"), 48)
        kb = 0
        for srcn, dstn, ntiles in (("Qtok", "Qf", 8), ("Ktok", "Kf", 2)):
            src = ctx.t[srcn]
            nw = ntiles * 128
            for tt in range(16):
                for j in range(ntiles):
                    pt = ctx.t["pt0"] if kb % 2 == 0 else ctx.t["pt1"]
                    if kb >= 2:
                        t.wait_ge(ctx.sem("q2_tev"), kb - 1)
                    t.transpose(pt[:, :], src[:, tt * nw + j * 128:tt * nw + j * 128 + 128],
                                ctx.t["ident_bf"][:, :]).then_inc(ctx.sem("q2_tp"), 1)
                    kb += 1
        return None

    def scalar(self, ctx, s):
        return None

    def vector(self, ctx, v):
        ce8, co8, se8, so8 = (ctx.t["ce8"], ctx.t["co8"], ctx.t["se8"], ctx.t["so8"])
        rt1, rt2 = ctx.t["ropetmp"][:, 0:256], ctx.t["ropetmp"][:, 256:512]
        for mmidx in range(64):
            p = ctx.t["pA"] if mmidx % 2 == 0 else ctx.t["pB"]
            v.wait_ge(ctx.sem("q2_mm"), mmidx + 1)
            if mmidx < 32:
                ci, tt = mmidx // 16, mmidx % 16
                w = 512
                dst = ctx.t["Qtok"][:, tt * 1024 + ci * 512:tt * 1024 + ci * 512 + 512]
            elif mmidx < 48:
                tt = mmidx % 16
                w = 256
                dst = ctx.t["Ktok"][:, tt * 256:tt * 256 + 256]
            else:
                tt = mmidx % 16
                w = 256
                dst = None
            if dst is not None:
                nh = w // 64
                tb = ce8[:, tt * 256:tt * 256 + 32 * nh]
                tbo = co8[:, tt * 256:tt * 256 + 32 * nh]
                tbs = se8[:, tt * 256:tt * 256 + 32 * nh]
                tbso = so8[:, tt * 256:tt * 256 + 32 * nh]
                pe = p[:, 0:w - 1:2]
                po = p[:, 1:w:2]
                v.tensor_tensor(rt1[:, 0:w // 2], pe, tb, op=ALU.mult)
                v.tensor_tensor(rt2[:, 0:w // 2], po, tbs, op=ALU.mult)
                v.drain()
                v.tensor_tensor(dst[:, 0:w - 1:2], rt1[:, 0:w // 2], rt2[:, 0:w // 2],
                                op=ALU.subtract)
                v.drain()
                v.tensor_tensor(rt1[:, 0:w // 2], po, tbo, op=ALU.mult)
                v.tensor_tensor(rt2[:, 0:w // 2], pe, tbso, op=ALU.mult)
                v.drain()
                v.tensor_tensor(dst[:, 1:w:2], rt1[:, 0:w // 2], rt2[:, 0:w // 2],
                                op=ALU.add)
                dinc(v, ctx.sem("q2_rope"))
            else:
                for h in range(4):
                    v.tensor_copy(ctx.t["Vtok"][:, tt * 260 + h * 65:tt * 260 + h * 65 + 64],
                                  p[:, h * 64:h * 64 + 64])
                dinc(v, ctx.sem("q2_rope"))
            dinc(v, ctx.sem("q2_ev"))
        for tt in range(16):
            for h in range(4):
                v.memset(ctx.t["Vtok"][:, tt * 260 + h * 65 + 64:tt * 260 + h * 65 + 65], 1.0)
        dinc(v, ctx.sem("q2_vones"))
        kb = 0
        for srcn, dstn, ntiles in (("Qtok", "Qf", 8), ("Ktok", "Kf", 2)):
            for tt in range(16):
                for j in range(ntiles):
                    pt = ctx.t["pt0"] if kb % 2 == 0 else ctx.t["pt1"]
                    v.wait_ge(ctx.sem("q2_tp"), kb + 1)
                    dst = ctx.t[dstn]
                    v.tensor_copy(dst[0:64, (2 * j) * TT2 + tt * 128:(2 * j) * TT2 + tt * 128 + 128],
                                  pt[0:64, :])
                    v.tensor_copy(dst[0:64, (2 * j + 1) * TT2 + tt * 128:(2 * j + 1) * TT2 + tt * 128 + 128],
                                  pt[64:128, :])
                    dinc(v, ctx.sem("q2_tev"))
                    kb += 1
        return None


class Q3Attn:

    def groups(self):
        out = []
        n = 0
        for qh in range(NQH):
            for qc in range(4):
                out.append((qh, qc, 4 * qc + 4, n))
                n += 4 * qc + 4
        return out

    def tensor(self, ctx, t):
        Qf, Kf, Vt = ctx.t["Qf"], ctx.t["Kf"], ctx.t["Vtok"]
        ones = ctx.t["ones_bf"]
        expS = ctx.t["expS"]
        for gi, (qh, qc, n_kt, n0) in enumerate(self.groups()):
            kvh = qh // 4
            qsl = Qf[0:64, qh * TT2 + qc * 512:qh * TT2 + qc * 512 + 512]
            for kt in range(n_kt):
                n = n0 + kt
                ps = ctx.t["pA"] if n % 2 == 0 else ctx.t["pB"]
                if n >= 2:
                    t.wait_ge(ctx.sem("q3_sev"), n - 1)
                ksl = Kf[0:64, kvh * TT2 + kt * 128:kvh * TT2 + kt * 128 + 128]
                t.matmul(ps, ksl, qsl, start=True, stop=True).then_inc(ctx.sem("q3_smm"), 1)
            pav = (ctx.t["bank4"] if gi % 2 == 0 else ctx.t["bank5"]).ap()[0:65, 0:512]
            if gi >= 2:
                t.wait_ge(ctx.sem("q3_aev"), gi - 1)
            ndiag = 0
            for kt in range(n_kt):
                t.wait_ge(ctx.sem("q3_sev"), n0 + kt + 1)
                if kt >= 4 * qc:
                    t.wait_ge(ctx.sem("q3_msk"), gi * 4 + (kt - 4 * qc) + 1)
                mm = t.matmul(pav, Vt[:, kt * 260 + (qh // 4) * 65:kt * 260 + (qh // 4) * 65 + 65],
                              expS[:, kt * 512:kt * 512 + 512],
                              start=(kt == 0), stop=(kt == n_kt - 1))
            mm.then_inc(ctx.sem("q3_avmm"), 1)
            t.wait_ge(ctx.sem("q3_den"), gi + 1)
            prb = (ctx.t["bank6"] if gi % 2 == 0 else ctx.t["bank7"]).ap()[0:64, 0:512]
            if gi >= 2:
                t.wait_ge(ctx.sem("q3_rbe"), gi - 1)
            t.matmul(prb, ones[0:1, 0:64], ctx.t["rrow1"][0:1, (gi % 2) * 512:(gi % 2) * 512 + 512],
                     start=True, stop=True).then_inc(ctx.sem("q3_rb"), 1)
        return None

    def scalar(self, ctx, s):
        expS = ctx.t["expS"]
        nbias = ctx.constf(-30.0)
        for gi, (qh, qc, n_kt, n0) in enumerate(self.groups()):
            for kt in range(n_kt):
                n = n0 + kt
                ps = ctx.t["pA"] if n % 2 == 0 else ctx.t["pB"]
                if gi >= 1:
                    s.wait_ge(ctx.sem("q3_avmm"), gi)
                s.wait_ge(ctx.sem("q3_smm"), n + 1)
                if kt >= 4 * qc:
                    m = kt - 4 * qc
                    if gi * 4 + m >= 2:
                        s.wait_ge(ctx.sem("q3_dtf"), gi * 4 + m - 1)
                    s.activation(ctx.t["dtmp"][:, (m % 2) * 512:(m % 2) * 512 + 512],
                                 ps, AF.Exp, scale=0.125, bias=nbias)
                    dinc(s, ctx.sem("q3_dex"))
                else:
                    s.activation(expS[:, kt * 512:kt * 512 + 512], ps, AF.Exp,
                                 scale=0.125, bias=nbias)
                    dinc(s, ctx.sem("q3_sev"))
            if gi >= 2:
                s.wait_ge(ctx.sem("q3_rb"), gi - 1)
            s.wait_ge(ctx.sem("q3_avmm"), gi + 1)
            pav = (ctx.t["bank4"] if gi % 2 == 0 else ctx.t["bank5"]).ap()
            s.activation(ctx.t["drow"][0:1, 0:512], pav[64:65, 0:512], AF.Ln)
            s.drain()
            s.activation(ctx.t["rrow1"][0:1, (gi % 2) * 512:(gi % 2) * 512 + 512],
                         ctx.t["drow"][0:1, 0:512], AF.Exp, scale=-1.0)
            dinc(s, ctx.sem("q3_den"))
        return None

    def vector(self, ctx, v):
        expS = ctx.t["expS"]
        attnO = ctx.t["attnO"]
        for gi, (qh, qc, n_kt, n0) in enumerate(self.groups()):
            for m in range(4):
                kt = 4 * qc + m
                v.wait_ge(ctx.sem("q3_dex"), gi * 4 + m + 1)
                v.tensor_tensor(expS[:, kt * 512:kt * 512 + 512],
                                ctx.t["dtmp"][:, (m % 2) * 512:(m % 2) * 512 + 512],
                                ctx.t["dmask"][:, m * 512:m * 512 + 512], op=ALU.mult)
                dinc(v, ctx.sem("q3_dtf"))
                dinc(v, ctx.sem("q3_sev"))
                dinc(v, ctx.sem("q3_msk"))
            v.wait_ge(ctx.sem("q3_rb"), gi + 1)
            prb = (ctx.t["bank6"] if gi % 2 == 0 else ctx.t["bank7"]).ap()[0:64, 0:512]
            v.tensor_copy(ctx.t["rBs"][0:64, 0:512], prb)
            dinc(v, ctx.sem("q3_rbe"))
            pav = (ctx.t["bank4"] if gi % 2 == 0 else ctx.t["bank5"]).ap()[0:64, 0:512]
            v.tensor_tensor(attnO[(qh % 2) * 64:(qh % 2) * 64 + 64,
                                  (qh // 2) * TT2 + qc * 512:(qh // 2) * TT2 + qc * 512 + 512],
                            pav, ctx.t["rBs"][0:64, 0:512], op=ALU.mult)
            dinc(v, ctx.sem("q3_aev"))
        return None


class Q4Exchange:
    def gpsimd(self, ctx, g):
        src = ctx.t["attnO"].rearrange("p (j t) -> p j t", j=8)
        dst = ctx.t["aOb"].ap().rearrange("(j p) t -> p j t", p=128)
        g.dma_start(out=dst, in_=src).then_inc(ctx.sem("q4_d"), 16)
        g.wait_ge(ctx.sem("q4_d"), 16)
        g.collective_compute("AllGather", ALU.bypass, replica_groups=PAIRS,
                             ins=[ctx.t["aOb"].ap().opt()],
                             outs=[ctx.t["aOg"].ap().opt()]).then_inc(ctx.sem("q4_cc"), 1)
        g.wait_ge(ctx.sem("q4_cc"), 1)
        for arm in g.switch_core_id(n=128):
            t0 = 0 if int(arm) % 2 == 0 else 1024
            src2 = ctx.t["aOg"].ap()[:, t0:t0 + T].rearrange("(j p) t -> p j t", p=128)
            dst2 = ctx.t["attnG"].rearrange("p (j t) -> p j t", j=16)
            g.dma_start(out=dst2, in_=src2).then_inc(ctx.sem("q4_d2"), 16)
        g.wait_ge(ctx.sem("q4_d2"), 16)
        return None


class Q5Wo:

    def sync(self, ctx, sy):
        for arm in sy.switch_core_id(n=128):
            t0 = 0 if int(arm) % 2 == 0 else 1024
            for k in range(16):
                sy.dma_start(out=ctx.t["xn1own"][:, k * T:(k + 1) * T],
                             in_=ctx.t["xn1g"].ap()[t0:t0 + T, k * 128:(k + 1) * 128],
                             transpose=True).then_inc(ctx.sem("q5_xo"), 16)
        sy.wait_ge(ctx.sem("q5_xo"), 16 * 16)
        return None

    def gpsimd(self, ctx, g):
        wo = ctx.t["WoT_d"]
        qi = 0
        for th in range(2):
            for oh in range(2):
                for et in range(16):
                    if qi >= 2:
                        g.wait_ge(ctx.sem("q5_wu"), qi - 1)
                    sl = ctx.t["woslabA"] if qi % 2 == 0 else ctx.t["woslabB"]
                    sem = ctx.sem("q5_wA" if qi % 2 == 0 else "q5_wB")
                    src = wo.ap()[et * 128:(et + 1) * 128, oh * 1024:(oh + 1) * 1024]
                    g.dma_start(out=sl[:, 0:1024], in_=src).then_inc(sem, 16)
                    qi += 1
        return None

    def tensor(self, ctx, t):
        banks = [ctx.t[f"bank{i}"] for i in range(8)]
        aG = ctx.t["attnG"]
        qi = 0
        for th in range(2):
            for oh in range(2):
                q = th * 2 + oh
                if q >= 1:
                    t.wait_ge(ctx.sem("q5_wev"), q)
                for et in range(16):
                    t.wait_ge(ctx.sem("q5_wA" if qi % 2 == 0 else "q5_wB"),
                              16 * (qi // 2 + 1))
                    sl = ctx.t["woslabA"] if qi % 2 == 0 else ctx.t["woslabB"]
                    for tt in range(4):
                        for oc in range(2):
                            mm = t.matmul(banks[tt * 2 + oc].ap(),
                                          aG[:, et * T + th * 512 + tt * 128:et * T + th * 512 + tt * 128 + 128],
                                          sl[:, oc * 512:oc * 512 + 512],
                                          start=(et == 0), stop=(et == 15))
                    mm.then_inc(ctx.sem("q5_wu"), 1)
                    qi += 1
        return None

    def vector(self, ctx, v):
        banks = [ctx.t[f"bank{i}"] for i in range(8)]
        for q in range(4):
            v.wait_ge(ctx.sem("q5_wu"), 16 * (q + 1))
            for tt in range(4):
                k = q * 4 + tt
                sl = k % 2
                if k >= 2:
                    v.wait_ge(ctx.sem("q5_oA" if sl == 0 else "q5_oB"), 16 * (k // 2))
                for oc in range(2):
                    v.tensor_copy(ctx.t["stage2"][:, sl * 1024 + oc * 512:sl * 1024 + oc * 512 + 512],
                                  banks[tt * 2 + oc].ap())
                dinc(v, ctx.sem("q5_wst"))
            dinc(v, ctx.sem("q5_wev"))
        return None

    def scalar(self, ctx, s):
        out = ctx.t["attn_s"]
        for q in range(4):
            th, oh = q // 2, q % 2
            for tt in range(4):
                k = q * 4 + tt
                sl = k % 2
                s.wait_ge(ctx.sem("q5_wst"), k + 1)
                dst = out.ap()[th * 512 + tt * 128:th * 512 + tt * 128 + 128,
                               oh * 1024:(oh + 1) * 1024]
                s.dma_start(out=dst, in_=ctx.t["stage2"][:, sl * 1024:sl * 1024 + 1024]) \
                    .then_inc(ctx.sem("q5_oA" if sl == 0 else "q5_oB"), 16)
        s.wait_ge(ctx.sem("q5_oA"), 16 * 8)
        s.wait_ge(ctx.sem("q5_oB"), 16 * 8)
        return None


class Q7Final:

    def tensor(self, ctx, t):
        for c in range(4):
            pb = (ctx.t["bank0"] if c % 2 == 0 else ctx.t["bank1"]).ap()
            if c >= 2:
                t.wait_ge(ctx.sem("q7_dse"), c - 1)
            t.matmul(pb, ctx.t["ones_bf"][0:1, 0:128],
                     ctx.t["dsrow"][0:1, c * 512:(c + 1) * 512],
                     start=True, stop=True).then_inc(ctx.sem("q7_dsb"), 1)
        return None

    def sync(self, ctx, sy):
        x1 = ctx.t["x1_in"]
        for tt in range(8):
            sl = tt % 2
            if tt >= 2:
                sy.wait_ge(ctx.sem("q7_used"), tt - 1)
            sy.dma_start(out=ctx.t["uR"][:, sl * D:(sl + 1) * D],
                         in_=x1.ap()[tt * 128:(tt + 1) * 128, :]) \
                .then_inc(ctx.sem("q7_uA" if sl == 0 else "q7_uB"), 16)
            sy.dma_start(out=ctx.t["mR"][:, sl * D:(sl + 1) * D],
                         in_=ctx.t["attn_s"].ap()[tt * 128:(tt + 1) * 128, :]) \
                .then_inc(ctx.sem("q7_mA" if sl == 0 else "q7_mB"), 16)
            sy.dma_start(out=ctx.t["fR"][:, sl * D:(sl + 1) * D],
                         in_=ctx.t["ffn1_s"].ap()[tt * 128:(tt + 1) * 128, :]) \
                .then_inc(ctx.sem("q7_fA" if sl == 0 else "q7_fB"), 16)
            sy.wait_ge(ctx.sem("q7_x1"), tt + 1)
            sy.dma_start(out=ctx.t["out_d"].ap()[tt * 128:(tt + 1) * 128, :],
                         in_=ctx.t["x1R"][:, sl * D:(sl + 1) * D]) \
                .then_inc(ctx.sem("q7_oA" if sl == 0 else "q7_oB"), 16)
        sy.wait_ge(ctx.sem("q7_oA"), 16 * 4)
        sy.wait_ge(ctx.sem("q7_oB"), 16 * 4)
        return None

    def vector(self, ctx, v):
        for c in range(4):
            v.wait_ge(ctx.sem("q7_dsb"), c + 1)
            pb = (ctx.t["bank0"] if c % 2 == 0 else ctx.t["bank1"]).ap()
            v.tensor_copy(ctx.t["dsB"][:, c * 512:(c + 1) * 512], pb[:, 0:512])
            dinc(v, ctx.sem("q7_dse"))
        for tt in range(8):
            sl = tt % 2
            v.wait_ge(ctx.sem("q7_uA" if sl == 0 else "q7_uB"), 16 * (tt // 2 + 1))
            v.wait_ge(ctx.sem("q7_mA" if sl == 0 else "q7_mB"), 16 * (tt // 2 + 1))
            v.wait_ge(ctx.sem("q7_fA" if sl == 0 else "q7_fB"), 16 * (tt // 2 + 1))
            if tt >= 2:
                v.wait_ge(ctx.sem("q7_oA" if sl == 0 else "q7_oB"), 16 * (tt // 2))
            x1 = ctx.t["x1R"][:, sl * D:(sl + 1) * D]
            x2 = ctx.t["x2R"][:, sl * D:(sl + 1) * D]
            v.tensor_tensor(x1, ctx.t["fR"][:, sl * D:(sl + 1) * D],
                            ctx.t["dsB"][:, 0:2048], op=ALU.mult)
            v.drain()
            v.tensor_tensor(x2, x1, ctx.t["mR"][:, sl * D:(sl + 1) * D], op=ALU.add)
            v.drain()
            v.tensor_tensor(x1, x2, ctx.t["uR"][:, sl * D:(sl + 1) * D], op=ALU.add)
            dinc(v, ctx.sem("q7_x1"))
            dinc(v, ctx.sem("q7_used"))
        return None


class Q0Consts:

    def gpsimd(self, ctx, g):
        ident = ctx.t["ident_bf"]
        g.memset(ident[:, :], 0.0)
        g.drain()
        g.affine_select(out=ident[:, :], in_=ident[:, :],
                        compare_op=ALU.not_equal, fill=1.0,
                        base=0, pattern=[[-1, 128]], channel_multiplier=1)
        for m in range(4):
            dm = ctx.t["dmask"][:, m * 512:(m + 1) * 512]
            g.memset(dm, 1.0)
            g.drain()
            g.affine_select(out=dm, in_=dm, compare_op=ALU.is_ge, fill=0.0,
                            base=-128 * m, pattern=[[1, 512]], channel_multiplier=-1)
            g.drain()
        return None

    def vector(self, ctx, v):
        last = v.memset(ctx.t["ones_bf"][:, :], 1.0)
        for val, idx in ctx.consts.items():
            last = v.memset(ctx.cpool[:, idx:idx + 1], val)
        return last

    def sync(self, ctx, sy):
        sem = ctx.sem("q0_ld")
        n = 0
        for sb_name, dram_name in [("ce8", "ce8_d"), ("co8", "co8_d"),
                                   ("se8", "se8_d"), ("so8", "so8_d"),
                                   ("gs1", "gs1_d"), ("us1", "us1_d"),
                                   ("dsrow", "ds_d")]:
            sy.dma_start(out=as_ap(ctx.t[sb_name])[:, :],
                         in_=ctx.t[dram_name].ap()).then_inc(sem, 16)
            n += 1
        sy.wait_ge(sem, 16 * n)
        return None


def alloc_l1(ctx, nc):
    arena = ctx.sb("arenaB", [128, 82048], BF16)
    ar32 = arena.bitcast(F32)
    ctx.t["u_tok8"] = ar32.ap()[:, 0:16384]
    ctx.t["xn1_tok"] = arena.ap()[:, 32768:32768 + 16384]
    ctx.t["ln_sq"] = ar32.ap()[:, 24576:24576 + 2048]
    ctx.t["ln_tmp2b"] = ar32.ap()[:, 26624:26624 + 2048]
    ctx.t["ln_aggr"] = ar32.ap()[:, 28672:28672 + 16]
    ctx.t["ln_rows"] = ar32.ap()[:, 28688:28688 + 32]
    ctx.t["xnF"] = arena.ap()[:, 0:32768]
    ctx.t["qkvslab"] = arena.ap()[:, 57408:57408 + 24576]
    ctx.t["Qtok"] = arena.ap()[:, 32768:32768 + 16384]
    ctx.t["Ktok"] = arena.ap()[:, 49152:49152 + 4096]
    ctx.t["Vtok"] = arena.ap()[:, 53248:53248 + 4160]
    pass
    ctx.t["Qf"] = arena.ap()[:, 0:32768]
    ctx.t["expS"] = arena.ap()[:, 32768:32768 + 8192]
    ctx.t["Kf"] = arena.ap()[:, 57408:57408 + 8192]
    ctx.t["attnO"] = arena.ap()[:, 65600:65600 + 16384]
    ctx.t["dtmp"] = arena.ap()[:, 40960:40960 + 1024]
    ctx.t["rBs"] = arena.ap()[:, 41984:41984 + 512]
    ctx.t["rrow1"] = arena.ap()[:, 42496:42496 + 1024]
    ctx.t["drow"] = ar32.ap()[:, 21760:21760 + 512]
    ctx.t["attnG"] = arena.ap()[:, 0:16384]
    ctx.t["woslabA"] = arena.ap()[:, 16384:16384 + 1024]
    ctx.t["woslabB"] = arena.ap()[:, 17408:17408 + 1024]
    ctx.t["stage2"] = ar32.ap()[:, 11264:11264 + 2048]
    ctx.t["xn1own"] = arena.ap()[:, 28672:28672 + 16384]
    ctx.t["h1"] = arena.ap()[:, 0:22528]
    ctx.t["slabA"] = arena.ap()[:, 45056:45056 + 8192]
    ctx.t["slabB"] = arena.ap()[:, 53248:53248 + 8192]
    ctx.t["slabC2"] = arena.ap()[:, 61440:61440 + 8192]
    ctx.t["slabD2"] = arena.ap()[:, 69632:69632 + 8192]
    ctx.t["slabC"] = arena.ap()[:, 26624:26624 + 1024]
    ctx.t["slabD"] = arena.ap()[:, 27648:27648 + 1024]
    ctx.t["ztmp"] = arena.ap()[:, 77824:77824 + 1024]
    ctx.t["uR"] = ar32.ap()[:, 0:4096]
    ctx.t["mR"] = ar32.ap()[:, 4096:8192]
    ctx.t["fR"] = ar32.ap()[:, 8192:12288]
    ctx.t["x1R"] = ar32.ap()[:, 12288:16384]
    ctx.t["x2R"] = ar32.ap()[:, 16384:20480]
    ctx.t["dsB"] = ar32.ap()[:, 20480:22528]
    ctx.sb("ident_bf", [128, 128], BF16)
    ctx.sb("ones_bf", [128, 128], BF16)
    ctx.sb("ropetmp", [128, 512], BF16)
    ctx.sb("dmask", [128, 4 * 512], BF16)
    ctx.sb("ce8", [128, 16 * 256], BF16)
    ctx.sb("co8", [128, 16 * 256], BF16)
    ctx.sb("se8", [128, 16 * 256], BF16)
    ctx.sb("so8", [128, 16 * 256], BF16)
    ctx.sb("gs1", [128, 44], F32)
    ctx.sb("us1", [128, 44], F32)
    ctx.sb("dsrow", [128, 2048], BF16)
    banks = [ctx.ps(f"bank{i}", [128, 512], F32) for i in range(8)]
    ctx.t["pA"] = banks[0].ap()
    ctx.t["pB"] = banks[1].ap()
    ctx.t["pt0"] = banks[6].bitcast(BF16).ap()[:, 0:128]
    ctx.t["pt1"] = banks[7].bitcast(BF16).ap()[:, 0:128]
    ctx.dram("x1_in", [T, D], F32, kind="ExternalInput")
    ctx.dram("WqT_d", [D, NQH * 64], F32, kind="ExternalInput")
    ctx.dram("WkT_d", [D, NKH * 64], F32, kind="ExternalInput")
    ctx.dram("WvT_d", [D, NKH * 64], F32, kind="ExternalInput")
    ctx.dram("WoT_d", [D, D], F32, kind="ExternalInput")
    ctx.dram("Wg1T_d", [D, FF1], F32, kind="ExternalInput")
    ctx.dram("Wu1T_d", [D, FF1], F32, kind="ExternalInput")
    ctx.dram("Wd1T_d", [FF1, D], F32, kind="ExternalInput")
    ctx.dram("ce8_d", [128, 16 * 256], BF16, kind="ExternalInput")
    ctx.dram("co8_d", [128, 16 * 256], BF16, kind="ExternalInput")
    ctx.dram("se8_d", [128, 16 * 256], BF16, kind="ExternalInput")
    ctx.dram("so8_d", [128, 16 * 256], BF16, kind="ExternalInput")
    ctx.dram("gs1_d", [128, 44], F32, kind="ExternalInput")
    ctx.dram("us1_d", [128, 44], F32, kind="ExternalInput")
    ctx.dram("ds_d", [128, 2048], BF16, kind="ExternalInput")
    ctx.dram("xn1b", [T, D], BF16)
    ctx.dram("xn1g", [TT2, D], BF16)
    ctx.dram("aOb", [T, TT2], BF16)
    ctx.dram("aOg", [TT2, TT2], BF16)
    ctx.dram("attn_s", [T, D], F32)
    ctx.dram("ffn1_s", [T, D], F32)
    ctx.dram("out_d", [T, D], F32, kind="ExternalOutput")


def build_l1():
    nc = bass.Bass("TRN2", target_bir_lowering=False)
    stack = contextlib.ExitStack()
    with stack:
        ctx = Ctx(nc, stack)
        alloc_l1(ctx, nc)
        ctx.constf(EPS)
        ctx.constf(-30.0)
        ffn1 = P6FFN0(xn_name="xn1own", xn_off=0, out_name="ffn1_s",
                      wg="Wg1T_d", wu="Wu1T_d", wd="Wd1T_d", pfx="q6",
                      ne=FF1, gate_scale="gs1", up_scale="us1")
        emit_phases(nc, ctx, [Q0Consts(), Q1LN(), Q2QKV(), Q3Attn(),
                              Q4Exchange(), Q5Wo(), ffn1, Q7Final()])
    return nc


def alloc_l0(ctx, nc):
    arena = ctx.sb("arena", [128, 82048], BF16)
    ar32 = arena.bitcast(F32)
    ctx.t["u_tok"] = ar32.ap()[:, 0:18432]
    ctx.t["u_bf"] = arena.ap()[:, 36896:36896 + 18432]
    ctx.t["ln_tmp"] = ar32.ap()[:, 27664:27664 + 2048]
    ctx.t["ln_tmp2"] = ar32.ap()[:, 29712:29712 + 2048]
    ctx.t["xBC"] = arena.ap()[:, 0:35 * TF]
    ctx.t["slabA"] = arena.ap()[:, 36896:36896 + 8192]
    ctx.t["slabB"] = arena.ap()[:, 45088:45088 + 8192]
    ctx.t["xdtT"] = arena.ap()[:, 36896:36896 + 32768]
    ctx.t["S_run"] = arena.ap()[:, 32800:32800 + 4096]
    ctx.t["y_feat"] = arena.ap()[:, 0:32768]
    ctx.t["h1"] = arena.ap()[:, 0:32768]
    ctx.t["conv_tmp"] = arena.ap()[:, 32896:32896 + 2048]
    ctx.t["slabC"] = arena.ap()[:, 36896:36896 + 1024]
    ctx.t["slabD"] = arena.ap()[:, 37920:37920 + 1024]
    ctx.t["stage2"] = ar32.ap()[:, 16400:16400 + 2048]
    ctx.t["slabC2"] = arena.ap()[:, 53280:53280 + 8192]
    ctx.t["slabD2"] = arena.ap()[:, 61472:61472 + 8192]
    ctx.t["uR"] = ar32.ap()[:, 0:4096]
    ctx.t["mR"] = ar32.ap()[:, 4096:8192]
    ctx.t["fR"] = ar32.ap()[:, 8192:12288]
    ctx.t["x1R"] = ar32.ap()[:, 12288:16384]
    ctx.t["x2R"] = ar32.ap()[:, 16384:20480]
    ctx.sb("norm_scale", [128, 32], F32)
    ctx.t["tmp_bf"] = arena.ap()[:, 34952:34952 + 1024]
    ctx.sb("ident_bf", [128, 128], BF16)
    ctx.sb("ident_f32", [128, 128], F32)
    ctx.sb("ones_bf", [128, 128], BF16)
    ctx.sb("gmaskA", [128, CH], BF16)
    ctx.sb("gmaskB", [128, CH], BF16)
    ctx.sb("xn", [128, 16 * TF], BF16)
    ctx.t["ln_aggr"] = ar32.ap()[:, 31760:31760 + 18]
    ctx.t["ln_rows"] = ar32.ap()[:, 31800:31800 + 36]
    dtbuf = ctx.sb("dtbuf", [128, T], F32)
    ctx.t["decay_bf"] = dtbuf.bitcast(BF16).ap()[:, 0:1024]
    ctx.t["sdo_bf"] = dtbuf.bitcast(BF16).ap()[:, 0:512]
    ctx.t["csdo"] = dtbuf.bitcast(BF16).ap()[:, 512:1024]
    ctx.t["xdtmp"] = dtbuf.bitcast(BF16).ap()[:, 1024:1280]
    ctx.t["sbf"] = dtbuf.bitcast(BF16).ap()[:, 1280:1408]
    ctx.t["stmpA"] = dtbuf.bitcast(BF16).ap()[:, 1408:1472]
    ctx.t["stmpB"] = dtbuf.bitcast(BF16).ap()[:, 1472:1536]
    acsA_h = ctx.sb("acsA", [128, T], F32)
    ctx.t["expL"] = acsA_h.bitcast(BF16).ap()[:, 0:1024]
    ctx.t["MT"] = acsA_h.bitcast(BF16).ap()[:, 1024:2048]
    acsB = ctx.sb("acsB", [128, T], F32)
    ctx.t["invdt"] = acsB.ap()[:, 0:512]
    ctx.t["convfG"] = acsB.ap()[:, 0:1024]
    ctx.t["dclamp"] = acsB.ap()[:, 0:512]
    ctx.t["acsrow_hi"] = acsB.bitcast(BF16).ap()[:, 1024:1536]
    ctx.t["acsrow_lo"] = acsB.bitcast(BF16).ap()[:, 1536:2048]
    ctx.t["rrow_bf"] = acsB.bitcast(BF16).ap()[:, 1024:2048]
    ctx.t["convfV"] = ar32.ap()[:, 39328:39328 + 1024]
    ctx.t["dtT"] = ar32.ap()[:, 40352:40352 + 512]
    ctx.t["rr_ln0"] = ctx.t["dtT"]
    ctx.t["acsrow"] = ctx.t["dtT"]
    ctx.t["Dinvdt"] = ar32.ap()[:, 38176:38176 + 512]
    ctx.t["rr_ln1"] = ctx.t["Dinvdt"]
    ctx.t["acsT"] = ar32.ap()[:, 37152:37152 + 512]
    ctx.t["decayT"] = ar32.ap()[:, 37664:37664 + 512]
    ctx.t["rstd_b"] = arena.ap()[:, 75328:75328 + 1024]
    ctx.t["alast_rows"] = ar32.ap()[0:1, 38944:38944 + 256]
    ctx.t["alast_erow"] = arena.ap()[0:1, 78400:78400 + 256]
    ctx.sb("expA", [128, 1], F32)
    ctx.t["Bfeat"] = arena.ap()[:, 70720:70720 + 1024]
    ctx.t["Cfeat"] = arena.ap()[:, 71744:71744 + 1024]
    ctx.t["BT"] = arena.ap()[:, 72768:72768 + 1024]
    ctx.t["bdec"] = arena.ap()[:, 73792:73792 + 512]
    ctx.t["expAlB"] = ar32.ap()[:, 38688:38688 + 256]
    ctx.sb("roleflag", [128, 1], F32)
    ctx.t["Gm"] = arena.ap()[:, 69696:69696 + 1024]
    ctx.t["sqbuf"] = ctx.t["Gm"]
    ctx.t["ztmp"] = acsA_h.bitcast(BF16).ap()[:, 0:1024]
    ctx.t["ztmp2"] = acsA_h.bitcast(BF16).ap()[:, 1024:2048]
    ctx.sb("convw", [128, 4 * 34], F32)
    ctx.sb("convb", [128, 34], F32)
    ctx.sb("dt_bias", [128, 1], F32)
    ctx.sb("A_log", [128, 1], F32)
    ctx.sb("D_tile", [128, 64], F32)
    banks = [ctx.ps(f"bank{i}", [128, 512], F32) for i in range(8)]
    ctx.t["pA"] = banks[0].ap()
    ctx.t["pB"] = banks[1].ap()
    ctx.t["pt0"] = banks[6].bitcast(BF16).ap()[:, 0:128]
    ctx.t["pt1"] = banks[7].bitcast(BF16).ap()[:, 0:128]
    ctx.t["pf0"] = banks[4].ap()[:, 0:64]
    ctx.t["pf1"] = banks[5].ap()[:, 0:64]
    ctx.t["pAl"] = banks[2].ap()[0:1, 0:64]
    ctx.t["pAcsB0"] = banks[0].ap()[:, 0:256]
    ctx.t["pAcsB1"] = banks[1].ap()[:, 0:256]
    ctx.t["pRow0"] = banks[2].ap()[0:1, 0:128]
    ctx.t["pRow1"] = banks[3].ap()[0:1, 0:128]
    ctx.t["pG0"] = banks[2].ap()[:, 0:256]
    ctx.t["pG1"] = banks[3].ap()[:, 0:256]
    ctx.t["pY0"] = banks[4].ap()[0:64, 0:256]
    ctx.t["pY1"] = banks[5].ap()[0:64, 0:256]
    ctx.t["pS0"] = banks[6].ap()[:, 0:64]
    ctx.t["pS1"] = banks[7].ap()[:, 0:64]
    ctx.dram("u_dram", [TP, D], F32, kind="ExternalInput")
    ctx.dram("W_in", [D, EIN], F32, kind="ExternalInput")
    ctx.dram("convw_d", [128, 4 * 34], F32, kind="ExternalInput")
    ctx.dram("convb_d", [128, 34], F32, kind="ExternalInput")
    ctx.dram("dtb_d", [64, 1], F32, kind="ExternalInput")
    ctx.dram("alog_d", [64, 1], F32, kind="ExternalInput")
    ctx.dram("D_d", [128, 64], F32, kind="ExternalInput")
    ctx.dram("role_d", [128, 1], F32, kind="ExternalInput")
    ctx.dram("W_out_d", [DIN, D], F32, kind="ExternalInput")
    ctx.dram("ns_d", [128, 32], F32, kind="ExternalInput")
    ctx.dram("mamba_s", [T, D], F32)
    ctx.dram("ffn0_s", [T, D], F32)
    ctx.dram("WgT_d", [D, FF0], F32, kind="ExternalInput")
    ctx.dram("WuT_d", [D, FF0], F32, kind="ExternalInput")
    ctx.dram("WdT_d", [FF0, D], F32, kind="ExternalInput")
    ctx.dram("x1_out", [T, D], F32, kind="ExternalOutput")
    ctx.dram("stout", [128, DIN], BF16)
    ctx.dram("stgath", [256, DIN], BF16)



P0_LOADS = [("convw", "convw_d"), ("convb", "convb_d"), ("D_tile", "D_d"), ("roleflag", "role_d"), ("norm_scale", "ns_d")]



class P0Small:

    def sync(self, ctx, sy):
        sem = ctx.sem("p0b_ld")
        sy.dma_start(out=ctx.t["dt_bias"][0:64, 0:1], in_=ctx.t["dtb_d"].ap()).then_inc(sem, 16)
        sy.dma_start(out=ctx.t["A_log"][0:64, 0:1], in_=ctx.t["alog_d"].ap()).then_inc(sem, 16)
        sy.wait_ge(sem, 32)
        return None




def build_l0():
    nc = bass.Bass("TRN2", target_bir_lowering=False)
    stack = contextlib.ExitStack()
    with stack:
        ctx = Ctx(nc, stack)
        alloc_l0(ctx, nc)
        ctx.constf(EPS)
        ctx.constf(1.0)
        emit_phases(nc, ctx, [P0Consts(P0_LOADS), P0Small(), P1LN(), P2InProjA(),
                              P3Conv(), P4aState(), P4xExchange(), P4bSSD(),
                              P5Gate(), P6FFN0(), P7Combine()])
    return nc


def _rope_tables():
    inv = 1.0 / 10000.0 ** (np.arange(0, 64, 2) / 64.0)
    t = np.arange(2048.0)
    emb = np.concatenate([np.outer(t, inv), np.outer(t, inv)], axis=1)
    return np.cos(emb), np.sin(emb)


def _rope_inputs():
    cos, sin = _rope_tables()
    def tile8(tab):
        out = np.zeros((128, 16 * 256), dtype=BF)
        for tt in range(16):
            out[:, tt * 256:(tt + 1) * 256] = np.tile(tab[tt * 128:(tt + 1) * 128], (1, 8))
        return out
    return (tile8(cos[:, 0::2]), tile8(cos[:, 1::2]),
            tile8(sin[:, 0::2]), tile8(sin[:, 1::2]))


_NC_CACHE = {}


def kernel(L0_attn_u, L0_attn_W_in, L0_attn_conv_w, L0_attn_conv_b, L0_attn_dt_bias,
           L0_attn_A_log, L0_attn_D, L0_attn_norm_scale, L0_attn_W_out,
           L0_ffn_w_gate, L0_ffn_w_up, L0_ffn_w_down,
           L1_attn_Wq, L1_attn_Wk, L1_attn_Wv, L1_attn_Wo,
           L1_ffn_gate_w, L1_ffn_up_w, L1_ffn_down_w,
           L1_ffn_gate_s, L1_ffn_up_s, L1_ffn_down_s):
    u = np.asarray(L0_attn_u, dtype=np.float32)
    convw_l = np.zeros((128, 136), dtype=np.float32)
    convb_l = np.zeros((128, 34), dtype=np.float32)
    cw = np.asarray(L0_attn_conv_w, dtype=np.float32)
    cb = np.asarray(L0_attn_conv_b, dtype=np.float32)
    for j in range(34):
        for i in range(4):
            convw_l[:, 4 * j + i] = cw[i, 128 * j:128 * j + 128]
        convb_l[:, j] = cb[128 * j:128 * j + 128]
    D_t = np.tile(np.asarray(L0_attn_D, np.float32)[None, :], (128, 1))
    ns_l = np.ascontiguousarray(np.asarray(L0_attn_norm_scale, np.float32).reshape(32, 128).T)
    WgT = np.ascontiguousarray(np.asarray(L0_ffn_w_gate, np.float32).T)
    WuT = np.ascontiguousarray(np.asarray(L0_ffn_w_up, np.float32).T)
    WdT = np.ascontiguousarray(np.asarray(L0_ffn_w_down, np.float32).T)
    if "fused" not in _NC_CACHE:
        _NC_CACHE["fused"] = build_fused()
    ce8, co8, se8, so8 = _rope_inputs()
    Wq = np.asarray(L1_attn_Wq, np.float32)
    Wk = np.asarray(L1_attn_Wk, np.float32)
    Wv = np.asarray(L1_attn_Wv, np.float32)
    WoT = np.ascontiguousarray(np.asarray(L1_attn_Wo, np.float32).T)
    Wg1T = np.ascontiguousarray(np.asarray(L1_ffn_gate_w, np.float32).T)
    Wu1T = np.ascontiguousarray(np.asarray(L1_ffn_up_w, np.float32).T)
    Wd1T = np.ascontiguousarray(np.asarray(L1_ffn_down_w, np.float32).T)
    gs_l = np.ascontiguousarray(np.asarray(L1_ffn_gate_s, np.float32).reshape(44, 128).T)
    us_l = np.ascontiguousarray(np.asarray(L1_ffn_up_s, np.float32).reshape(44, 128).T)
    ds_t = np.tile(np.asarray(L1_ffn_down_s, np.float32)[None, :], (128, 1)).astype(BF)
    in_maps = []
    for c in range(8):
        half = c % 2
        hsl = slice(0, 1024) if half == 0 else slice(1024, 2048)
        ksl = slice(0, 256) if half == 0 else slice(256, 512)
        role = np.full((128, 1), float(half), dtype=np.float32)
        in_maps.append({
            "u_dram": prep_u_core(u, c),
            "W_in": np.asarray(L0_attn_W_in, np.float32),
            "convw_d": convw_l, "convb_d": convb_l,
            "dtb_d": np.asarray(L0_attn_dt_bias, np.float32)[:, None],
            "alog_d": np.asarray(L0_attn_A_log, np.float32)[:, None],
            "D_d": D_t, "role_d": role,
            "W_out_d": np.asarray(L0_attn_W_out, np.float32), "ns_d": ns_l,
            "WgT_d": WgT, "WuT_d": WuT, "WdT_d": WdT,
            "WqT_d": np.ascontiguousarray(Wq[hsl].T),
            "WkT_d": np.ascontiguousarray(Wk[ksl].T),
            "WvT_d": np.ascontiguousarray(Wv[ksl].T),
            "WoT_d": WoT,
            "Wg1T_d": Wg1T, "Wu1T_d": Wu1T, "Wd1T_d": Wd1T,
            "ce8_d": ce8, "co8_d": co8, "se8_d": se8, "so8_d": so8,
            "gs1_d": gs_l, "us1_d": us_l, "ds_d": ds_t,
        })
    res = run_bass_kernel_spmd(_NC_CACHE["fused"], in_maps, core_ids=list(range(8)))
    out = np.zeros((4, 2048, 2048), dtype=np.float32)
    for c in range(8):
        b, half = c // 2, c % 2
        out[b, half * 1024:half * 1024 + 1024] = np.asarray(res.results[c]["out_d"])
    return out

def alloc_l1_fused(ctx, nc):
    arena = ctx.t["arena"]
    ar32 = arena.bitcast(F32)
    b = {}
    b["u_tok8"] = ar32.ap()[:, 0:16384]
    b["xn1_tok"] = arena.ap()[:, 32768:32768 + 16384]
    b["ln_sq"] = ar32.ap()[:, 24576:24576 + 2048]
    b["ln_tmp2b"] = ar32.ap()[:, 26624:26624 + 2048]
    b["ln_aggr"] = ar32.ap()[:, 28672:28672 + 16]
    b["ln_rows"] = ar32.ap()[:, 28688:28688 + 32]
    b["xnF"] = arena.ap()[:, 0:32768]
    b["qkvslab"] = arena.ap()[:, 57408:57408 + 24576]
    b["Qtok"] = arena.ap()[:, 32768:32768 + 16384]
    b["Ktok"] = arena.ap()[:, 49152:49152 + 4096]
    b["Vtok"] = arena.ap()[:, 53248:53248 + 4160]
    b["Qf"] = arena.ap()[:, 0:32768]
    b["expS"] = arena.ap()[:, 32768:32768 + 8192]
    b["Kf"] = arena.ap()[:, 57408:57408 + 8192]
    b["attnO"] = arena.ap()[:, 65600:65600 + 16384]
    b["dtmp"] = arena.ap()[:, 40960:40960 + 1024]
    b["rBs"] = arena.ap()[:, 41984:41984 + 512]
    b["rrow1"] = arena.ap()[:, 42496:42496 + 1024]
    b["drow"] = ar32.ap()[:, 21760:21760 + 512]
    b["attnG"] = arena.ap()[:, 0:16384]
    b["woslabA"] = arena.ap()[:, 16384:16384 + 1024]
    b["woslabB"] = arena.ap()[:, 17408:17408 + 1024]
    b["stage2"] = ar32.ap()[:, 11264:11264 + 2048]
    b["xn1own"] = arena.ap()[:, 28672:28672 + 16384]
    b["h1"] = arena.ap()[:, 0:22528]
    b["slabA"] = arena.ap()[:, 45056:45056 + 8192]
    b["slabB"] = arena.ap()[:, 53248:53248 + 8192]
    b["slabC2"] = arena.ap()[:, 61440:61440 + 8192]
    b["slabD2"] = arena.ap()[:, 69632:69632 + 8192]
    b["slabC"] = arena.ap()[:, 26624:26624 + 1024]
    b["slabD"] = arena.ap()[:, 27648:27648 + 1024]
    b["ztmp"] = arena.ap()[:, 77824:77824 + 1024]
    b["uR"] = ar32.ap()[:, 0:4096]
    b["mR"] = ar32.ap()[:, 4096:8192]
    b["fR"] = ar32.ap()[:, 8192:12288]
    b["x1R"] = ar32.ap()[:, 12288:16384]
    b["x2R"] = ar32.ap()[:, 16384:20480]
    b["dsB"] = ar32.ap()[:, 20480:22528]
    xn = ctx.t["xn"]
    b["ce8"] = xn.ap()[:, 0:4096]
    b["co8"] = xn.ap()[:, 4096:8192]
    b["se8"] = xn.ap()[:, 8192:12288]
    b["so8"] = xn.ap()[:, 12288:16384]
    b["dmask"] = ctx.t["acsA"].bitcast(BF16).ap()[:, 0:2048]
    b["ropetmp"] = ctx.t["acsB"].bitcast(BF16).ap()[:, 0:512]
    b["gs1"] = ctx.t["acsB"].ap()[:, 256:300]
    b["us1"] = ctx.t["acsB"].ap()[:, 300:344]
    b["dsrow"] = ctx.t["dtbuf"].bitcast(BF16).ap()[:, 0:2048]
    ctx.dram("x1_mid", [T, D], F32)
    b["x1_in"] = ctx.t["x1_mid"]
    ctx.t["x1_out"] = ctx.t["x1_mid"]
    ctx.dram("WqT_d", [D, NQH * 64], F32, kind="ExternalInput")
    ctx.dram("WkT_d", [D, NKH * 64], F32, kind="ExternalInput")
    ctx.dram("WvT_d", [D, NKH * 64], F32, kind="ExternalInput")
    ctx.dram("WoT_d", [D, D], F32, kind="ExternalInput")
    ctx.dram("Wg1T_d", [D, FF1], F32, kind="ExternalInput")
    ctx.dram("Wu1T_d", [D, FF1], F32, kind="ExternalInput")
    ctx.dram("Wd1T_d", [FF1, D], F32, kind="ExternalInput")
    ctx.dram("ce8_d", [128, 16 * 256], BF16, kind="ExternalInput")
    ctx.dram("co8_d", [128, 16 * 256], BF16, kind="ExternalInput")
    ctx.dram("se8_d", [128, 16 * 256], BF16, kind="ExternalInput")
    ctx.dram("so8_d", [128, 16 * 256], BF16, kind="ExternalInput")
    ctx.dram("gs1_d", [128, 44], F32, kind="ExternalInput")
    ctx.dram("us1_d", [128, 44], F32, kind="ExternalInput")
    ctx.dram("ds_d", [128, 2048], BF16, kind="ExternalInput")
    ctx.dram("xn1b", [T, D], BF16)
    ctx.dram("xn1g", [TT2, D], BF16)
    ctx.dram("aOb", [T, TT2], BF16)
    ctx.dram("aOg", [TT2, TT2], BF16)
    ctx.dram("attn_s", [T, D], F32)
    ctx.dram("ffn1_s", [T, D], F32)
    ctx.dram("out_d", [T, D], F32, kind="ExternalOutput")
    return b


def build_fused():
    nc = bass.Bass("TRN2", target_bir_lowering=False)
    stack = contextlib.ExitStack()
    with stack:
        ctx = Ctx(nc, stack)
        alloc_l0(ctx, nc)
        rebind = alloc_l1_fused(ctx, nc)
        ctx.constf(EPS)
        ctx.constf(1.0)
        ctx.constf(-30.0)
        ffn1 = P6FFN0(xn_name="xn1own", xn_off=0, out_name="ffn1_s",
                      wg="Wg1T_d", wu="Wu1T_d", wd="Wd1T_d", pfx="q6",
                      ne=FF1, gate_scale="gs1", up_scale="us1")
        phases = [P0Consts(P0_LOADS), P0Small(), P1LN(), P2InProjA(), P3Conv(),
                  P4aState(), P4xExchange(), P4bSSD(), P5Gate(), P6FFN0(),
                  P7Combine(), rebind, Q0Consts(), Q1LN(), Q2QKV(), Q3Attn(),
                  Q4Exchange(), Q5Wo(), ffn1, Q7Final()]
        emit_phases(nc, ctx, phases)
    return nc


# version 08
# speedup vs baseline: 1.2537x; 1.2537x over previous
import contextlib
import numpy as np
import ml_dtypes
import concourse.bass as bass
import concourse.mybir as mybir
from concourse.bass_utils import run_bass_kernel_spmd

F32 = mybir.dt.float32
BF16 = mybir.dt.bfloat16
AF = mybir.ActivationFunctionType
ALU = mybir.AluOpType
AX = mybir.AxisListType

BF = ml_dtypes.bfloat16

D = 2048
T = 1024
HALO = 4
TF = T + HALO
TP = 1152
DIN = 4096
NH = 64
HD = 64
NST = 128
CH = 256
NCH = T // CH
CONVD = DIN + 2 * NST
EIN = 2 * DIN + 2 * NST + NH
XBCD = CONVD + NH
FF0 = 8192
FF1 = 5632
EPS = 1e-5

PAIRS = [[0, 1], [2, 3], [4, 5], [6, 7]]
ENGS = ["tensor", "vector", "scalar", "gpsimd", "sync"]


def cdiv(a, b):
    return (a + b - 1) // b


class Ctx:
    def __init__(self, nc, stack):
        self.nc = nc
        self.t = {}
        self.sems = {}
        self.stack = stack
        self.phase_idx = -1
        self.pool = []
        self.phase_map = {}
        self.consts = {}
        self.cpool = None

    def init_pool(self, n=88):
        for i in range(n):
            self.pool.append(self.stack.enter_context(self.nc.semaphore(f"pool{i}")))

    def constf(self, value):
        if self.cpool is None:
            self.cpool = self.sb("cpool", [128, 32], F32)
        v = float(value)
        if v not in self.consts:
            self.consts[v] = len(self.consts)
            assert len(self.consts) <= 32
        return self.cpool[:, self.consts[v]:self.consts[v] + 1]

    def constf_r(self, value, rows):
        ap = self.constf(value)
        return self.cpool[0:rows, self.consts[float(value)]:self.consts[float(value)] + 1]

    def sem(self, name):
        if self.phase_idx < 0 or name.startswith("g_"):
            if name not in self.sems:
                self.sems[name] = self.stack.enter_context(self.nc.semaphore(name))
            return self.sems[name]
        key = (self.phase_idx, name)
        if key not in self.phase_map:
            used = len([1 for (p, _) in self.phase_map if p == self.phase_idx])
            assert used < len(self.pool), "phase sem pool exhausted"
            self.phase_map[key] = used
        return self.pool[self.phase_map[key]]

    def sb(self, name, shape, dtype, **kw):
        h = self.stack.enter_context(self.nc.sbuf_tensor(name, shape, dtype, **kw))
        self.t[name] = h
        return h

    def ps(self, name, shape, dtype=F32):
        h = self.stack.enter_context(self.nc.psum_tensor(name, shape, dtype))
        self.t[name] = h
        return h

    def dram(self, name, shape, dtype, kind=None, **kw):
        if kind:
            h = self.nc.dram_tensor(name, shape, dtype, kind=kind, **kw)
        else:
            h = self.nc.dram_tensor(name, shape, dtype, **kw)
        self.t[name] = h
        return h


def inc_last(eng, inst, sem):
    eng.drain()
    eng.sem_inc(sem, 1)


def dinc(eng, sem, n=1):
    eng.drain()
    eng.sem_inc(sem, n)


def emit_phases(nc, ctx, phases):
    ctx.init_pool()
    pool_nums = sorted(sm.num for sm in ctx.pool)
    lo, hi = pool_nums[0], pool_nums[-1] + 1
    assert hi - lo == len(pool_nums), "pool sems not contiguous"
    emitted = 0
    for p in phases:
        if isinstance(p, dict):
            ctx.t.update(p)
            continue
        i = emitted
        emitted += 1
        ctx.phase_idx = i
        if i > 0:
            with nc.Block() as bc:
                def clr(g):
                    g.dma_reset(range(lo, hi))
                    g.sem_clear(range(lo, hi))
                bc.gpsimd(clr)
            nc.all_engine_barrier()
        with nc.Block() as block:
            for eng_name in ENGS:
                fn = getattr(p, eng_name, None)
                if fn is None:
                    continue
                def make_body(fn=fn):
                    def body(eng):
                        fn(ctx, eng)
                    return body
                getattr(block, eng_name)(make_body())
        nc.all_engine_barrier()
    ctx.phase_idx = -1


class PDebugStore:

    def __init__(self, names):
        self.names = names

    def sync(self, ctx, sy):
        s_o = ctx.sem("dbg_out")
        k = 0
        for n in self.names:
            sy.dma_start(out=ctx.t["dbg_" + n].ap(),
                         in_=as_ap(ctx.t[n])[:, :]).then_inc(s_o, 16)
            k += 1
        sy.wait_ge(s_o, 16 * k)
        return None


def as_ap(obj):
    return obj.ap() if callable(getattr(obj, "ap", None)) else obj


def add_debug(ctx, names):
    for n in names:
        ap = as_ap(ctx.t[n])
        ctx.dram("dbg_" + n, list(ap.shape), ap.dtype, kind="ExternalOutput")
    return PDebugStore(names)


class P0Consts:
    def __init__(self, loads):
        self.loads = loads

    def gpsimd(self, ctx, g):
        ident = ctx.t["ident_bf"]
        g.memset(ident[:, :], 0.0)
        g.drain()
        g.affine_select(out=ident[:, :], in_=ident[:, :],
                        compare_op=ALU.not_equal, fill=1.0,
                        base=0, pattern=[[-1, 128]], channel_multiplier=1)
        ident32 = ctx.t["ident_f32"]
        g.memset(ident32[:, :], 0.0)
        g.drain()
        g.affine_select(out=ident32[:, :], in_=ident32[:, :],
                        compare_op=ALU.not_equal, fill=1.0,
                        base=0, pattern=[[-1, 128]], channel_multiplier=1)
        ma = ctx.t["gmaskA"]
        g.memset(ma[:, :], 1.0)
        g.drain()
        g.affine_select(out=ma[:, :], in_=ma[:, :], compare_op=ALU.is_ge,
                        fill=0.0, base=0, pattern=[[1, CH]], channel_multiplier=-1)
        mb = ctx.t["gmaskB"]
        g.memset(mb[:, :], 1.0)
        g.drain()
        last = g.affine_select(out=mb[:, :], in_=mb[:, :], compare_op=ALU.is_ge,
                               fill=0.0, base=-128, pattern=[[1, CH]],
                               channel_multiplier=-1)
        return last

    def vector(self, ctx, v):
        last = v.memset(ctx.t["ones_bf"][:, :], 1.0)
        for val, idx in ctx.consts.items():
            last = v.memset(ctx.cpool[:, idx:idx + 1], val)
        return last

    def sync(self, ctx, sy):
        sem = ctx.sem("p0_ld")
        n = 0
        for sb_name, dram_name in self.loads:
            sb = ctx.t[sb_name]
            dr = ctx.t[dram_name]
            sy.dma_start(out=sb[:, :], in_=dr.ap()).then_inc(sem, 16)
            n += 1
        if n:
            sy.wait_ge(sem, 16 * n)
        return None


class P1LN:

    def sync(self, ctx, sy):
        u = ctx.t["u_dram"]
        u_tok = ctx.t["u_tok"]
        s_u = ctx.sem("p1_u_ld")
        src = u.ap().rearrange("(tt p) d -> p tt d", p=128)
        dst = u_tok.rearrange("p (tt d) -> p tt d", tt=9)
        sy.dma_start(out=dst, in_=src).then_inc(s_u, 16)
        return None

    def vector(self, ctx, v):
        u_tok = ctx.t["u_tok"]
        u_bf = ctx.t["u_bf"]
        aggr = ctx.t["ln_aggr"]
        rows = ctx.t["ln_rows"]
        sq = ctx.t["ln_tmp"]
        s_u = ctx.sem("p1_u_ld")
        s_ready = ctx.sem("p1_tok_ready")
        s_rstd = ctx.sem("p1_rstd")
        s_sq = ctx.sem("p1_sq")
        for tt in range(9):
            if tt == 0:
                v.wait_ge(s_u, 16)
            ut = u_tok[:, tt * D:(tt + 1) * D]
            a = aggr[:, 2 * tt:2 * tt + 2]
            r = rows[:, 4 * tt:4 * tt + 4]
            v.reduce_sum(r[:, 2:3], ut, axis=AX.X)
            v.wait_ge(s_sq, tt + 1)
            v.reduce_sum(r[:, 3:4], sq[:, :], axis=AX.X)
            dinc(v, ctx.sem("p1_sq_free"))
            v.drain()
            v.tensor_scalar(a[:, 0:1], r[:, 2:3], 1.0 / D, None, op0=ALU.mult)
            v.tensor_scalar(r[:, 3:4], r[:, 3:4], 1.0 / D, None, op0=ALU.mult)
            v.drain()
            v.tensor_tensor(a[:, 1:2], a[:, 0:1], a[:, 0:1], op=ALU.mult)
            v.drain()
            v.tensor_tensor(a[:, 1:2], r[:, 3:4], a[:, 1:2], op=ALU.subtract)
            dinc(v, ctx.sem("p1_var_ready"))
        tmp2 = ctx.t["ln_tmp2"]
        for tt in range(9):
            ut = u_tok[:, tt * D:(tt + 1) * D]
            a = aggr[:, 2 * tt:2 * tt + 2]
            r = rows[:, 4 * tt:4 * tt + 4]
            v.wait_ge(s_rstd, tt + 1)
            v.tensor_scalar(tmp2[:, :], ut, a[:, 0:1], r[:, 0:1],
                            op0=ALU.subtract, op1=ALU.mult)
            v.drain()
            v.tensor_copy(u_bf[:, tt * D:(tt + 1) * D], tmp2[:, :])
            dinc(v, s_ready)
        return None

    def scalar(self, ctx, s):
        aggr = ctx.t["ln_aggr"]
        rows = ctx.t["ln_rows"]
        s_var = ctx.sem("p1_var_ready")
        s_rstd = ctx.sem("p1_rstd")
        xn = ctx.t["xn"]
        s_tp = ctx.sem("p1_tp")
        s_ev = ctx.sem("p1_ev")
        sq = ctx.t["ln_tmp"]
        u_tok = ctx.t["u_tok"]
        s_sq = ctx.sem("p1_sq")
        s_sq_free = ctx.sem("p1_sq_free")
        last = None
        for tt in range(9):
            if tt == 0:
                s.wait_ge(ctx.sem("p1_u_ld"), 16)
            if tt > 0:
                s.wait_ge(s_sq_free, tt)
            s.activation(sq[:, :], u_tok[:, tt * D:(tt + 1) * D], AF.Square)
            dinc(s, s_sq)
        for tt in range(9):
            a = aggr[:, 2 * tt:2 * tt + 2]
            r = rows[:, 4 * tt:4 * tt + 4]
            s.wait_ge(s_var, tt + 1)
            s.activation(r[:, 1:2], a[:, 1:2], AF.Ln, bias=ctx.constf(EPS))
            s.drain()
            s.activation(r[:, 0:1], r[:, 1:2], AF.Exp, scale=-0.5)
            dinc(s, s_rstd)
        for tt in range(9):
            w = 128 if tt < 8 else HALO
            for dt in range(16):
                k = tt * 16 + dt
                s.wait_ge(s_tp, k + 1)
                pt = ctx.t["pt0"] if k % 2 == 0 else ctx.t["pt1"]
                last = s.copy(xn[:, dt * TF + tt * 128: dt * TF + tt * 128 + w],
                              pt[:, 0:w])
                last.then_inc(s_ev, 1)
        return None

    def tensor(self, ctx, t):
        u_bf = ctx.t["u_bf"]
        ident = ctx.t["ident_bf"]
        s_ready = ctx.sem("p1_tok_ready")
        s_tp = ctx.sem("p1_tp")
        s_ev = ctx.sem("p1_ev")
        last = None
        for tt in range(9):
            t.wait_ge(s_ready, tt + 1)
            for dt in range(16):
                k = tt * 16 + dt
                if k >= 2:
                    t.wait_ge(s_ev, k - 1)
                pt = ctx.t["pt0"] if k % 2 == 0 else ctx.t["pt1"]
                last = t.transpose(pt[:, :],
                                   u_bf[:, tt * D + dt * 128: tt * D + dt * 128 + 128],
                                   ident[:, :])
                last.then_inc(s_tp, 1)
        return None


def prep_u_core(u, core):
    b, half = core // 2, core % 2
    t0 = half * 1024
    out = np.zeros((TP, D), dtype=np.float32)
    lo = t0 - HALO
    src_lo = max(lo, 0)
    off = src_lo - lo
    out[off:HALO + 1024, :] = u[b, src_lo:t0 + 1024, :]
    return np.ascontiguousarray(out)


P2_CHUNKS = [(DIN + 512 * i, 512) for i in range(8)] + [(DIN + 4096, 320)]
P2_TCS = [(0, 512), (512, 512), (1024, HALO)]
CONV_ORDER = [32, 33] + list(range(32))


def etiles(w):
    out = []
    o = 0
    while o < w:
        out.append((o, min(128, w - o)))
        o += 128
    return out


class P2InProjA:

    def groups(self):
        idx = 0
        for ci, (e0, ew) in enumerate(P2_CHUNKS):
            for j, (jo, jw) in enumerate(etiles(ew)):
                jglob = (e0 - DIN + jo) // 128
                for tc0, tw in P2_TCS:
                    yield idx, ci, ew, jo, jw, jglob, tc0, tw
                    idx += 1

    def chunk_last(self):
        last = {}
        for idx, ci, *_ in self.groups():
            last[ci] = idx
        return last

    def gpsimd(self, ctx, g):
        w = ctx.t["W_in"]
        s_mm = ctx.sem("p2_mm")
        last = self.chunk_last()
        for ci, (e0, ew) in enumerate(P2_CHUNKS):
            if ci >= 2:
                g.wait_ge(s_mm, last[ci - 2] + 1)
            slab = ctx.t["slabA"] if ci % 2 == 0 else ctx.t["slabB"]
            s_wl = ctx.sem("p2_wlA" if ci % 2 == 0 else "p2_wlB")
            src = w.ap()[:, e0:e0 + ew].rearrange("(k p) e -> p k e", p=128)
            dst = slab[:, 0:16 * ew].rearrange("p (k e) -> p k e", k=16)
            g.dma_start(out=dst, in_=src).then_inc(s_wl, 16)
        return None

    def tensor(self, ctx, t):
        xn = ctx.t["xn"]
        s_mm = ctx.sem("p2_mm")
        s_evV = ctx.sem("p2_evV")
        s_evS = ctx.sem("p2_evS")
        last_ci = -1
        for idx, ci, ew, jo, jw, jglob, tc0, tw in self.groups():
            if ci != last_ci:
                t.wait_ge(ctx.sem("p2_wlA" if ci % 2 == 0 else "p2_wlB"),
                          16 * (ci // 2 + 1))
                last_ci = ci
            slab = ctx.t["slabA"] if ci % 2 == 0 else ctx.t["slabB"]
            p = ctx.t["pA"] if idx % 2 == 0 else ctx.t["pB"]
            sev = s_evV if idx % 2 == 0 else s_evS
            if idx >= 2:
                t.wait_ge(sev, idx // 2)
            for k in range(16):
                mm = t.matmul(p[0:jw, 0:tw],
                              slab[:, k * ew + jo:k * ew + jo + jw],
                              xn[:, k * TF + tc0:k * TF + tc0 + tw],
                              start=(k == 0), stop=(k == 15))
            mm.then_inc(s_mm, 1)
        return None

    def _evac(self, ctx, eng, parity):
        xBC = ctx.t["xBC"]
        s_mm = ctx.sem("p2_mm")
        sev = ctx.sem("p2_evV" if parity == 0 else "p2_evS")
        for idx, ci, ew, jo, jw, jglob, tc0, tw in self.groups():
            if idx % 2 != parity:
                continue
            p = ctx.t["pA"] if parity == 0 else ctx.t["pB"]
            eng.wait_ge(s_mm, idx + 1)
            eng.tensor_copy(xBC[0:jw, jglob * TF + tc0:jglob * TF + tc0 + tw],
                            p[0:jw, 0:tw])
            dinc(eng, sev)
        return None

    def vector(self, ctx, v):
        return self._evac(ctx, v, 0)

    def scalar(self, ctx, s):
        xBC = ctx.t["xBC"]
        s_mm = ctx.sem("p2_mm")
        sev = ctx.sem("p2_evS")
        for idx, ci, ew, jo, jw, jglob, tc0, tw in self.groups():
            if idx % 2 != 1:
                continue
            p = ctx.t["pB"]
            s.wait_ge(s_mm, idx + 1)
            s.copy(xBC[0:jw, jglob * TF + tc0:jglob * TF + tc0 + tw], p[0:jw, 0:tw])
            dinc(s, sev)
        return None


class P3Conv:

    def vector(self, ctx, v):
        xBC = ctx.t["xBC"]
        dtbuf, acsA, acsB = ctx.t["dtbuf"], ctx.t["acsA"], ctx.t["acsB"]
        dt34 = xBC[0:64, 34 * TF + HALO:34 * TF + HALO + T]
        v.tensor_scalar(dtbuf[0:64, :], dt34, ctx.t["dt_bias"][0:64, 0:1], None,
                        op0=ALU.add)
        dinc(v, ctx.sem("p3_a"))
        v.wait_ge(ctx.sem("p3_sp"), 1)
        v.tensor_tensor(dtbuf[0:64, :], acsA[0:64, :], acsB[0:64, :], op=ALU.add)
        v.drain()
        v.tensor_copy(ctx.t["tmp_bf"][0:64, :], dtbuf[0:64, :])
        v.wait_ge(ctx.sem("p3_expA"), 1)
        v.tensor_scalar(acsA[0:64, :], dtbuf[0:64, :], ctx.t["expA"][0:64, 0:1],
                        None, op0=ALU.mult)
        dinc(v, ctx.sem("p3_dtbf"))
        dtT = ctx.t["dtT"]
        for tb in range(8):
            pt = ctx.t["pt0"] if tb % 2 == 0 else ctx.t["pt1"]
            v.wait_ge(ctx.sem("p3_dtT"), tb + 1)
            v.tensor_copy(dtT[:, tb * 64:(tb + 1) * 64], pt[:, 0:64])
            dinc(v, ctx.sem("p3_evdt"))
        for lv, sh in enumerate([1, 2, 4, 8, 16, 32, 64, 128]):
            src = acsA if lv % 2 == 0 else acsB
            dst = acsB if lv % 2 == 0 else acsA
            v.drain()
            for c in range(NCH):
                c0 = c * CH
                v.tensor_tensor(dst[0:64, c0 + sh:c0 + CH],
                                src[0:64, c0 + sh:c0 + CH],
                                src[0:64, c0:c0 + CH - sh], op=ALU.add)
                v.tensor_copy(dst[0:64, c0:c0 + sh], src[0:64, c0:c0 + sh])
        dinc(v, ctx.sem("p3_acsdone"))
        for c in range(NCH):
            c0 = c * CH
            v.tensor_scalar(acsB[0:64, c0:c0 + CH], acsA[0:64, c0:c0 + CH],
                            acsA[0:64, c0 + CH - 1:c0 + CH], None, op0=ALU.subtract)
        dinc(v, ctx.sem("p3_dexp"))
        acsT = ctx.t["acsT"].rearrange("p (cc hh tb) -> p cc hh tb", cc=NCH, tb=2)
        for tb in range(8):
            pf = ctx.t["pf0"] if tb % 2 == 0 else ctx.t["pf1"]
            v.wait_ge(ctx.sem("p3_acsTtp"), tb + 1)
            v.tensor_copy(acsT[:, tb // 2, :, tb % 2], pf[:, 0:64])
            dinc(v, ctx.sem("p3_evacs"))
        for c in range(NCH):
            v.wait_ge(ctx.sem("p3_alT"), c + 1)
            v.tensor_copy(ctx.t["alast_rows"][0:1, c * 64:(c + 1) * 64],
                          ctx.t["pAl"][0:1, 0:64])
            dinc(v, ctx.sem("p3_eval"))
        decayT = ctx.t["decayT"]
        for tb in range(8):
            pt = ctx.t["pt0"] if tb % 2 == 0 else ctx.t["pt1"]
            v.wait_ge(ctx.sem("p3_decT"), tb + 1)
            v.tensor_copy(decayT[:, tb * 64:(tb + 1) * 64], pt[:, 0:64])
            dinc(v, ctx.sem("p3_evdec"))
        v.reciprocal(ctx.t["invdt"][:, :], dtT[:, :])
        v.drain()
        for tb in range(8):
            v.tensor_tensor(ctx.t["Dinvdt"][:, tb * 64:tb * 64 + 64],
                            ctx.t["invdt"][:, tb * 64:tb * 64 + 64],
                            ctx.t["D_tile"][:, 0:64], op=ALU.mult)
        dinc(v, ctx.sem("p3_dinv"))
        wcol = ctx.t["convw"]
        bcol = ctx.t["convb"]
        s_cd = ctx.sem("p3_convV")
        s_cfree = ctx.sem("p3_freeV")
        xdtT = ctx.t["xdtT"]
        BT = ctx.t["BT"]
        s_tp3 = ctx.sem("p3_tp")
        s_ev3 = ctx.sem("p3_ev")

        def evac_B(kb):
            for tb in range(8):
                pt = ctx.t["pt0"] if kb % 2 == 0 else ctx.t["pt1"]
                v.wait_ge(s_tp3, kb + 1)
                v.tensor_copy(BT[:, tb * 128:(tb + 1) * 128], pt[:, :])
                dinc(v, s_ev3)
                kb += 1
            return kb

        def evac_x(m, kb):
            for tb in range(8):
                pt = ctx.t["pt0"] if kb % 2 == 0 else ctx.t["pt1"]
                v.wait_ge(s_tp3, kb + 1)
                v.tensor_scalar(xdtT[:, tb * DIN + (2 * m) * 64:tb * DIN + (2 * m) * 64 + 64],
                                pt[:, 0:64], dtT[:, tb * 64 + 2 * m:tb * 64 + 2 * m + 1],
                                None, op0=ALU.mult)
                v.tensor_scalar(xdtT[:, tb * DIN + (2 * m + 1) * 64:tb * DIN + (2 * m + 1) * 64 + 64],
                                pt[:, 64:128], dtT[:, tb * 64 + 2 * m + 1:tb * 64 + 2 * m + 2],
                                None, op0=ALU.mult)
                dinc(v, s_ev3)
                kb += 1
            return kb

        kb = 0
        for i_o, j in enumerate(CONV_ORDER):
            cf = ctx.t["convfV"] if i_o % 2 == 0 else ctx.t["convfG"]
            if i_o >= 2:
                v.wait_ge(s_cfree, i_o - 1)
            xj = xBC[:, j * TF:(j + 1) * TF]
            v.drain()
            v.tensor_scalar(cf[:, :], xj[:, 1:1 + T], wcol[:, 4 * j:4 * j + 1],
                            bcol[:, j:j + 1], op0=ALU.mult, op1=ALU.add)
            for i in range(1, 4):
                v.drain()
                v.scalar_tensor_tensor(cf[:, :], xj[:, 1 + i:1 + i + T],
                                       wcol[:, 4 * j + i:4 * j + i + 1], cf[:, :],
                                       op0=ALU.mult, op1=ALU.add)
            dinc(v, s_cd)
            if i_o == 2:
                kb = evac_B(kb)
            if i_o >= 3:
                kb = evac_x(i_o - 3, kb)
        kb = evac_x(31, kb)
        return None

    def scalar(self, ctx, s):
        dtbuf, acsA, acsB = ctx.t["dtbuf"], ctx.t["acsA"], ctx.t["acsB"]
        s.activation(ctx.t["expA"][0:64, 0:1], ctx.t["A_log"][0:64, 0:1], AF.Exp)
        dinc(s, ctx.sem("p3_expA"))
        s.wait_ge(ctx.sem("p3_a"), 1)
        s.activation(acsA[0:64, :], dtbuf[0:64, :], AF.Abs)
        s.drain()
        s.activation(acsB[0:64, :], acsA[0:64, :], AF.Exp, scale=-1.0)
        s.drain()
        s.activation(acsA[0:64, :], acsB[0:64, :], AF.Ln, bias=ctx.constf_r(1.0, 64))
        s.drain()
        s.activation(acsB[0:64, :], dtbuf[0:64, :], AF.Relu)
        dinc(s, ctx.sem("p3_sp"))
        s.wait_ge(ctx.sem("p3_dexp"), 1)
        s.activation(ctx.t["decay_bf"][0:64, :], acsB[0:64, :], AF.Exp)
        dinc(s, ctx.sem("p3_dbf"))
        s.wait_ge(ctx.sem("p3_eval"), NCH)
        s.activation(ctx.t["alast_erow"][0:1, 0:NCH * 64],
                     ctx.t["alast_rows"][0:1, 0:NCH * 64], AF.Exp, scale=-1.0)
        dinc(s, ctx.sem("p3_alerow"))
        s_cdV = ctx.sem("p3_convV")
        s_silu = ctx.sem("p3_silu")
        sfree = ctx.sem("p3_freeV")
        for i_o, j in enumerate(CONV_ORDER):
            s.wait_ge(s_cdV, i_o + 1)
            cf = ctx.t["convfV"] if i_o % 2 == 0 else ctx.t["convfG"]
            if j == 32:
                dst = ctx.t["Bfeat"][:, :]
            elif j == 33:
                dst = ctx.t["Cfeat"][:, :]
            else:
                if j >= 2:
                    s.wait_ge(ctx.sem("p3_tp"), 8 * j)
                dst = ctx.t["conv_tmp"][:, (j % 2) * T:(j % 2) * T + T]
            s.activation(dst, cf[:, :], AF.Silu)
            dinc(s, s_silu)
            dinc(s, sfree)
        return None

    def tensor(self, ctx, t):
        ident = ctx.t["ident_bf"]
        ident32 = ctx.t["ident_f32"]
        tmp_bf = ctx.t["tmp_bf"]
        acsA = ctx.t["acsA"]
        t.wait_ge(ctx.sem("p3_dtbf"), 1)
        for tb in range(8):
            pt = ctx.t["pt0"] if tb % 2 == 0 else ctx.t["pt1"]
            if tb >= 2:
                t.wait_ge(ctx.sem("p3_evdt"), tb - 1)
            t.transpose(pt[:, 0:64], tmp_bf[0:64, tb * 128:(tb + 1) * 128],
                        ident[0:64, 0:64]).then_inc(ctx.sem("p3_dtT"), 1)
        t.wait_ge(ctx.sem("p3_acsdone"), 1)
        for tb in range(8):
            pf = ctx.t["pf0"] if tb % 2 == 0 else ctx.t["pf1"]
            if tb >= 2:
                t.wait_ge(ctx.sem("p3_evacs"), tb - 1)
            t.transpose(pf[:, 0:64], acsA[0:64, tb * 128:(tb + 1) * 128],
                        ident32[0:64, 0:64]).then_inc(ctx.sem("p3_acsTtp"), 1)
        for c in range(NCH):
            if c >= 1:
                t.wait_ge(ctx.sem("p3_eval"), c)
            t.transpose(ctx.t["pAl"][0:1, 0:64],
                        acsA[0:64, c * CH + CH - 1:c * CH + CH],
                        ident32[0:64, 0:64]).then_inc(ctx.sem("p3_alT"), 1)
        t.wait_ge(ctx.sem("p3_evdt"), 8)
        t.wait_ge(ctx.sem("p3_dbf"), 1)
        for tb in range(8):
            pt = ctx.t["pt0"] if tb % 2 == 0 else ctx.t["pt1"]
            if tb >= 2:
                t.wait_ge(ctx.sem("p3_evdec"), tb - 1)
            t.transpose(pt[:, 0:64], ctx.t["decay_bf"][0:64, tb * 128:(tb + 1) * 128],
                        ident[0:64, 0:64]).then_inc(ctx.sem("p3_decT"), 1)
        s_silu = ctx.sem("p3_silu")
        s_tp3 = ctx.sem("p3_tp")
        s_ev3 = ctx.sem("p3_ev")
        t.wait_ge(ctx.sem("p3_evdec"), 8)
        kb = 0
        t.wait_ge(s_silu, 1)
        for tb in range(8):
            pt = ctx.t["pt0"] if kb % 2 == 0 else ctx.t["pt1"]
            if kb >= 2:
                t.wait_ge(s_ev3, kb - 1)
            t.transpose(pt[:, :], ctx.t["Bfeat"][:, tb * 128:tb * 128 + 128],
                        ident[:, :]).then_inc(s_tp3, 1)
            kb += 1
        for j in range(32):
            t.wait_ge(s_silu, j + 3)
            for tb in range(8):
                pt = ctx.t["pt0"] if kb % 2 == 0 else ctx.t["pt1"]
                if kb >= 2:
                    t.wait_ge(s_ev3, kb - 1)
                t.transpose(pt[:, :],
                            ctx.t["conv_tmp"][:, (j % 2) * T + tb * 128:(j % 2) * T + tb * 128 + 128],
                            ident[:, :]).then_inc(s_tp3, 1)
                kb += 1
        return None


class P4aState:
    def gpsimd(self, ctx, g):
        sel4 = ctx.t["sel4"]
        g.memset(sel4[:, :], 1.0)
        g.drain()
        g.affine_select(out=sel4[:, :], in_=sel4[:, :], compare_op=ALU.is_ge,
                        fill=0.0, base=0, pattern=[[1, 512]], channel_multiplier=-128)
        g.drain()
        g.affine_select(out=sel4[:, :], in_=sel4[:, :], compare_op=ALU.is_ge,
                        fill=0.0, base=127, pattern=[[-1, 512]], channel_multiplier=128)
        g.drain()
        return None

    def tensor(self, ctx, t):
        s_bd = ctx.sem("p4a_bd")
        s_mm = ctx.sem("p4a_mm")
        s_rec = ctx.sem("p4a_rec")
        s_alb = ctx.sem("p4a_alb")
        for c in range(NCH):
            p = ctx.t["pAcsB0"] if c % 2 == 0 else ctx.t["pAcsB1"]
            if c >= 2:
                t.wait_ge(s_alb, c - 1)
            t.matmul(p[:, 0:64], ctx.t["ones_bf"][0:1, 0:128],
                     ctx.t["alast_erow"][0:1, c * 64:(c + 1) * 64],
                     start=True, stop=True).then_inc(ctx.sem("p4a_albtp"), 1)
        xdtT = ctx.t["xdtT"]
        bdec = ctx.t["bdec"]
        for idx in range(NCH * NH):
            c, h = idx // NH, idx % NH
            slot = idx % 2
            pS = ctx.t["pS0"] if slot == 0 else ctx.t["pS1"]
            t.wait_ge(s_bd, idx + 1)
            if idx >= 2:
                t.wait_ge(s_rec, idx - 1)
            for tb in (2 * c, 2 * c + 1):
                mm = t.matmul(pS, bdec[:, slot * 256 + (tb % 2) * 128:slot * 256 + (tb % 2) * 128 + 128],
                              xdtT[:, tb * DIN + h * 64:tb * DIN + h * 64 + 64],
                              start=(tb % 2 == 0), stop=(tb % 2 == 1))
            mm.then_inc(s_mm, 1)
        return None

    def vector(self, ctx, v):
        s_bd = ctx.sem("p4a_bd")
        s_mm = ctx.sem("p4a_mm")
        s_rec = ctx.sem("p4a_rec")
        s_alb = ctx.sem("p4a_alb")
        BT = ctx.t["BT"]
        decayT = ctx.t["decayT"]
        bdec = ctx.t["bdec"]
        S_run = ctx.t["S_run"]
        for c in range(NCH):
            p = ctx.t["pAcsB0"] if c % 2 == 0 else ctx.t["pAcsB1"]
            v.wait_ge(ctx.sem("p4a_albtp"), c + 1)
            v.tensor_copy(ctx.t["expAlB"][:, c * 64:(c + 1) * 64], p[:, 0:64])
            dinc(v, s_alb)
        for idx in range(NCH * NH):
            c, h = idx // NH, idx % NH
            slot = idx % 2
            pS = ctx.t["pS0"] if slot == 0 else ctx.t["pS1"]
            if idx >= 2:
                v.wait_ge(s_mm, idx - 1)
            for tb in (2 * c, 2 * c + 1):
                v.tensor_scalar(bdec[:, slot * 256 + (tb % 2) * 128:slot * 256 + (tb % 2) * 128 + 128],
                                BT[:, tb * 128:(tb + 1) * 128],
                                decayT[:, tb * 64 + h:tb * 64 + h + 1], None, op0=ALU.mult)
            dinc(v, s_bd)
            v.wait_ge(s_mm, idx + 1)
            sh = S_run[:, h * 32:(h + 1) * 32].bitcast(BF16) if False else None
            srun_h = ctx.t["S_run"][:, h * 64:(h + 1) * 64]
            if c == 0:
                v.tensor_copy(srun_h, pS)
            else:
                stmp = ctx.t["stmpA"]
                v.tensor_scalar(stmp[:, :], srun_h, ctx.t["expAlB"][:, c * 64 + h:c * 64 + h + 1],
                                None, op0=ALU.mult)
                v.drain()
                v.tensor_tensor(srun_h, stmp[:, :], pS, op=ALU.add)
            dinc(v, s_rec)
        return None


class P4xExchange:
    def gpsimd(self, ctx, g):
        sem = ctx.sem("p4x_dma")
        sem2 = ctx.sem("p4x_dma2")
        cc = ctx.sem("p4x_cc")
        g.dma_start(out=ctx.t["stout"].ap(), in_=ctx.t["S_run"][:, :]).then_inc(sem, 16)
        g.wait_ge(sem, 16)
        g.collective_compute(
            "AllGather", ALU.bypass,
            replica_groups=PAIRS,
            ins=[ctx.t["stout"].ap().opt()],
            outs=[ctx.t["stgath"].ap().opt()],
        ).then_inc(cc, 1)
        g.wait_ge(cc, 1)
        g.dma_start(out=ctx.t["S_run"][:, :], in_=ctx.t["stgath"].ap()[0:128, :]).then_inc(sem2, 16)
        g.wait_ge(sem2, 16)
        g.sem_inc(ctx.sem("p4x_ld"), 1)
        return None

    def vector(self, ctx, v):
        S_run = ctx.t["S_run"]
        role = ctx.t["roleflag"]
        bnc = ctx.t["roleb"]
        v.wait_ge(ctx.sem("p4x_ld"), 1)
        for half in range(2):
            v.tensor_copy(bnc[:, 0:2048], S_run[:, half * 2048:(half + 1) * 2048])
            v.drain()
            v.tensor_scalar(S_run[:, half * 2048:(half + 1) * 2048], bnc[:, 0:2048],
                            role[:, 0:1], None, op0=ALU.mult)
            v.drain()
        return None


class P4bSSD:

    def tensor(self, ctx, t):
        ident32 = ctx.t["ident_f32"]
        identb = ctx.t["ident_bf"]
        acsT = ctx.t["acsT"]
        xdtT = ctx.t["xdtT"]
        MT = ctx.t["MT"]
        bdec = ctx.t["bdec"]
        S_run = ctx.t["S_run"]
        sel4 = ctx.t["sel4"]
        s_gm = ctx.sem("p4b_gm")
        s_gtp = ctx.sem("p4b_gtp")
        s_rtp = ctx.sem("p4b_rtp")
        s_rowv = ctx.sem("p4b_rowv")
        s_hl = ctx.sem("p4b_hl")
        s_acsbtp = ctx.sem("p4b_acsbtp")
        s_acsb = ctx.sem("p4b_acsb")
        s_mtg = ctx.sem("p4b_mtg")
        s_mtv = ctx.sem("p4b_mtv")
        s_ymm = ctx.sem("p4b_ymm")
        s_yev = ctx.sem("p4b_yev")
        s_smm = ctx.sem("p4b_smm")
        s_rec = ctx.sem("p4b_rec")

        def ybl(k):
            c2, h2 = k // NH, k % NH
            sl = k % 2
            t.wait_ge(s_mtg, k + 1)
            t.wait_ge(s_mtv, k + 1)
            if k >= 2:
                t.wait_ge(s_yev, k - 1)
            pY = ctx.t["pY0"] if sl == 0 else ctx.t["pY1"]
            t.matmul(pY, xdtT[:, (2 * c2) * DIN + h2 * 64:(2 * c2) * DIN + h2 * 64 + 64],
                     MT[:, sl * 512:sl * 512 + 256], start=True, stop=False)
            t.matmul(pY, xdtT[:, (2 * c2 + 1) * DIN + h2 * 64:(2 * c2 + 1) * DIN + h2 * 64 + 64],
                     MT[:, sl * 512 + 256:sl * 512 + 512], start=False, stop=False)
            t.matmul(pY, S_run[:, h2 * 64:(h2 + 1) * 64],
                     ctx.t["csdo"][:, sl * 256:sl * 256 + 256], start=False, stop=False)
            t.matmul(pY[0:64, 0:128], ctx.t["xdtmp"][:, sl * 128:sl * 128 + 64],
                     identb[:, :], start=False, stop=False)
            t.matmul(pY[0:64, 128:256], ctx.t["xdtmp"][:, sl * 128 + 64:sl * 128 + 128],
                     identb[:, :], start=False, stop=True).then_inc(s_ymm, 1)
            if k >= 2:
                t.wait_ge(s_rec, k - 1)
            pS = ctx.t["pS0"] if sl == 0 else ctx.t["pS1"]
            for tb in (2 * c2, 2 * c2 + 1):
                mm = t.matmul(pS, bdec[:, sl * 256 + (tb % 2) * 128:sl * 256 + (tb % 2) * 128 + 128],
                              xdtT[:, tb * DIN + h2 * 64:tb * DIN + h2 * 64 + 64],
                              start=(tb % 2 == 0), stop=(tb % 2 == 1))
            mm.then_inc(s_smm, 1)

        for i in range(NCH * NH):
            c, h = i // NH, i % NH
            slot = i % 2
            p = i // 2
            if h == 0:
                if c > 0:
                    t.wait_ge(s_rowv, c * 32)
                for st in range(2):
                    pG = ctx.t["pG0"] if st == 0 else ctx.t["pG1"]
                    t.matmul(pG, ctx.t["Bfeat"][:, c * CH + st * 128:c * CH + st * 128 + 128],
                             ctx.t["Cfeat"][:, c * CH:(c + 1) * CH],
                             start=True, stop=True).then_inc(s_gtp, 1)
            if h % 2 == 0:
                t.wait_ge(s_gm, 2 * (c + 1))
                if p >= 2:
                    t.wait_ge(s_rowv, p - 1)
                pR = ctx.t["pRT0"] if p % 2 == 0 else ctx.t["pRT1"]
                t.matmul(pR, acsT[:, c * 128 + 2 * h:c * 128 + 2 * h + 4], ident32[:, :],
                         start=True, stop=True).then_inc(s_rtp, 1)
            t.wait_ge(s_hl, p + 1)
            if i >= 2:
                t.wait_ge(s_acsb, i - 1)
            pB_ = ctx.t["pAcsB0"] if slot == 0 else ctx.t["pAcsB1"]
            hiB = ctx.t["hiB0"] if p % 2 == 0 else ctx.t["hiB1"]
            loB = ctx.t["loB0"] if p % 2 == 0 else ctx.t["loB1"]
            hp2 = h % 2
            r0, r1 = 2 * hp2, 2 * hp2 + 1
            t.matmul(pB_[:, 0:128], sel4[:, r0 * 128:r0 * 128 + 128], hiB,
                     start=True, stop=False)
            t.matmul(pB_[:, 0:128], sel4[:, r0 * 128:r0 * 128 + 128], loB,
                     start=False, stop=True)
            t.matmul(pB_[:, 128:256], sel4[:, r1 * 128:r1 * 128 + 128], hiB,
                     start=True, stop=False)
            t.matmul(pB_[:, 128:256], sel4[:, r1 * 128:r1 * 128 + 128], loB,
                     start=False, stop=True).then_inc(s_acsbtp, 1)
            if i >= 2:
                ybl(i - 2)
        ybl(NCH * NH - 2)
        ybl(NCH * NH - 1)
        return None

    def gpsimd(self, ctx, g):
        MT = ctx.t["MT"]
        expLmin = ctx.t["expLmin"]
        csdo = ctx.t["csdo"]
        sdo = ctx.t["sdo_bf"]
        Cfeat = ctx.t["Cfeat"]
        s_mtv = ctx.sem("p4b_mtv")
        s_gm = ctx.sem("p4b_gm")
        s_mtg = ctx.sem("p4b_mtg")
        for j in range(NCH * NH):
            sl = j % 2
            c2 = j // NH
            g.wait_ge(s_mtv, j + 1)
            g.wait_ge(s_gm, 2 * (c2 + 1))
            Gmb = ctx.t["Gm"] if c2 % 2 == 0 else ctx.t["GmB"]
            g.tensor_tensor(MT[:, sl * 512:sl * 512 + 256],
                            expLmin[:, sl * 512:sl * 512 + 256],
                            Gmb[:, 0:256], op=ALU.mult)
            g.tensor_tensor(MT[:, sl * 512 + 256:sl * 512 + 512],
                            expLmin[:, sl * 512 + 256:sl * 512 + 512],
                            Gmb[:, 256:512], op=ALU.mult)
            g.tensor_tensor(csdo[:, sl * 256:sl * 256 + 256],
                            Cfeat[:, c2 * CH:(c2 + 1) * CH],
                            sdo[:, (j % 4) * 256:(j % 4) * 256 + 256],
                            op=ALU.mult).then_inc(s_mtg, 1)
        return None

    def scalar(self, ctx, s):
        acsT = ctx.t["acsT"]
        expL = ctx.t["expL"]
        sdo = ctx.t["sdo_bf"]
        S_run = ctx.t["S_run"]
        expAlB = ctx.t["expAlB"]
        y = ctx.t["y_feat"]
        s_acsbtp = ctx.sem("p4b_acsbtp")
        s_mtg = ctx.sem("p4b_mtg")
        s_expl = ctx.sem("p4b_expl")
        s_acsb = ctx.sem("p4b_acsb")
        s_stm = ctx.sem("p4b_stm")
        s_rec = ctx.sem("p4b_rec")
        s_ymm = ctx.sem("p4b_ymm")
        s_yev = ctx.sem("p4b_yev")

        def yev(k):
            c2, h2 = k // NH, k % NH
            sl = k % 2
            s.wait_ge(s_ymm, k + 1)
            pY = ctx.t["pY0"] if sl == 0 else ctx.t["pY1"]
            r0 = (h2 % 2) * 64
            s.copy(y[r0:r0 + 64, (h2 // 2) * T + c2 * CH:(h2 // 2) * T + (c2 + 1) * CH], pY)
            s.drain()
            s.sem_inc(s_yev, 1)

        for i in range(NCH * NH):
            c, h = i // NH, i % NH
            slot = i % 2
            pB_ = ctx.t["pAcsB0"] if slot == 0 else ctx.t["pAcsB1"]
            s.wait_ge(s_acsbtp, i + 1)
            if i >= 2:
                s.wait_ge(ctx.sem("p4b_mtv"), i - 1)
            if i >= 4:
                s.wait_ge(s_mtg, i - 3)
            s.activation(expL[:, slot * 512:slot * 512 + 256], pB_, AF.Exp,
                         scale=-1.0, bias=acsT[:, c * 128 + 2 * h:c * 128 + 2 * h + 1])
            s.activation(expL[:, slot * 512 + 256:slot * 512 + 512], pB_, AF.Exp,
                         scale=-1.0, bias=acsT[:, c * 128 + 2 * h + 1:c * 128 + 2 * h + 2])
            s.activation(sdo[:, (i % 4) * 256:(i % 4) * 256 + 256], pB_, AF.Exp,
                         scale=-1.0)
            if c < NCH - 1:
                if i >= 4:
                    s.wait_ge(s_rec, i - 3)
                sl4 = (i % 4) * 64
                s.activation(ctx.t["stmp4"][:, sl4:sl4 + 64],
                             S_run[:, h * 64:(h + 1) * 64], AF.Copy,
                             scale=expAlB[:, c * 64 + h:c * 64 + h + 1])
            s.drain()
            s.sem_inc(s_expl, 1)
            s.sem_inc(s_acsb, 1)
            s.sem_inc(s_stm, 1)
            if i >= 2:
                yev(i - 2)
        yev(NCH * NH - 2)
        yev(NCH * NH - 1)
        return None

    def vector(self, ctx, v):
        sel4 = ctx.t["sel4"]
        S_run = ctx.t["S_run"]
        BT = ctx.t["BT"]
        decayT = ctx.t["decayT"]
        bdec = ctx.t["bdec"]
        xdtT = ctx.t["xdtT"]
        s_gtp = ctx.sem("p4b_gtp")
        s_gm = ctx.sem("p4b_gm")
        s_rtp = ctx.sem("p4b_rtp")
        s_rowv = ctx.sem("p4b_rowv")
        s_hl = ctx.sem("p4b_hl")
        s_acsbtp = ctx.sem("p4b_acsbtp")
        s_mtg = ctx.sem("p4b_mtg")
        s_mtv = ctx.sem("p4b_mtv")
        s_smm = ctx.sem("p4b_smm")
        s_stm = ctx.sem("p4b_stm")
        s_rec = ctx.sem("p4b_rec")
        def rec(k):
            c2, h2 = k // NH, k % NH
            if c2 < NCH - 1:
                sl4 = (k % 4) * 64
                pS = ctx.t["pS0"] if k % 2 == 0 else ctx.t["pS1"]
                v.wait_ge(s_stm, k + 1)
                v.wait_ge(s_smm, k + 1)
                v.tensor_tensor(S_run[:, h2 * 64:(h2 + 1) * 64],
                                ctx.t["stmp4"][:, sl4:sl4 + 64], pS, op=ALU.add)
                v.drain()
            v.sem_inc(s_rec, 1)

        for c in range(NCH):
            Gmb = ctx.t["Gm"] if c % 2 == 0 else ctx.t["GmB"]
            if c >= 2:
                v.wait_ge(s_mtg, (c - 1) * NH)
            for st in range(2):
                pG = ctx.t["pG0"] if st == 0 else ctx.t["pG1"]
                v.wait_ge(s_gtp, 2 * c + st + 1)
                v.tensor_tensor(Gmb[:, st * CH:(st + 1) * CH], pG,
                                ctx.t["gmaskA" if st == 0 else "gmaskB"][:, :], op=ALU.mult)
            v.drain()
            v.sem_inc(s_gm, 2)
            for h in range(NH):
                i = c * NH + h
                slot = i % 2
                p = i // 2
                if h % 2 == 0:
                    v.wait_ge(s_rtp, p + 1)
                    if p >= 2:
                        v.wait_ge(s_acsbtp, 2 * p - 2)
                    pR = ctx.t["pRT0"] if p % 2 == 0 else ctx.t["pRT1"]
                    rowF = ctx.t["rowF0"] if p % 2 == 0 else ctx.t["rowF1"]
                    hiB = ctx.t["hiB0"] if p % 2 == 0 else ctx.t["hiB1"]
                    loB = ctx.t["loB0"] if p % 2 == 0 else ctx.t["loB1"]
                    v.tensor_copy(rowF, pR)
                    v.tensor_copy(hiB, pR)
                    v.drain()
                    v.sem_inc(s_rowv, 1)
                    v.tensor_tensor(loB, rowF, hiB, op=ALU.subtract)
                    v.drain()
                    v.sem_inc(s_hl, 1)
                if i >= 2:
                    v.wait_ge(s_smm, i - 1)
                v.wait_ge(ctx.sem("p4b_expl"), i + 1)
                v.tensor_scalar(ctx.t["expLmin"][:, slot * 512:slot * 512 + 512],
                                ctx.t["expL"][:, slot * 512:slot * 512 + 512],
                                1.0, None, op0=ALU.min)
                for tb in range(2):
                    v.tensor_scalar(ctx.t["xdtmp"][:, slot * 128 + tb * 64:slot * 128 + tb * 64 + 64],
                                    xdtT[:, (2 * c + tb) * DIN + h * 64:(2 * c + tb) * DIN + h * 64 + 64],
                                    ctx.t["Dinvdt"][:, (2 * c + tb) * 64 + h:(2 * c + tb) * 64 + h + 1],
                                    None, op0=ALU.mult)
                for tb in (2 * c, 2 * c + 1):
                    v.tensor_scalar(bdec[:, slot * 256 + (tb % 2) * 128:slot * 256 + (tb % 2) * 128 + 128],
                                    BT[:, tb * 128:(tb + 1) * 128],
                                    decayT[:, tb * 64 + h:tb * 64 + h + 1], None, op0=ALU.mult)
                v.drain()
                v.sem_inc(s_mtv, 1)
                if i >= 2:
                    rec(i - 2)
        rec(NCH * NH - 2)
        rec(NCH * NH - 1)
        return None


P5_CHUNKS = [(512 * i, 512) for i in range(8)]


class P5Gate:

    def groups(self):
        idx = 0
        for ci, (e0, ew) in enumerate(P5_CHUNKS):
            for j in range(4):
                for tc in range(2):
                    yield idx, ci, ew, j, ci * 4 + j, tc
                    idx += 1

    def gpsimd(self, ctx, g):
        w = ctx.t["W_in"]
        s_mm = ctx.sem("p5_mm")
        for ci, (e0, ew) in enumerate(P5_CHUNKS):
            if ci >= 2:
                g.wait_ge(s_mm, 8 * (ci - 1))
            slab = ctx.t["slabA"] if ci % 2 == 0 else ctx.t["slabB"]
            s_wl = ctx.sem("p5_wlA" if ci % 2 == 0 else "p5_wlB")
            src = w.ap()[:, e0:e0 + ew].rearrange("(k p) e -> p k e", p=128)
            dst = slab[:, 0:16 * ew].rearrange("p (k e) -> p k e", k=16)
            g.dma_start(out=dst, in_=src).then_inc(s_wl, 16)
        g.wait_ge(s_mm, 64)
        wo = ctx.t["W_out_d"]
        qi = 0
        for th in range(2):
            for oh in range(2):
                for et in range(32):
                    if qi >= 2:
                        g.wait_ge(ctx.sem("p5_wou"), qi - 1)
                    slab = ctx.t["slabC"] if qi % 2 == 0 else ctx.t["slabD"]
                    s_wl = ctx.sem("p5_woA" if qi % 2 == 0 else "p5_woB")
                    src = wo.ap()[et * 128:(et + 1) * 128, oh * 1024:(oh + 1) * 1024]
                    g.dma_start(out=slab[:, 0:1024], in_=src).then_inc(s_wl, 16)
                    qi += 1
        return None

    def tensor(self, ctx, t):
        xn = ctx.t["xn"]
        s_silu = ctx.sem("p5_silu")
        s_mm = ctx.sem("p5_mm")
        s_sq = ctx.sem("p5_sq")
        last_ci = -1
        for idx, ci, ew, j, jglob, tc in self.groups():
            if ci != last_ci:
                t.wait_ge(ctx.sem("p5_wlA" if ci % 2 == 0 else "p5_wlB"),
                          16 * (ci // 2 + 1))
                last_ci = ci
            slab = ctx.t["slabA"] if ci % 2 == 0 else ctx.t["slabB"]
            p = ctx.t["pA"] if idx % 2 == 0 else ctx.t["pB"]
            if idx >= 2:
                t.wait_ge(s_silu, idx - 1)
            for k in range(16):
                mm = t.matmul(p, slab[:, k * ew + j * 128:k * ew + j * 128 + 128],
                              xn[:, k * TF + HALO + tc * 512:k * TF + HALO + tc * 512 + 512],
                              start=(k == 0), stop=(k == 15))
            mm.then_inc(s_mm, 1)
            if idx >= 2:
                self._sq_mm(ctx, t, idx - 2)
        self._sq_mm(ctx, t, 62)
        self._sq_mm(ctx, t, 63)
        t.wait_ge(ctx.sem("p5_rrow"), 1)
        for tc in range(2):
            t.matmul(ctx.t["bank0" if tc == 0 else "bank1"].ap()[:, 0:512],
                     ctx.t["ones_bf"][0:1, 0:128],
                     ctx.t["rrow_bf"][0:1, tc * 512:tc * 512 + 512],
                     start=True, stop=True).then_inc(ctx.sem("p5_rbtp"), 1)
        t.wait_ge(ctx.sem("p5_fin"), 32)
        banks = [ctx.t[f"bank{i}"] for i in range(8)]
        yfin = ctx.t["y_feat"]
        qi = 0
        for th in range(2):
            for oh in range(2):
                q = th * 2 + oh
                if q >= 1:
                    t.wait_ge(ctx.sem("p5_wev"), q)
                for et in range(32):
                    t.wait_ge(ctx.sem("p5_woA" if qi % 2 == 0 else "p5_woB"),
                              16 * (qi // 2 + 1))
                    slab = ctx.t["slabC"] if qi % 2 == 0 else ctx.t["slabD"]
                    for tt in range(4):
                        for oc in range(2):
                            mm = t.matmul(banks[tt * 2 + oc].ap(),
                                          yfin[:, et * T + th * 512 + tt * 128:et * T + th * 512 + tt * 128 + 128],
                                          slab[:, oc * 512:oc * 512 + 512],
                                          start=(et == 0), stop=(et == 31))
                    mm.then_inc(ctx.sem("p5_wou"), 1)
                    qi += 1
        return None

    def _sq_mm(self, ctx, t, idx):
        jglob, tc = idx // 2, idx % 2
        t.wait_ge(ctx.sem("p5_sq"), idx + 1)
        psq = ctx.t["bank2"].ap() if tc == 0 else ctx.t["bank3"].ap()
        t.matmul(psq[0:1, 0:512], ctx.t["ones_bf"][:, 0:1],
                 ctx.t["sqbuf"][:, (idx % 2) * 512:(idx % 2) * 512 + 512],
                 start=(jglob == 0), stop=(jglob == 31)).then_inc(ctx.sem("p5_sqmm"), 1)

    def scalar(self, ctx, s):
        s_mm = ctx.sem("p5_mm")
        for idx, ci, ew, j, jglob, tc in self.groups():
            p = ctx.t["pA"] if idx % 2 == 0 else ctx.t["pB"]
            s.wait_ge(s_mm, idx + 1)
            if idx >= 2:
                s.wait_ge(ctx.sem("p5_zt"), idx - 1)
            s.activation(ctx.t["ztmp"][:, (idx % 2) * 512:(idx % 2) * 512 + 512],
                         p, AF.Silu)
            dinc(s, ctx.sem("p5_silu"))
        s.wait_ge(ctx.sem("p5_sqmm"), 64)
        s.activation(ctx.t["rr_ln0"][0:1, 0:512], ctx.t["bank2"].ap()[0:1, 0:512], AF.Ln,
                     scale=1.0 / DIN, bias=ctx.constf_r(EPS, 1))
        s.activation(ctx.t["rr_ln1"][0:1, 0:512], ctx.t["bank3"].ap()[0:1, 0:512], AF.Ln,
                     scale=1.0 / DIN, bias=ctx.constf_r(EPS, 1))
        s.drain()
        s.activation(ctx.t["rrow_bf"][0:1, 0:512], ctx.t["rr_ln0"][0:1, 0:512],
                     AF.Exp, scale=-0.5)
        s.activation(ctx.t["rrow_bf"][0:1, 512:1024], ctx.t["rr_ln1"][0:1, 0:512],
                     AF.Exp, scale=-0.5)
        dinc(s, ctx.sem("p5_rrow"))
        return None

    def vector(self, ctx, v):
        y = ctx.t["y_feat"]
        for idx, ci, ew, j, jglob, tc in self.groups():
            v.wait_ge(ctx.sem("p5_silu"), idx + 1)
            ysl = y[:, jglob * T + tc * 512:jglob * T + tc * 512 + 512]
            zt = ctx.t["ztmp"][:, (idx % 2) * 512:(idx % 2) * 512 + 512]
            bounce = ctx.t["ztmp2"][:, (idx % 2) * 512:(idx % 2) * 512 + 512]
            v.tensor_tensor(bounce, ysl, zt, op=ALU.mult)
            dinc(v, ctx.sem("p5_zt"))
            if idx >= 2:
                v.wait_ge(ctx.sem("p5_sqmm"), idx - 1)
            v.tensor_tensor(ctx.t["sqbuf"][:, (idx % 2) * 512:(idx % 2) * 512 + 512],
                            bounce, bounce, op=ALU.mult)
            dinc(v, ctx.sem("p5_sq"))
            v.tensor_copy(ysl, bounce)
            dinc(v, ctx.sem("p5_yg"))
            v.drain()
        for tc in range(2):
            v.wait_ge(ctx.sem("p5_rbtp"), tc + 1)
            pb = ctx.t["bank0" if tc == 0 else "bank1"].ap()
            v.tensor_copy(ctx.t["rstd_b"][:, tc * 512:tc * 512 + 512], pb[:, 0:512])
        v.drain()
        for jglob in range(32):
            gs = y[:, jglob * T:(jglob + 1) * T]
            v.tensor_tensor(ctx.t["ztmp2"][:, 0:1024], gs, ctx.t["rstd_b"][:, 0:1024],
                            op=ALU.mult)
            v.drain()
            v.tensor_scalar(gs, ctx.t["ztmp2"][:, 0:1024],
                            ctx.t["norm_scale"][:, jglob:jglob + 1], None, op0=ALU.mult)
            dinc(v, ctx.sem("p5_fin"))
        banks = [ctx.t[f"bank{i}"] for i in range(8)]
        for q in range(4):
            v.wait_ge(ctx.sem("p5_wou"), 32 * (q + 1))
            for tt in range(4):
                k = q * 4 + tt
                sl = k % 2
                if k >= 2:
                    v.wait_ge(ctx.sem("p5_oA" if sl == 0 else "p5_oB"), 16 * (k // 2))
                for oc in range(2):
                    v.tensor_copy(ctx.t["stage2"][:, sl * 1024 + oc * 512:sl * 1024 + oc * 512 + 512],
                                  banks[tt * 2 + oc].ap())
                dinc(v, ctx.sem("p5_wst"))
            dinc(v, ctx.sem("p5_wev"))
        return None

    def sync(self, ctx, sy):
        ms = ctx.t["mamba_s"]
        for q in range(4):
            th, oh = q // 2, q % 2
            for tt in range(4):
                k = q * 4 + tt
                sl = k % 2
                sy.wait_ge(ctx.sem("p5_wst"), k + 1)
                dst = ms.ap()[th * 512 + tt * 128:th * 512 + tt * 128 + 128,
                              oh * 1024:(oh + 1) * 1024]
                sy.dma_start(out=dst, in_=ctx.t["stage2"][:, sl * 1024:sl * 1024 + 1024]) \
                    .then_inc(ctx.sem("p5_oA" if sl == 0 else "p5_oB"), 16)
        sy.wait_ge(ctx.sem("p5_oA"), 16 * 8)
        sy.wait_ge(ctx.sem("p5_oB"), 16 * 8)
        return None


class P6FFN0:

    def __init__(self, xn_name="xn", xn_off=HALO, out_name="ffn0_s",
                 wg="WgT_d", wu="WuT_d", wd="WdT_d", pfx="p6",
                 ne=FF0, gate_scale=None, up_scale=None, down_scale=None):
        self.xn_name, self.xn_off, self.out_name = xn_name, xn_off, out_name
        self.wg, self.wu, self.wd, self.pfx = wg, wu, wd, pfx
        self.ne = ne
        self.n_ch = ne // 512
        self.n_et = ne // 128
        self.gate_scale = gate_scale
        self.up_scale = up_scale
        self.down_scale = down_scale

    def gpsimd(self, ctx, g):
        x = self.pfx
        for tc in range(2):
            for ci in range(self.n_ch):
                qi = tc * self.n_ch + ci
                if qi >= 2:
                    g.wait_ge(ctx.sem(x + "_mm"), 4 * (qi - 1))
                sg = ctx.sem(x + ("_wgA" if qi % 2 == 0 else "_wgB"))
                su = ctx.sem(x + ("_wuA" if qi % 2 == 0 else "_wuB"))
                slg = ctx.t["slabA"] if qi % 2 == 0 else ctx.t["slabB"]
                slu = ctx.t["slabC2"] if qi % 2 == 0 else ctx.t["slabD2"]
                for wname, sl, sem in ((self.wg, slg, sg), (self.wu, slu, su)):
                    src = ctx.t[wname].ap()[:, ci * 512:(ci + 1) * 512] \
                        .rearrange("(k p) e -> p k e", p=128)
                    dst = sl[:, 0:8192].rearrange("p (k e) -> p k e", k=16)
                    g.dma_start(out=dst, in_=src).then_inc(sem, 16)
            g.wait_ge(ctx.sem(x + "_mm"), 4 * (tc + 1) * self.n_ch)
            di = 0
            for dh in range(2):
                for et in range(self.n_et):
                    qd = tc * 2 * self.n_et + dh * self.n_et + et
                    if di >= 2:
                        g.wait_ge(ctx.sem(x + "_du"), qd - 1)
                    sl = ctx.t["slabC"] if qd % 2 == 0 else ctx.t["slabD"]
                    sem = ctx.sem(x + ("_wdA" if qd % 2 == 0 else "_wdB"))
                    src = ctx.t[self.wd].ap()[et * 128:(et + 1) * 128,
                                              dh * 1024:(dh + 1) * 1024]
                    g.dma_start(out=sl[:, 0:1024], in_=src).then_inc(sem, 16)
                    di += 1
            if tc == 0:
                g.wait_ge(ctx.sem(x + "_du"), 2 * self.n_et)
        return None

    def tensor(self, ctx, t):
        x = self.pfx
        xn = ctx.t[self.xn_name]
        xw = TF if self.xn_name == "xn" else T
        banks = [ctx.t[f"bank{i}"] for i in range(8)]
        h1 = ctx.t["h1"]
        for tc in range(2):
            if tc > 0:
                t.wait_ge(ctx.sem(x + "_wev"), 2 * tc)
            for ci in range(self.n_ch):
                qi = tc * self.n_ch + ci
                t.wait_ge(ctx.sem(x + ("_wgA" if qi % 2 == 0 else "_wgB")),
                          16 * (qi // 2 + 1))
                t.wait_ge(ctx.sem(x + ("_wuA" if qi % 2 == 0 else "_wuB")),
                          16 * (qi // 2 + 1))
                slg = ctx.t["slabA"] if qi % 2 == 0 else ctx.t["slabB"]
                slu = ctx.t["slabC2"] if qi % 2 == 0 else ctx.t["slabD2"]
                for j in range(4):
                    gi = qi * 4 + j
                    pg = banks[gi % 2].ap()
                    pu = banks[2 + gi % 2].ap()
                    if gi >= 2:
                        t.wait_ge(ctx.sem(x + "_h1"), gi - 1)
                    for k in range(16):
                        t.matmul(pg, slg[:, k * 512 + j * 128:k * 512 + j * 128 + 128],
                                 xn[:, k * xw + self.xn_off + tc * 512:k * xw + self.xn_off + tc * 512 + 512],
                                 start=(k == 0), stop=(k == 15))
                        mm = t.matmul(pu, slu[:, k * 512 + j * 128:k * 512 + j * 128 + 128],
                                      xn[:, k * xw + self.xn_off + tc * 512:k * xw + self.xn_off + tc * 512 + 512],
                                      start=(k == 0), stop=(k == 15))
                    mm.then_inc(ctx.sem(x + "_mm"), 1)
            t.wait_ge(ctx.sem(x + "_h1"), (tc + 1) * self.n_ch * 4)
            for dh in range(2):
                dq = tc * 2 + dh
                if dq >= 1:
                    t.wait_ge(ctx.sem(x + "_wev"), dq)
                for et in range(self.n_et):
                    qd = tc * 2 * self.n_et + dh * self.n_et + et
                    t.wait_ge(ctx.sem(x + ("_wdA" if qd % 2 == 0 else "_wdB")),
                              16 * (qd // 2 + 1))
                    sl = ctx.t["slabC"] if qd % 2 == 0 else ctx.t["slabD"]
                    for tt in range(4):
                        for dc in range(2):
                            mm = t.matmul(banks[tt * 2 + dc].ap(),
                                          h1[:, et * 512 + tt * 128:et * 512 + tt * 128 + 128],
                                          sl[:, dc * 512:dc * 512 + 512],
                                          start=(et == 0), stop=(et == self.n_et - 1))
                    mm.then_inc(ctx.sem(x + "_du"), 1)
        return None

    def scalar(self, ctx, s):
        x = self.pfx
        banks = [ctx.t[f"bank{i}"] for i in range(8)]
        for tc in range(2):
            for gl in range(self.n_ch * 4):
                gi = tc * self.n_ch * 4 + gl
                s.wait_ge(ctx.sem(x + "_mm"), gi + 1)
                if gi >= 2:
                    s.wait_ge(ctx.sem(x + "_zt"), gi - 1)
                if self.gate_scale:
                    et_local = gi % self.n_et
                    s.activation(ctx.t["ztmp"][:, (gi % 2) * 512:(gi % 2) * 512 + 512],
                                 banks[gi % 2].ap(), AF.Silu,
                                 scale=ctx.t[self.gate_scale][:, et_local:et_local + 1])
                else:
                    s.activation(ctx.t["ztmp"][:, (gi % 2) * 512:(gi % 2) * 512 + 512],
                                 banks[gi % 2].ap(), AF.Silu)
                dinc(s, ctx.sem(x + "_silu"))
        return None

    def vector(self, ctx, v):
        x = self.pfx
        banks = [ctx.t[f"bank{i}"] for i in range(8)]
        h1 = ctx.t["h1"]
        for tc in range(2):
            for gl in range(self.n_ch * 4):
                gi = tc * self.n_ch * 4 + gl
                et_local = gi % (self.n_et)
                v.wait_ge(ctx.sem(x + "_silu"), gi + 1)
                dst = h1[:, et_local * 512:(et_local + 1) * 512]
                if self.up_scale:
                    v.scalar_tensor_tensor(dst, banks[2 + gi % 2].ap(),
                                           ctx.t[self.up_scale][:, et_local:et_local + 1],
                                           ctx.t["ztmp"][:, (gi % 2) * 512:(gi % 2) * 512 + 512],
                                           op0=ALU.mult, op1=ALU.mult)
                else:
                    v.tensor_tensor(dst, banks[2 + gi % 2].ap(),
                                    ctx.t["ztmp"][:, (gi % 2) * 512:(gi % 2) * 512 + 512],
                                    op=ALU.mult)
                dinc(v, ctx.sem(x + "_zt"))
                dinc(v, ctx.sem(x + "_h1"))
            for dh in range(2):
                dq = tc * 2 + dh
                v.wait_ge(ctx.sem(x + "_du"), self.n_et * (dq + 1))
                for tt in range(4):
                    k = dq * 4 + tt
                    sl = k % 2
                    if k >= 2:
                        v.wait_ge(ctx.sem(x + ("_oA" if sl == 0 else "_oB")), 16 * (k // 2))
                    for dc in range(2):
                        v.tensor_copy(ctx.t["stage2"][:, sl * 1024 + dc * 512:sl * 1024 + dc * 512 + 512],
                                      banks[tt * 2 + dc].ap())
                    dinc(v, ctx.sem(x + "_wst"))
                dinc(v, ctx.sem(x + "_wev"))
        return None

    def sync(self, ctx, sy):
        x = self.pfx
        out = ctx.t[self.out_name]
        for dq in range(4):
            tc, dh = dq // 2, dq % 2
            for tt in range(4):
                k = dq * 4 + tt
                sl = k % 2
                sy.wait_ge(ctx.sem(x + "_wst"), k + 1)
                dst = out.ap()[tc * 512 + tt * 128:tc * 512 + tt * 128 + 128,
                               dh * 1024:(dh + 1) * 1024]
                sy.dma_start(out=dst, in_=ctx.t["stage2"][:, sl * 1024:sl * 1024 + 1024]) \
                    .then_inc(ctx.sem(x + ("_oA" if sl == 0 else "_oB")), 16)
        sy.wait_ge(ctx.sem(x + "_oA"), 16 * 8)
        sy.wait_ge(ctx.sem(x + "_oB"), 16 * 8)
        return None


class P7Combine:
    def sync(self, ctx, sy):
        u = ctx.t["u_dram"]
        for tt in range(8):
            sl = tt % 2
            if tt >= 2:
                sy.wait_ge(ctx.sem("p7_used"), tt - 1)
            sy.dma_start(out=ctx.t["uR"][:, sl * D:(sl + 1) * D],
                         in_=u.ap()[HALO + tt * 128:HALO + tt * 128 + 128, :]) \
                .then_inc(ctx.sem("p7_uA" if sl == 0 else "p7_uB"), 16)
            sy.dma_start(out=ctx.t["mR"][:, sl * D:(sl + 1) * D],
                         in_=ctx.t["mamba_s"].ap()[tt * 128:(tt + 1) * 128, :]) \
                .then_inc(ctx.sem("p7_mA" if sl == 0 else "p7_mB"), 16)
            sy.dma_start(out=ctx.t["fR"][:, sl * D:(sl + 1) * D],
                         in_=ctx.t["ffn0_s"].ap()[tt * 128:(tt + 1) * 128, :]) \
                .then_inc(ctx.sem("p7_fA" if sl == 0 else "p7_fB"), 16)
            sy.wait_ge(ctx.sem("p7_x1"), tt + 1)
            sy.dma_start(out=ctx.t["x1_out"].ap()[tt * 128:(tt + 1) * 128, :],
                         in_=ctx.t["x1R"][:, sl * D:(sl + 1) * D]) \
                .then_inc(ctx.sem("p7_oA" if sl == 0 else "p7_oB"), 16)
        sy.wait_ge(ctx.sem("p7_oA"), 16 * 4)
        sy.wait_ge(ctx.sem("p7_oB"), 16 * 4)
        return None

    def vector(self, ctx, v):
        for tt in range(8):
            sl = tt % 2
            v.wait_ge(ctx.sem("p7_uA" if sl == 0 else "p7_uB"), 16 * (tt // 2 + 1))
            v.wait_ge(ctx.sem("p7_mA" if sl == 0 else "p7_mB"), 16 * (tt // 2 + 1))
            v.wait_ge(ctx.sem("p7_fA" if sl == 0 else "p7_fB"), 16 * (tt // 2 + 1))
            if tt >= 2:
                v.wait_ge(ctx.sem("p7_oA" if sl == 0 else "p7_oB"), 16 * (tt // 2))
            x1 = ctx.t["x1R"][:, sl * D:(sl + 1) * D]
            x2 = ctx.t["x2R"][:, sl * D:(sl + 1) * D]
            v.tensor_tensor(x2, ctx.t["uR"][:, sl * D:(sl + 1) * D],
                            ctx.t["mR"][:, sl * D:(sl + 1) * D], op=ALU.add)
            v.drain()
            v.tensor_tensor(x1, x2, ctx.t["fR"][:, sl * D:(sl + 1) * D], op=ALU.add)
            dinc(v, ctx.sem("p7_x1"))
            dinc(v, ctx.sem("p7_used"))
        return None


NQH = 16
NKH = 4
TT2 = 2048


class Q1LN:

    def sync(self, ctx, sy):
        x1 = ctx.t["x1_in"]
        s_u = ctx.sem("q1_ld")
        src = x1.ap().rearrange("(tt p) d -> p tt d", p=128)
        dst = ctx.t["u_tok8"].rearrange("p (tt d) -> p tt d", tt=8)
        sy.dma_start(out=dst, in_=src).then_inc(s_u, 16)
        return None

    def vector(self, ctx, v):
        u_tok = ctx.t["u_tok8"]
        xn_tok = ctx.t["xn1_tok"]
        aggr = ctx.t["ln_aggr"]
        rows = ctx.t["ln_rows"]
        sq = ctx.t["ln_sq"]
        v.wait_ge(ctx.sem("q1_ld"), 16)
        for tt in range(8):
            ut = u_tok[:, tt * D:(tt + 1) * D]
            a = aggr[:, 2 * tt:2 * tt + 2]
            r = rows[:, 4 * tt:4 * tt + 4]
            v.reduce_sum(r[:, 2:3], ut, axis=AX.X)
            v.wait_ge(ctx.sem("q1_sq"), tt + 1)
            v.reduce_sum(r[:, 3:4], sq[:, :], axis=AX.X)
            dinc(v, ctx.sem("q1_sqfree"))
            v.drain()
            v.tensor_scalar(a[:, 0:1], r[:, 2:3], 1.0 / D, None, op0=ALU.mult)
            v.tensor_scalar(r[:, 3:4], r[:, 3:4], 1.0 / D, None, op0=ALU.mult)
            v.drain()
            v.tensor_tensor(a[:, 1:2], a[:, 0:1], a[:, 0:1], op=ALU.mult)
            v.drain()
            v.tensor_tensor(a[:, 1:2], r[:, 3:4], a[:, 1:2], op=ALU.subtract)
            dinc(v, ctx.sem("q1_var"))
        tmp2 = ctx.t["ln_tmp2b"]
        for tt in range(8):
            ut = u_tok[:, tt * D:(tt + 1) * D]
            a = aggr[:, 2 * tt:2 * tt + 2]
            r = rows[:, 4 * tt:4 * tt + 4]
            v.wait_ge(ctx.sem("q1_rstd"), tt + 1)
            v.tensor_scalar(tmp2[:, :], ut, a[:, 0:1], r[:, 0:1],
                            op0=ALU.subtract, op1=ALU.mult)
            v.drain()
            v.tensor_copy(xn_tok[:, tt * D:(tt + 1) * D], tmp2[:, :])
            dinc(v, ctx.sem("q1_xn"))
        return None

    def scalar(self, ctx, s):
        u_tok = ctx.t["u_tok8"]
        sq = ctx.t["ln_sq"]
        aggr = ctx.t["ln_aggr"]
        rows = ctx.t["ln_rows"]
        s.wait_ge(ctx.sem("q1_ld"), 16)
        for tt in range(8):
            if tt > 0:
                s.wait_ge(ctx.sem("q1_sqfree"), tt)
            s.activation(sq[:, :], u_tok[:, tt * D:(tt + 1) * D], AF.Square)
            dinc(s, ctx.sem("q1_sq"))
        for tt in range(8):
            a = aggr[:, 2 * tt:2 * tt + 2]
            r = rows[:, 4 * tt:4 * tt + 4]
            s.wait_ge(ctx.sem("q1_var"), tt + 1)
            s.activation(r[:, 1:2], a[:, 1:2], AF.Ln, bias=ctx.constf(EPS))
            s.drain()
            s.activation(r[:, 0:1], r[:, 1:2], AF.Exp, scale=-0.5)
            dinc(s, ctx.sem("q1_rstd"))
        return None

    def gpsimd(self, ctx, g):
        g.wait_ge(ctx.sem("q1_xn"), 8)
        src = ctx.t["xn1_tok"].rearrange("p (tt d) -> p tt d", tt=8)
        dst = ctx.t["xn1b"].ap().rearrange("(tt p) d -> p tt d", p=128)
        g.dma_start(out=dst, in_=src).then_inc(ctx.sem("q1_bd"), 16)
        g.wait_ge(ctx.sem("q1_bd"), 16)
        g.collective_compute("AllGather", ALU.bypass, replica_groups=PAIRS,
                             ins=[ctx.t["xn1b"].ap().opt()],
                             outs=[ctx.t["xn1g"].ap().opt()]).then_inc(ctx.sem("q1_cc"), 1)
        g.wait_ge(ctx.sem("q1_cc"), 1)
        return None


class Q2QKV:

    def sync(self, ctx, sy):
        xg = ctx.t["xn1g"]
        for k in range(16):
            sy.dma_start(out=ctx.t["xnF"][:, k * TT2:(k + 1) * TT2],
                         in_=xg.ap()[:, k * 128:(k + 1) * 128],
                         transpose=True).then_inc(ctx.sem("q2_xf"), 16)
        return None

    def gpsimd(self, ctx, g):
        s_w = ctx.sem("q2_w")
        offs = [0, 8192, 16384, 20480]
        for i, (wname, w) in enumerate([("WqT_d", 512), ("WqT2", 512),
                                        ("WkT_d", 256), ("WvT_d", 256)]):
            src_t = ctx.t["WqT_d"] if wname in ("WqT_d", "WqT2") else ctx.t[wname]
            off = 512 if wname == "WqT2" else 0
            src = src_t.ap()[:, off:off + w].rearrange("(k p) e -> p k e", p=128)
            dst = ctx.t["qkvslab"][:, offs[i]:offs[i] + 16 * w] \
                .rearrange("p (k e) -> p k e", k=16)
            g.dma_start(out=dst, in_=src).then_inc(s_w, 16)
        return None

    def tensor(self, ctx, t):
        xnF = ctx.t["xnF"]
        t.wait_ge(ctx.sem("q2_xf"), 16 * 16)
        t.wait_ge(ctx.sem("q2_w"), 16 * 4)
        mmidx = 0
        offs = [0, 8192, 16384, 20480]
        for ci, w in [(0, 512), (1, 512), (2, 256), (3, 256)]:
            for tt in range(16):
                p = ctx.t["pA"] if mmidx % 2 == 0 else ctx.t["pB"]
                if mmidx >= 2:
                    t.wait_ge(ctx.sem("q2_ev"), mmidx - 1)
                for k in range(16):
                    mm = t.matmul(p[:, 0:w],
                                  xnF[:, k * TT2 + tt * 128:k * TT2 + tt * 128 + 128],
                                  ctx.t["qkvslab"][:, offs[ci] + k * w:offs[ci] + k * w + w],
                                  start=(k == 0), stop=(k == 15))
                mm.then_inc(ctx.sem("q2_mm"), 1)
                mmidx += 1
        t.wait_ge(ctx.sem("q2_rope"), 48)
        kb = 0
        for srcn, dstn, ntiles in (("Qtok", "Qf", 8), ("Ktok", "Kf", 2)):
            src = ctx.t[srcn]
            nw = ntiles * 128
            for tt in range(16):
                for j in range(ntiles):
                    pt = ctx.t["pt0"] if kb % 2 == 0 else ctx.t["pt1"]
                    if kb >= 2:
                        t.wait_ge(ctx.sem("q2_tev"), kb - 1)
                    t.transpose(pt[:, :], src[:, tt * nw + j * 128:tt * nw + j * 128 + 128],
                                ctx.t["ident_bf"][:, :]).then_inc(ctx.sem("q2_tp"), 1)
                    kb += 1
        return None

    def scalar(self, ctx, s):
        return None

    def vector(self, ctx, v):
        ce8, co8, se8, so8 = (ctx.t["ce8"], ctx.t["co8"], ctx.t["se8"], ctx.t["so8"])
        rt1, rt2 = ctx.t["ropetmp"][:, 0:256], ctx.t["ropetmp"][:, 256:512]
        for mmidx in range(64):
            p = ctx.t["pA"] if mmidx % 2 == 0 else ctx.t["pB"]
            v.wait_ge(ctx.sem("q2_mm"), mmidx + 1)
            if mmidx < 32:
                ci, tt = mmidx // 16, mmidx % 16
                w = 512
                dst = ctx.t["Qtok"][:, tt * 1024 + ci * 512:tt * 1024 + ci * 512 + 512]
            elif mmidx < 48:
                tt = mmidx % 16
                w = 256
                dst = ctx.t["Ktok"][:, tt * 256:tt * 256 + 256]
            else:
                tt = mmidx % 16
                w = 256
                dst = None
            if dst is not None:
                nh = w // 64
                tb = ce8[:, tt * 256:tt * 256 + 32 * nh]
                tbo = co8[:, tt * 256:tt * 256 + 32 * nh]
                tbs = se8[:, tt * 256:tt * 256 + 32 * nh]
                tbso = so8[:, tt * 256:tt * 256 + 32 * nh]
                pe = p[:, 0:w - 1:2]
                po = p[:, 1:w:2]
                v.tensor_tensor(rt1[:, 0:w // 2], pe, tb, op=ALU.mult)
                v.tensor_tensor(rt2[:, 0:w // 2], po, tbs, op=ALU.mult)
                v.drain()
                v.tensor_tensor(dst[:, 0:w - 1:2], rt1[:, 0:w // 2], rt2[:, 0:w // 2],
                                op=ALU.subtract)
                v.drain()
                v.tensor_tensor(rt1[:, 0:w // 2], po, tbo, op=ALU.mult)
                v.tensor_tensor(rt2[:, 0:w // 2], pe, tbso, op=ALU.mult)
                v.drain()
                v.tensor_tensor(dst[:, 1:w:2], rt1[:, 0:w // 2], rt2[:, 0:w // 2],
                                op=ALU.add)
                dinc(v, ctx.sem("q2_rope"))
            else:
                for h in range(4):
                    v.tensor_copy(ctx.t["Vtok"][:, tt * 260 + h * 65:tt * 260 + h * 65 + 64],
                                  p[:, h * 64:h * 64 + 64])
                dinc(v, ctx.sem("q2_rope"))
            dinc(v, ctx.sem("q2_ev"))
        for tt in range(16):
            for h in range(4):
                v.memset(ctx.t["Vtok"][:, tt * 260 + h * 65 + 64:tt * 260 + h * 65 + 65], 1.0)
        dinc(v, ctx.sem("q2_vones"))
        kb = 0
        for srcn, dstn, ntiles in (("Qtok", "Qf", 8), ("Ktok", "Kf", 2)):
            for tt in range(16):
                for j in range(ntiles):
                    pt = ctx.t["pt0"] if kb % 2 == 0 else ctx.t["pt1"]
                    v.wait_ge(ctx.sem("q2_tp"), kb + 1)
                    dst = ctx.t[dstn]
                    v.tensor_copy(dst[0:64, (2 * j) * TT2 + tt * 128:(2 * j) * TT2 + tt * 128 + 128],
                                  pt[0:64, :])
                    v.tensor_copy(dst[0:64, (2 * j + 1) * TT2 + tt * 128:(2 * j + 1) * TT2 + tt * 128 + 128],
                                  pt[64:128, :])
                    dinc(v, ctx.sem("q2_tev"))
                    kb += 1
        return None


class Q3Attn:

    def groups(self):
        out = []
        n = 0
        for qh in range(NQH):
            for qc in range(4):
                out.append((qh, qc, 4 * qc + 4, n))
                n += 4 * qc + 4
        return out

    def tensor(self, ctx, t):
        Qf, Kf, Vt = ctx.t["Qf"], ctx.t["Kf"], ctx.t["Vtok"]
        ones = ctx.t["ones_bf"]
        expS = ctx.t["expS"]
        for gi, (qh, qc, n_kt, n0) in enumerate(self.groups()):
            kvh = qh // 4
            qsl = Qf[0:64, qh * TT2 + qc * 512:qh * TT2 + qc * 512 + 512]
            for kt in range(n_kt):
                n = n0 + kt
                ps = ctx.t["pA"] if n % 2 == 0 else ctx.t["pB"]
                if n >= 2:
                    t.wait_ge(ctx.sem("q3_sev"), n - 1)
                ksl = Kf[0:64, kvh * TT2 + kt * 128:kvh * TT2 + kt * 128 + 128]
                t.matmul(ps, ksl, qsl, start=True, stop=True).then_inc(ctx.sem("q3_smm"), 1)
            pav = (ctx.t["bank4"] if gi % 2 == 0 else ctx.t["bank5"]).ap()[0:65, 0:512]
            if gi >= 2:
                t.wait_ge(ctx.sem("q3_aev"), gi - 1)
            ndiag = 0
            for kt in range(n_kt):
                t.wait_ge(ctx.sem("q3_sev"), n0 + kt + 1)
                if kt >= 4 * qc:
                    t.wait_ge(ctx.sem("q3_msk"), gi * 4 + (kt - 4 * qc) + 1)
                mm = t.matmul(pav, Vt[:, kt * 260 + (qh // 4) * 65:kt * 260 + (qh // 4) * 65 + 65],
                              expS[:, kt * 512:kt * 512 + 512],
                              start=(kt == 0), stop=(kt == n_kt - 1))
            mm.then_inc(ctx.sem("q3_avmm"), 1)
            t.wait_ge(ctx.sem("q3_den"), gi + 1)
            prb = (ctx.t["bank6"] if gi % 2 == 0 else ctx.t["bank7"]).ap()[0:64, 0:512]
            if gi >= 2:
                t.wait_ge(ctx.sem("q3_rbe"), gi - 1)
            t.matmul(prb, ones[0:1, 0:64], ctx.t["rrow1"][0:1, (gi % 2) * 512:(gi % 2) * 512 + 512],
                     start=True, stop=True).then_inc(ctx.sem("q3_rb"), 1)
        return None

    def scalar(self, ctx, s):
        expS = ctx.t["expS"]
        nbias = ctx.constf(-30.0)
        for gi, (qh, qc, n_kt, n0) in enumerate(self.groups()):
            for kt in range(n_kt):
                n = n0 + kt
                ps = ctx.t["pA"] if n % 2 == 0 else ctx.t["pB"]
                if gi >= 1:
                    s.wait_ge(ctx.sem("q3_avmm"), gi)
                s.wait_ge(ctx.sem("q3_smm"), n + 1)
                if kt >= 4 * qc:
                    m = kt - 4 * qc
                    if gi * 4 + m >= 2:
                        s.wait_ge(ctx.sem("q3_dtf"), gi * 4 + m - 1)
                    s.activation(ctx.t["dtmp"][:, (m % 2) * 512:(m % 2) * 512 + 512],
                                 ps, AF.Exp, scale=0.125, bias=nbias)
                    dinc(s, ctx.sem("q3_dex"))
                else:
                    s.activation(expS[:, kt * 512:kt * 512 + 512], ps, AF.Exp,
                                 scale=0.125, bias=nbias)
                    dinc(s, ctx.sem("q3_sev"))
            if gi >= 2:
                s.wait_ge(ctx.sem("q3_rb"), gi - 1)
            s.wait_ge(ctx.sem("q3_avmm"), gi + 1)
            pav = (ctx.t["bank4"] if gi % 2 == 0 else ctx.t["bank5"]).ap()
            s.activation(ctx.t["drow"][0:1, 0:512], pav[64:65, 0:512], AF.Ln)
            s.drain()
            s.activation(ctx.t["rrow1"][0:1, (gi % 2) * 512:(gi % 2) * 512 + 512],
                         ctx.t["drow"][0:1, 0:512], AF.Exp, scale=-1.0)
            dinc(s, ctx.sem("q3_den"))
        return None

    def vector(self, ctx, v):
        expS = ctx.t["expS"]
        attnO = ctx.t["attnO"]
        for gi, (qh, qc, n_kt, n0) in enumerate(self.groups()):
            for m in range(4):
                kt = 4 * qc + m
                v.wait_ge(ctx.sem("q3_dex"), gi * 4 + m + 1)
                v.tensor_tensor(expS[:, kt * 512:kt * 512 + 512],
                                ctx.t["dtmp"][:, (m % 2) * 512:(m % 2) * 512 + 512],
                                ctx.t["dmask"][:, m * 512:m * 512 + 512], op=ALU.mult)
                dinc(v, ctx.sem("q3_dtf"))
                dinc(v, ctx.sem("q3_sev"))
                dinc(v, ctx.sem("q3_msk"))
            v.wait_ge(ctx.sem("q3_rb"), gi + 1)
            prb = (ctx.t["bank6"] if gi % 2 == 0 else ctx.t["bank7"]).ap()[0:64, 0:512]
            v.tensor_copy(ctx.t["rBs"][0:64, 0:512], prb)
            dinc(v, ctx.sem("q3_rbe"))
            pav = (ctx.t["bank4"] if gi % 2 == 0 else ctx.t["bank5"]).ap()[0:64, 0:512]
            v.tensor_tensor(attnO[(qh % 2) * 64:(qh % 2) * 64 + 64,
                                  (qh // 2) * TT2 + qc * 512:(qh // 2) * TT2 + qc * 512 + 512],
                            pav, ctx.t["rBs"][0:64, 0:512], op=ALU.mult)
            dinc(v, ctx.sem("q3_aev"))
        return None


class Q4Exchange:
    def gpsimd(self, ctx, g):
        src = ctx.t["attnO"].rearrange("p (j t) -> p j t", j=8)
        dst = ctx.t["aOb"].ap().rearrange("(j p) t -> p j t", p=128)
        g.dma_start(out=dst, in_=src).then_inc(ctx.sem("q4_d"), 16)
        g.wait_ge(ctx.sem("q4_d"), 16)
        g.collective_compute("AllGather", ALU.bypass, replica_groups=PAIRS,
                             ins=[ctx.t["aOb"].ap().opt()],
                             outs=[ctx.t["aOg"].ap().opt()]).then_inc(ctx.sem("q4_cc"), 1)
        g.wait_ge(ctx.sem("q4_cc"), 1)
        for arm in g.switch_core_id(n=128):
            t0 = 0 if int(arm) % 2 == 0 else 1024
            src2 = ctx.t["aOg"].ap()[:, t0:t0 + T].rearrange("(j p) t -> p j t", p=128)
            dst2 = ctx.t["attnG"].rearrange("p (j t) -> p j t", j=16)
            g.dma_start(out=dst2, in_=src2).then_inc(ctx.sem("q4_d2"), 16)
        g.wait_ge(ctx.sem("q4_d2"), 16)
        return None


class Q5Wo:

    def sync(self, ctx, sy):
        for arm in sy.switch_core_id(n=128):
            t0 = 0 if int(arm) % 2 == 0 else 1024
            for k in range(16):
                sy.dma_start(out=ctx.t["xn1own"][:, k * T:(k + 1) * T],
                             in_=ctx.t["xn1g"].ap()[t0:t0 + T, k * 128:(k + 1) * 128],
                             transpose=True).then_inc(ctx.sem("q5_xo"), 16)
        sy.wait_ge(ctx.sem("q5_xo"), 16 * 16)
        return None

    def gpsimd(self, ctx, g):
        wo = ctx.t["WoT_d"]
        qi = 0
        for th in range(2):
            for oh in range(2):
                for et in range(16):
                    if qi >= 2:
                        g.wait_ge(ctx.sem("q5_wu"), qi - 1)
                    sl = ctx.t["woslabA"] if qi % 2 == 0 else ctx.t["woslabB"]
                    sem = ctx.sem("q5_wA" if qi % 2 == 0 else "q5_wB")
                    src = wo.ap()[et * 128:(et + 1) * 128, oh * 1024:(oh + 1) * 1024]
                    g.dma_start(out=sl[:, 0:1024], in_=src).then_inc(sem, 16)
                    qi += 1
        return None

    def tensor(self, ctx, t):
        banks = [ctx.t[f"bank{i}"] for i in range(8)]
        aG = ctx.t["attnG"]
        qi = 0
        for th in range(2):
            for oh in range(2):
                q = th * 2 + oh
                if q >= 1:
                    t.wait_ge(ctx.sem("q5_wev"), q)
                for et in range(16):
                    t.wait_ge(ctx.sem("q5_wA" if qi % 2 == 0 else "q5_wB"),
                              16 * (qi // 2 + 1))
                    sl = ctx.t["woslabA"] if qi % 2 == 0 else ctx.t["woslabB"]
                    for tt in range(4):
                        for oc in range(2):
                            mm = t.matmul(banks[tt * 2 + oc].ap(),
                                          aG[:, et * T + th * 512 + tt * 128:et * T + th * 512 + tt * 128 + 128],
                                          sl[:, oc * 512:oc * 512 + 512],
                                          start=(et == 0), stop=(et == 15))
                    mm.then_inc(ctx.sem("q5_wu"), 1)
                    qi += 1
        return None

    def vector(self, ctx, v):
        banks = [ctx.t[f"bank{i}"] for i in range(8)]
        for q in range(4):
            v.wait_ge(ctx.sem("q5_wu"), 16 * (q + 1))
            for tt in range(4):
                k = q * 4 + tt
                sl = k % 2
                if k >= 2:
                    v.wait_ge(ctx.sem("q5_oA" if sl == 0 else "q5_oB"), 16 * (k // 2))
                for oc in range(2):
                    v.tensor_copy(ctx.t["stage2"][:, sl * 1024 + oc * 512:sl * 1024 + oc * 512 + 512],
                                  banks[tt * 2 + oc].ap())
                dinc(v, ctx.sem("q5_wst"))
            dinc(v, ctx.sem("q5_wev"))
        return None

    def scalar(self, ctx, s):
        out = ctx.t["attn_s"]
        for q in range(4):
            th, oh = q // 2, q % 2
            for tt in range(4):
                k = q * 4 + tt
                sl = k % 2
                s.wait_ge(ctx.sem("q5_wst"), k + 1)
                dst = out.ap()[th * 512 + tt * 128:th * 512 + tt * 128 + 128,
                               oh * 1024:(oh + 1) * 1024]
                s.dma_start(out=dst, in_=ctx.t["stage2"][:, sl * 1024:sl * 1024 + 1024]) \
                    .then_inc(ctx.sem("q5_oA" if sl == 0 else "q5_oB"), 16)
        s.wait_ge(ctx.sem("q5_oA"), 16 * 8)
        s.wait_ge(ctx.sem("q5_oB"), 16 * 8)
        return None


class Q7Final:

    def tensor(self, ctx, t):
        for c in range(4):
            pb = (ctx.t["bank0"] if c % 2 == 0 else ctx.t["bank1"]).ap()
            if c >= 2:
                t.wait_ge(ctx.sem("q7_dse"), c - 1)
            t.matmul(pb, ctx.t["ones_bf"][0:1, 0:128],
                     ctx.t["dsrow"][0:1, c * 512:(c + 1) * 512],
                     start=True, stop=True).then_inc(ctx.sem("q7_dsb"), 1)
        return None

    def sync(self, ctx, sy):
        x1 = ctx.t["x1_in"]
        for tt in range(8):
            sl = tt % 2
            if tt >= 2:
                sy.wait_ge(ctx.sem("q7_used"), tt - 1)
            sy.dma_start(out=ctx.t["uR"][:, sl * D:(sl + 1) * D],
                         in_=x1.ap()[tt * 128:(tt + 1) * 128, :]) \
                .then_inc(ctx.sem("q7_uA" if sl == 0 else "q7_uB"), 16)
            sy.dma_start(out=ctx.t["mR"][:, sl * D:(sl + 1) * D],
                         in_=ctx.t["attn_s"].ap()[tt * 128:(tt + 1) * 128, :]) \
                .then_inc(ctx.sem("q7_mA" if sl == 0 else "q7_mB"), 16)
            sy.dma_start(out=ctx.t["fR"][:, sl * D:(sl + 1) * D],
                         in_=ctx.t["ffn1_s"].ap()[tt * 128:(tt + 1) * 128, :]) \
                .then_inc(ctx.sem("q7_fA" if sl == 0 else "q7_fB"), 16)
            sy.wait_ge(ctx.sem("q7_x1"), tt + 1)
            sy.dma_start(out=ctx.t["out_d"].ap()[tt * 128:(tt + 1) * 128, :],
                         in_=ctx.t["x1R"][:, sl * D:(sl + 1) * D]) \
                .then_inc(ctx.sem("q7_oA" if sl == 0 else "q7_oB"), 16)
        sy.wait_ge(ctx.sem("q7_oA"), 16 * 4)
        sy.wait_ge(ctx.sem("q7_oB"), 16 * 4)
        return None

    def vector(self, ctx, v):
        for c in range(4):
            v.wait_ge(ctx.sem("q7_dsb"), c + 1)
            pb = (ctx.t["bank0"] if c % 2 == 0 else ctx.t["bank1"]).ap()
            v.tensor_copy(ctx.t["dsB"][:, c * 512:(c + 1) * 512], pb[:, 0:512])
            dinc(v, ctx.sem("q7_dse"))
        for tt in range(8):
            sl = tt % 2
            v.wait_ge(ctx.sem("q7_uA" if sl == 0 else "q7_uB"), 16 * (tt // 2 + 1))
            v.wait_ge(ctx.sem("q7_mA" if sl == 0 else "q7_mB"), 16 * (tt // 2 + 1))
            v.wait_ge(ctx.sem("q7_fA" if sl == 0 else "q7_fB"), 16 * (tt // 2 + 1))
            if tt >= 2:
                v.wait_ge(ctx.sem("q7_oA" if sl == 0 else "q7_oB"), 16 * (tt // 2))
            x1 = ctx.t["x1R"][:, sl * D:(sl + 1) * D]
            x2 = ctx.t["x2R"][:, sl * D:(sl + 1) * D]
            v.tensor_tensor(x1, ctx.t["fR"][:, sl * D:(sl + 1) * D],
                            ctx.t["dsB"][:, 0:2048], op=ALU.mult)
            v.drain()
            v.tensor_tensor(x2, x1, ctx.t["mR"][:, sl * D:(sl + 1) * D], op=ALU.add)
            v.drain()
            v.tensor_tensor(x1, x2, ctx.t["uR"][:, sl * D:(sl + 1) * D], op=ALU.add)
            dinc(v, ctx.sem("q7_x1"))
            dinc(v, ctx.sem("q7_used"))
        return None


class Q0Consts:

    def gpsimd(self, ctx, g):
        ident = ctx.t["ident_bf"]
        g.memset(ident[:, :], 0.0)
        g.drain()
        g.affine_select(out=ident[:, :], in_=ident[:, :],
                        compare_op=ALU.not_equal, fill=1.0,
                        base=0, pattern=[[-1, 128]], channel_multiplier=1)
        for m in range(4):
            dm = ctx.t["dmask"][:, m * 512:(m + 1) * 512]
            g.memset(dm, 1.0)
            g.drain()
            g.affine_select(out=dm, in_=dm, compare_op=ALU.is_ge, fill=0.0,
                            base=-128 * m, pattern=[[1, 512]], channel_multiplier=-1)
            g.drain()
        return None

    def vector(self, ctx, v):
        last = v.memset(ctx.t["ones_bf"][:, :], 1.0)
        for val, idx in ctx.consts.items():
            last = v.memset(ctx.cpool[:, idx:idx + 1], val)
        return last

    def sync(self, ctx, sy):
        sem = ctx.sem("q0_ld")
        n = 0
        for sb_name, dram_name in [("ce8", "ce8_d"), ("co8", "co8_d"),
                                   ("se8", "se8_d"), ("so8", "so8_d"),
                                   ("gs1", "gs1_d"), ("us1", "us1_d"),
                                   ("dsrow", "ds_d")]:
            sy.dma_start(out=as_ap(ctx.t[sb_name])[:, :],
                         in_=ctx.t[dram_name].ap()).then_inc(sem, 16)
            n += 1
        sy.wait_ge(sem, 16 * n)
        return None


def alloc_l1(ctx, nc):
    arena = ctx.sb("arenaB", [128, 82048], BF16)
    ar32 = arena.bitcast(F32)
    ctx.t["u_tok8"] = ar32.ap()[:, 0:16384]
    ctx.t["xn1_tok"] = arena.ap()[:, 32768:32768 + 16384]
    ctx.t["ln_sq"] = ar32.ap()[:, 24576:24576 + 2048]
    ctx.t["ln_tmp2b"] = ar32.ap()[:, 26624:26624 + 2048]
    ctx.t["ln_aggr"] = ar32.ap()[:, 28672:28672 + 16]
    ctx.t["ln_rows"] = ar32.ap()[:, 28688:28688 + 32]
    ctx.t["xnF"] = arena.ap()[:, 0:32768]
    ctx.t["qkvslab"] = arena.ap()[:, 57408:57408 + 24576]
    ctx.t["Qtok"] = arena.ap()[:, 32768:32768 + 16384]
    ctx.t["Ktok"] = arena.ap()[:, 49152:49152 + 4096]
    ctx.t["Vtok"] = arena.ap()[:, 53248:53248 + 4160]
    pass
    ctx.t["Qf"] = arena.ap()[:, 0:32768]
    ctx.t["expS"] = arena.ap()[:, 32768:32768 + 8192]
    ctx.t["Kf"] = arena.ap()[:, 57408:57408 + 8192]
    ctx.t["attnO"] = arena.ap()[:, 65600:65600 + 16384]
    ctx.t["dtmp"] = arena.ap()[:, 40960:40960 + 1024]
    ctx.t["rBs"] = arena.ap()[:, 41984:41984 + 512]
    ctx.t["rrow1"] = arena.ap()[:, 42496:42496 + 1024]
    ctx.t["drow"] = ar32.ap()[:, 21760:21760 + 512]
    ctx.t["attnG"] = arena.ap()[:, 0:16384]
    ctx.t["woslabA"] = arena.ap()[:, 16384:16384 + 1024]
    ctx.t["woslabB"] = arena.ap()[:, 17408:17408 + 1024]
    ctx.t["stage2"] = ar32.ap()[:, 11264:11264 + 2048]
    ctx.t["xn1own"] = arena.ap()[:, 28672:28672 + 16384]
    ctx.t["h1"] = arena.ap()[:, 0:22528]
    ctx.t["slabA"] = arena.ap()[:, 45056:45056 + 8192]
    ctx.t["slabB"] = arena.ap()[:, 53248:53248 + 8192]
    ctx.t["slabC2"] = arena.ap()[:, 61440:61440 + 8192]
    ctx.t["slabD2"] = arena.ap()[:, 69632:69632 + 8192]
    ctx.t["slabC"] = arena.ap()[:, 26624:26624 + 1024]
    ctx.t["slabD"] = arena.ap()[:, 27648:27648 + 1024]
    ctx.t["ztmp"] = arena.ap()[:, 77824:77824 + 1024]
    ctx.t["uR"] = ar32.ap()[:, 0:4096]
    ctx.t["mR"] = ar32.ap()[:, 4096:8192]
    ctx.t["fR"] = ar32.ap()[:, 8192:12288]
    ctx.t["x1R"] = ar32.ap()[:, 12288:16384]
    ctx.t["x2R"] = ar32.ap()[:, 16384:20480]
    ctx.t["dsB"] = ar32.ap()[:, 20480:22528]
    ctx.sb("ident_bf", [128, 128], BF16)
    ctx.sb("ones_bf", [128, 128], BF16)
    ctx.sb("ropetmp", [128, 512], BF16)
    ctx.sb("dmask", [128, 4 * 512], BF16)
    ctx.sb("ce8", [128, 16 * 256], BF16)
    ctx.sb("co8", [128, 16 * 256], BF16)
    ctx.sb("se8", [128, 16 * 256], BF16)
    ctx.sb("so8", [128, 16 * 256], BF16)
    ctx.sb("gs1", [128, 44], F32)
    ctx.sb("us1", [128, 44], F32)
    ctx.sb("dsrow", [128, 2048], BF16)
    banks = [ctx.ps(f"bank{i}", [128, 512], F32) for i in range(8)]
    ctx.t["pA"] = banks[0].ap()
    ctx.t["pB"] = banks[1].ap()
    ctx.t["pt0"] = banks[6].bitcast(BF16).ap()[:, 0:128]
    ctx.t["pt1"] = banks[7].bitcast(BF16).ap()[:, 0:128]
    ctx.dram("x1_in", [T, D], F32, kind="ExternalInput")
    ctx.dram("WqT_d", [D, NQH * 64], F32, kind="ExternalInput")
    ctx.dram("WkT_d", [D, NKH * 64], F32, kind="ExternalInput")
    ctx.dram("WvT_d", [D, NKH * 64], F32, kind="ExternalInput")
    ctx.dram("WoT_d", [D, D], F32, kind="ExternalInput")
    ctx.dram("Wg1T_d", [D, FF1], F32, kind="ExternalInput")
    ctx.dram("Wu1T_d", [D, FF1], F32, kind="ExternalInput")
    ctx.dram("Wd1T_d", [FF1, D], F32, kind="ExternalInput")
    ctx.dram("ce8_d", [128, 16 * 256], BF16, kind="ExternalInput")
    ctx.dram("co8_d", [128, 16 * 256], BF16, kind="ExternalInput")
    ctx.dram("se8_d", [128, 16 * 256], BF16, kind="ExternalInput")
    ctx.dram("so8_d", [128, 16 * 256], BF16, kind="ExternalInput")
    ctx.dram("gs1_d", [128, 44], F32, kind="ExternalInput")
    ctx.dram("us1_d", [128, 44], F32, kind="ExternalInput")
    ctx.dram("ds_d", [128, 2048], BF16, kind="ExternalInput")
    ctx.dram("xn1b", [T, D], BF16)
    ctx.dram("xn1g", [TT2, D], BF16)
    ctx.dram("aOb", [T, TT2], BF16)
    ctx.dram("aOg", [TT2, TT2], BF16)
    ctx.dram("attn_s", [T, D], F32)
    ctx.dram("ffn1_s", [T, D], F32)
    ctx.dram("out_d", [T, D], F32, kind="ExternalOutput")


def build_l1():
    nc = bass.Bass("TRN2", target_bir_lowering=False)
    stack = contextlib.ExitStack()
    with stack:
        ctx = Ctx(nc, stack)
        alloc_l1(ctx, nc)
        ctx.constf(EPS)
        ctx.constf(-30.0)
        ffn1 = P6FFN0(xn_name="xn1own", xn_off=0, out_name="ffn1_s",
                      wg="Wg1T_d", wu="Wu1T_d", wd="Wd1T_d", pfx="q6",
                      ne=FF1, gate_scale="gs1", up_scale="us1")
        emit_phases(nc, ctx, [Q0Consts(), Q1LN(), Q2QKV(), Q3Attn(),
                              Q4Exchange(), Q5Wo(), ffn1, Q7Final()])
    return nc


def alloc_l0(ctx, nc):
    arena = ctx.sb("arena", [128, 82048], BF16)
    ar32 = arena.bitcast(F32)
    ctx.t["u_tok"] = ar32.ap()[:, 0:18432]
    ctx.t["u_bf"] = arena.ap()[:, 36896:36896 + 18432]
    ctx.t["ln_tmp"] = ar32.ap()[:, 27664:27664 + 2048]
    ctx.t["ln_tmp2"] = ar32.ap()[:, 29712:29712 + 2048]
    ctx.t["xBC"] = arena.ap()[:, 0:35 * TF]
    ctx.t["slabA"] = arena.ap()[:, 36896:36896 + 8192]
    ctx.t["slabB"] = arena.ap()[:, 45088:45088 + 8192]
    ctx.t["xdtT"] = arena.ap()[:, 36896:36896 + 32768]
    ctx.t["S_run"] = arena.ap()[:, 32800:32800 + 4096]
    ctx.t["y_feat"] = arena.ap()[:, 0:32768]
    ctx.t["h1"] = arena.ap()[:, 0:32768]
    ctx.t["conv_tmp"] = arena.ap()[:, 32896:32896 + 2048]
    ctx.t["slabC"] = arena.ap()[:, 36896:36896 + 1024]
    ctx.t["slabD"] = arena.ap()[:, 37920:37920 + 1024]
    ctx.t["stage2"] = ar32.ap()[:, 16400:16400 + 2048]
    ctx.t["slabC2"] = arena.ap()[:, 53280:53280 + 8192]
    ctx.t["slabD2"] = arena.ap()[:, 61472:61472 + 8192]
    ctx.t["uR"] = ar32.ap()[:, 0:4096]
    ctx.t["mR"] = ar32.ap()[:, 4096:8192]
    ctx.t["fR"] = ar32.ap()[:, 8192:12288]
    ctx.t["x1R"] = ar32.ap()[:, 12288:16384]
    ctx.t["x2R"] = ar32.ap()[:, 16384:20480]
    ctx.sb("norm_scale", [128, 32], F32)
    ctx.t["tmp_bf"] = arena.ap()[:, 34952:34952 + 1024]
    ctx.sb("ident_bf", [128, 128], BF16)
    ctx.sb("ident_f32", [128, 128], F32)
    ctx.sb("ones_bf", [128, 128], BF16)
    ctx.sb("gmaskA", [128, CH], BF16)
    ctx.sb("gmaskB", [128, CH], BF16)
    ctx.sb("xn", [128, 16 * TF], BF16)
    ctx.t["ln_aggr"] = ar32.ap()[:, 31760:31760 + 18]
    ctx.t["ln_rows"] = ar32.ap()[:, 31800:31800 + 36]
    dtbuf = ctx.sb("dtbuf", [128, T], F32)
    ctx.t["decay_bf"] = dtbuf.bitcast(BF16).ap()[:, 0:1024]
    ctx.t["sdo_bf"] = dtbuf.bitcast(BF16).ap()[:, 0:1024]
    ctx.t["csdo"] = dtbuf.bitcast(BF16).ap()[:, 1024:1536]
    ctx.t["xdtmp"] = dtbuf.bitcast(BF16).ap()[:, 1536:1792]
    ctx.t["stmp4"] = dtbuf.bitcast(BF16).ap()[:, 1792:2048]
    ctx.t["stmpA"] = dtbuf.bitcast(BF16).ap()[:, 1792:1856]
    ctx.t["stmpB"] = dtbuf.bitcast(BF16).ap()[:, 1856:1920]
    acsA_h = ctx.sb("acsA", [128, T], F32)
    ctx.t["expL"] = acsA_h.bitcast(BF16).ap()[:, 0:1024]
    ctx.t["MT"] = acsA_h.bitcast(BF16).ap()[:, 1024:2048]
    ctx.t["roleb"] = acsA_h.bitcast(BF16).ap()[:, 0:2048]
    acsB = ctx.sb("acsB", [128, T], F32)
    ctx.t["invdt"] = acsB.ap()[:, 0:512]
    ctx.t["convfG"] = acsB.ap()[:, 0:1024]
    ctx.t["expLmin"] = acsB.bitcast(BF16).ap()[:, 0:2048]
    ctx.t["rrow_bf"] = acsB.bitcast(BF16).ap()[:, 1024:2048]
    ctx.t["convfV"] = ar32.ap()[:, 39328:39328 + 1024]
    ctx.t["sel4"] = arena.ap()[0:4, 78656:79168]
    ctx.t["rowF0"] = ar32.ap()[0:4, 39584:39712]
    ctx.t["rowF1"] = ar32.ap()[0:4, 39712:39840]
    ctx.t["hiB0"] = arena.ap()[0:4, 79680:79808]
    ctx.t["hiB1"] = arena.ap()[0:4, 79808:79936]
    ctx.t["loB0"] = arena.ap()[0:4, 79936:80064]
    ctx.t["loB1"] = arena.ap()[0:4, 80064:80192]
    ctx.t["dtT"] = ar32.ap()[:, 40352:40352 + 512]
    ctx.t["rr_ln0"] = ctx.t["dtT"]
    ctx.t["GmB"] = arena.ap()[:, 80704:81728]
    ctx.t["Dinvdt"] = ar32.ap()[:, 38176:38176 + 512]
    ctx.t["rr_ln1"] = ctx.t["Dinvdt"]
    ctx.t["acsT"] = ar32.ap()[:, 37152:37152 + 512]
    ctx.t["decayT"] = ar32.ap()[:, 37664:37664 + 512]
    ctx.t["rstd_b"] = arena.ap()[:, 75328:75328 + 1024]
    ctx.t["alast_rows"] = ar32.ap()[0:1, 38944:38944 + 256]
    ctx.t["alast_erow"] = arena.ap()[0:1, 78400:78400 + 256]
    ctx.sb("expA", [128, 1], F32)
    ctx.t["Bfeat"] = arena.ap()[:, 70720:70720 + 1024]
    ctx.t["Cfeat"] = arena.ap()[:, 71744:71744 + 1024]
    ctx.t["BT"] = arena.ap()[:, 72768:72768 + 1024]
    ctx.t["bdec"] = arena.ap()[:, 73792:73792 + 512]
    ctx.t["expAlB"] = ar32.ap()[:, 38688:38688 + 256]
    ctx.sb("roleflag", [128, 1], F32)
    ctx.t["Gm"] = arena.ap()[:, 69696:69696 + 1024]
    ctx.t["sqbuf"] = ctx.t["Gm"]
    ctx.t["ztmp"] = acsA_h.bitcast(BF16).ap()[:, 0:1024]
    ctx.t["ztmp2"] = acsA_h.bitcast(BF16).ap()[:, 1024:2048]
    ctx.sb("convw", [128, 4 * 34], F32)
    ctx.sb("convb", [128, 34], F32)
    ctx.sb("dt_bias", [128, 1], F32)
    ctx.sb("A_log", [128, 1], F32)
    ctx.sb("D_tile", [128, 64], F32)
    banks = [ctx.ps(f"bank{i}", [128, 512], F32) for i in range(8)]
    ctx.t["pA"] = banks[0].ap()
    ctx.t["pB"] = banks[1].ap()
    ctx.t["pt0"] = banks[6].bitcast(BF16).ap()[:, 0:128]
    ctx.t["pt1"] = banks[7].bitcast(BF16).ap()[:, 0:128]
    ctx.t["pf0"] = banks[4].ap()[:, 0:64]
    ctx.t["pf1"] = banks[5].ap()[:, 0:64]
    ctx.t["pAl"] = banks[2].ap()[0:1, 0:64]
    ctx.t["pAcsB0"] = banks[0].ap()[:, 0:256]
    ctx.t["pAcsB1"] = banks[1].ap()[:, 0:256]
    ctx.t["pRT0"] = banks[2].ap()[0:4, 0:128]
    ctx.t["pRT1"] = banks[3].ap()[0:4, 0:128]
    ctx.t["pG0"] = banks[2].ap()[:, 0:256]
    ctx.t["pG1"] = banks[3].ap()[:, 0:256]
    ctx.t["pY0"] = banks[4].ap()[0:64, 0:256]
    ctx.t["pY1"] = banks[5].ap()[0:64, 0:256]
    ctx.t["pS0"] = banks[6].ap()[:, 0:64]
    ctx.t["pS1"] = banks[7].ap()[:, 0:64]
    ctx.dram("u_dram", [TP, D], F32, kind="ExternalInput")
    ctx.dram("W_in", [D, EIN], F32, kind="ExternalInput")
    ctx.dram("convw_d", [128, 4 * 34], F32, kind="ExternalInput")
    ctx.dram("convb_d", [128, 34], F32, kind="ExternalInput")
    ctx.dram("dtb_d", [64, 1], F32, kind="ExternalInput")
    ctx.dram("alog_d", [64, 1], F32, kind="ExternalInput")
    ctx.dram("D_d", [128, 64], F32, kind="ExternalInput")
    ctx.dram("role_d", [128, 1], F32, kind="ExternalInput")
    ctx.dram("W_out_d", [DIN, D], F32, kind="ExternalInput")
    ctx.dram("ns_d", [128, 32], F32, kind="ExternalInput")
    ctx.dram("mamba_s", [T, D], F32)
    ctx.dram("ffn0_s", [T, D], F32)
    ctx.dram("WgT_d", [D, FF0], F32, kind="ExternalInput")
    ctx.dram("WuT_d", [D, FF0], F32, kind="ExternalInput")
    ctx.dram("WdT_d", [FF0, D], F32, kind="ExternalInput")
    ctx.dram("x1_out", [T, D], F32, kind="ExternalOutput")
    ctx.dram("stout", [128, DIN], BF16)
    ctx.dram("stgath", [256, DIN], BF16)



P0_LOADS = [("convw", "convw_d"), ("convb", "convb_d"), ("D_tile", "D_d"), ("roleflag", "role_d"), ("norm_scale", "ns_d")]



class P0Small:

    def sync(self, ctx, sy):
        sem = ctx.sem("p0b_ld")
        sy.dma_start(out=ctx.t["dt_bias"][0:64, 0:1], in_=ctx.t["dtb_d"].ap()).then_inc(sem, 16)
        sy.dma_start(out=ctx.t["A_log"][0:64, 0:1], in_=ctx.t["alog_d"].ap()).then_inc(sem, 16)
        sy.wait_ge(sem, 32)
        return None




def build_l0():
    nc = bass.Bass("TRN2", target_bir_lowering=False)
    stack = contextlib.ExitStack()
    with stack:
        ctx = Ctx(nc, stack)
        alloc_l0(ctx, nc)
        ctx.constf(EPS)
        ctx.constf(1.0)
        emit_phases(nc, ctx, [P0Consts(P0_LOADS), P0Small(), P1LN(), P2InProjA(),
                              P3Conv(), P4aState(), P4xExchange(), P4bSSD(),
                              P5Gate(), P6FFN0(), P7Combine()])
    return nc


def _rope_tables():
    inv = 1.0 / 10000.0 ** (np.arange(0, 64, 2) / 64.0)
    t = np.arange(2048.0)
    emb = np.concatenate([np.outer(t, inv), np.outer(t, inv)], axis=1)
    return np.cos(emb), np.sin(emb)


def _rope_inputs():
    cos, sin = _rope_tables()
    def tile8(tab):
        out = np.zeros((128, 16 * 256), dtype=BF)
        for tt in range(16):
            out[:, tt * 256:(tt + 1) * 256] = np.tile(tab[tt * 128:(tt + 1) * 128], (1, 8))
        return out
    return (tile8(cos[:, 0::2]), tile8(cos[:, 1::2]),
            tile8(sin[:, 0::2]), tile8(sin[:, 1::2]))


_NC_CACHE = {}


def kernel(L0_attn_u, L0_attn_W_in, L0_attn_conv_w, L0_attn_conv_b, L0_attn_dt_bias,
           L0_attn_A_log, L0_attn_D, L0_attn_norm_scale, L0_attn_W_out,
           L0_ffn_w_gate, L0_ffn_w_up, L0_ffn_w_down,
           L1_attn_Wq, L1_attn_Wk, L1_attn_Wv, L1_attn_Wo,
           L1_ffn_gate_w, L1_ffn_up_w, L1_ffn_down_w,
           L1_ffn_gate_s, L1_ffn_up_s, L1_ffn_down_s):
    u = np.asarray(L0_attn_u, dtype=np.float32)
    convw_l = np.zeros((128, 136), dtype=np.float32)
    convb_l = np.zeros((128, 34), dtype=np.float32)
    cw = np.asarray(L0_attn_conv_w, dtype=np.float32)
    cb = np.asarray(L0_attn_conv_b, dtype=np.float32)
    for j in range(34):
        for i in range(4):
            convw_l[:, 4 * j + i] = cw[i, 128 * j:128 * j + 128]
        convb_l[:, j] = cb[128 * j:128 * j + 128]
    D_t = np.tile(np.asarray(L0_attn_D, np.float32)[None, :], (128, 1))
    ns_l = np.ascontiguousarray(np.asarray(L0_attn_norm_scale, np.float32).reshape(32, 128).T)
    WgT = np.ascontiguousarray(np.asarray(L0_ffn_w_gate, np.float32).T)
    WuT = np.ascontiguousarray(np.asarray(L0_ffn_w_up, np.float32).T)
    WdT = np.ascontiguousarray(np.asarray(L0_ffn_w_down, np.float32).T)
    if "fused" not in _NC_CACHE:
        _NC_CACHE["fused"] = build_fused()
    ce8, co8, se8, so8 = _rope_inputs()
    Wq = np.asarray(L1_attn_Wq, np.float32)
    Wk = np.asarray(L1_attn_Wk, np.float32)
    Wv = np.asarray(L1_attn_Wv, np.float32)
    WoT = np.ascontiguousarray(np.asarray(L1_attn_Wo, np.float32).T)
    Wg1T = np.ascontiguousarray(np.asarray(L1_ffn_gate_w, np.float32).T)
    Wu1T = np.ascontiguousarray(np.asarray(L1_ffn_up_w, np.float32).T)
    Wd1T = np.ascontiguousarray(np.asarray(L1_ffn_down_w, np.float32).T)
    gs_l = np.ascontiguousarray(np.asarray(L1_ffn_gate_s, np.float32).reshape(44, 128).T)
    us_l = np.ascontiguousarray(np.asarray(L1_ffn_up_s, np.float32).reshape(44, 128).T)
    ds_t = np.tile(np.asarray(L1_ffn_down_s, np.float32)[None, :], (128, 1)).astype(BF)
    in_maps = []
    for c in range(8):
        half = c % 2
        hsl = slice(0, 1024) if half == 0 else slice(1024, 2048)
        ksl = slice(0, 256) if half == 0 else slice(256, 512)
        role = np.full((128, 1), float(half), dtype=np.float32)
        in_maps.append({
            "u_dram": prep_u_core(u, c),
            "W_in": np.asarray(L0_attn_W_in, np.float32),
            "convw_d": convw_l, "convb_d": convb_l,
            "dtb_d": np.asarray(L0_attn_dt_bias, np.float32)[:, None],
            "alog_d": np.asarray(L0_attn_A_log, np.float32)[:, None],
            "D_d": D_t, "role_d": role,
            "W_out_d": np.asarray(L0_attn_W_out, np.float32), "ns_d": ns_l,
            "WgT_d": WgT, "WuT_d": WuT, "WdT_d": WdT,
            "WqT_d": np.ascontiguousarray(Wq[hsl].T),
            "WkT_d": np.ascontiguousarray(Wk[ksl].T),
            "WvT_d": np.ascontiguousarray(Wv[ksl].T),
            "WoT_d": WoT,
            "Wg1T_d": Wg1T, "Wu1T_d": Wu1T, "Wd1T_d": Wd1T,
            "ce8_d": ce8, "co8_d": co8, "se8_d": se8, "so8_d": so8,
            "gs1_d": gs_l, "us1_d": us_l, "ds_d": ds_t,
        })
    res = run_bass_kernel_spmd(_NC_CACHE["fused"], in_maps, core_ids=list(range(8)))
    out = np.zeros((4, 2048, 2048), dtype=np.float32)
    for c in range(8):
        b, half = c // 2, c % 2
        out[b, half * 1024:half * 1024 + 1024] = np.asarray(res.results[c]["out_d"])
    return out

def alloc_l1_fused(ctx, nc):
    arena = ctx.t["arena"]
    ar32 = arena.bitcast(F32)
    b = {}
    b["u_tok8"] = ar32.ap()[:, 0:16384]
    b["xn1_tok"] = arena.ap()[:, 32768:32768 + 16384]
    b["ln_sq"] = ar32.ap()[:, 24576:24576 + 2048]
    b["ln_tmp2b"] = ar32.ap()[:, 26624:26624 + 2048]
    b["ln_aggr"] = ar32.ap()[:, 28672:28672 + 16]
    b["ln_rows"] = ar32.ap()[:, 28688:28688 + 32]
    b["xnF"] = arena.ap()[:, 0:32768]
    b["qkvslab"] = arena.ap()[:, 57408:57408 + 24576]
    b["Qtok"] = arena.ap()[:, 32768:32768 + 16384]
    b["Ktok"] = arena.ap()[:, 49152:49152 + 4096]
    b["Vtok"] = arena.ap()[:, 53248:53248 + 4160]
    b["Qf"] = arena.ap()[:, 0:32768]
    b["expS"] = arena.ap()[:, 32768:32768 + 8192]
    b["Kf"] = arena.ap()[:, 57408:57408 + 8192]
    b["attnO"] = arena.ap()[:, 65600:65600 + 16384]
    b["dtmp"] = arena.ap()[:, 40960:40960 + 1024]
    b["rBs"] = arena.ap()[:, 41984:41984 + 512]
    b["rrow1"] = arena.ap()[:, 42496:42496 + 1024]
    b["drow"] = ar32.ap()[:, 21760:21760 + 512]
    b["attnG"] = arena.ap()[:, 0:16384]
    b["woslabA"] = arena.ap()[:, 16384:16384 + 1024]
    b["woslabB"] = arena.ap()[:, 17408:17408 + 1024]
    b["stage2"] = ar32.ap()[:, 11264:11264 + 2048]
    b["xn1own"] = arena.ap()[:, 28672:28672 + 16384]
    b["h1"] = arena.ap()[:, 0:22528]
    b["slabA"] = arena.ap()[:, 45056:45056 + 8192]
    b["slabB"] = arena.ap()[:, 53248:53248 + 8192]
    b["slabC2"] = arena.ap()[:, 61440:61440 + 8192]
    b["slabD2"] = arena.ap()[:, 69632:69632 + 8192]
    b["slabC"] = arena.ap()[:, 26624:26624 + 1024]
    b["slabD"] = arena.ap()[:, 27648:27648 + 1024]
    b["ztmp"] = arena.ap()[:, 77824:77824 + 1024]
    b["uR"] = ar32.ap()[:, 0:4096]
    b["mR"] = ar32.ap()[:, 4096:8192]
    b["fR"] = ar32.ap()[:, 8192:12288]
    b["x1R"] = ar32.ap()[:, 12288:16384]
    b["x2R"] = ar32.ap()[:, 16384:20480]
    b["dsB"] = ar32.ap()[:, 20480:22528]
    xn = ctx.t["xn"]
    b["ce8"] = xn.ap()[:, 0:4096]
    b["co8"] = xn.ap()[:, 4096:8192]
    b["se8"] = xn.ap()[:, 8192:12288]
    b["so8"] = xn.ap()[:, 12288:16384]
    b["dmask"] = ctx.t["acsA"].bitcast(BF16).ap()[:, 0:2048]
    b["ropetmp"] = ctx.t["acsB"].bitcast(BF16).ap()[:, 0:512]
    b["gs1"] = ctx.t["acsB"].ap()[:, 256:300]
    b["us1"] = ctx.t["acsB"].ap()[:, 300:344]
    b["dsrow"] = ctx.t["dtbuf"].bitcast(BF16).ap()[:, 0:2048]
    ctx.dram("x1_mid", [T, D], F32)
    b["x1_in"] = ctx.t["x1_mid"]
    ctx.t["x1_out"] = ctx.t["x1_mid"]
    ctx.dram("WqT_d", [D, NQH * 64], F32, kind="ExternalInput")
    ctx.dram("WkT_d", [D, NKH * 64], F32, kind="ExternalInput")
    ctx.dram("WvT_d", [D, NKH * 64], F32, kind="ExternalInput")
    ctx.dram("WoT_d", [D, D], F32, kind="ExternalInput")
    ctx.dram("Wg1T_d", [D, FF1], F32, kind="ExternalInput")
    ctx.dram("Wu1T_d", [D, FF1], F32, kind="ExternalInput")
    ctx.dram("Wd1T_d", [FF1, D], F32, kind="ExternalInput")
    ctx.dram("ce8_d", [128, 16 * 256], BF16, kind="ExternalInput")
    ctx.dram("co8_d", [128, 16 * 256], BF16, kind="ExternalInput")
    ctx.dram("se8_d", [128, 16 * 256], BF16, kind="ExternalInput")
    ctx.dram("so8_d", [128, 16 * 256], BF16, kind="ExternalInput")
    ctx.dram("gs1_d", [128, 44], F32, kind="ExternalInput")
    ctx.dram("us1_d", [128, 44], F32, kind="ExternalInput")
    ctx.dram("ds_d", [128, 2048], BF16, kind="ExternalInput")
    ctx.dram("xn1b", [T, D], BF16)
    ctx.dram("xn1g", [TT2, D], BF16)
    ctx.dram("aOb", [T, TT2], BF16)
    ctx.dram("aOg", [TT2, TT2], BF16)
    ctx.dram("attn_s", [T, D], F32)
    ctx.dram("ffn1_s", [T, D], F32)
    ctx.dram("out_d", [T, D], F32, kind="ExternalOutput")
    return b


def build_fused():
    nc = bass.Bass("TRN2", target_bir_lowering=False)
    stack = contextlib.ExitStack()
    with stack:
        ctx = Ctx(nc, stack)
        alloc_l0(ctx, nc)
        rebind = alloc_l1_fused(ctx, nc)
        ctx.constf(EPS)
        ctx.constf(1.0)
        ctx.constf(-30.0)
        ffn1 = P6FFN0(xn_name="xn1own", xn_off=0, out_name="ffn1_s",
                      wg="Wg1T_d", wu="Wu1T_d", wd="Wd1T_d", pfx="q6",
                      ne=FF1, gate_scale="gs1", up_scale="us1")
        phases = [P0Consts(P0_LOADS), P0Small(), P1LN(), P2InProjA(), P3Conv(),
                  P4aState(), P4xExchange(), P4bSSD(), P5Gate(), P6FFN0(),
                  P7Combine(), rebind, Q0Consts(), Q1LN(), Q2QKV(), Q3Attn(),
                  Q4Exchange(), Q5Wo(), ffn1, Q7Final()]
        emit_phases(nc, ctx, phases)
    return nc


# version 09 (final)
# speedup vs baseline: 1.3423x; 1.0706x over previous
import contextlib
import numpy as np
import ml_dtypes
import concourse.bass as bass
import concourse.mybir as mybir
from concourse.bass_utils import run_bass_kernel_spmd

F32 = mybir.dt.float32
BF16 = mybir.dt.bfloat16
AF = mybir.ActivationFunctionType
ALU = mybir.AluOpType
AX = mybir.AxisListType

BF = ml_dtypes.bfloat16

D = 2048
T = 1024
HALO = 4
TF = T + HALO
TP = 1152
DIN = 4096
NH = 64
HD = 64
NST = 128
CH = 256
NCH = T // CH
CONVD = DIN + 2 * NST
EIN = 2 * DIN + 2 * NST + NH
XBCD = CONVD + NH
FF0 = 8192
FF1 = 5632
EPS = 1e-5

PAIRS = [[0, 1], [2, 3], [4, 5], [6, 7]]
ENGS = ["tensor", "vector", "scalar", "gpsimd", "sync"]


def cdiv(a, b):
    return (a + b - 1) // b


class Ctx:
    def __init__(self, nc, stack):
        self.nc = nc
        self.t = {}
        self.sems = {}
        self.stack = stack
        self.phase_idx = -1
        self.pool = []
        self.phase_map = {}
        self.consts = {}
        self.cpool = None

    def init_pool(self, n=88):
        for i in range(n):
            self.pool.append(self.stack.enter_context(self.nc.semaphore(f"pool{i}")))

    def constf(self, value):
        if self.cpool is None:
            self.cpool = self.sb("cpool", [128, 32], F32)
        v = float(value)
        if v not in self.consts:
            self.consts[v] = len(self.consts)
            assert len(self.consts) <= 32
        return self.cpool[:, self.consts[v]:self.consts[v] + 1]

    def constf_r(self, value, rows):
        ap = self.constf(value)
        return self.cpool[0:rows, self.consts[float(value)]:self.consts[float(value)] + 1]

    def sem(self, name):
        if self.phase_idx < 0 or name.startswith("g_"):
            if name not in self.sems:
                self.sems[name] = self.stack.enter_context(self.nc.semaphore(name))
            return self.sems[name]
        key = (self.phase_idx, name)
        if key not in self.phase_map:
            used = len([1 for (p, _) in self.phase_map if p == self.phase_idx])
            assert used < len(self.pool), "phase sem pool exhausted"
            self.phase_map[key] = used
        return self.pool[self.phase_map[key]]

    def sb(self, name, shape, dtype, **kw):
        h = self.stack.enter_context(self.nc.sbuf_tensor(name, shape, dtype, **kw))
        self.t[name] = h
        return h

    def ps(self, name, shape, dtype=F32):
        h = self.stack.enter_context(self.nc.psum_tensor(name, shape, dtype))
        self.t[name] = h
        return h

    def dram(self, name, shape, dtype, kind=None, **kw):
        if kind:
            h = self.nc.dram_tensor(name, shape, dtype, kind=kind, **kw)
        else:
            h = self.nc.dram_tensor(name, shape, dtype, **kw)
        self.t[name] = h
        return h


def inc_last(eng, inst, sem):
    eng.drain()
    eng.sem_inc(sem, 1)


def dinc(eng, sem, n=1):
    eng.drain()
    eng.sem_inc(sem, n)


def emit_phases(nc, ctx, phases):
    ctx.init_pool()
    pool_nums = sorted(sm.num for sm in ctx.pool)
    lo, hi = pool_nums[0], pool_nums[-1] + 1
    assert hi - lo == len(pool_nums), "pool sems not contiguous"
    emitted = 0
    for p in phases:
        if isinstance(p, dict):
            ctx.t.update(p)
            continue
        i = emitted
        emitted += 1
        ctx.phase_idx = i
        if i > 0:
            with nc.Block() as bc:
                def clr(g):
                    g.dma_reset(range(lo, hi))
                    g.sem_clear(range(lo, hi))
                bc.gpsimd(clr)
            nc.all_engine_barrier()
        with nc.Block() as block:
            for eng_name in ENGS:
                fn = getattr(p, eng_name, None)
                if fn is None:
                    continue
                def make_body(fn=fn):
                    def body(eng):
                        fn(ctx, eng)
                    return body
                getattr(block, eng_name)(make_body())
        nc.all_engine_barrier()
    ctx.phase_idx = -1


class PDebugStore:

    def __init__(self, names):
        self.names = names

    def sync(self, ctx, sy):
        s_o = ctx.sem("dbg_out")
        k = 0
        for n in self.names:
            sy.dma_start(out=ctx.t["dbg_" + n].ap(),
                         in_=as_ap(ctx.t[n])[:, :]).then_inc(s_o, 16)
            k += 1
        sy.wait_ge(s_o, 16 * k)
        return None


def as_ap(obj):
    return obj.ap() if callable(getattr(obj, "ap", None)) else obj


def add_debug(ctx, names):
    for n in names:
        ap = as_ap(ctx.t[n])
        ctx.dram("dbg_" + n, list(ap.shape), ap.dtype, kind="ExternalOutput")
    return PDebugStore(names)


class P0Consts:
    def __init__(self, loads):
        self.loads = loads

    def gpsimd(self, ctx, g):
        ident = ctx.t["ident_bf"]
        g.memset(ident[:, :], 0.0)
        g.drain()
        g.affine_select(out=ident[:, :], in_=ident[:, :],
                        compare_op=ALU.not_equal, fill=1.0,
                        base=0, pattern=[[-1, 128]], channel_multiplier=1)
        ident32 = ctx.t["ident_f32"]
        g.memset(ident32[:, :], 0.0)
        g.drain()
        g.affine_select(out=ident32[:, :], in_=ident32[:, :],
                        compare_op=ALU.not_equal, fill=1.0,
                        base=0, pattern=[[-1, 128]], channel_multiplier=1)
        ma = ctx.t["gmaskA"]
        g.memset(ma[:, :], 1.0)
        g.drain()
        g.affine_select(out=ma[:, :], in_=ma[:, :], compare_op=ALU.is_ge,
                        fill=0.0, base=0, pattern=[[1, CH]], channel_multiplier=-1)
        mb = ctx.t["gmaskB"]
        g.memset(mb[:, :], 1.0)
        g.drain()
        last = g.affine_select(out=mb[:, :], in_=mb[:, :], compare_op=ALU.is_ge,
                               fill=0.0, base=-128, pattern=[[1, CH]],
                               channel_multiplier=-1)
        return last

    def vector(self, ctx, v):
        last = v.memset(ctx.t["ones_bf"][:, :], 1.0)
        for val, idx in ctx.consts.items():
            last = v.memset(ctx.cpool[:, idx:idx + 1], val)
        return last

    def sync(self, ctx, sy):
        sem = ctx.sem("p0_ld")
        n = 0
        for sb_name, dram_name in self.loads:
            sb = ctx.t[sb_name]
            dr = ctx.t[dram_name]
            sy.dma_start(out=sb[:, :], in_=dr.ap()).then_inc(sem, 16)
            n += 1
        if n:
            sy.wait_ge(sem, 16 * n)
        return None


class P1LN:

    def sync(self, ctx, sy):
        u = ctx.t["u_dram"]
        u_tok = ctx.t["u_tok"]
        s_u = ctx.sem("p1_u_ld")
        src = u.ap().rearrange("(tt p) d -> p tt d", p=128)
        dst = u_tok.rearrange("p (tt d) -> p tt d", tt=9)
        sy.dma_start(out=dst, in_=src).then_inc(s_u, 16)
        return None

    def vector(self, ctx, v):
        u_tok = ctx.t["u_tok"]
        u_bf = ctx.t["u_bf"]
        aggr = ctx.t["ln_aggr"]
        rows = ctx.t["ln_rows"]
        sq = ctx.t["ln_tmp"]
        s_u = ctx.sem("p1_u_ld")
        s_ready = ctx.sem("p1_tok_ready")
        s_rstd = ctx.sem("p1_rstd")
        s_sq = ctx.sem("p1_sq")
        for tt in range(9):
            if tt == 0:
                v.wait_ge(s_u, 16)
            ut = u_tok[:, tt * D:(tt + 1) * D]
            a = aggr[:, 2 * tt:2 * tt + 2]
            r = rows[:, 4 * tt:4 * tt + 4]
            v.reduce_sum(r[:, 2:3], ut, axis=AX.X)
            v.wait_ge(s_sq, tt + 1)
            v.reduce_sum(r[:, 3:4], sq[:, :], axis=AX.X)
            dinc(v, ctx.sem("p1_sq_free"))
            v.drain()
            v.tensor_scalar(a[:, 0:1], r[:, 2:3], 1.0 / D, None, op0=ALU.mult)
            v.tensor_scalar(r[:, 3:4], r[:, 3:4], 1.0 / D, None, op0=ALU.mult)
            v.drain()
            v.tensor_tensor(a[:, 1:2], a[:, 0:1], a[:, 0:1], op=ALU.mult)
            v.drain()
            v.tensor_tensor(a[:, 1:2], r[:, 3:4], a[:, 1:2], op=ALU.subtract)
            dinc(v, ctx.sem("p1_var_ready"))
        tmp2 = ctx.t["ln_tmp2"]
        for tt in range(9):
            ut = u_tok[:, tt * D:(tt + 1) * D]
            a = aggr[:, 2 * tt:2 * tt + 2]
            r = rows[:, 4 * tt:4 * tt + 4]
            v.wait_ge(s_rstd, tt + 1)
            v.tensor_scalar(tmp2[:, :], ut, a[:, 0:1], r[:, 0:1],
                            op0=ALU.subtract, op1=ALU.mult)
            v.drain()
            v.tensor_copy(u_bf[:, tt * D:(tt + 1) * D], tmp2[:, :])
            dinc(v, s_ready)
        return None

    def scalar(self, ctx, s):
        aggr = ctx.t["ln_aggr"]
        rows = ctx.t["ln_rows"]
        s_var = ctx.sem("p1_var_ready")
        s_rstd = ctx.sem("p1_rstd")
        xn = ctx.t["xn"]
        s_tp = ctx.sem("p1_tp")
        s_ev = ctx.sem("p1_ev")
        sq = ctx.t["ln_tmp"]
        u_tok = ctx.t["u_tok"]
        s_sq = ctx.sem("p1_sq")
        s_sq_free = ctx.sem("p1_sq_free")
        last = None
        for tt in range(9):
            if tt == 0:
                s.wait_ge(ctx.sem("p1_u_ld"), 16)
            if tt > 0:
                s.wait_ge(s_sq_free, tt)
            s.activation(sq[:, :], u_tok[:, tt * D:(tt + 1) * D], AF.Square)
            dinc(s, s_sq)
        for tt in range(9):
            a = aggr[:, 2 * tt:2 * tt + 2]
            r = rows[:, 4 * tt:4 * tt + 4]
            s.wait_ge(s_var, tt + 1)
            s.activation(r[:, 1:2], a[:, 1:2], AF.Ln, bias=ctx.constf(EPS))
            s.drain()
            s.activation(r[:, 0:1], r[:, 1:2], AF.Exp, scale=-0.5)
            dinc(s, s_rstd)
        for tt in range(9):
            w = 128 if tt < 8 else HALO
            for dt in range(16):
                k = tt * 16 + dt
                s.wait_ge(s_tp, k + 1)
                pt = ctx.t["pt0"] if k % 2 == 0 else ctx.t["pt1"]
                last = s.copy(xn[:, dt * TF + tt * 128: dt * TF + tt * 128 + w],
                              pt[:, 0:w])
                last.then_inc(s_ev, 1)
        return None

    def tensor(self, ctx, t):
        u_bf = ctx.t["u_bf"]
        ident = ctx.t["ident_bf"]
        s_ready = ctx.sem("p1_tok_ready")
        s_tp = ctx.sem("p1_tp")
        s_ev = ctx.sem("p1_ev")
        last = None
        for tt in range(9):
            t.wait_ge(s_ready, tt + 1)
            for dt in range(16):
                k = tt * 16 + dt
                if k >= 2:
                    t.wait_ge(s_ev, k - 1)
                pt = ctx.t["pt0"] if k % 2 == 0 else ctx.t["pt1"]
                last = t.transpose(pt[:, :],
                                   u_bf[:, tt * D + dt * 128: tt * D + dt * 128 + 128],
                                   ident[:, :])
                last.then_inc(s_tp, 1)
        return None


def prep_u_core(u, core):
    b, half = core // 2, core % 2
    t0 = half * 1024
    out = np.zeros((TP, D), dtype=np.float32)
    lo = t0 - HALO
    src_lo = max(lo, 0)
    off = src_lo - lo
    out[off:HALO + 1024, :] = u[b, src_lo:t0 + 1024, :]
    return np.ascontiguousarray(out)


P2_CHUNKS = [(DIN + 512 * i, 512) for i in range(8)] + [(DIN + 4096, 320)]
P2_TCS = [(0, 512), (512, 512), (1024, HALO)]
CONV_ORDER = [32, 33] + list(range(32))


def etiles(w):
    out = []
    o = 0
    while o < w:
        out.append((o, min(128, w - o)))
        o += 128
    return out


class P2InProjA:

    def groups(self):
        idx = 0
        for ci, (e0, ew) in enumerate(P2_CHUNKS):
            for j, (jo, jw) in enumerate(etiles(ew)):
                jglob = (e0 - DIN + jo) // 128
                for tc0, tw in P2_TCS:
                    yield idx, ci, ew, jo, jw, jglob, tc0, tw
                    idx += 1

    def chunk_last(self):
        last = {}
        for idx, ci, *_ in self.groups():
            last[ci] = idx
        return last

    def gpsimd(self, ctx, g):
        w = ctx.t["W_in"]
        s_mm = ctx.sem("p2_mm")
        last = self.chunk_last()
        for ci, (e0, ew) in enumerate(P2_CHUNKS):
            if ci >= 2:
                g.wait_ge(s_mm, last[ci - 2] + 1)
            slab = ctx.t["slabA"] if ci % 2 == 0 else ctx.t["slabB"]
            s_wl = ctx.sem("p2_wlA" if ci % 2 == 0 else "p2_wlB")
            src = w.ap()[:, e0:e0 + ew].rearrange("(k p) e -> p k e", p=128)
            dst = slab[:, 0:16 * ew].rearrange("p (k e) -> p k e", k=16)
            g.dma_start(out=dst, in_=src).then_inc(s_wl, 16)
        return None

    def tensor(self, ctx, t):
        xn = ctx.t["xn"]
        s_mm = ctx.sem("p2_mm")
        s_evV = ctx.sem("p2_evV")
        s_evS = ctx.sem("p2_evS")
        last_ci = -1
        for idx, ci, ew, jo, jw, jglob, tc0, tw in self.groups():
            if ci != last_ci:
                t.wait_ge(ctx.sem("p2_wlA" if ci % 2 == 0 else "p2_wlB"),
                          16 * (ci // 2 + 1))
                last_ci = ci
            slab = ctx.t["slabA"] if ci % 2 == 0 else ctx.t["slabB"]
            p = ctx.t["pA"] if idx % 2 == 0 else ctx.t["pB"]
            sev = s_evV if idx % 2 == 0 else s_evS
            if idx >= 2:
                t.wait_ge(sev, idx // 2)
            for k in range(16):
                mm = t.matmul(p[0:jw, 0:tw],
                              slab[:, k * ew + jo:k * ew + jo + jw],
                              xn[:, k * TF + tc0:k * TF + tc0 + tw],
                              start=(k == 0), stop=(k == 15))
            mm.then_inc(s_mm, 1)
        return None

    def _evac(self, ctx, eng, parity):
        xBC = ctx.t["xBC"]
        s_mm = ctx.sem("p2_mm")
        sev = ctx.sem("p2_evV" if parity == 0 else "p2_evS")
        for idx, ci, ew, jo, jw, jglob, tc0, tw in self.groups():
            if idx % 2 != parity:
                continue
            p = ctx.t["pA"] if parity == 0 else ctx.t["pB"]
            eng.wait_ge(s_mm, idx + 1)
            eng.tensor_copy(xBC[0:jw, jglob * TF + tc0:jglob * TF + tc0 + tw],
                            p[0:jw, 0:tw])
            dinc(eng, sev)
        return None

    def vector(self, ctx, v):
        return self._evac(ctx, v, 0)

    def scalar(self, ctx, s):
        xBC = ctx.t["xBC"]
        s_mm = ctx.sem("p2_mm")
        sev = ctx.sem("p2_evS")
        for idx, ci, ew, jo, jw, jglob, tc0, tw in self.groups():
            if idx % 2 != 1:
                continue
            p = ctx.t["pB"]
            s.wait_ge(s_mm, idx + 1)
            s.copy(xBC[0:jw, jglob * TF + tc0:jglob * TF + tc0 + tw], p[0:jw, 0:tw])
            dinc(s, sev)
        return None


class P3Conv:

    def vector(self, ctx, v):
        xBC = ctx.t["xBC"]
        dtbuf, acsA, acsB = ctx.t["dtbuf"], ctx.t["acsA"], ctx.t["acsB"]
        dt34 = xBC[0:64, 34 * TF + HALO:34 * TF + HALO + T]
        v.tensor_scalar(dtbuf[0:64, :], dt34, ctx.t["dt_bias"][0:64, 0:1], None,
                        op0=ALU.add)
        dinc(v, ctx.sem("p3_a"))
        v.wait_ge(ctx.sem("p3_sp"), 1)
        v.tensor_tensor(dtbuf[0:64, :], acsA[0:64, :], acsB[0:64, :], op=ALU.add)
        v.drain()
        v.tensor_copy(ctx.t["tmp_bf"][0:64, :], dtbuf[0:64, :])
        v.wait_ge(ctx.sem("p3_expA"), 1)
        v.tensor_scalar(acsA[0:64, :], dtbuf[0:64, :], ctx.t["expA"][0:64, 0:1],
                        None, op0=ALU.mult)
        dinc(v, ctx.sem("p3_dtbf"))
        dtT = ctx.t["dtT"]
        for tb in range(8):
            pt = ctx.t["pt0"] if tb % 2 == 0 else ctx.t["pt1"]
            v.wait_ge(ctx.sem("p3_dtT"), tb + 1)
            v.tensor_copy(dtT[:, tb * 64:(tb + 1) * 64], pt[:, 0:64])
            dinc(v, ctx.sem("p3_evdt"))
        for lv, sh in enumerate([1, 2, 4, 8, 16, 32, 64, 128]):
            src = acsA if lv % 2 == 0 else acsB
            dst = acsB if lv % 2 == 0 else acsA
            v.drain()
            for c in range(NCH):
                c0 = c * CH
                v.tensor_tensor(dst[0:64, c0 + sh:c0 + CH],
                                src[0:64, c0 + sh:c0 + CH],
                                src[0:64, c0:c0 + CH - sh], op=ALU.add)
                v.tensor_copy(dst[0:64, c0:c0 + sh], src[0:64, c0:c0 + sh])
        dinc(v, ctx.sem("p3_acsdone"))
        for c in range(NCH):
            c0 = c * CH
            v.tensor_scalar(acsB[0:64, c0:c0 + CH], acsA[0:64, c0:c0 + CH],
                            acsA[0:64, c0 + CH - 1:c0 + CH], None, op0=ALU.subtract)
        dinc(v, ctx.sem("p3_dexp"))
        acsT = ctx.t["acsT"].rearrange("p (cc hh tb) -> p cc hh tb", cc=NCH, tb=2)
        for tb in range(8):
            pf = ctx.t["pf0"] if tb % 2 == 0 else ctx.t["pf1"]
            v.wait_ge(ctx.sem("p3_acsTtp"), tb + 1)
            v.tensor_copy(acsT[:, tb // 2, :, tb % 2], pf[:, 0:64])
            dinc(v, ctx.sem("p3_evacs"))
        for c in range(NCH):
            v.wait_ge(ctx.sem("p3_alT"), c + 1)
            v.tensor_copy(ctx.t["alast_rows"][0:1, c * 64:(c + 1) * 64],
                          ctx.t["pAl"][0:1, 0:64])
            dinc(v, ctx.sem("p3_eval"))
        decayT = ctx.t["decayT"]
        for tb in range(8):
            pt = ctx.t["pt0"] if tb % 2 == 0 else ctx.t["pt1"]
            v.wait_ge(ctx.sem("p3_decT"), tb + 1)
            v.tensor_copy(decayT[:, tb * 64:(tb + 1) * 64], pt[:, 0:64])
            dinc(v, ctx.sem("p3_evdec"))
        v.reciprocal(ctx.t["invdt"][:, :], dtT[:, :])
        v.drain()
        for tb in range(8):
            v.tensor_tensor(ctx.t["Dinvdt"][:, tb * 64:tb * 64 + 64],
                            ctx.t["invdt"][:, tb * 64:tb * 64 + 64],
                            ctx.t["D_tile"][:, 0:64], op=ALU.mult)
        dinc(v, ctx.sem("p3_dinv"))
        wcol = ctx.t["convw"]
        bcol = ctx.t["convb"]
        s_cd = ctx.sem("p3_convV")
        s_cfree = ctx.sem("p3_freeV")
        xdtT = ctx.t["xdtT"]
        BT = ctx.t["BT"]
        s_tp3 = ctx.sem("p3_tp")
        s_ev3 = ctx.sem("p3_ev")

        def evac_B(kb):
            for tb in range(8):
                pt = ctx.t["pt0"] if kb % 2 == 0 else ctx.t["pt1"]
                v.wait_ge(s_tp3, kb + 1)
                v.tensor_copy(BT[:, tb * 128:(tb + 1) * 128], pt[:, :])
                dinc(v, s_ev3)
                kb += 1
            return kb

        def evac_x(m, kb):
            for tb in range(8):
                pt = ctx.t["pt0"] if kb % 2 == 0 else ctx.t["pt1"]
                v.wait_ge(s_tp3, kb + 1)
                v.tensor_scalar(xdtT[:, tb * DIN + (2 * m) * 64:tb * DIN + (2 * m) * 64 + 64],
                                pt[:, 0:64], dtT[:, tb * 64 + 2 * m:tb * 64 + 2 * m + 1],
                                None, op0=ALU.mult)
                v.tensor_scalar(xdtT[:, tb * DIN + (2 * m + 1) * 64:tb * DIN + (2 * m + 1) * 64 + 64],
                                pt[:, 64:128], dtT[:, tb * 64 + 2 * m + 1:tb * 64 + 2 * m + 2],
                                None, op0=ALU.mult)
                dinc(v, s_ev3)
                kb += 1
            return kb

        kb = 0
        for i_o, j in enumerate(CONV_ORDER):
            cf = ctx.t["convfV"] if i_o % 2 == 0 else ctx.t["convfG"]
            if i_o >= 2:
                v.wait_ge(s_cfree, i_o - 1)
            xj = xBC[:, j * TF:(j + 1) * TF]
            v.drain()
            v.tensor_scalar(cf[:, :], xj[:, 1:1 + T], wcol[:, 4 * j:4 * j + 1],
                            bcol[:, j:j + 1], op0=ALU.mult, op1=ALU.add)
            for i in range(1, 4):
                v.drain()
                v.scalar_tensor_tensor(cf[:, :], xj[:, 1 + i:1 + i + T],
                                       wcol[:, 4 * j + i:4 * j + i + 1], cf[:, :],
                                       op0=ALU.mult, op1=ALU.add)
            dinc(v, s_cd)
            if i_o == 2:
                kb = evac_B(kb)
            if i_o >= 3:
                kb = evac_x(i_o - 3, kb)
        kb = evac_x(31, kb)
        return None

    def scalar(self, ctx, s):
        dtbuf, acsA, acsB = ctx.t["dtbuf"], ctx.t["acsA"], ctx.t["acsB"]
        s.activation(ctx.t["expA"][0:64, 0:1], ctx.t["A_log"][0:64, 0:1], AF.Exp)
        dinc(s, ctx.sem("p3_expA"))
        s.wait_ge(ctx.sem("p3_a"), 1)
        s.activation(acsA[0:64, :], dtbuf[0:64, :], AF.Abs)
        s.drain()
        s.activation(acsB[0:64, :], acsA[0:64, :], AF.Exp, scale=-1.0)
        s.drain()
        s.activation(acsA[0:64, :], acsB[0:64, :], AF.Ln, bias=ctx.constf_r(1.0, 64))
        s.drain()
        s.activation(acsB[0:64, :], dtbuf[0:64, :], AF.Relu)
        dinc(s, ctx.sem("p3_sp"))
        s.wait_ge(ctx.sem("p3_dexp"), 1)
        s.activation(ctx.t["decay_bf"][0:64, :], acsB[0:64, :], AF.Exp)
        dinc(s, ctx.sem("p3_dbf"))
        s.wait_ge(ctx.sem("p3_eval"), NCH)
        s.activation(ctx.t["alast_erow"][0:1, 0:NCH * 64],
                     ctx.t["alast_rows"][0:1, 0:NCH * 64], AF.Exp, scale=-1.0)
        dinc(s, ctx.sem("p3_alerow"))
        s_cdV = ctx.sem("p3_convV")
        s_silu = ctx.sem("p3_silu")
        sfree = ctx.sem("p3_freeV")
        for i_o, j in enumerate(CONV_ORDER):
            s.wait_ge(s_cdV, i_o + 1)
            cf = ctx.t["convfV"] if i_o % 2 == 0 else ctx.t["convfG"]
            if j == 32:
                dst = ctx.t["Bfeat"][:, :]
            elif j == 33:
                dst = ctx.t["Cfeat"][:, :]
            else:
                if j >= 2:
                    s.wait_ge(ctx.sem("p3_tp"), 8 * j)
                dst = ctx.t["conv_tmp"][:, (j % 2) * T:(j % 2) * T + T]
            s.activation(dst, cf[:, :], AF.Silu)
            dinc(s, s_silu)
            dinc(s, sfree)
        return None

    def tensor(self, ctx, t):
        ident = ctx.t["ident_bf"]
        ident32 = ctx.t["ident_f32"]
        tmp_bf = ctx.t["tmp_bf"]
        acsA = ctx.t["acsA"]
        t.wait_ge(ctx.sem("p3_dtbf"), 1)
        for tb in range(8):
            pt = ctx.t["pt0"] if tb % 2 == 0 else ctx.t["pt1"]
            if tb >= 2:
                t.wait_ge(ctx.sem("p3_evdt"), tb - 1)
            t.transpose(pt[:, 0:64], tmp_bf[0:64, tb * 128:(tb + 1) * 128],
                        ident[0:64, 0:64]).then_inc(ctx.sem("p3_dtT"), 1)
        t.wait_ge(ctx.sem("p3_acsdone"), 1)
        for tb in range(8):
            pf = ctx.t["pf0"] if tb % 2 == 0 else ctx.t["pf1"]
            if tb >= 2:
                t.wait_ge(ctx.sem("p3_evacs"), tb - 1)
            t.transpose(pf[:, 0:64], acsA[0:64, tb * 128:(tb + 1) * 128],
                        ident32[0:64, 0:64]).then_inc(ctx.sem("p3_acsTtp"), 1)
        for c in range(NCH):
            if c >= 1:
                t.wait_ge(ctx.sem("p3_eval"), c)
            t.transpose(ctx.t["pAl"][0:1, 0:64],
                        acsA[0:64, c * CH + CH - 1:c * CH + CH],
                        ident32[0:64, 0:64]).then_inc(ctx.sem("p3_alT"), 1)
        t.wait_ge(ctx.sem("p3_evdt"), 8)
        t.wait_ge(ctx.sem("p3_dbf"), 1)
        for tb in range(8):
            pt = ctx.t["pt0"] if tb % 2 == 0 else ctx.t["pt1"]
            if tb >= 2:
                t.wait_ge(ctx.sem("p3_evdec"), tb - 1)
            t.transpose(pt[:, 0:64], ctx.t["decay_bf"][0:64, tb * 128:(tb + 1) * 128],
                        ident[0:64, 0:64]).then_inc(ctx.sem("p3_decT"), 1)
        s_silu = ctx.sem("p3_silu")
        s_tp3 = ctx.sem("p3_tp")
        s_ev3 = ctx.sem("p3_ev")
        t.wait_ge(ctx.sem("p3_evdec"), 8)
        kb = 0
        t.wait_ge(s_silu, 1)
        for tb in range(8):
            pt = ctx.t["pt0"] if kb % 2 == 0 else ctx.t["pt1"]
            if kb >= 2:
                t.wait_ge(s_ev3, kb - 1)
            t.transpose(pt[:, :], ctx.t["Bfeat"][:, tb * 128:tb * 128 + 128],
                        ident[:, :]).then_inc(s_tp3, 1)
            kb += 1
        for j in range(32):
            t.wait_ge(s_silu, j + 3)
            for tb in range(8):
                pt = ctx.t["pt0"] if kb % 2 == 0 else ctx.t["pt1"]
                if kb >= 2:
                    t.wait_ge(s_ev3, kb - 1)
                t.transpose(pt[:, :],
                            ctx.t["conv_tmp"][:, (j % 2) * T + tb * 128:(j % 2) * T + tb * 128 + 128],
                            ident[:, :]).then_inc(s_tp3, 1)
                kb += 1
        return None


class P4aState:
    def gpsimd(self, ctx, g):
        sel4 = ctx.t["sel4"]
        g.memset(sel4[:, :], 1.0)
        g.drain()
        g.affine_select(out=sel4[:, :], in_=sel4[:, :], compare_op=ALU.is_ge,
                        fill=0.0, base=0, pattern=[[1, 512]], channel_multiplier=-128)
        g.drain()
        g.affine_select(out=sel4[:, :], in_=sel4[:, :], compare_op=ALU.is_ge,
                        fill=0.0, base=127, pattern=[[-1, 512]], channel_multiplier=128)
        g.drain()
        return None

    def tensor(self, ctx, t):
        s_bd = ctx.sem("p4a_bd")
        s_mm = ctx.sem("p4a_mm")
        s_rec = ctx.sem("p4a_rec")
        s_alb = ctx.sem("p4a_alb")
        for c in range(NCH):
            p = ctx.t["pAcsB0"] if c % 2 == 0 else ctx.t["pAcsB1"]
            if c >= 2:
                t.wait_ge(s_alb, c - 1)
            t.matmul(p[:, 0:64], ctx.t["ones_bf"][0:1, 0:128],
                     ctx.t["alast_erow"][0:1, c * 64:(c + 1) * 64],
                     start=True, stop=True).then_inc(ctx.sem("p4a_albtp"), 1)
        xdtT = ctx.t["xdtT"]
        bdec = ctx.t["bdec"]
        for idx in range(NCH * NH):
            c, h = idx // NH, idx % NH
            slot = idx % 2
            pS = ctx.t["pS0"] if slot == 0 else ctx.t["pS1"]
            t.wait_ge(s_bd, idx + 1)
            if idx >= 2:
                t.wait_ge(s_rec, idx - 1)
            for tb in (2 * c, 2 * c + 1):
                mm = t.matmul(pS, bdec[:, slot * 256 + (tb % 2) * 128:slot * 256 + (tb % 2) * 128 + 128],
                              xdtT[:, tb * DIN + h * 64:tb * DIN + h * 64 + 64],
                              start=(tb % 2 == 0), stop=(tb % 2 == 1))
            mm.then_inc(s_mm, 1)
        return None

    def vector(self, ctx, v):
        s_bd = ctx.sem("p4a_bd")
        s_mm = ctx.sem("p4a_mm")
        s_rec = ctx.sem("p4a_rec")
        s_alb = ctx.sem("p4a_alb")
        BT = ctx.t["BT"]
        decayT = ctx.t["decayT"]
        bdec = ctx.t["bdec"]
        S_run = ctx.t["S_run"]
        for c in range(NCH):
            p = ctx.t["pAcsB0"] if c % 2 == 0 else ctx.t["pAcsB1"]
            v.wait_ge(ctx.sem("p4a_albtp"), c + 1)
            v.tensor_copy(ctx.t["expAlB"][:, c * 64:(c + 1) * 64], p[:, 0:64])
            dinc(v, s_alb)
        for idx in range(NCH * NH):
            c, h = idx // NH, idx % NH
            slot = idx % 2
            pS = ctx.t["pS0"] if slot == 0 else ctx.t["pS1"]
            if idx >= 2:
                v.wait_ge(s_mm, idx - 1)
            for tb in (2 * c, 2 * c + 1):
                v.tensor_scalar(bdec[:, slot * 256 + (tb % 2) * 128:slot * 256 + (tb % 2) * 128 + 128],
                                BT[:, tb * 128:(tb + 1) * 128],
                                decayT[:, tb * 64 + h:tb * 64 + h + 1], None, op0=ALU.mult)
            dinc(v, s_bd)
            v.wait_ge(s_mm, idx + 1)
            sh = S_run[:, h * 32:(h + 1) * 32].bitcast(BF16) if False else None
            srun_h = ctx.t["S_run"][:, h * 64:(h + 1) * 64]
            if c == 0:
                v.tensor_copy(srun_h, pS)
            else:
                stmp = ctx.t["stmpA"]
                v.tensor_scalar(stmp[:, :], srun_h, ctx.t["expAlB"][:, c * 64 + h:c * 64 + h + 1],
                                None, op0=ALU.mult)
                v.drain()
                v.tensor_tensor(srun_h, stmp[:, :], pS, op=ALU.add)
            dinc(v, s_rec)
        return None


class P4xExchange:
    def gpsimd(self, ctx, g):
        sem = ctx.sem("p4x_dma")
        sem2 = ctx.sem("p4x_dma2")
        cc = ctx.sem("p4x_cc")
        g.dma_start(out=ctx.t["stout"].ap(), in_=ctx.t["S_run"][:, :]).then_inc(sem, 16)
        g.wait_ge(sem, 16)
        g.collective_compute(
            "AllGather", ALU.bypass,
            replica_groups=PAIRS,
            ins=[ctx.t["stout"].ap().opt()],
            outs=[ctx.t["stgath"].ap().opt()],
        ).then_inc(cc, 1)
        g.wait_ge(cc, 1)
        g.dma_start(out=ctx.t["S_run"][:, :], in_=ctx.t["stgath"].ap()[0:128, :]).then_inc(sem2, 16)
        g.wait_ge(sem2, 16)
        g.sem_inc(ctx.sem("p4x_ld"), 1)
        return None

    def vector(self, ctx, v):
        S_run = ctx.t["S_run"]
        role = ctx.t["roleflag"]
        bnc = ctx.t["roleb"]
        v.wait_ge(ctx.sem("p4x_ld"), 1)
        for half in range(2):
            v.tensor_copy(bnc[:, 0:2048], S_run[:, half * 2048:(half + 1) * 2048])
            v.drain()
            v.tensor_scalar(S_run[:, half * 2048:(half + 1) * 2048], bnc[:, 0:2048],
                            role[:, 0:1], None, op0=ALU.mult)
            v.drain()
        return None


class P4bSSD:

    def tensor(self, ctx, t):
        ident32 = ctx.t["ident_f32"]
        identb = ctx.t["ident_bf"]
        acsT = ctx.t["acsT"]
        xdtT = ctx.t["xdtT"]
        MT = ctx.t["MT"]
        bdec = ctx.t["bdec"]
        S_run = ctx.t["S_run"]
        sel4 = ctx.t["sel4"]
        s_gm = ctx.sem("p4b_gm")
        s_gtp = ctx.sem("p4b_gtp")
        s_rtp = ctx.sem("p4b_rtp")
        s_rowv = ctx.sem("p4b_rowv")
        s_hl = ctx.sem("p4b_hl")
        s_acsbtp = ctx.sem("p4b_acsbtp")
        s_acsb = ctx.sem("p4b_acsb")
        s_mtg = ctx.sem("p4b_mtg")
        s_mtv = ctx.sem("p4b_mtv")
        s_ymm = ctx.sem("p4b_ymm")
        s_yev = ctx.sem("p4b_yev")
        s_smm = ctx.sem("p4b_smm")
        s_rec = ctx.sem("p4b_rec")

        def ybl(k):
            c2, h2 = k // NH, k % NH
            sl = k % 2
            t.wait_ge(s_mtg, k + 1)
            t.wait_ge(s_mtv, k + 1)
            if k >= 2:
                t.wait_ge(s_yev, k - 1)
            pY = ctx.t["pY0"] if sl == 0 else ctx.t["pY1"]
            t.matmul(pY, xdtT[:, (2 * c2) * DIN + h2 * 64:(2 * c2) * DIN + h2 * 64 + 64],
                     MT[:, sl * 512:sl * 512 + 256], start=True, stop=False)
            t.matmul(pY, xdtT[:, (2 * c2 + 1) * DIN + h2 * 64:(2 * c2 + 1) * DIN + h2 * 64 + 64],
                     MT[:, sl * 512 + 256:sl * 512 + 512], start=False, stop=False)
            t.matmul(pY, S_run[:, h2 * 64:(h2 + 1) * 64],
                     ctx.t["csdo"][:, sl * 256:sl * 256 + 256], start=False, stop=False)
            t.matmul(pY[0:64, 0:128], ctx.t["xdtmp"][:, sl * 128:sl * 128 + 64],
                     identb[:, :], start=False, stop=False)
            t.matmul(pY[0:64, 128:256], ctx.t["xdtmp"][:, sl * 128 + 64:sl * 128 + 128],
                     identb[:, :], start=False, stop=True).then_inc(s_ymm, 1)
            if k >= 2:
                t.wait_ge(s_rec, k - 1)
            pS = ctx.t["pS0"] if sl == 0 else ctx.t["pS1"]
            for tb in (2 * c2, 2 * c2 + 1):
                mm = t.matmul(pS, bdec[:, sl * 256 + (tb % 2) * 128:sl * 256 + (tb % 2) * 128 + 128],
                              xdtT[:, tb * DIN + h2 * 64:tb * DIN + h2 * 64 + 64],
                              start=(tb % 2 == 0), stop=(tb % 2 == 1))
            mm.then_inc(s_smm, 1)

        for i in range(NCH * NH):
            c, h = i // NH, i % NH
            slot = i % 2
            p = i // 2
            if h == 0:
                if c > 0:
                    t.wait_ge(s_rowv, c * 32)
                for st in range(2):
                    pG = ctx.t["pG0"] if st == 0 else ctx.t["pG1"]
                    t.matmul(pG, ctx.t["Bfeat"][:, c * CH + st * 128:c * CH + st * 128 + 128],
                             ctx.t["Cfeat"][:, c * CH:(c + 1) * CH],
                             start=True, stop=True).then_inc(s_gtp, 1)
            if h % 2 == 0:
                t.wait_ge(s_gm, 2 * (c + 1))
                if p >= 2:
                    t.wait_ge(s_rowv, p - 1)
                pR = ctx.t["pRT0"] if p % 2 == 0 else ctx.t["pRT1"]
                t.matmul(pR, acsT[:, c * 128 + 2 * h:c * 128 + 2 * h + 4], ident32[:, :],
                         start=True, stop=True).then_inc(s_rtp, 1)
            t.wait_ge(s_hl, p + 1)
            if i >= 2:
                t.wait_ge(s_acsb, i - 1)
            pB_ = ctx.t["pAcsB0"] if slot == 0 else ctx.t["pAcsB1"]
            hiB = ctx.t["hiB0"] if p % 2 == 0 else ctx.t["hiB1"]
            loB = ctx.t["loB0"] if p % 2 == 0 else ctx.t["loB1"]
            hp2 = h % 2
            r0, r1 = 2 * hp2, 2 * hp2 + 1
            t.matmul(pB_[:, 0:128], sel4[:, r0 * 128:r0 * 128 + 128], hiB,
                     start=True, stop=False)
            t.matmul(pB_[:, 0:128], sel4[:, r0 * 128:r0 * 128 + 128], loB,
                     start=False, stop=True)
            t.matmul(pB_[:, 128:256], sel4[:, r1 * 128:r1 * 128 + 128], hiB,
                     start=True, stop=False)
            t.matmul(pB_[:, 128:256], sel4[:, r1 * 128:r1 * 128 + 128], loB,
                     start=False, stop=True).then_inc(s_acsbtp, 1)
            if i >= 2:
                ybl(i - 2)
        ybl(NCH * NH - 2)
        ybl(NCH * NH - 1)
        return None

    def gpsimd(self, ctx, g):
        MT = ctx.t["MT"]
        expLmin = ctx.t["expLmin"]
        csdo = ctx.t["csdo"]
        sdo = ctx.t["sdo_bf"]
        Cfeat = ctx.t["Cfeat"]
        s_mtv = ctx.sem("p4b_mtv")
        s_gm = ctx.sem("p4b_gm")
        s_mtg = ctx.sem("p4b_mtg")
        for j in range(NCH * NH):
            sl = j % 2
            c2 = j // NH
            g.wait_ge(s_mtv, j + 1)
            g.wait_ge(s_gm, 2 * (c2 + 1))
            Gmb = ctx.t["Gm"] if c2 % 2 == 0 else ctx.t["GmB"]
            g.tensor_tensor(MT[:, sl * 512:sl * 512 + 256],
                            expLmin[:, sl * 512:sl * 512 + 256],
                            Gmb[:, 0:256], op=ALU.mult)
            g.tensor_tensor(MT[:, sl * 512 + 256:sl * 512 + 512],
                            expLmin[:, sl * 512 + 256:sl * 512 + 512],
                            Gmb[:, 256:512], op=ALU.mult)
            g.tensor_tensor(csdo[:, sl * 256:sl * 256 + 256],
                            Cfeat[:, c2 * CH:(c2 + 1) * CH],
                            sdo[:, (j % 4) * 256:(j % 4) * 256 + 256],
                            op=ALU.mult).then_inc(s_mtg, 1)
        return None

    def scalar(self, ctx, s):
        acsT = ctx.t["acsT"]
        expL = ctx.t["expL"]
        sdo = ctx.t["sdo_bf"]
        S_run = ctx.t["S_run"]
        expAlB = ctx.t["expAlB"]
        y = ctx.t["y_feat"]
        s_acsbtp = ctx.sem("p4b_acsbtp")
        s_mtg = ctx.sem("p4b_mtg")
        s_expl = ctx.sem("p4b_expl")
        s_acsb = ctx.sem("p4b_acsb")
        s_stm = ctx.sem("p4b_stm")
        s_rec = ctx.sem("p4b_rec")
        s_ymm = ctx.sem("p4b_ymm")
        s_yev = ctx.sem("p4b_yev")

        def yev(k):
            c2, h2 = k // NH, k % NH
            sl = k % 2
            s.wait_ge(s_ymm, k + 1)
            pY = ctx.t["pY0"] if sl == 0 else ctx.t["pY1"]
            r0 = (h2 % 2) * 64
            s.copy(y[r0:r0 + 64, (h2 // 2) * T + c2 * CH:(h2 // 2) * T + (c2 + 1) * CH], pY)
            s.drain()
            s.sem_inc(s_yev, 1)

        for i in range(NCH * NH):
            c, h = i // NH, i % NH
            slot = i % 2
            pB_ = ctx.t["pAcsB0"] if slot == 0 else ctx.t["pAcsB1"]
            s.wait_ge(s_acsbtp, i + 1)
            if i >= 2:
                s.wait_ge(ctx.sem("p4b_mtv"), i - 1)
            if i >= 4:
                s.wait_ge(s_mtg, i - 3)
            s.activation(expL[:, slot * 512:slot * 512 + 256], pB_, AF.Exp,
                         scale=-1.0, bias=acsT[:, c * 128 + 2 * h:c * 128 + 2 * h + 1])
            s.activation(expL[:, slot * 512 + 256:slot * 512 + 512], pB_, AF.Exp,
                         scale=-1.0, bias=acsT[:, c * 128 + 2 * h + 1:c * 128 + 2 * h + 2])
            s.activation(sdo[:, (i % 4) * 256:(i % 4) * 256 + 256], pB_, AF.Exp,
                         scale=-1.0)
            if c < NCH - 1:
                if i >= 4:
                    s.wait_ge(s_rec, i - 3)
                sl4 = (i % 4) * 64
                s.activation(ctx.t["stmp4"][:, sl4:sl4 + 64],
                             S_run[:, h * 64:(h + 1) * 64], AF.Copy,
                             scale=expAlB[:, c * 64 + h:c * 64 + h + 1])
            s.drain()
            s.sem_inc(s_expl, 1)
            s.sem_inc(s_acsb, 1)
            s.sem_inc(s_stm, 1)
            if i >= 2:
                yev(i - 2)
        yev(NCH * NH - 2)
        yev(NCH * NH - 1)
        return None

    def vector(self, ctx, v):
        sel4 = ctx.t["sel4"]
        S_run = ctx.t["S_run"]
        BT = ctx.t["BT"]
        decayT = ctx.t["decayT"]
        bdec = ctx.t["bdec"]
        xdtT = ctx.t["xdtT"]
        s_gtp = ctx.sem("p4b_gtp")
        s_gm = ctx.sem("p4b_gm")
        s_rtp = ctx.sem("p4b_rtp")
        s_rowv = ctx.sem("p4b_rowv")
        s_hl = ctx.sem("p4b_hl")
        s_acsbtp = ctx.sem("p4b_acsbtp")
        s_mtg = ctx.sem("p4b_mtg")
        s_mtv = ctx.sem("p4b_mtv")
        s_smm = ctx.sem("p4b_smm")
        s_stm = ctx.sem("p4b_stm")
        s_rec = ctx.sem("p4b_rec")
        def rec(k):
            c2, h2 = k // NH, k % NH
            if c2 < NCH - 1:
                sl4 = (k % 4) * 64
                pS = ctx.t["pS0"] if k % 2 == 0 else ctx.t["pS1"]
                v.wait_ge(s_stm, k + 1)
                v.wait_ge(s_smm, k + 1)
                v.tensor_tensor(S_run[:, h2 * 64:(h2 + 1) * 64],
                                ctx.t["stmp4"][:, sl4:sl4 + 64], pS, op=ALU.add)
                v.drain()
            v.sem_inc(s_rec, 1)

        for c in range(NCH):
            Gmb = ctx.t["Gm"] if c % 2 == 0 else ctx.t["GmB"]
            if c >= 2:
                v.wait_ge(s_mtg, (c - 1) * NH)
            for st in range(2):
                pG = ctx.t["pG0"] if st == 0 else ctx.t["pG1"]
                v.wait_ge(s_gtp, 2 * c + st + 1)
                v.tensor_tensor(Gmb[:, st * CH:(st + 1) * CH], pG,
                                ctx.t["gmaskA" if st == 0 else "gmaskB"][:, :], op=ALU.mult)
            v.drain()
            v.sem_inc(s_gm, 2)
            for h in range(NH):
                i = c * NH + h
                slot = i % 2
                p = i // 2
                if h % 2 == 0:
                    v.wait_ge(s_rtp, p + 1)
                    if p >= 2:
                        v.wait_ge(s_acsbtp, 2 * p - 2)
                    pR = ctx.t["pRT0"] if p % 2 == 0 else ctx.t["pRT1"]
                    rowF = ctx.t["rowF0"] if p % 2 == 0 else ctx.t["rowF1"]
                    hiB = ctx.t["hiB0"] if p % 2 == 0 else ctx.t["hiB1"]
                    loB = ctx.t["loB0"] if p % 2 == 0 else ctx.t["loB1"]
                    v.tensor_copy(rowF, pR)
                    v.tensor_copy(hiB, pR)
                    v.drain()
                    v.sem_inc(s_rowv, 1)
                    v.tensor_tensor(loB, rowF, hiB, op=ALU.subtract)
                    v.drain()
                    v.sem_inc(s_hl, 1)
                if i >= 2:
                    v.wait_ge(s_smm, i - 1)
                v.wait_ge(ctx.sem("p4b_expl"), i + 1)
                v.tensor_scalar(ctx.t["expLmin"][:, slot * 512:slot * 512 + 512],
                                ctx.t["expL"][:, slot * 512:slot * 512 + 512],
                                1.0, None, op0=ALU.min)
                for tb in range(2):
                    v.tensor_scalar(ctx.t["xdtmp"][:, slot * 128 + tb * 64:slot * 128 + tb * 64 + 64],
                                    xdtT[:, (2 * c + tb) * DIN + h * 64:(2 * c + tb) * DIN + h * 64 + 64],
                                    ctx.t["Dinvdt"][:, (2 * c + tb) * 64 + h:(2 * c + tb) * 64 + h + 1],
                                    None, op0=ALU.mult)
                for tb in (2 * c, 2 * c + 1):
                    v.tensor_scalar(bdec[:, slot * 256 + (tb % 2) * 128:slot * 256 + (tb % 2) * 128 + 128],
                                    BT[:, tb * 128:(tb + 1) * 128],
                                    decayT[:, tb * 64 + h:tb * 64 + h + 1], None, op0=ALU.mult)
                v.drain()
                v.sem_inc(s_mtv, 1)
                if i >= 2:
                    rec(i - 2)
        rec(NCH * NH - 2)
        rec(NCH * NH - 1)
        return None


P5_CHUNKS = [(512 * i, 512) for i in range(8)]


class P5Gate:

    def groups(self):
        idx = 0
        for ci, (e0, ew) in enumerate(P5_CHUNKS):
            for j in range(4):
                for tc in range(2):
                    yield idx, ci, ew, j, ci * 4 + j, tc
                    idx += 1

    def gpsimd(self, ctx, g):
        w = ctx.t["W_in"]
        s_mm = ctx.sem("p5_mm")
        for ci, (e0, ew) in enumerate(P5_CHUNKS):
            if ci >= 2:
                g.wait_ge(s_mm, 8 * (ci - 1))
            slab = ctx.t["slabA"] if ci % 2 == 0 else ctx.t["slabB"]
            s_wl = ctx.sem("p5_wlA" if ci % 2 == 0 else "p5_wlB")
            src = w.ap()[:, e0:e0 + ew].rearrange("(k p) e -> p k e", p=128)
            dst = slab[:, 0:16 * ew].rearrange("p (k e) -> p k e", k=16)
            g.dma_start(out=dst, in_=src).then_inc(s_wl, 16)
        g.wait_ge(s_mm, 64)
        wo = ctx.t["W_out_d"]
        qi = 0
        for th in range(2):
            for oh in range(2):
                for et in range(32):
                    if qi >= 2:
                        g.wait_ge(ctx.sem("p5_wou"), qi - 1)
                    slab = ctx.t["slabC"] if qi % 2 == 0 else ctx.t["slabD"]
                    s_wl = ctx.sem("p5_woA" if qi % 2 == 0 else "p5_woB")
                    src = wo.ap()[et * 128:(et + 1) * 128, oh * 1024:(oh + 1) * 1024]
                    g.dma_start(out=slab[:, 0:1024], in_=src).then_inc(s_wl, 16)
                    qi += 1
        return None

    def tensor(self, ctx, t):
        xn = ctx.t["xn"]
        s_silu = ctx.sem("p5_silu")
        s_mm = ctx.sem("p5_mm")
        s_sq = ctx.sem("p5_sq")
        last_ci = -1
        for idx, ci, ew, j, jglob, tc in self.groups():
            if ci != last_ci:
                t.wait_ge(ctx.sem("p5_wlA" if ci % 2 == 0 else "p5_wlB"),
                          16 * (ci // 2 + 1))
                last_ci = ci
            slab = ctx.t["slabA"] if ci % 2 == 0 else ctx.t["slabB"]
            p = ctx.t["pA"] if idx % 2 == 0 else ctx.t["pB"]
            if idx >= 2:
                t.wait_ge(s_silu, idx - 1)
            for k in range(16):
                mm = t.matmul(p, slab[:, k * ew + j * 128:k * ew + j * 128 + 128],
                              xn[:, k * TF + HALO + tc * 512:k * TF + HALO + tc * 512 + 512],
                              start=(k == 0), stop=(k == 15))
            mm.then_inc(s_mm, 1)
            if idx >= 2:
                self._sq_mm(ctx, t, idx - 2)
        self._sq_mm(ctx, t, 62)
        self._sq_mm(ctx, t, 63)
        t.wait_ge(ctx.sem("p5_rrow"), 1)
        for tc in range(2):
            t.matmul(ctx.t["bank0" if tc == 0 else "bank1"].ap()[:, 0:512],
                     ctx.t["ones_bf"][0:1, 0:128],
                     ctx.t["rrow_bf"][0:1, tc * 512:tc * 512 + 512],
                     start=True, stop=True).then_inc(ctx.sem("p5_rbtp"), 1)
        t.wait_ge(ctx.sem("p5_fin"), 32)
        banks = [ctx.t[f"bank{i}"] for i in range(8)]
        yfin = ctx.t["y_feat"]
        qi = 0
        for th in range(2):
            for oh in range(2):
                q = th * 2 + oh
                if q >= 1:
                    t.wait_ge(ctx.sem("p5_wev"), q)
                for et in range(32):
                    t.wait_ge(ctx.sem("p5_woA" if qi % 2 == 0 else "p5_woB"),
                              16 * (qi // 2 + 1))
                    slab = ctx.t["slabC"] if qi % 2 == 0 else ctx.t["slabD"]
                    for tt in range(4):
                        for oc in range(2):
                            mm = t.matmul(banks[tt * 2 + oc].ap(),
                                          yfin[:, et * T + th * 512 + tt * 128:et * T + th * 512 + tt * 128 + 128],
                                          slab[:, oc * 512:oc * 512 + 512],
                                          start=(et == 0), stop=(et == 31))
                    mm.then_inc(ctx.sem("p5_wou"), 1)
                    qi += 1
        return None

    def _sq_mm(self, ctx, t, idx):
        jglob, tc = idx // 2, idx % 2
        t.wait_ge(ctx.sem("p5_sq"), idx + 1)
        psq = ctx.t["bank2"].ap() if tc == 0 else ctx.t["bank3"].ap()
        t.matmul(psq[0:1, 0:512], ctx.t["ones_bf"][:, 0:1],
                 ctx.t["sqbuf"][:, (idx % 2) * 512:(idx % 2) * 512 + 512],
                 start=(jglob == 0), stop=(jglob == 31)).then_inc(ctx.sem("p5_sqmm"), 1)

    def scalar(self, ctx, s):
        s_mm = ctx.sem("p5_mm")
        for idx, ci, ew, j, jglob, tc in self.groups():
            p = ctx.t["pA"] if idx % 2 == 0 else ctx.t["pB"]
            s.wait_ge(s_mm, idx + 1)
            if idx >= 2:
                s.wait_ge(ctx.sem("p5_zt"), idx - 1)
            s.activation(ctx.t["ztmp"][:, (idx % 2) * 512:(idx % 2) * 512 + 512],
                         p, AF.Silu)
            dinc(s, ctx.sem("p5_silu"))
        s.wait_ge(ctx.sem("p5_sqmm"), 64)
        s.activation(ctx.t["rr_ln0"][0:1, 0:512], ctx.t["bank2"].ap()[0:1, 0:512], AF.Ln,
                     scale=1.0 / DIN, bias=ctx.constf_r(EPS, 1))
        s.activation(ctx.t["rr_ln1"][0:1, 0:512], ctx.t["bank3"].ap()[0:1, 0:512], AF.Ln,
                     scale=1.0 / DIN, bias=ctx.constf_r(EPS, 1))
        s.drain()
        s.activation(ctx.t["rrow_bf"][0:1, 0:512], ctx.t["rr_ln0"][0:1, 0:512],
                     AF.Exp, scale=-0.5)
        s.activation(ctx.t["rrow_bf"][0:1, 512:1024], ctx.t["rr_ln1"][0:1, 0:512],
                     AF.Exp, scale=-0.5)
        dinc(s, ctx.sem("p5_rrow"))
        return None

    def vector(self, ctx, v):
        y = ctx.t["y_feat"]
        for idx, ci, ew, j, jglob, tc in self.groups():
            v.wait_ge(ctx.sem("p5_silu"), idx + 1)
            ysl = y[:, jglob * T + tc * 512:jglob * T + tc * 512 + 512]
            zt = ctx.t["ztmp"][:, (idx % 2) * 512:(idx % 2) * 512 + 512]
            bounce = ctx.t["ztmp2"][:, (idx % 2) * 512:(idx % 2) * 512 + 512]
            v.tensor_tensor(bounce, ysl, zt, op=ALU.mult)
            dinc(v, ctx.sem("p5_zt"))
            if idx >= 2:
                v.wait_ge(ctx.sem("p5_sqmm"), idx - 1)
            v.tensor_tensor(ctx.t["sqbuf"][:, (idx % 2) * 512:(idx % 2) * 512 + 512],
                            bounce, bounce, op=ALU.mult)
            dinc(v, ctx.sem("p5_sq"))
            v.tensor_copy(ysl, bounce)
            dinc(v, ctx.sem("p5_yg"))
            v.drain()
        for tc in range(2):
            v.wait_ge(ctx.sem("p5_rbtp"), tc + 1)
            pb = ctx.t["bank0" if tc == 0 else "bank1"].ap()
            v.tensor_copy(ctx.t["rstd_b"][:, tc * 512:tc * 512 + 512], pb[:, 0:512])
        v.drain()
        for jglob in range(32):
            gs = y[:, jglob * T:(jglob + 1) * T]
            v.tensor_tensor(ctx.t["ztmp2"][:, 0:1024], gs, ctx.t["rstd_b"][:, 0:1024],
                            op=ALU.mult)
            v.drain()
            v.tensor_scalar(gs, ctx.t["ztmp2"][:, 0:1024],
                            ctx.t["norm_scale"][:, jglob:jglob + 1], None, op0=ALU.mult)
            dinc(v, ctx.sem("p5_fin"))
        banks = [ctx.t[f"bank{i}"] for i in range(8)]
        for q in range(4):
            v.wait_ge(ctx.sem("p5_wou"), 32 * (q + 1))
            for tt in range(4):
                k = q * 4 + tt
                sl = k % 2
                if k >= 2:
                    v.wait_ge(ctx.sem("p5_oA" if sl == 0 else "p5_oB"), 16 * (k // 2))
                for oc in range(2):
                    v.tensor_copy(ctx.t["stage2"][:, sl * 1024 + oc * 512:sl * 1024 + oc * 512 + 512],
                                  banks[tt * 2 + oc].ap())
                dinc(v, ctx.sem("p5_wst"))
            dinc(v, ctx.sem("p5_wev"))
        return None

    def sync(self, ctx, sy):
        ms = ctx.t["mamba_s"]
        for q in range(4):
            th, oh = q // 2, q % 2
            for tt in range(4):
                k = q * 4 + tt
                sl = k % 2
                sy.wait_ge(ctx.sem("p5_wst"), k + 1)
                dst = ms.ap()[th * 512 + tt * 128:th * 512 + tt * 128 + 128,
                              oh * 1024:(oh + 1) * 1024]
                sy.dma_start(out=dst, in_=ctx.t["stage2"][:, sl * 1024:sl * 1024 + 1024]) \
                    .then_inc(ctx.sem("p5_oA" if sl == 0 else "p5_oB"), 16)
        sy.wait_ge(ctx.sem("p5_oA"), 16 * 8)
        sy.wait_ge(ctx.sem("p5_oB"), 16 * 8)
        return None


class P6FFN0:

    def __init__(self, xn_name="xn", xn_off=HALO, out_name="ffn0_s",
                 wg="WgT_d", wu="WuT_d", wd="WdT_d", pfx="p6",
                 ne=FF0, gate_scale=None, up_scale=None, slab0=65536,
                 xw=None):
        self.xn_name, self.xn_off, self.out_name = xn_name, xn_off, out_name
        self.wg, self.wu, self.wd, self.pfx = wg, wu, wd, pfx
        self.ne = ne
        self.n_ch = ne // 256
        self.n_et = ne // 128
        self.gate_scale = gate_scale
        self.up_scale = up_scale
        self.slab0 = slab0
        self.xw = xw if xw is not None else (TF if xn_name == "xn" else T)

    def _slabs(self, ctx):
        a = ctx.t["arena"]
        s0 = self.slab0
        return ([a.ap()[:, s0 + i * 4096:s0 + (i + 1) * 4096] for i in range(2)],
                [a.ap()[:, s0 + 8192 + i * 4096:s0 + 8192 + (i + 1) * 4096] for i in range(2)],
                [a.ap()[:, s0 + i * 512:s0 + (i + 1) * 512] for i in range(2)])

    def gpsimd(self, ctx, g):
        x = self.pfx
        slG, slU, slD = self._slabs(ctx)
        for ci in range(self.n_ch):
            if ci >= 2:
                g.wait_ge(ctx.sem(x + "_mm"), 4 * (ci - 1))
            sg = ctx.sem(x + ("_wgA" if ci % 2 == 0 else "_wgB"))
            su = ctx.sem(x + ("_wuA" if ci % 2 == 0 else "_wuB"))
            for wname, sl, sem in ((self.wg, slG[ci % 2], sg), (self.wu, slU[ci % 2], su)):
                src = ctx.t[wname].ap()[:, ci * 256:(ci + 1) * 256] \
                    .rearrange("(k p) e -> p k e", p=128)
                dst = sl.rearrange("p (k e) -> p k e", k=16)
                g.dma_start(out=dst, in_=src).then_inc(sem, 16)
        g.wait_ge(ctx.sem(x + "_mm"), 4 * self.n_ch)
        for dq in range(4):
            for et in range(self.n_et):
                qd = dq * self.n_et + et
                if qd >= 2:
                    g.wait_ge(ctx.sem(x + "_du"), qd - 1)
                sem = ctx.sem(x + ("_wdA" if qd % 2 == 0 else "_wdB"))
                src = ctx.t[self.wd].ap()[et * 128:(et + 1) * 128,
                                          dq * 512:(dq + 1) * 512]
                g.dma_start(out=slD[qd % 2], in_=src).then_inc(sem, 16)
        return None

    def tensor(self, ctx, t):
        x = self.pfx
        xn = ctx.t[self.xn_name]
        xw, off = self.xw, self.xn_off
        banks = [ctx.t[f"bank{i}"] for i in range(8)]
        h1 = ctx.t["h1f"]
        slG, slU, slD = self._slabs(ctx)
        for gi in range(4 * self.n_ch):
            et, tc = gi // 2, gi % 2
            ci, j = et // 2, et % 2
            if tc == 0 and j == 0:
                t.wait_ge(ctx.sem(x + ("_wgA" if ci % 2 == 0 else "_wgB")),
                          16 * (ci // 2 + 1))
                t.wait_ge(ctx.sem(x + ("_wuA" if ci % 2 == 0 else "_wuB")),
                          16 * (ci // 2 + 1))
            pg = banks[gi % 2].ap()
            pu = banks[2 + gi % 2].ap()
            if gi >= 2:
                t.wait_ge(ctx.sem(x + "_h1"), gi - 1)
            for k in range(16):
                t.matmul(pg, slG[ci % 2][:, k * 256 + j * 128:k * 256 + j * 128 + 128],
                         xn[:, k * xw + off + tc * 512:k * xw + off + tc * 512 + 512],
                         start=(k == 0), stop=(k == 15))
                mm = t.matmul(pu, slU[ci % 2][:, k * 256 + j * 128:k * 256 + j * 128 + 128],
                              xn[:, k * xw + off + tc * 512:k * xw + off + tc * 512 + 512],
                              start=(k == 0), stop=(k == 15))
            mm.then_inc(ctx.sem(x + "_mm"), 1)
        t.wait_ge(ctx.sem(x + "_h1"), 4 * self.n_ch)
        for dq in range(4):
            if dq >= 1:
                t.wait_ge(ctx.sem(x + "_wst"), 8 * dq)
            for et in range(self.n_et):
                qd = dq * self.n_et + et
                t.wait_ge(ctx.sem(x + ("_wdA" if qd % 2 == 0 else "_wdB")),
                          16 * (qd // 2 + 1))
                for tt in range(8):
                    mm = t.matmul(banks[tt].ap(),
                                  h1[:, et * 1024 + tt * 128:et * 1024 + tt * 128 + 128],
                                  slD[qd % 2],
                                  start=(et == 0), stop=(et == self.n_et - 1))
                mm.then_inc(ctx.sem(x + "_du"), 1)
        return None

    def scalar(self, ctx, s):
        x = self.pfx
        banks = [ctx.t[f"bank{i}"] for i in range(8)]
        for gi in range(4 * self.n_ch):
            et = gi // 2
            s.wait_ge(ctx.sem(x + "_mm"), gi + 1)
            if gi >= 2:
                s.wait_ge(ctx.sem(x + "_zt"), gi - 1)
            if self.gate_scale:
                s.activation(ctx.t["ztmp"][:, (gi % 2) * 512:(gi % 2) * 512 + 512],
                             banks[gi % 2].ap(), AF.Silu,
                             scale=ctx.t[self.gate_scale][:, et:et + 1])
            else:
                s.activation(ctx.t["ztmp"][:, (gi % 2) * 512:(gi % 2) * 512 + 512],
                             banks[gi % 2].ap(), AF.Silu)
            dinc(s, ctx.sem(x + "_silu"))
        return None

    def vector(self, ctx, v):
        x = self.pfx
        banks = [ctx.t[f"bank{i}"] for i in range(8)]
        h1 = ctx.t["h1f"]
        for gi in range(4 * self.n_ch):
            et, tc = gi // 2, gi % 2
            v.wait_ge(ctx.sem(x + "_silu"), gi + 1)
            dst = h1[:, et * 1024 + tc * 512:et * 1024 + tc * 512 + 512]
            if self.up_scale:
                v.scalar_tensor_tensor(dst, banks[2 + gi % 2].ap(),
                                       ctx.t[self.up_scale][:, et:et + 1],
                                       ctx.t["ztmp"][:, (gi % 2) * 512:(gi % 2) * 512 + 512],
                                       op0=ALU.mult, op1=ALU.mult)
            else:
                v.tensor_tensor(dst, banks[2 + gi % 2].ap(),
                                ctx.t["ztmp"][:, (gi % 2) * 512:(gi % 2) * 512 + 512],
                                op=ALU.mult)
            v.drain()
            v.sem_inc(ctx.sem(x + "_zt"), 1)
            v.sem_inc(ctx.sem(x + "_h1"), 1)
        stage = ctx.t["dstage"]
        for dq in range(4):
            v.wait_ge(ctx.sem(x + "_du"), self.n_et * (dq + 1))
            for tt in range(8):
                k = dq * 8 + tt
                sl = k % 2
                if k >= 2:
                    v.wait_ge(ctx.sem(x + ("_oA" if sl == 0 else "_oB")), 16 * (k // 2))
                v.tensor_copy(stage[:, sl * 512:sl * 512 + 512], banks[tt].ap())
                dinc(v, ctx.sem(x + "_wst"))
        return None

    def sync(self, ctx, sy):
        x = self.pfx
        stage = ctx.t["dstage"]
        out = ctx.t[self.out_name]
        for dq in range(4):
            for tt in range(8):
                k = dq * 8 + tt
                sl = k % 2
                sy.wait_ge(ctx.sem(x + "_wst"), k + 1)
                dst = out.ap()[tt * 128:(tt + 1) * 128, dq * 512:(dq + 1) * 512]
                sy.dma_start(out=dst, in_=stage[:, sl * 512:sl * 512 + 512]) \
                    .then_inc(ctx.sem(x + ("_oA" if sl == 0 else "_oB")), 16)
        sy.wait_ge(ctx.sem(x + "_oA"), 16 * 16)
        sy.wait_ge(ctx.sem(x + "_oB"), 16 * 16)
        return None


class P7Combine:
    def sync(self, ctx, sy):
        u = ctx.t["u_dram"]
        for tt in range(8):
            sl = tt % 2
            if tt >= 2:
                sy.wait_ge(ctx.sem("p7_used"), tt - 1)
            sy.dma_start(out=ctx.t["uR"][:, sl * D:(sl + 1) * D],
                         in_=u.ap()[HALO + tt * 128:HALO + tt * 128 + 128, :]) \
                .then_inc(ctx.sem("p7_uA" if sl == 0 else "p7_uB"), 16)
            sy.dma_start(out=ctx.t["mR"][:, sl * D:(sl + 1) * D],
                         in_=ctx.t["mamba_s"].ap()[tt * 128:(tt + 1) * 128, :]) \
                .then_inc(ctx.sem("p7_mA" if sl == 0 else "p7_mB"), 16)
            sy.dma_start(out=ctx.t["fR"][:, sl * D:(sl + 1) * D],
                         in_=ctx.t["ffn0_s"].ap()[tt * 128:(tt + 1) * 128, :]) \
                .then_inc(ctx.sem("p7_fA" if sl == 0 else "p7_fB"), 16)
            sy.wait_ge(ctx.sem("p7_x1"), tt + 1)
            sy.dma_start(out=ctx.t["x1_out"].ap()[tt * 128:(tt + 1) * 128, :],
                         in_=ctx.t["x1R"][:, sl * D:(sl + 1) * D]) \
                .then_inc(ctx.sem("p7_oA" if sl == 0 else "p7_oB"), 16)
        sy.wait_ge(ctx.sem("p7_oA"), 16 * 4)
        sy.wait_ge(ctx.sem("p7_oB"), 16 * 4)
        return None

    def vector(self, ctx, v):
        for tt in range(8):
            sl = tt % 2
            v.wait_ge(ctx.sem("p7_uA" if sl == 0 else "p7_uB"), 16 * (tt // 2 + 1))
            v.wait_ge(ctx.sem("p7_mA" if sl == 0 else "p7_mB"), 16 * (tt // 2 + 1))
            v.wait_ge(ctx.sem("p7_fA" if sl == 0 else "p7_fB"), 16 * (tt // 2 + 1))
            if tt >= 2:
                v.wait_ge(ctx.sem("p7_oA" if sl == 0 else "p7_oB"), 16 * (tt // 2))
            x1 = ctx.t["x1R"][:, sl * D:(sl + 1) * D]
            x2 = ctx.t["x2R"][:, sl * D:(sl + 1) * D]
            v.tensor_tensor(x2, ctx.t["uR"][:, sl * D:(sl + 1) * D],
                            ctx.t["mR"][:, sl * D:(sl + 1) * D], op=ALU.add)
            v.drain()
            v.tensor_tensor(x1, x2, ctx.t["fR"][:, sl * D:(sl + 1) * D], op=ALU.add)
            dinc(v, ctx.sem("p7_x1"))
            dinc(v, ctx.sem("p7_used"))
        return None


NQH = 16
NKH = 4
TT2 = 2048


class Q1LN:

    def sync(self, ctx, sy):
        x1 = ctx.t["x1_in"]
        s_u = ctx.sem("q1_ld")
        src = x1.ap().rearrange("(tt p) d -> p tt d", p=128)
        dst = ctx.t["u_tok8"].rearrange("p (tt d) -> p tt d", tt=8)
        sy.dma_start(out=dst, in_=src).then_inc(s_u, 16)
        return None

    def vector(self, ctx, v):
        u_tok = ctx.t["u_tok8"]
        xn_tok = ctx.t["xn1_tok"]
        aggr = ctx.t["ln_aggr"]
        rows = ctx.t["ln_rows"]
        sq = ctx.t["ln_sq"]
        v.wait_ge(ctx.sem("q1_ld"), 16)
        for tt in range(8):
            ut = u_tok[:, tt * D:(tt + 1) * D]
            a = aggr[:, 2 * tt:2 * tt + 2]
            r = rows[:, 4 * tt:4 * tt + 4]
            v.reduce_sum(r[:, 2:3], ut, axis=AX.X)
            v.wait_ge(ctx.sem("q1_sq"), tt + 1)
            v.reduce_sum(r[:, 3:4], sq[:, :], axis=AX.X)
            dinc(v, ctx.sem("q1_sqfree"))
            v.drain()
            v.tensor_scalar(a[:, 0:1], r[:, 2:3], 1.0 / D, None, op0=ALU.mult)
            v.tensor_scalar(r[:, 3:4], r[:, 3:4], 1.0 / D, None, op0=ALU.mult)
            v.drain()
            v.tensor_tensor(a[:, 1:2], a[:, 0:1], a[:, 0:1], op=ALU.mult)
            v.drain()
            v.tensor_tensor(a[:, 1:2], r[:, 3:4], a[:, 1:2], op=ALU.subtract)
            dinc(v, ctx.sem("q1_var"))
        tmp2 = ctx.t["ln_tmp2b"]
        for tt in range(8):
            ut = u_tok[:, tt * D:(tt + 1) * D]
            a = aggr[:, 2 * tt:2 * tt + 2]
            r = rows[:, 4 * tt:4 * tt + 4]
            v.wait_ge(ctx.sem("q1_rstd"), tt + 1)
            v.tensor_scalar(tmp2[:, :], ut, a[:, 0:1], r[:, 0:1],
                            op0=ALU.subtract, op1=ALU.mult)
            v.drain()
            v.tensor_copy(xn_tok[:, tt * D:(tt + 1) * D], tmp2[:, :])
            dinc(v, ctx.sem("q1_xn"))
        return None

    def scalar(self, ctx, s):
        u_tok = ctx.t["u_tok8"]
        sq = ctx.t["ln_sq"]
        aggr = ctx.t["ln_aggr"]
        rows = ctx.t["ln_rows"]
        s.wait_ge(ctx.sem("q1_ld"), 16)
        for tt in range(8):
            if tt > 0:
                s.wait_ge(ctx.sem("q1_sqfree"), tt)
            s.activation(sq[:, :], u_tok[:, tt * D:(tt + 1) * D], AF.Square)
            dinc(s, ctx.sem("q1_sq"))
        for tt in range(8):
            a = aggr[:, 2 * tt:2 * tt + 2]
            r = rows[:, 4 * tt:4 * tt + 4]
            s.wait_ge(ctx.sem("q1_var"), tt + 1)
            s.activation(r[:, 1:2], a[:, 1:2], AF.Ln, bias=ctx.constf(EPS))
            s.drain()
            s.activation(r[:, 0:1], r[:, 1:2], AF.Exp, scale=-0.5)
            dinc(s, ctx.sem("q1_rstd"))
        return None

    def gpsimd(self, ctx, g):
        g.wait_ge(ctx.sem("q1_xn"), 8)
        src = ctx.t["xn1_tok"].rearrange("p (tt d) -> p tt d", tt=8)
        dst = ctx.t["xn1b"].ap().rearrange("(tt p) d -> p tt d", p=128)
        g.dma_start(out=dst, in_=src).then_inc(ctx.sem("q1_bd"), 16)
        g.wait_ge(ctx.sem("q1_bd"), 16)
        g.collective_compute("AllGather", ALU.bypass, replica_groups=PAIRS,
                             ins=[ctx.t["xn1b"].ap().opt()],
                             outs=[ctx.t["xn1g"].ap().opt()]).then_inc(ctx.sem("q1_cc"), 1)
        g.wait_ge(ctx.sem("q1_cc"), 1)
        return None


class Q2QKV:

    def sync(self, ctx, sy):
        xg = ctx.t["xn1g"]
        for k in range(16):
            sy.dma_start(out=ctx.t["xnF"][:, k * TT2:(k + 1) * TT2],
                         in_=xg.ap()[:, k * 128:(k + 1) * 128],
                         transpose=True).then_inc(ctx.sem("q2_xf"), 16)
        return None

    def gpsimd(self, ctx, g):
        s_w = ctx.sem("q2_w")
        offs = [0, 8192, 16384, 20480]
        for i, (wname, w) in enumerate([("WqT_d", 512), ("WqT2", 512),
                                        ("WkT_d", 256), ("WvT_d", 256)]):
            src_t = ctx.t["WqT_d"] if wname in ("WqT_d", "WqT2") else ctx.t[wname]
            off = 512 if wname == "WqT2" else 0
            src = src_t.ap()[:, off:off + w].rearrange("(k p) e -> p k e", p=128)
            dst = ctx.t["qkvslab"][:, offs[i]:offs[i] + 16 * w] \
                .rearrange("p (k e) -> p k e", k=16)
            g.dma_start(out=dst, in_=src).then_inc(s_w, 16)
        return None

    def tensor(self, ctx, t):
        xnF = ctx.t["xnF"]
        t.wait_ge(ctx.sem("q2_xf"), 16 * 16)
        t.wait_ge(ctx.sem("q2_w"), 16 * 4)
        mmidx = 0
        offs = [0, 8192, 16384, 20480]
        for ci, w in [(0, 512), (1, 512), (2, 256), (3, 256)]:
            for tt in range(16):
                p = ctx.t["pA"] if mmidx % 2 == 0 else ctx.t["pB"]
                if mmidx >= 2:
                    t.wait_ge(ctx.sem("q2_ev"), mmidx - 1)
                for k in range(16):
                    mm = t.matmul(p[:, 0:w],
                                  xnF[:, k * TT2 + tt * 128:k * TT2 + tt * 128 + 128],
                                  ctx.t["qkvslab"][:, offs[ci] + k * w:offs[ci] + k * w + w],
                                  start=(k == 0), stop=(k == 15))
                mm.then_inc(ctx.sem("q2_mm"), 1)
                mmidx += 1
        t.wait_ge(ctx.sem("q2_rope"), 48)
        kb = 0
        for srcn, dstn, ntiles in (("Qtok", "Qf", 8), ("Ktok", "Kf", 2)):
            src = ctx.t[srcn]
            nw = ntiles * 128
            for tt in range(16):
                for j in range(ntiles):
                    pt = ctx.t["pt0"] if kb % 2 == 0 else ctx.t["pt1"]
                    if kb >= 2:
                        t.wait_ge(ctx.sem("q2_tev"), kb - 1)
                    t.transpose(pt[:, :], src[:, tt * nw + j * 128:tt * nw + j * 128 + 128],
                                ctx.t["ident_bf"][:, :]).then_inc(ctx.sem("q2_tp"), 1)
                    kb += 1
        return None

    def scalar(self, ctx, s):
        return None

    def vector(self, ctx, v):
        ce8, co8, se8, so8 = (ctx.t["ce8"], ctx.t["co8"], ctx.t["se8"], ctx.t["so8"])
        rt1, rt2 = ctx.t["ropetmp"][:, 0:256], ctx.t["ropetmp"][:, 256:512]
        for mmidx in range(64):
            p = ctx.t["pA"] if mmidx % 2 == 0 else ctx.t["pB"]
            v.wait_ge(ctx.sem("q2_mm"), mmidx + 1)
            if mmidx < 32:
                ci, tt = mmidx // 16, mmidx % 16
                w = 512
                dst = ctx.t["Qtok"][:, tt * 1024 + ci * 512:tt * 1024 + ci * 512 + 512]
            elif mmidx < 48:
                tt = mmidx % 16
                w = 256
                dst = ctx.t["Ktok"][:, tt * 256:tt * 256 + 256]
            else:
                tt = mmidx % 16
                w = 256
                dst = None
            if dst is not None:
                nh = w // 64
                tb = ce8[:, tt * 256:tt * 256 + 32 * nh]
                tbo = co8[:, tt * 256:tt * 256 + 32 * nh]
                tbs = se8[:, tt * 256:tt * 256 + 32 * nh]
                tbso = so8[:, tt * 256:tt * 256 + 32 * nh]
                pe = p[:, 0:w - 1:2]
                po = p[:, 1:w:2]
                v.tensor_tensor(rt1[:, 0:w // 2], pe, tb, op=ALU.mult)
                v.tensor_tensor(rt2[:, 0:w // 2], po, tbs, op=ALU.mult)
                v.drain()
                v.tensor_tensor(dst[:, 0:w - 1:2], rt1[:, 0:w // 2], rt2[:, 0:w // 2],
                                op=ALU.subtract)
                v.drain()
                v.tensor_tensor(rt1[:, 0:w // 2], po, tbo, op=ALU.mult)
                v.tensor_tensor(rt2[:, 0:w // 2], pe, tbso, op=ALU.mult)
                v.drain()
                v.tensor_tensor(dst[:, 1:w:2], rt1[:, 0:w // 2], rt2[:, 0:w // 2],
                                op=ALU.add)
                dinc(v, ctx.sem("q2_rope"))
            else:
                for h in range(4):
                    v.tensor_copy(ctx.t["Vtok"][:, tt * 260 + h * 65:tt * 260 + h * 65 + 64],
                                  p[:, h * 64:h * 64 + 64])
                dinc(v, ctx.sem("q2_rope"))
            dinc(v, ctx.sem("q2_ev"))
        for tt in range(16):
            for h in range(4):
                v.memset(ctx.t["Vtok"][:, tt * 260 + h * 65 + 64:tt * 260 + h * 65 + 65], 1.0)
        dinc(v, ctx.sem("q2_vones"))
        kb = 0
        for srcn, dstn, ntiles in (("Qtok", "Qf", 8), ("Ktok", "Kf", 2)):
            for tt in range(16):
                for j in range(ntiles):
                    pt = ctx.t["pt0"] if kb % 2 == 0 else ctx.t["pt1"]
                    v.wait_ge(ctx.sem("q2_tp"), kb + 1)
                    dst = ctx.t[dstn]
                    v.tensor_copy(dst[0:64, (2 * j) * TT2 + tt * 128:(2 * j) * TT2 + tt * 128 + 128],
                                  pt[0:64, :])
                    v.tensor_copy(dst[0:64, (2 * j + 1) * TT2 + tt * 128:(2 * j + 1) * TT2 + tt * 128 + 128],
                                  pt[64:128, :])
                    dinc(v, ctx.sem("q2_tev"))
                    kb += 1
        return None


class Q3Attn:

    def groups(self):
        out = []
        n = 0
        for qh in range(NQH):
            for qc in range(4):
                out.append((qh, qc, 4 * qc + 4, n))
                n += 4 * qc + 4
        return out

    def tensor(self, ctx, t):
        Qf, Kf, Vt = ctx.t["Qf"], ctx.t["Kf"], ctx.t["Vtok"]
        ones = ctx.t["ones_bf"]
        expS = ctx.t["expS"]
        for gi, (qh, qc, n_kt, n0) in enumerate(self.groups()):
            kvh = qh // 4
            qsl = Qf[0:64, qh * TT2 + qc * 512:qh * TT2 + qc * 512 + 512]
            for kt in range(n_kt):
                n = n0 + kt
                ps = ctx.t["pA"] if n % 2 == 0 else ctx.t["pB"]
                if n >= 2:
                    t.wait_ge(ctx.sem("q3_sev"), n - 1)
                ksl = Kf[0:64, kvh * TT2 + kt * 128:kvh * TT2 + kt * 128 + 128]
                t.matmul(ps, ksl, qsl, start=True, stop=True).then_inc(ctx.sem("q3_smm"), 1)
            pav = (ctx.t["bank4"] if gi % 2 == 0 else ctx.t["bank5"]).ap()[0:65, 0:512]
            if gi >= 2:
                t.wait_ge(ctx.sem("q3_aev"), gi - 1)
            ndiag = 0
            for kt in range(n_kt):
                t.wait_ge(ctx.sem("q3_sev"), n0 + kt + 1)
                if kt >= 4 * qc:
                    t.wait_ge(ctx.sem("q3_msk"), gi * 4 + (kt - 4 * qc) + 1)
                mm = t.matmul(pav, Vt[:, kt * 260 + (qh // 4) * 65:kt * 260 + (qh // 4) * 65 + 65],
                              expS[:, kt * 512:kt * 512 + 512],
                              start=(kt == 0), stop=(kt == n_kt - 1))
            mm.then_inc(ctx.sem("q3_avmm"), 1)
            t.wait_ge(ctx.sem("q3_den"), gi + 1)
            prb = (ctx.t["bank6"] if gi % 2 == 0 else ctx.t["bank7"]).ap()[0:64, 0:512]
            if gi >= 2:
                t.wait_ge(ctx.sem("q3_rbe"), gi - 1)
            t.matmul(prb, ones[0:1, 0:64], ctx.t["rrow1"][0:1, (gi % 2) * 512:(gi % 2) * 512 + 512],
                     start=True, stop=True).then_inc(ctx.sem("q3_rb"), 1)
        return None

    def scalar(self, ctx, s):
        expS = ctx.t["expS"]
        nbias = ctx.constf(-30.0)
        for gi, (qh, qc, n_kt, n0) in enumerate(self.groups()):
            for kt in range(n_kt):
                n = n0 + kt
                ps = ctx.t["pA"] if n % 2 == 0 else ctx.t["pB"]
                if gi >= 1:
                    s.wait_ge(ctx.sem("q3_avmm"), gi)
                s.wait_ge(ctx.sem("q3_smm"), n + 1)
                if kt >= 4 * qc:
                    m = kt - 4 * qc
                    if gi * 4 + m >= 2:
                        s.wait_ge(ctx.sem("q3_dtf"), gi * 4 + m - 1)
                    s.activation(ctx.t["dtmp"][:, (m % 2) * 512:(m % 2) * 512 + 512],
                                 ps, AF.Exp, scale=0.125, bias=nbias)
                    dinc(s, ctx.sem("q3_dex"))
                else:
                    s.activation(expS[:, kt * 512:kt * 512 + 512], ps, AF.Exp,
                                 scale=0.125, bias=nbias)
                    dinc(s, ctx.sem("q3_sev"))
            if gi >= 2:
                s.wait_ge(ctx.sem("q3_rb"), gi - 1)
            s.wait_ge(ctx.sem("q3_avmm"), gi + 1)
            pav = (ctx.t["bank4"] if gi % 2 == 0 else ctx.t["bank5"]).ap()
            s.activation(ctx.t["drow"][0:1, 0:512], pav[64:65, 0:512], AF.Ln)
            s.drain()
            s.activation(ctx.t["rrow1"][0:1, (gi % 2) * 512:(gi % 2) * 512 + 512],
                         ctx.t["drow"][0:1, 0:512], AF.Exp, scale=-1.0)
            dinc(s, ctx.sem("q3_den"))
        return None

    def vector(self, ctx, v):
        expS = ctx.t["expS"]
        attnO = ctx.t["attnO"]
        for gi, (qh, qc, n_kt, n0) in enumerate(self.groups()):
            for m in range(4):
                kt = 4 * qc + m
                v.wait_ge(ctx.sem("q3_dex"), gi * 4 + m + 1)
                v.tensor_tensor(expS[:, kt * 512:kt * 512 + 512],
                                ctx.t["dtmp"][:, (m % 2) * 512:(m % 2) * 512 + 512],
                                ctx.t["dmask"][:, m * 512:m * 512 + 512], op=ALU.mult)
                dinc(v, ctx.sem("q3_dtf"))
                dinc(v, ctx.sem("q3_sev"))
                dinc(v, ctx.sem("q3_msk"))
            v.wait_ge(ctx.sem("q3_rb"), gi + 1)
            prb = (ctx.t["bank6"] if gi % 2 == 0 else ctx.t["bank7"]).ap()[0:64, 0:512]
            v.tensor_copy(ctx.t["rBs"][0:64, 0:512], prb)
            dinc(v, ctx.sem("q3_rbe"))
            pav = (ctx.t["bank4"] if gi % 2 == 0 else ctx.t["bank5"]).ap()[0:64, 0:512]
            v.tensor_tensor(attnO[(qh % 2) * 64:(qh % 2) * 64 + 64,
                                  (qh // 2) * TT2 + qc * 512:(qh // 2) * TT2 + qc * 512 + 512],
                            pav, ctx.t["rBs"][0:64, 0:512], op=ALU.mult)
            dinc(v, ctx.sem("q3_aev"))
        return None


class Q4Exchange:
    def gpsimd(self, ctx, g):
        src = ctx.t["attnO"].rearrange("p (j t) -> p j t", j=8)
        dst = ctx.t["aOb"].ap().rearrange("(j p) t -> p j t", p=128)
        g.dma_start(out=dst, in_=src).then_inc(ctx.sem("q4_d"), 16)
        g.wait_ge(ctx.sem("q4_d"), 16)
        g.collective_compute("AllGather", ALU.bypass, replica_groups=PAIRS,
                             ins=[ctx.t["aOb"].ap().opt()],
                             outs=[ctx.t["aOg"].ap().opt()]).then_inc(ctx.sem("q4_cc"), 1)
        g.wait_ge(ctx.sem("q4_cc"), 1)
        for arm in g.switch_core_id(n=128):
            t0 = 0 if int(arm) % 2 == 0 else 1024
            src2 = ctx.t["aOg"].ap()[:, t0:t0 + T].rearrange("(j p) t -> p j t", p=128)
            dst2 = ctx.t["attnG"].rearrange("p (j t) -> p j t", j=16)
            g.dma_start(out=dst2, in_=src2).then_inc(ctx.sem("q4_d2"), 16)
        g.wait_ge(ctx.sem("q4_d2"), 16)
        return None


class Q5Wo:

    def sync(self, ctx, sy):
        for arm in sy.switch_core_id(n=128):
            t0 = 0 if int(arm) % 2 == 0 else 1024
            for k in range(16):
                sy.dma_start(out=ctx.t["xn1own"][:, k * T:(k + 1) * T],
                             in_=ctx.t["xn1g"].ap()[t0:t0 + T, k * 128:(k + 1) * 128],
                             transpose=True).then_inc(ctx.sem("q5_xo"), 16)
        sy.wait_ge(ctx.sem("q5_xo"), 16 * 16)
        return None

    def gpsimd(self, ctx, g):
        wo = ctx.t["WoT_d"]
        qi = 0
        for th in range(2):
            for oh in range(2):
                for et in range(16):
                    if qi >= 2:
                        g.wait_ge(ctx.sem("q5_wu"), qi - 1)
                    sl = ctx.t["woslabA"] if qi % 2 == 0 else ctx.t["woslabB"]
                    sem = ctx.sem("q5_wA" if qi % 2 == 0 else "q5_wB")
                    src = wo.ap()[et * 128:(et + 1) * 128, oh * 1024:(oh + 1) * 1024]
                    g.dma_start(out=sl[:, 0:1024], in_=src).then_inc(sem, 16)
                    qi += 1
        return None

    def tensor(self, ctx, t):
        banks = [ctx.t[f"bank{i}"] for i in range(8)]
        aG = ctx.t["attnG"]
        qi = 0
        for th in range(2):
            for oh in range(2):
                q = th * 2 + oh
                if q >= 1:
                    t.wait_ge(ctx.sem("q5_wev"), q)
                for et in range(16):
                    t.wait_ge(ctx.sem("q5_wA" if qi % 2 == 0 else "q5_wB"),
                              16 * (qi // 2 + 1))
                    sl = ctx.t["woslabA"] if qi % 2 == 0 else ctx.t["woslabB"]
                    for tt in range(4):
                        for oc in range(2):
                            mm = t.matmul(banks[tt * 2 + oc].ap(),
                                          aG[:, et * T + th * 512 + tt * 128:et * T + th * 512 + tt * 128 + 128],
                                          sl[:, oc * 512:oc * 512 + 512],
                                          start=(et == 0), stop=(et == 15))
                    mm.then_inc(ctx.sem("q5_wu"), 1)
                    qi += 1
        return None

    def vector(self, ctx, v):
        banks = [ctx.t[f"bank{i}"] for i in range(8)]
        for q in range(4):
            v.wait_ge(ctx.sem("q5_wu"), 16 * (q + 1))
            for tt in range(4):
                k = q * 4 + tt
                sl = k % 2
                if k >= 2:
                    v.wait_ge(ctx.sem("q5_oA" if sl == 0 else "q5_oB"), 16 * (k // 2))
                for oc in range(2):
                    v.tensor_copy(ctx.t["stage2"][:, sl * 1024 + oc * 512:sl * 1024 + oc * 512 + 512],
                                  banks[tt * 2 + oc].ap())
                dinc(v, ctx.sem("q5_wst"))
            dinc(v, ctx.sem("q5_wev"))
        return None

    def scalar(self, ctx, s):
        out = ctx.t["attn_s"]
        for q in range(4):
            th, oh = q // 2, q % 2
            for tt in range(4):
                k = q * 4 + tt
                sl = k % 2
                s.wait_ge(ctx.sem("q5_wst"), k + 1)
                dst = out.ap()[th * 512 + tt * 128:th * 512 + tt * 128 + 128,
                               oh * 1024:(oh + 1) * 1024]
                s.dma_start(out=dst, in_=ctx.t["stage2"][:, sl * 1024:sl * 1024 + 1024]) \
                    .then_inc(ctx.sem("q5_oA" if sl == 0 else "q5_oB"), 16)
        s.wait_ge(ctx.sem("q5_oA"), 16 * 8)
        s.wait_ge(ctx.sem("q5_oB"), 16 * 8)
        return None


class Q7Final:

    def tensor(self, ctx, t):
        for c in range(4):
            pb = (ctx.t["bank0"] if c % 2 == 0 else ctx.t["bank1"]).ap()
            if c >= 2:
                t.wait_ge(ctx.sem("q7_dse"), c - 1)
            t.matmul(pb, ctx.t["ones_bf"][0:1, 0:128],
                     ctx.t["dsrow"][0:1, c * 512:(c + 1) * 512],
                     start=True, stop=True).then_inc(ctx.sem("q7_dsb"), 1)
        return None

    def sync(self, ctx, sy):
        x1 = ctx.t["x1_in"]
        for tt in range(8):
            sl = tt % 2
            if tt >= 2:
                sy.wait_ge(ctx.sem("q7_used"), tt - 1)
            sy.dma_start(out=ctx.t["uR"][:, sl * D:(sl + 1) * D],
                         in_=x1.ap()[tt * 128:(tt + 1) * 128, :]) \
                .then_inc(ctx.sem("q7_uA" if sl == 0 else "q7_uB"), 16)
            sy.dma_start(out=ctx.t["mR"][:, sl * D:(sl + 1) * D],
                         in_=ctx.t["attn_s"].ap()[tt * 128:(tt + 1) * 128, :]) \
                .then_inc(ctx.sem("q7_mA" if sl == 0 else "q7_mB"), 16)
            sy.dma_start(out=ctx.t["fR"][:, sl * D:(sl + 1) * D],
                         in_=ctx.t["ffn1_s"].ap()[tt * 128:(tt + 1) * 128, :]) \
                .then_inc(ctx.sem("q7_fA" if sl == 0 else "q7_fB"), 16)
            sy.wait_ge(ctx.sem("q7_x1"), tt + 1)
            sy.dma_start(out=ctx.t["out_d"].ap()[tt * 128:(tt + 1) * 128, :],
                         in_=ctx.t["x1R"][:, sl * D:(sl + 1) * D]) \
                .then_inc(ctx.sem("q7_oA" if sl == 0 else "q7_oB"), 16)
        sy.wait_ge(ctx.sem("q7_oA"), 16 * 4)
        sy.wait_ge(ctx.sem("q7_oB"), 16 * 4)
        return None

    def vector(self, ctx, v):
        for c in range(4):
            v.wait_ge(ctx.sem("q7_dsb"), c + 1)
            pb = (ctx.t["bank0"] if c % 2 == 0 else ctx.t["bank1"]).ap()
            v.tensor_copy(ctx.t["dsB"][:, c * 512:(c + 1) * 512], pb[:, 0:512])
            dinc(v, ctx.sem("q7_dse"))
        for tt in range(8):
            sl = tt % 2
            v.wait_ge(ctx.sem("q7_uA" if sl == 0 else "q7_uB"), 16 * (tt // 2 + 1))
            v.wait_ge(ctx.sem("q7_mA" if sl == 0 else "q7_mB"), 16 * (tt // 2 + 1))
            v.wait_ge(ctx.sem("q7_fA" if sl == 0 else "q7_fB"), 16 * (tt // 2 + 1))
            if tt >= 2:
                v.wait_ge(ctx.sem("q7_oA" if sl == 0 else "q7_oB"), 16 * (tt // 2))
            x1 = ctx.t["x1R"][:, sl * D:(sl + 1) * D]
            x2 = ctx.t["x2R"][:, sl * D:(sl + 1) * D]
            v.tensor_tensor(x1, ctx.t["fR"][:, sl * D:(sl + 1) * D],
                            ctx.t["dsB"][:, 0:2048], op=ALU.mult)
            v.drain()
            v.tensor_tensor(x2, x1, ctx.t["mR"][:, sl * D:(sl + 1) * D], op=ALU.add)
            v.drain()
            v.tensor_tensor(x1, x2, ctx.t["uR"][:, sl * D:(sl + 1) * D], op=ALU.add)
            dinc(v, ctx.sem("q7_x1"))
            dinc(v, ctx.sem("q7_used"))
        return None


class Q0Consts:

    def gpsimd(self, ctx, g):
        ident = ctx.t["ident_bf"]
        g.memset(ident[:, :], 0.0)
        g.drain()
        g.affine_select(out=ident[:, :], in_=ident[:, :],
                        compare_op=ALU.not_equal, fill=1.0,
                        base=0, pattern=[[-1, 128]], channel_multiplier=1)
        for m in range(4):
            dm = ctx.t["dmask"][:, m * 512:(m + 1) * 512]
            g.memset(dm, 1.0)
            g.drain()
            g.affine_select(out=dm, in_=dm, compare_op=ALU.is_ge, fill=0.0,
                            base=-128 * m, pattern=[[1, 512]], channel_multiplier=-1)
            g.drain()
        return None

    def vector(self, ctx, v):
        last = v.memset(ctx.t["ones_bf"][:, :], 1.0)
        for val, idx in ctx.consts.items():
            last = v.memset(ctx.cpool[:, idx:idx + 1], val)
        return last

    def sync(self, ctx, sy):
        sem = ctx.sem("q0_ld")
        n = 0
        for sb_name, dram_name in [("ce8", "ce8_d"), ("co8", "co8_d"),
                                   ("se8", "se8_d"), ("so8", "so8_d"),
                                   ("gs1", "gs1_d"), ("us1", "us1_d"),
                                   ("dsrow", "ds_d")]:
            sy.dma_start(out=as_ap(ctx.t[sb_name])[:, :],
                         in_=ctx.t[dram_name].ap()).then_inc(sem, 16)
            n += 1
        sy.wait_ge(sem, 16 * n)
        return None


def alloc_l1(ctx, nc):
    arena = ctx.sb("arenaB", [128, 82048], BF16)
    ar32 = arena.bitcast(F32)
    ctx.t["u_tok8"] = ar32.ap()[:, 0:16384]
    ctx.t["xn1_tok"] = arena.ap()[:, 32768:32768 + 16384]
    ctx.t["ln_sq"] = ar32.ap()[:, 24576:24576 + 2048]
    ctx.t["ln_tmp2b"] = ar32.ap()[:, 26624:26624 + 2048]
    ctx.t["ln_aggr"] = ar32.ap()[:, 28672:28672 + 16]
    ctx.t["ln_rows"] = ar32.ap()[:, 28688:28688 + 32]
    ctx.t["xnF"] = arena.ap()[:, 0:32768]
    ctx.t["qkvslab"] = arena.ap()[:, 57408:57408 + 24576]
    ctx.t["Qtok"] = arena.ap()[:, 32768:32768 + 16384]
    ctx.t["Ktok"] = arena.ap()[:, 49152:49152 + 4096]
    ctx.t["Vtok"] = arena.ap()[:, 53248:53248 + 4160]
    pass
    ctx.t["Qf"] = arena.ap()[:, 0:32768]
    ctx.t["expS"] = arena.ap()[:, 32768:32768 + 8192]
    ctx.t["Kf"] = arena.ap()[:, 57408:57408 + 8192]
    ctx.t["attnO"] = arena.ap()[:, 65600:65600 + 16384]
    ctx.t["dtmp"] = arena.ap()[:, 40960:40960 + 1024]
    ctx.t["rBs"] = arena.ap()[:, 41984:41984 + 512]
    ctx.t["rrow1"] = arena.ap()[:, 42496:42496 + 1024]
    ctx.t["drow"] = ar32.ap()[:, 21760:21760 + 512]
    ctx.t["attnG"] = arena.ap()[:, 0:16384]
    ctx.t["woslabA"] = arena.ap()[:, 16384:16384 + 1024]
    ctx.t["woslabB"] = arena.ap()[:, 17408:17408 + 1024]
    ctx.t["stage2"] = ar32.ap()[:, 11264:11264 + 2048]
    ctx.t["xn1own"] = arena.ap()[:, 45056:45056 + 16384]
    ctx.t["h1f"] = arena.ap()[:, 0:45056]
    ctx.t["ztmp"] = arena.ap()[:, 77824:77824 + 1024]
    ctx.t["uR"] = ar32.ap()[:, 0:4096]
    ctx.t["mR"] = ar32.ap()[:, 4096:8192]
    ctx.t["fR"] = ar32.ap()[:, 8192:12288]
    ctx.t["x1R"] = ar32.ap()[:, 12288:16384]
    ctx.t["x2R"] = ar32.ap()[:, 16384:20480]
    ctx.t["dsB"] = ar32.ap()[:, 20480:22528]
    ctx.sb("ident_bf", [128, 128], BF16)
    ctx.sb("ones_bf", [128, 128], BF16)
    ctx.sb("ropetmp", [128, 512], BF16)
    ctx.sb("dmask", [128, 4 * 512], BF16)
    ctx.sb("ce8", [128, 16 * 256], BF16)
    ctx.sb("co8", [128, 16 * 256], BF16)
    ctx.sb("se8", [128, 16 * 256], BF16)
    ctx.sb("so8", [128, 16 * 256], BF16)
    ctx.sb("gs1", [128, 44], F32)
    ctx.sb("us1", [128, 44], F32)
    ctx.sb("dsrow", [128, 2048], BF16)
    banks = [ctx.ps(f"bank{i}", [128, 512], F32) for i in range(8)]
    ctx.t["pA"] = banks[0].ap()
    ctx.t["pB"] = banks[1].ap()
    ctx.t["pt0"] = banks[6].bitcast(BF16).ap()[:, 0:128]
    ctx.t["pt1"] = banks[7].bitcast(BF16).ap()[:, 0:128]
    ctx.dram("x1_in", [T, D], F32, kind="ExternalInput")
    ctx.dram("WqT_d", [D, NQH * 64], F32, kind="ExternalInput")
    ctx.dram("WkT_d", [D, NKH * 64], F32, kind="ExternalInput")
    ctx.dram("WvT_d", [D, NKH * 64], F32, kind="ExternalInput")
    ctx.dram("WoT_d", [D, D], F32, kind="ExternalInput")
    ctx.dram("Wg1T_d", [D, FF1], F32, kind="ExternalInput")
    ctx.dram("Wu1T_d", [D, FF1], F32, kind="ExternalInput")
    ctx.dram("Wd1T_d", [FF1, D], F32, kind="ExternalInput")
    ctx.dram("ce8_d", [128, 16 * 256], BF16, kind="ExternalInput")
    ctx.dram("co8_d", [128, 16 * 256], BF16, kind="ExternalInput")
    ctx.dram("se8_d", [128, 16 * 256], BF16, kind="ExternalInput")
    ctx.dram("so8_d", [128, 16 * 256], BF16, kind="ExternalInput")
    ctx.dram("gs1_d", [128, 44], F32, kind="ExternalInput")
    ctx.dram("us1_d", [128, 44], F32, kind="ExternalInput")
    ctx.dram("ds_d", [128, 2048], BF16, kind="ExternalInput")
    ctx.dram("xn1b", [T, D], BF16)
    ctx.dram("xn1g", [TT2, D], BF16)
    ctx.dram("aOb", [T, TT2], BF16)
    ctx.dram("aOg", [TT2, TT2], BF16)
    ctx.dram("attn_s", [T, D], F32)
    ctx.dram("ffn1_s", [T, D], F32)
    ctx.dram("out_d", [T, D], F32, kind="ExternalOutput")


def build_l1():
    nc = bass.Bass("TRN2", target_bir_lowering=False)
    stack = contextlib.ExitStack()
    with stack:
        ctx = Ctx(nc, stack)
        alloc_l1(ctx, nc)
        ctx.constf(EPS)
        ctx.constf(-30.0)
        ffn1 = P6FFN0(xn_name="xn1own", xn_off=0, out_name="ffn1_s",
                      wg="Wg1T_d", wu="Wu1T_d", wd="Wd1T_d", pfx="q6",
                      ne=FF1, gate_scale="gs1", up_scale="us1", slab0=61440)
        emit_phases(nc, ctx, [Q0Consts(), Q1LN(), Q2QKV(), Q3Attn(),
                              Q4Exchange(), Q5Wo(), ffn1, Q7Final()])
    return nc


def alloc_l0(ctx, nc):
    arena = ctx.sb("arena", [128, 82048], BF16)
    ar32 = arena.bitcast(F32)
    ctx.t["u_tok"] = ar32.ap()[:, 0:18432]
    ctx.t["u_bf"] = arena.ap()[:, 36896:36896 + 18432]
    ctx.t["ln_tmp"] = ar32.ap()[:, 27664:27664 + 2048]
    ctx.t["ln_tmp2"] = ar32.ap()[:, 29712:29712 + 2048]
    ctx.t["xBC"] = arena.ap()[:, 0:35 * TF]
    ctx.t["slabA"] = arena.ap()[:, 36896:36896 + 8192]
    ctx.t["slabB"] = arena.ap()[:, 45088:45088 + 8192]
    ctx.t["xdtT"] = arena.ap()[:, 36896:36896 + 32768]
    ctx.t["S_run"] = arena.ap()[:, 32800:32800 + 4096]
    ctx.t["y_feat"] = arena.ap()[:, 0:32768]
    ctx.t["h1f"] = arena.ap()[:, 0:65536]
    ctx.t["conv_tmp"] = arena.ap()[:, 32896:32896 + 2048]
    ctx.t["slabC"] = arena.ap()[:, 36896:36896 + 1024]
    ctx.t["slabD"] = arena.ap()[:, 37920:37920 + 1024]
    ctx.t["stage2"] = ar32.ap()[:, 16400:16400 + 2048]
    ctx.t["slabC2"] = arena.ap()[:, 53280:53280 + 8192]
    ctx.t["slabD2"] = arena.ap()[:, 61472:61472 + 8192]
    ctx.t["uR"] = ar32.ap()[:, 0:4096]
    ctx.t["mR"] = ar32.ap()[:, 4096:8192]
    ctx.t["fR"] = ar32.ap()[:, 8192:12288]
    ctx.t["x1R"] = ar32.ap()[:, 12288:16384]
    ctx.t["x2R"] = ar32.ap()[:, 16384:20480]
    ctx.sb("norm_scale", [128, 32], F32)
    ctx.t["tmp_bf"] = arena.ap()[:, 34952:34952 + 1024]
    ctx.sb("ident_bf", [128, 128], BF16)
    ctx.sb("ident_f32", [128, 128], F32)
    ctx.sb("ones_bf", [128, 128], BF16)
    ctx.sb("gmaskA", [128, CH], BF16)
    ctx.sb("gmaskB", [128, CH], BF16)
    xn_h = ctx.sb("xn", [128, 16 * TF], BF16)
    ctx.t["dstage"] = xn_h.bitcast(F32).ap()[:, 0:1024]
    ctx.t["ln_aggr"] = ar32.ap()[:, 31760:31760 + 18]
    ctx.t["ln_rows"] = ar32.ap()[:, 31800:31800 + 36]
    dtbuf = ctx.sb("dtbuf", [128, T], F32)
    ctx.t["decay_bf"] = dtbuf.bitcast(BF16).ap()[:, 0:1024]
    ctx.t["sdo_bf"] = dtbuf.bitcast(BF16).ap()[:, 0:1024]
    ctx.t["csdo"] = dtbuf.bitcast(BF16).ap()[:, 1024:1536]
    ctx.t["xdtmp"] = dtbuf.bitcast(BF16).ap()[:, 1536:1792]
    ctx.t["stmp4"] = dtbuf.bitcast(BF16).ap()[:, 1792:2048]
    ctx.t["stmpA"] = dtbuf.bitcast(BF16).ap()[:, 1792:1856]
    ctx.t["stmpB"] = dtbuf.bitcast(BF16).ap()[:, 1856:1920]
    acsA_h = ctx.sb("acsA", [128, T], F32)
    ctx.t["expL"] = acsA_h.bitcast(BF16).ap()[:, 0:1024]
    ctx.t["MT"] = acsA_h.bitcast(BF16).ap()[:, 1024:2048]
    ctx.t["roleb"] = acsA_h.bitcast(BF16).ap()[:, 0:2048]
    acsB = ctx.sb("acsB", [128, T], F32)
    ctx.t["invdt"] = acsB.ap()[:, 0:512]
    ctx.t["convfG"] = acsB.ap()[:, 0:1024]
    ctx.t["expLmin"] = acsB.bitcast(BF16).ap()[:, 0:2048]
    ctx.t["rrow_bf"] = acsB.bitcast(BF16).ap()[:, 1024:2048]
    ctx.t["convfV"] = ar32.ap()[:, 39328:39328 + 1024]
    ctx.t["sel4"] = arena.ap()[0:4, 78656:79168]
    ctx.t["rowF0"] = ar32.ap()[0:4, 39584:39712]
    ctx.t["rowF1"] = ar32.ap()[0:4, 39712:39840]
    ctx.t["hiB0"] = arena.ap()[0:4, 79680:79808]
    ctx.t["hiB1"] = arena.ap()[0:4, 79808:79936]
    ctx.t["loB0"] = arena.ap()[0:4, 79936:80064]
    ctx.t["loB1"] = arena.ap()[0:4, 80064:80192]
    ctx.t["dtT"] = ar32.ap()[:, 40352:40352 + 512]
    ctx.t["rr_ln0"] = ctx.t["dtT"]
    ctx.t["GmB"] = arena.ap()[:, 80704:81728]
    ctx.t["Dinvdt"] = ar32.ap()[:, 38176:38176 + 512]
    ctx.t["rr_ln1"] = ctx.t["Dinvdt"]
    ctx.t["acsT"] = ar32.ap()[:, 37152:37152 + 512]
    ctx.t["decayT"] = ar32.ap()[:, 37664:37664 + 512]
    ctx.t["rstd_b"] = arena.ap()[:, 75328:75328 + 1024]
    ctx.t["alast_rows"] = ar32.ap()[0:1, 38944:38944 + 256]
    ctx.t["alast_erow"] = arena.ap()[0:1, 78400:78400 + 256]
    ctx.sb("expA", [128, 1], F32)
    ctx.t["Bfeat"] = arena.ap()[:, 70720:70720 + 1024]
    ctx.t["Cfeat"] = arena.ap()[:, 71744:71744 + 1024]
    ctx.t["BT"] = arena.ap()[:, 72768:72768 + 1024]
    ctx.t["bdec"] = arena.ap()[:, 73792:73792 + 512]
    ctx.t["expAlB"] = ar32.ap()[:, 38688:38688 + 256]
    ctx.sb("roleflag", [128, 1], F32)
    ctx.t["Gm"] = arena.ap()[:, 69696:69696 + 1024]
    ctx.t["sqbuf"] = ctx.t["Gm"]
    ctx.t["ztmp"] = acsA_h.bitcast(BF16).ap()[:, 0:1024]
    ctx.t["ztmp2"] = acsA_h.bitcast(BF16).ap()[:, 1024:2048]
    ctx.sb("convw", [128, 4 * 34], F32)
    ctx.sb("convb", [128, 34], F32)
    ctx.sb("dt_bias", [128, 1], F32)
    ctx.sb("A_log", [128, 1], F32)
    ctx.sb("D_tile", [128, 64], F32)
    banks = [ctx.ps(f"bank{i}", [128, 512], F32) for i in range(8)]
    ctx.t["pA"] = banks[0].ap()
    ctx.t["pB"] = banks[1].ap()
    ctx.t["pt0"] = banks[6].bitcast(BF16).ap()[:, 0:128]
    ctx.t["pt1"] = banks[7].bitcast(BF16).ap()[:, 0:128]
    ctx.t["pf0"] = banks[4].ap()[:, 0:64]
    ctx.t["pf1"] = banks[5].ap()[:, 0:64]
    ctx.t["pAl"] = banks[2].ap()[0:1, 0:64]
    ctx.t["pAcsB0"] = banks[0].ap()[:, 0:256]
    ctx.t["pAcsB1"] = banks[1].ap()[:, 0:256]
    ctx.t["pRT0"] = banks[2].ap()[0:4, 0:128]
    ctx.t["pRT1"] = banks[3].ap()[0:4, 0:128]
    ctx.t["pG0"] = banks[2].ap()[:, 0:256]
    ctx.t["pG1"] = banks[3].ap()[:, 0:256]
    ctx.t["pY0"] = banks[4].ap()[0:64, 0:256]
    ctx.t["pY1"] = banks[5].ap()[0:64, 0:256]
    ctx.t["pS0"] = banks[6].ap()[:, 0:64]
    ctx.t["pS1"] = banks[7].ap()[:, 0:64]
    ctx.dram("u_dram", [TP, D], F32, kind="ExternalInput")
    ctx.dram("W_in", [D, EIN], F32, kind="ExternalInput")
    ctx.dram("convw_d", [128, 4 * 34], F32, kind="ExternalInput")
    ctx.dram("convb_d", [128, 34], F32, kind="ExternalInput")
    ctx.dram("dtb_d", [64, 1], F32, kind="ExternalInput")
    ctx.dram("alog_d", [64, 1], F32, kind="ExternalInput")
    ctx.dram("D_d", [128, 64], F32, kind="ExternalInput")
    ctx.dram("role_d", [128, 1], F32, kind="ExternalInput")
    ctx.dram("W_out_d", [DIN, D], F32, kind="ExternalInput")
    ctx.dram("ns_d", [128, 32], F32, kind="ExternalInput")
    ctx.dram("mamba_s", [T, D], F32)
    ctx.dram("ffn0_s", [T, D], F32)
    ctx.dram("WgT_d", [D, FF0], F32, kind="ExternalInput")
    ctx.dram("WuT_d", [D, FF0], F32, kind="ExternalInput")
    ctx.dram("WdT_d", [FF0, D], F32, kind="ExternalInput")
    ctx.dram("x1_out", [T, D], F32, kind="ExternalOutput")
    ctx.dram("stout", [128, DIN], BF16)
    ctx.dram("stgath", [256, DIN], BF16)



P0_LOADS = [("convw", "convw_d"), ("convb", "convb_d"), ("D_tile", "D_d"), ("roleflag", "role_d"), ("norm_scale", "ns_d")]



class P0Small:

    def sync(self, ctx, sy):
        sem = ctx.sem("p0b_ld")
        sy.dma_start(out=ctx.t["dt_bias"][0:64, 0:1], in_=ctx.t["dtb_d"].ap()).then_inc(sem, 16)
        sy.dma_start(out=ctx.t["A_log"][0:64, 0:1], in_=ctx.t["alog_d"].ap()).then_inc(sem, 16)
        sy.wait_ge(sem, 32)
        return None




def build_l0():
    nc = bass.Bass("TRN2", target_bir_lowering=False)
    stack = contextlib.ExitStack()
    with stack:
        ctx = Ctx(nc, stack)
        alloc_l0(ctx, nc)
        ctx.constf(EPS)
        ctx.constf(1.0)
        emit_phases(nc, ctx, [P0Consts(P0_LOADS), P0Small(), P1LN(), P2InProjA(),
                              P3Conv(), P4aState(), P4xExchange(), P4bSSD(),
                              P5Gate(), P6FFN0(), P7Combine()])
    return nc


def _rope_tables():
    inv = 1.0 / 10000.0 ** (np.arange(0, 64, 2) / 64.0)
    t = np.arange(2048.0)
    emb = np.concatenate([np.outer(t, inv), np.outer(t, inv)], axis=1)
    return np.cos(emb), np.sin(emb)


def _rope_inputs():
    cos, sin = _rope_tables()
    def tile8(tab):
        out = np.zeros((128, 16 * 256), dtype=BF)
        for tt in range(16):
            out[:, tt * 256:(tt + 1) * 256] = np.tile(tab[tt * 128:(tt + 1) * 128], (1, 8))
        return out
    return (tile8(cos[:, 0::2]), tile8(cos[:, 1::2]),
            tile8(sin[:, 0::2]), tile8(sin[:, 1::2]))


_NC_CACHE = {}


def kernel(L0_attn_u, L0_attn_W_in, L0_attn_conv_w, L0_attn_conv_b, L0_attn_dt_bias,
           L0_attn_A_log, L0_attn_D, L0_attn_norm_scale, L0_attn_W_out,
           L0_ffn_w_gate, L0_ffn_w_up, L0_ffn_w_down,
           L1_attn_Wq, L1_attn_Wk, L1_attn_Wv, L1_attn_Wo,
           L1_ffn_gate_w, L1_ffn_up_w, L1_ffn_down_w,
           L1_ffn_gate_s, L1_ffn_up_s, L1_ffn_down_s):
    u = np.asarray(L0_attn_u, dtype=np.float32)
    convw_l = np.zeros((128, 136), dtype=np.float32)
    convb_l = np.zeros((128, 34), dtype=np.float32)
    cw = np.asarray(L0_attn_conv_w, dtype=np.float32)
    cb = np.asarray(L0_attn_conv_b, dtype=np.float32)
    for j in range(34):
        for i in range(4):
            convw_l[:, 4 * j + i] = cw[i, 128 * j:128 * j + 128]
        convb_l[:, j] = cb[128 * j:128 * j + 128]
    D_t = np.tile(np.asarray(L0_attn_D, np.float32)[None, :], (128, 1))
    ns_l = np.ascontiguousarray(np.asarray(L0_attn_norm_scale, np.float32).reshape(32, 128).T)
    WgT = np.ascontiguousarray(np.asarray(L0_ffn_w_gate, np.float32).T)
    WuT = np.ascontiguousarray(np.asarray(L0_ffn_w_up, np.float32).T)
    WdT = np.ascontiguousarray(np.asarray(L0_ffn_w_down, np.float32).T)
    if "fused" not in _NC_CACHE:
        _NC_CACHE["fused"] = build_fused()
    ce8, co8, se8, so8 = _rope_inputs()
    Wq = np.asarray(L1_attn_Wq, np.float32)
    Wk = np.asarray(L1_attn_Wk, np.float32)
    Wv = np.asarray(L1_attn_Wv, np.float32)
    WoT = np.ascontiguousarray(np.asarray(L1_attn_Wo, np.float32).T)
    Wg1T = np.ascontiguousarray(np.asarray(L1_ffn_gate_w, np.float32).T)
    Wu1T = np.ascontiguousarray(np.asarray(L1_ffn_up_w, np.float32).T)
    Wd1T = np.ascontiguousarray(np.asarray(L1_ffn_down_w, np.float32).T)
    gs_l = np.ascontiguousarray(np.asarray(L1_ffn_gate_s, np.float32).reshape(44, 128).T)
    us_l = np.ascontiguousarray(np.asarray(L1_ffn_up_s, np.float32).reshape(44, 128).T)
    ds_t = np.tile(np.asarray(L1_ffn_down_s, np.float32)[None, :], (128, 1)).astype(BF)
    in_maps = []
    for c in range(8):
        half = c % 2
        hsl = slice(0, 1024) if half == 0 else slice(1024, 2048)
        ksl = slice(0, 256) if half == 0 else slice(256, 512)
        role = np.full((128, 1), float(half), dtype=np.float32)
        in_maps.append({
            "u_dram": prep_u_core(u, c),
            "W_in": np.asarray(L0_attn_W_in, np.float32),
            "convw_d": convw_l, "convb_d": convb_l,
            "dtb_d": np.asarray(L0_attn_dt_bias, np.float32)[:, None],
            "alog_d": np.asarray(L0_attn_A_log, np.float32)[:, None],
            "D_d": D_t, "role_d": role,
            "W_out_d": np.asarray(L0_attn_W_out, np.float32), "ns_d": ns_l,
            "WgT_d": WgT, "WuT_d": WuT, "WdT_d": WdT,
            "WqT_d": np.ascontiguousarray(Wq[hsl].T),
            "WkT_d": np.ascontiguousarray(Wk[ksl].T),
            "WvT_d": np.ascontiguousarray(Wv[ksl].T),
            "WoT_d": WoT,
            "Wg1T_d": Wg1T, "Wu1T_d": Wu1T, "Wd1T_d": Wd1T,
            "ce8_d": ce8, "co8_d": co8, "se8_d": se8, "so8_d": so8,
            "gs1_d": gs_l, "us1_d": us_l, "ds_d": ds_t,
        })
    res = run_bass_kernel_spmd(_NC_CACHE["fused"], in_maps, core_ids=list(range(8)))
    out = np.zeros((4, 2048, 2048), dtype=np.float32)
    for c in range(8):
        b, half = c // 2, c % 2
        out[b, half * 1024:half * 1024 + 1024] = np.asarray(res.results[c]["out_d"])
    return out

def alloc_l1_fused(ctx, nc):
    arena = ctx.t["arena"]
    ar32 = arena.bitcast(F32)
    b = {}
    b["u_tok8"] = ar32.ap()[:, 0:16384]
    b["xn1_tok"] = arena.ap()[:, 32768:32768 + 16384]
    b["ln_sq"] = ar32.ap()[:, 24576:24576 + 2048]
    b["ln_tmp2b"] = ar32.ap()[:, 26624:26624 + 2048]
    b["ln_aggr"] = ar32.ap()[:, 28672:28672 + 16]
    b["ln_rows"] = ar32.ap()[:, 28688:28688 + 32]
    b["xnF"] = arena.ap()[:, 0:32768]
    b["qkvslab"] = arena.ap()[:, 57408:57408 + 24576]
    b["Qtok"] = arena.ap()[:, 32768:32768 + 16384]
    b["Ktok"] = arena.ap()[:, 49152:49152 + 4096]
    b["Vtok"] = arena.ap()[:, 53248:53248 + 4160]
    b["Qf"] = arena.ap()[:, 0:32768]
    b["expS"] = arena.ap()[:, 32768:32768 + 8192]
    b["Kf"] = arena.ap()[:, 57408:57408 + 8192]
    b["attnO"] = arena.ap()[:, 65600:65600 + 16384]
    b["dtmp"] = arena.ap()[:, 40960:40960 + 1024]
    b["rBs"] = arena.ap()[:, 41984:41984 + 512]
    b["rrow1"] = arena.ap()[:, 42496:42496 + 1024]
    b["drow"] = ar32.ap()[:, 21760:21760 + 512]
    b["attnG"] = arena.ap()[:, 0:16384]
    b["woslabA"] = arena.ap()[:, 16384:16384 + 1024]
    b["woslabB"] = arena.ap()[:, 17408:17408 + 1024]
    b["stage2"] = ar32.ap()[:, 11264:11264 + 2048]
    b["xn1own"] = arena.ap()[:, 45056:45056 + 16384]
    b["h1f"] = arena.ap()[:, 0:45056]
    b["ztmp"] = arena.ap()[:, 77824:77824 + 1024]
    b["uR"] = ar32.ap()[:, 0:4096]
    b["mR"] = ar32.ap()[:, 4096:8192]
    b["fR"] = ar32.ap()[:, 8192:12288]
    b["x1R"] = ar32.ap()[:, 12288:16384]
    b["x2R"] = ar32.ap()[:, 16384:20480]
    b["dsB"] = ar32.ap()[:, 20480:22528]
    xn = ctx.t["xn"]
    b["ce8"] = xn.ap()[:, 0:4096]
    b["co8"] = xn.ap()[:, 4096:8192]
    b["se8"] = xn.ap()[:, 8192:12288]
    b["so8"] = xn.ap()[:, 12288:16384]
    b["dmask"] = ctx.t["acsA"].bitcast(BF16).ap()[:, 0:2048]
    b["ropetmp"] = ctx.t["acsB"].bitcast(BF16).ap()[:, 0:512]
    b["gs1"] = ctx.t["acsB"].ap()[:, 256:300]
    b["us1"] = ctx.t["acsB"].ap()[:, 300:344]
    b["dsrow"] = ctx.t["dtbuf"].bitcast(BF16).ap()[:, 0:2048]
    ctx.dram("x1_mid", [T, D], F32)
    b["x1_in"] = ctx.t["x1_mid"]
    ctx.t["x1_out"] = ctx.t["x1_mid"]
    ctx.dram("WqT_d", [D, NQH * 64], F32, kind="ExternalInput")
    ctx.dram("WkT_d", [D, NKH * 64], F32, kind="ExternalInput")
    ctx.dram("WvT_d", [D, NKH * 64], F32, kind="ExternalInput")
    ctx.dram("WoT_d", [D, D], F32, kind="ExternalInput")
    ctx.dram("Wg1T_d", [D, FF1], F32, kind="ExternalInput")
    ctx.dram("Wu1T_d", [D, FF1], F32, kind="ExternalInput")
    ctx.dram("Wd1T_d", [FF1, D], F32, kind="ExternalInput")
    ctx.dram("ce8_d", [128, 16 * 256], BF16, kind="ExternalInput")
    ctx.dram("co8_d", [128, 16 * 256], BF16, kind="ExternalInput")
    ctx.dram("se8_d", [128, 16 * 256], BF16, kind="ExternalInput")
    ctx.dram("so8_d", [128, 16 * 256], BF16, kind="ExternalInput")
    ctx.dram("gs1_d", [128, 44], F32, kind="ExternalInput")
    ctx.dram("us1_d", [128, 44], F32, kind="ExternalInput")
    ctx.dram("ds_d", [128, 2048], BF16, kind="ExternalInput")
    ctx.dram("xn1b", [T, D], BF16)
    ctx.dram("xn1g", [TT2, D], BF16)
    ctx.dram("aOb", [T, TT2], BF16)
    ctx.dram("aOg", [TT2, TT2], BF16)
    ctx.dram("attn_s", [T, D], F32)
    ctx.dram("ffn1_s", [T, D], F32)
    ctx.dram("out_d", [T, D], F32, kind="ExternalOutput")
    return b


def build_fused():
    nc = bass.Bass("TRN2", target_bir_lowering=False)
    stack = contextlib.ExitStack()
    with stack:
        ctx = Ctx(nc, stack)
        alloc_l0(ctx, nc)
        rebind = alloc_l1_fused(ctx, nc)
        ctx.constf(EPS)
        ctx.constf(1.0)
        ctx.constf(-30.0)
        ffn1 = P6FFN0(xn_name="xn1own", xn_off=0, out_name="ffn1_s",
                      wg="Wg1T_d", wu="Wu1T_d", wd="Wd1T_d", pfx="q6",
                      ne=FF1, gate_scale="gs1", up_scale="us1", slab0=61440)
        phases = [P0Consts(P0_LOADS), P0Small(), P1LN(), P2InProjA(), P3Conv(),
                  P4aState(), P4xExchange(), P4bSSD(), P5Gate(), P6FFN0(),
                  P7Combine(), rebind, Q0Consts(), Q1LN(), Q2QKV(), Q3Attn(),
                  Q4Exchange(), Q5Wo(), ffn1, Q7Final()]
        emit_phases(nc, ctx, phases)
    return nc


# version 10
# speedup vs baseline: 1.3517x; 1.0070x over previous
import contextlib
import numpy as np
import ml_dtypes
import concourse.bass as bass
import concourse.mybir as mybir
from concourse.bass_utils import run_bass_kernel_spmd

F32 = mybir.dt.float32
BF16 = mybir.dt.bfloat16
AF = mybir.ActivationFunctionType
ALU = mybir.AluOpType
AX = mybir.AxisListType

BF = ml_dtypes.bfloat16

D = 2048
T = 1024
HALO = 4
TF = T + HALO
TP = 1152
DIN = 4096
NH = 64
HD = 64
NST = 128
CH = 256
NCH = T // CH
CONVD = DIN + 2 * NST
EIN = 2 * DIN + 2 * NST + NH
XBCD = CONVD + NH
FF0 = 8192
FF1 = 5632
EPS = 1e-5

PAIRS = [[0, 1], [2, 3], [4, 5], [6, 7]]
ENGS = ["tensor", "vector", "scalar", "gpsimd", "sync"]


def cdiv(a, b):
    return (a + b - 1) // b


class Ctx:
    def __init__(self, nc, stack):
        self.nc = nc
        self.t = {}
        self.sems = {}
        self.stack = stack
        self.phase_idx = -1
        self.pool = []
        self.phase_map = {}
        self.consts = {}
        self.cpool = None

    def init_pool(self, n=88):
        for i in range(n):
            self.pool.append(self.stack.enter_context(self.nc.semaphore(f"pool{i}")))

    def constf(self, value):
        if self.cpool is None:
            self.cpool = self.sb("cpool", [128, 32], F32)
        v = float(value)
        if v not in self.consts:
            self.consts[v] = len(self.consts)
            assert len(self.consts) <= 32
        return self.cpool[:, self.consts[v]:self.consts[v] + 1]

    def constf_r(self, value, rows):
        ap = self.constf(value)
        return self.cpool[0:rows, self.consts[float(value)]:self.consts[float(value)] + 1]

    def sem(self, name):
        if self.phase_idx < 0 or name.startswith("g_"):
            if name not in self.sems:
                self.sems[name] = self.stack.enter_context(self.nc.semaphore(name))
            return self.sems[name]
        key = (self.phase_idx, name)
        if key not in self.phase_map:
            used = len([1 for (p, _) in self.phase_map if p == self.phase_idx])
            assert used < len(self.pool), "phase sem pool exhausted"
            self.phase_map[key] = used
        return self.pool[self.phase_map[key]]

    def sb(self, name, shape, dtype, **kw):
        h = self.stack.enter_context(self.nc.sbuf_tensor(name, shape, dtype, **kw))
        self.t[name] = h
        return h

    def ps(self, name, shape, dtype=F32):
        h = self.stack.enter_context(self.nc.psum_tensor(name, shape, dtype))
        self.t[name] = h
        return h

    def dram(self, name, shape, dtype, kind=None, **kw):
        if kind:
            h = self.nc.dram_tensor(name, shape, dtype, kind=kind, **kw)
        else:
            h = self.nc.dram_tensor(name, shape, dtype, **kw)
        self.t[name] = h
        return h


def inc_last(eng, inst, sem):
    eng.drain()
    eng.sem_inc(sem, 1)


def dinc(eng, sem, n=1):
    eng.drain()
    eng.sem_inc(sem, n)


def emit_phases(nc, ctx, phases):
    ctx.init_pool()
    pool_nums = sorted(sm.num for sm in ctx.pool)
    lo, hi = pool_nums[0], pool_nums[-1] + 1
    assert hi - lo == len(pool_nums), "pool sems not contiguous"
    emitted = 0
    for p in phases:
        if isinstance(p, dict):
            ctx.t.update(p)
            continue
        i = emitted
        emitted += 1
        ctx.phase_idx = i
        if i > 0:
            with nc.Block() as bc:
                def clr(g):
                    g.dma_reset(range(lo, hi))
                    g.sem_clear(range(lo, hi))
                bc.gpsimd(clr)
            nc.all_engine_barrier()
        with nc.Block() as block:
            for eng_name in ENGS:
                fn = getattr(p, eng_name, None)
                if fn is None:
                    continue
                def make_body(fn=fn):
                    def body(eng):
                        fn(ctx, eng)
                    return body
                getattr(block, eng_name)(make_body())
        nc.all_engine_barrier()
    ctx.phase_idx = -1


class PDebugStore:

    def __init__(self, names):
        self.names = names

    def sync(self, ctx, sy):
        s_o = ctx.sem("dbg_out")
        k = 0
        for n in self.names:
            sy.dma_start(out=ctx.t["dbg_" + n].ap(),
                         in_=as_ap(ctx.t[n])[:, :]).then_inc(s_o, 16)
            k += 1
        sy.wait_ge(s_o, 16 * k)
        return None


def as_ap(obj):
    return obj.ap() if callable(getattr(obj, "ap", None)) else obj


def add_debug(ctx, names):
    for n in names:
        ap = as_ap(ctx.t[n])
        ctx.dram("dbg_" + n, list(ap.shape), ap.dtype, kind="ExternalOutput")
    return PDebugStore(names)


class P0Consts:
    def __init__(self, loads):
        self.loads = loads

    def gpsimd(self, ctx, g):
        ident = ctx.t["ident_bf"]
        g.memset(ident[:, :], 0.0)
        g.drain()
        g.affine_select(out=ident[:, :], in_=ident[:, :],
                        compare_op=ALU.not_equal, fill=1.0,
                        base=0, pattern=[[-1, 128]], channel_multiplier=1)
        ident32 = ctx.t["ident_f32"]
        g.memset(ident32[:, :], 0.0)
        g.drain()
        g.affine_select(out=ident32[:, :], in_=ident32[:, :],
                        compare_op=ALU.not_equal, fill=1.0,
                        base=0, pattern=[[-1, 128]], channel_multiplier=1)
        ma = ctx.t["gmaskA"]
        g.memset(ma[:, :], 1.0)
        g.drain()
        g.affine_select(out=ma[:, :], in_=ma[:, :], compare_op=ALU.is_ge,
                        fill=0.0, base=0, pattern=[[1, CH]], channel_multiplier=-1)
        mb = ctx.t["gmaskB"]
        g.memset(mb[:, :], 1.0)
        g.drain()
        last = g.affine_select(out=mb[:, :], in_=mb[:, :], compare_op=ALU.is_ge,
                               fill=0.0, base=-128, pattern=[[1, CH]],
                               channel_multiplier=-1)
        return last

    def vector(self, ctx, v):
        last = v.memset(ctx.t["ones_bf"][:, :], 1.0)
        for val, idx in ctx.consts.items():
            last = v.memset(ctx.cpool[:, idx:idx + 1], val)
        return last

    def sync(self, ctx, sy):
        sem = ctx.sem("p0_ld")
        n = 0
        for sb_name, dram_name in self.loads:
            sb = ctx.t[sb_name]
            dr = ctx.t[dram_name]
            sy.dma_start(out=sb[:, :], in_=dr.ap()).then_inc(sem, 16)
            n += 1
        if n:
            sy.wait_ge(sem, 16 * n)
        return None


class P1LN:

    def sync(self, ctx, sy):
        u = ctx.t["u_dram"]
        u_tok = ctx.t["u_tok"]
        s_u = ctx.sem("p1_u_ld")
        src = u.ap().rearrange("(tt p) d -> p tt d", p=128)
        dst = u_tok.rearrange("p (tt d) -> p tt d", tt=9)
        sy.dma_start(out=dst, in_=src).then_inc(s_u, 16)
        return None

    def vector(self, ctx, v):
        u_tok = ctx.t["u_tok"]
        u_bf = ctx.t["u_bf"]
        aggr = ctx.t["ln_aggr"]
        rows = ctx.t["ln_rows"]
        sq = ctx.t["ln_tmp"]
        s_u = ctx.sem("p1_u_ld")
        s_ready = ctx.sem("p1_tok_ready")
        s_rstd = ctx.sem("p1_rstd")
        s_sq = ctx.sem("p1_sq")
        for tt in range(9):
            if tt == 0:
                v.wait_ge(s_u, 16)
            ut = u_tok[:, tt * D:(tt + 1) * D]
            a = aggr[:, 2 * tt:2 * tt + 2]
            r = rows[:, 4 * tt:4 * tt + 4]
            v.reduce_sum(r[:, 2:3], ut, axis=AX.X)
            v.wait_ge(s_sq, tt + 1)
            v.reduce_sum(r[:, 3:4], sq[:, :], axis=AX.X)
            dinc(v, ctx.sem("p1_sq_free"))
            v.drain()
            v.tensor_scalar(a[:, 0:1], r[:, 2:3], 1.0 / D, None, op0=ALU.mult)
            v.tensor_scalar(r[:, 3:4], r[:, 3:4], 1.0 / D, None, op0=ALU.mult)
            v.drain()
            v.tensor_tensor(a[:, 1:2], a[:, 0:1], a[:, 0:1], op=ALU.mult)
            v.drain()
            v.tensor_tensor(a[:, 1:2], r[:, 3:4], a[:, 1:2], op=ALU.subtract)
            dinc(v, ctx.sem("p1_var_ready"))
        tmp2 = ctx.t["ln_tmp2"]
        for tt in range(9):
            ut = u_tok[:, tt * D:(tt + 1) * D]
            a = aggr[:, 2 * tt:2 * tt + 2]
            r = rows[:, 4 * tt:4 * tt + 4]
            v.wait_ge(s_rstd, tt + 1)
            v.tensor_scalar(tmp2[:, :], ut, a[:, 0:1], r[:, 0:1],
                            op0=ALU.subtract, op1=ALU.mult)
            v.drain()
            v.tensor_copy(u_bf[:, tt * D:(tt + 1) * D], tmp2[:, :])
            dinc(v, s_ready)
        return None

    def scalar(self, ctx, s):
        aggr = ctx.t["ln_aggr"]
        rows = ctx.t["ln_rows"]
        s_var = ctx.sem("p1_var_ready")
        s_rstd = ctx.sem("p1_rstd")
        xn = ctx.t["xn"]
        s_tp = ctx.sem("p1_tp")
        s_ev = ctx.sem("p1_ev")
        sq = ctx.t["ln_tmp"]
        u_tok = ctx.t["u_tok"]
        s_sq = ctx.sem("p1_sq")
        s_sq_free = ctx.sem("p1_sq_free")
        last = None
        for tt in range(9):
            if tt == 0:
                s.wait_ge(ctx.sem("p1_u_ld"), 16)
            if tt > 0:
                s.wait_ge(s_sq_free, tt)
            s.activation(sq[:, :], u_tok[:, tt * D:(tt + 1) * D], AF.Square)
            dinc(s, s_sq)
        for tt in range(9):
            a = aggr[:, 2 * tt:2 * tt + 2]
            r = rows[:, 4 * tt:4 * tt + 4]
            s.wait_ge(s_var, tt + 1)
            s.activation(r[:, 1:2], a[:, 1:2], AF.Ln, bias=ctx.constf(EPS))
            s.drain()
            s.activation(r[:, 0:1], r[:, 1:2], AF.Exp, scale=-0.5)
            dinc(s, s_rstd)
        for tt in range(9):
            w = 128 if tt < 8 else HALO
            for dt in range(16):
                k = tt * 16 + dt
                s.wait_ge(s_tp, k + 1)
                pt = ctx.t["pt0"] if k % 2 == 0 else ctx.t["pt1"]
                last = s.copy(xn[:, dt * TF + tt * 128: dt * TF + tt * 128 + w],
                              pt[:, 0:w])
                last.then_inc(s_ev, 1)
        return None

    def tensor(self, ctx, t):
        u_bf = ctx.t["u_bf"]
        ident = ctx.t["ident_bf"]
        s_ready = ctx.sem("p1_tok_ready")
        s_tp = ctx.sem("p1_tp")
        s_ev = ctx.sem("p1_ev")
        last = None
        for tt in range(9):
            t.wait_ge(s_ready, tt + 1)
            for dt in range(16):
                k = tt * 16 + dt
                if k >= 2:
                    t.wait_ge(s_ev, k - 1)
                pt = ctx.t["pt0"] if k % 2 == 0 else ctx.t["pt1"]
                last = t.transpose(pt[:, :],
                                   u_bf[:, tt * D + dt * 128: tt * D + dt * 128 + 128],
                                   ident[:, :])
                last.then_inc(s_tp, 1)
        return None


def prep_u_core(u, core):
    b, half = core // 2, core % 2
    t0 = half * 1024
    out = np.zeros((TP, D), dtype=np.float32)
    lo = t0 - HALO
    src_lo = max(lo, 0)
    off = src_lo - lo
    out[off:HALO + 1024, :] = u[b, src_lo:t0 + 1024, :]
    return np.ascontiguousarray(out)


P2_CHUNKS = [(DIN + 512 * i, 512) for i in range(8)] + [(DIN + 4096, 320)]
P2_TCS = [(0, 512), (512, 512), (1024, HALO)]
CONV_ORDER = [32, 33] + list(range(32))


def etiles(w):
    out = []
    o = 0
    while o < w:
        out.append((o, min(128, w - o)))
        o += 128
    return out


class P2InProjA:

    def groups(self):
        idx = 0
        for ci, (e0, ew) in enumerate(P2_CHUNKS):
            for j, (jo, jw) in enumerate(etiles(ew)):
                jglob = (e0 - DIN + jo) // 128
                for tc0, tw in P2_TCS:
                    yield idx, ci, ew, jo, jw, jglob, tc0, tw
                    idx += 1

    def chunk_last(self):
        last = {}
        for idx, ci, *_ in self.groups():
            last[ci] = idx
        return last

    def gpsimd(self, ctx, g):
        w = ctx.t["W_in"]
        s_mm = ctx.sem("p2_mm")
        last = self.chunk_last()
        for ci, (e0, ew) in enumerate(P2_CHUNKS):
            if ci >= 2:
                g.wait_ge(s_mm, last[ci - 2] + 1)
            slab = ctx.t["slabA"] if ci % 2 == 0 else ctx.t["slabB"]
            s_wl = ctx.sem("p2_wlA" if ci % 2 == 0 else "p2_wlB")
            src = w.ap()[:, e0:e0 + ew].rearrange("(k p) e -> p k e", p=128)
            dst = slab[:, 0:16 * ew].rearrange("p (k e) -> p k e", k=16)
            g.dma_start(out=dst, in_=src).then_inc(s_wl, 16)
        return None

    def tensor(self, ctx, t):
        xn = ctx.t["xn"]
        s_mm = ctx.sem("p2_mm")
        s_evV = ctx.sem("p2_evV")
        s_evS = ctx.sem("p2_evS")
        last_ci = -1
        for idx, ci, ew, jo, jw, jglob, tc0, tw in self.groups():
            if ci != last_ci:
                t.wait_ge(ctx.sem("p2_wlA" if ci % 2 == 0 else "p2_wlB"),
                          16 * (ci // 2 + 1))
                last_ci = ci
            slab = ctx.t["slabA"] if ci % 2 == 0 else ctx.t["slabB"]
            p = ctx.t["pA"] if idx % 2 == 0 else ctx.t["pB"]
            sev = s_evV if idx % 2 == 0 else s_evS
            if idx >= 2:
                t.wait_ge(sev, idx // 2)
            for k in range(16):
                mm = t.matmul(p[0:jw, 0:tw],
                              slab[:, k * ew + jo:k * ew + jo + jw],
                              xn[:, k * TF + tc0:k * TF + tc0 + tw],
                              start=(k == 0), stop=(k == 15))
            mm.then_inc(s_mm, 1)
        return None

    def _evac(self, ctx, eng, parity):
        xBC = ctx.t["xBC"]
        s_mm = ctx.sem("p2_mm")
        sev = ctx.sem("p2_evV" if parity == 0 else "p2_evS")
        for idx, ci, ew, jo, jw, jglob, tc0, tw in self.groups():
            if idx % 2 != parity:
                continue
            p = ctx.t["pA"] if parity == 0 else ctx.t["pB"]
            eng.wait_ge(s_mm, idx + 1)
            eng.tensor_copy(xBC[0:jw, jglob * TF + tc0:jglob * TF + tc0 + tw],
                            p[0:jw, 0:tw])
            dinc(eng, sev)
        return None

    def vector(self, ctx, v):
        return self._evac(ctx, v, 0)

    def scalar(self, ctx, s):
        xBC = ctx.t["xBC"]
        s_mm = ctx.sem("p2_mm")
        sev = ctx.sem("p2_evS")
        for idx, ci, ew, jo, jw, jglob, tc0, tw in self.groups():
            if idx % 2 != 1:
                continue
            p = ctx.t["pB"]
            s.wait_ge(s_mm, idx + 1)
            s.copy(xBC[0:jw, jglob * TF + tc0:jglob * TF + tc0 + tw], p[0:jw, 0:tw])
            dinc(s, sev)
        return None


class P3Conv:

    def vector(self, ctx, v):
        xBC = ctx.t["xBC"]
        dtbuf, acsA, acsB = ctx.t["dtbuf"], ctx.t["acsA"], ctx.t["acsB"]
        dt34 = xBC[0:64, 34 * TF + HALO:34 * TF + HALO + T]
        v.tensor_scalar(dtbuf[0:64, :], dt34, ctx.t["dt_bias"][0:64, 0:1], None,
                        op0=ALU.add)
        dinc(v, ctx.sem("p3_a"))
        v.wait_ge(ctx.sem("p3_sp"), 1)
        v.tensor_tensor(dtbuf[0:64, :], acsA[0:64, :], acsB[0:64, :], op=ALU.add)
        v.drain()
        v.tensor_copy(ctx.t["tmp_bf"][0:64, :], dtbuf[0:64, :])
        v.wait_ge(ctx.sem("p3_expA"), 1)
        v.tensor_scalar(acsA[0:64, :], dtbuf[0:64, :], ctx.t["expA"][0:64, 0:1],
                        None, op0=ALU.mult)
        dinc(v, ctx.sem("p3_dtbf"))
        dtT = ctx.t["dtT"]
        for tb in range(8):
            pt = ctx.t["pt0"] if tb % 2 == 0 else ctx.t["pt1"]
            v.wait_ge(ctx.sem("p3_dtT"), tb + 1)
            v.tensor_copy(dtT[:, tb * 64:(tb + 1) * 64], pt[:, 0:64])
            dinc(v, ctx.sem("p3_evdt"))
        for lv, sh in enumerate([1, 2, 4, 8, 16, 32, 64, 128]):
            src = acsA if lv % 2 == 0 else acsB
            dst = acsB if lv % 2 == 0 else acsA
            v.drain()
            for c in range(NCH):
                c0 = c * CH
                v.tensor_tensor(dst[0:64, c0 + sh:c0 + CH],
                                src[0:64, c0 + sh:c0 + CH],
                                src[0:64, c0:c0 + CH - sh], op=ALU.add)
                v.tensor_copy(dst[0:64, c0:c0 + sh], src[0:64, c0:c0 + sh])
        dinc(v, ctx.sem("p3_acsdone"))
        for c in range(NCH):
            c0 = c * CH
            v.tensor_scalar(acsB[0:64, c0:c0 + CH], acsA[0:64, c0:c0 + CH],
                            acsA[0:64, c0 + CH - 1:c0 + CH], None, op0=ALU.subtract)
        dinc(v, ctx.sem("p3_dexp"))
        acsT = ctx.t["acsT"].rearrange("p (cc hh tb) -> p cc hh tb", cc=NCH, tb=2)
        for tb in range(8):
            pf = ctx.t["pf0"] if tb % 2 == 0 else ctx.t["pf1"]
            v.wait_ge(ctx.sem("p3_acsTtp"), tb + 1)
            v.tensor_copy(acsT[:, tb // 2, :, tb % 2], pf[:, 0:64])
            dinc(v, ctx.sem("p3_evacs"))
        for c in range(NCH):
            v.wait_ge(ctx.sem("p3_alT"), c + 1)
            v.tensor_copy(ctx.t["alast_rows"][0:1, c * 64:(c + 1) * 64],
                          ctx.t["pAl"][0:1, 0:64])
            dinc(v, ctx.sem("p3_eval"))
        decayT = ctx.t["decayT"]
        for tb in range(8):
            pt = ctx.t["pt0"] if tb % 2 == 0 else ctx.t["pt1"]
            v.wait_ge(ctx.sem("p3_decT"), tb + 1)
            v.tensor_copy(decayT[:, tb * 64:(tb + 1) * 64], pt[:, 0:64])
            dinc(v, ctx.sem("p3_evdec"))
        v.reciprocal(ctx.t["invdt"][:, :], dtT[:, :])
        v.drain()
        for tb in range(8):
            v.tensor_tensor(ctx.t["Dinvdt"][:, tb * 64:tb * 64 + 64],
                            ctx.t["invdt"][:, tb * 64:tb * 64 + 64],
                            ctx.t["D_tile"][:, 0:64], op=ALU.mult)
        dinc(v, ctx.sem("p3_dinv"))
        wcol = ctx.t["convw"]
        bcol = ctx.t["convb"]
        s_cd = ctx.sem("p3_convV")
        s_cfree = ctx.sem("p3_freeV")
        xdtT = ctx.t["xdtT"]
        BT = ctx.t["BT"]
        s_tp3 = ctx.sem("p3_tp")
        s_ev3 = ctx.sem("p3_ev")

        def evac_B(kb):
            for tb in range(8):
                pt = ctx.t["pt0"] if kb % 2 == 0 else ctx.t["pt1"]
                v.wait_ge(s_tp3, kb + 1)
                v.tensor_copy(BT[:, tb * 128:(tb + 1) * 128], pt[:, :])
                dinc(v, s_ev3)
                kb += 1
            return kb

        def evac_x(m, kb):
            for tb in range(8):
                pt = ctx.t["pt0"] if kb % 2 == 0 else ctx.t["pt1"]
                v.wait_ge(s_tp3, kb + 1)
                v.tensor_scalar(xdtT[:, tb * DIN + (2 * m) * 64:tb * DIN + (2 * m) * 64 + 64],
                                pt[:, 0:64], dtT[:, tb * 64 + 2 * m:tb * 64 + 2 * m + 1],
                                None, op0=ALU.mult)
                v.tensor_scalar(xdtT[:, tb * DIN + (2 * m + 1) * 64:tb * DIN + (2 * m + 1) * 64 + 64],
                                pt[:, 64:128], dtT[:, tb * 64 + 2 * m + 1:tb * 64 + 2 * m + 2],
                                None, op0=ALU.mult)
                dinc(v, s_ev3)
                kb += 1
            return kb

        kb = 0
        for i_o, j in enumerate(CONV_ORDER):
            cf = ctx.t["convfV"] if i_o % 2 == 0 else ctx.t["convfG"]
            if i_o >= 2:
                v.wait_ge(s_cfree, i_o - 1)
            xj = xBC[:, j * TF:(j + 1) * TF]
            v.drain()
            v.tensor_scalar(cf[:, :], xj[:, 1:1 + T], wcol[:, 4 * j:4 * j + 1],
                            bcol[:, j:j + 1], op0=ALU.mult, op1=ALU.add)
            for i in range(1, 4):
                v.drain()
                v.scalar_tensor_tensor(cf[:, :], xj[:, 1 + i:1 + i + T],
                                       wcol[:, 4 * j + i:4 * j + i + 1], cf[:, :],
                                       op0=ALU.mult, op1=ALU.add)
            dinc(v, s_cd)
            if i_o == 2:
                kb = evac_B(kb)
            if i_o >= 3:
                kb = evac_x(i_o - 3, kb)
        kb = evac_x(31, kb)
        return None

    def scalar(self, ctx, s):
        dtbuf, acsA, acsB = ctx.t["dtbuf"], ctx.t["acsA"], ctx.t["acsB"]
        s.activation(ctx.t["expA"][0:64, 0:1], ctx.t["A_log"][0:64, 0:1], AF.Exp)
        dinc(s, ctx.sem("p3_expA"))
        s.wait_ge(ctx.sem("p3_a"), 1)
        s.activation(acsA[0:64, :], dtbuf[0:64, :], AF.Abs)
        s.drain()
        s.activation(acsB[0:64, :], acsA[0:64, :], AF.Exp, scale=-1.0)
        s.drain()
        s.activation(acsA[0:64, :], acsB[0:64, :], AF.Ln, bias=ctx.constf_r(1.0, 64))
        s.drain()
        s.activation(acsB[0:64, :], dtbuf[0:64, :], AF.Relu)
        dinc(s, ctx.sem("p3_sp"))
        s.wait_ge(ctx.sem("p3_dexp"), 1)
        s.activation(ctx.t["decay_bf"][0:64, :], acsB[0:64, :], AF.Exp)
        dinc(s, ctx.sem("p3_dbf"))
        s.wait_ge(ctx.sem("p3_eval"), NCH)
        s.activation(ctx.t["alast_erow"][0:1, 0:NCH * 64],
                     ctx.t["alast_rows"][0:1, 0:NCH * 64], AF.Exp, scale=-1.0)
        dinc(s, ctx.sem("p3_alerow"))
        s_cdV = ctx.sem("p3_convV")
        s_silu = ctx.sem("p3_silu")
        sfree = ctx.sem("p3_freeV")
        for i_o, j in enumerate(CONV_ORDER):
            s.wait_ge(s_cdV, i_o + 1)
            cf = ctx.t["convfV"] if i_o % 2 == 0 else ctx.t["convfG"]
            if j == 32:
                dst = ctx.t["Bfeat"][:, :]
            elif j == 33:
                dst = ctx.t["Cfeat"][:, :]
            else:
                if j >= 2:
                    s.wait_ge(ctx.sem("p3_tp"), 8 * j)
                dst = ctx.t["conv_tmp"][:, (j % 2) * T:(j % 2) * T + T]
            s.activation(dst, cf[:, :], AF.Silu)
            dinc(s, s_silu)
            dinc(s, sfree)
        return None

    def tensor(self, ctx, t):
        ident = ctx.t["ident_bf"]
        ident32 = ctx.t["ident_f32"]
        tmp_bf = ctx.t["tmp_bf"]
        acsA = ctx.t["acsA"]
        t.wait_ge(ctx.sem("p3_dtbf"), 1)
        for tb in range(8):
            pt = ctx.t["pt0"] if tb % 2 == 0 else ctx.t["pt1"]
            if tb >= 2:
                t.wait_ge(ctx.sem("p3_evdt"), tb - 1)
            t.transpose(pt[:, 0:64], tmp_bf[0:64, tb * 128:(tb + 1) * 128],
                        ident[0:64, 0:64]).then_inc(ctx.sem("p3_dtT"), 1)
        t.wait_ge(ctx.sem("p3_acsdone"), 1)
        for tb in range(8):
            pf = ctx.t["pf0"] if tb % 2 == 0 else ctx.t["pf1"]
            if tb >= 2:
                t.wait_ge(ctx.sem("p3_evacs"), tb - 1)
            t.transpose(pf[:, 0:64], acsA[0:64, tb * 128:(tb + 1) * 128],
                        ident32[0:64, 0:64]).then_inc(ctx.sem("p3_acsTtp"), 1)
        for c in range(NCH):
            if c >= 1:
                t.wait_ge(ctx.sem("p3_eval"), c)
            t.transpose(ctx.t["pAl"][0:1, 0:64],
                        acsA[0:64, c * CH + CH - 1:c * CH + CH],
                        ident32[0:64, 0:64]).then_inc(ctx.sem("p3_alT"), 1)
        t.wait_ge(ctx.sem("p3_evdt"), 8)
        t.wait_ge(ctx.sem("p3_dbf"), 1)
        for tb in range(8):
            pt = ctx.t["pt0"] if tb % 2 == 0 else ctx.t["pt1"]
            if tb >= 2:
                t.wait_ge(ctx.sem("p3_evdec"), tb - 1)
            t.transpose(pt[:, 0:64], ctx.t["decay_bf"][0:64, tb * 128:(tb + 1) * 128],
                        ident[0:64, 0:64]).then_inc(ctx.sem("p3_decT"), 1)
        s_silu = ctx.sem("p3_silu")
        s_tp3 = ctx.sem("p3_tp")
        s_ev3 = ctx.sem("p3_ev")
        t.wait_ge(ctx.sem("p3_evdec"), 8)
        kb = 0
        t.wait_ge(s_silu, 1)
        for tb in range(8):
            pt = ctx.t["pt0"] if kb % 2 == 0 else ctx.t["pt1"]
            if kb >= 2:
                t.wait_ge(s_ev3, kb - 1)
            t.transpose(pt[:, :], ctx.t["Bfeat"][:, tb * 128:tb * 128 + 128],
                        ident[:, :]).then_inc(s_tp3, 1)
            kb += 1
        for j in range(32):
            t.wait_ge(s_silu, j + 3)
            for tb in range(8):
                pt = ctx.t["pt0"] if kb % 2 == 0 else ctx.t["pt1"]
                if kb >= 2:
                    t.wait_ge(s_ev3, kb - 1)
                t.transpose(pt[:, :],
                            ctx.t["conv_tmp"][:, (j % 2) * T + tb * 128:(j % 2) * T + tb * 128 + 128],
                            ident[:, :]).then_inc(s_tp3, 1)
                kb += 1
        return None


class P4aState:
    def gpsimd(self, ctx, g):
        sel4 = ctx.t["sel4"]
        g.memset(sel4[:, :], 1.0)
        g.drain()
        g.affine_select(out=sel4[:, :], in_=sel4[:, :], compare_op=ALU.is_ge,
                        fill=0.0, base=0, pattern=[[1, 512]], channel_multiplier=-128)
        g.drain()
        g.affine_select(out=sel4[:, :], in_=sel4[:, :], compare_op=ALU.is_ge,
                        fill=0.0, base=127, pattern=[[-1, 512]], channel_multiplier=128)
        g.drain()
        return None

    def tensor(self, ctx, t):
        s_bd = ctx.sem("p4a_bd")
        s_mm = ctx.sem("p4a_mm")
        s_rec = ctx.sem("p4a_rec")
        s_alb = ctx.sem("p4a_alb")
        for c in range(NCH):
            p = ctx.t["pAcsB0"] if c % 2 == 0 else ctx.t["pAcsB1"]
            if c >= 2:
                t.wait_ge(s_alb, c - 1)
            t.matmul(p[:, 0:64], ctx.t["ones_bf"][0:1, 0:128],
                     ctx.t["alast_erow"][0:1, c * 64:(c + 1) * 64],
                     start=True, stop=True).then_inc(ctx.sem("p4a_albtp"), 1)
        xdtT = ctx.t["xdtT"]
        bdec = ctx.t["bdec"]
        for idx in range(NCH * NH):
            c, h = idx // NH, idx % NH
            slot = idx % 2
            pS = ctx.t["pS0"] if slot == 0 else ctx.t["pS1"]
            t.wait_ge(s_bd, idx + 1)
            if idx >= 2:
                t.wait_ge(s_rec, idx - 1)
            for tb in (2 * c, 2 * c + 1):
                mm = t.matmul(pS, bdec[:, slot * 256 + (tb % 2) * 128:slot * 256 + (tb % 2) * 128 + 128],
                              xdtT[:, tb * DIN + h * 64:tb * DIN + h * 64 + 64],
                              start=(tb % 2 == 0), stop=(tb % 2 == 1))
            mm.then_inc(s_mm, 1)
        return None

    def scalar(self, ctx, s):
        s_rec = ctx.sem("p4a_rec")
        s_alb = ctx.sem("p4a_alb")
        s_st4 = ctx.sem("p4a_st4")
        expAlB = ctx.t["expAlB"]
        S_run = ctx.t["S_run"]
        for idx in range(NCH * NH):
            c, h = idx // NH, idx % NH
            if c >= 1:
                s.wait_ge(s_alb, c + 1)
                if idx >= 4:
                    s.wait_ge(s_rec, idx - 3)
                sl4 = (idx % 4) * 64
                s.activation(ctx.t["stmp4"][:, sl4:sl4 + 64],
                             S_run[:, h * 64:(h + 1) * 64], AF.Copy,
                             scale=expAlB[:, c * 64 + h:c * 64 + h + 1])
                s.drain()
            s.sem_inc(s_st4, 1)
        return None

    def vector(self, ctx, v):
        s_bd = ctx.sem("p4a_bd")
        s_mm = ctx.sem("p4a_mm")
        s_rec = ctx.sem("p4a_rec")
        s_alb = ctx.sem("p4a_alb")
        s_st4 = ctx.sem("p4a_st4")
        BT = ctx.t["BT"]
        decayT = ctx.t["decayT"]
        bdec = ctx.t["bdec"]
        S_run = ctx.t["S_run"]
        for c in range(NCH):
            p = ctx.t["pAcsB0"] if c % 2 == 0 else ctx.t["pAcsB1"]
            v.wait_ge(ctx.sem("p4a_albtp"), c + 1)
            v.tensor_copy(ctx.t["expAlB"][:, c * 64:(c + 1) * 64], p[:, 0:64])
            dinc(v, s_alb)
        for idx in range(NCH * NH):
            c, h = idx // NH, idx % NH
            slot = idx % 2
            pS = ctx.t["pS0"] if slot == 0 else ctx.t["pS1"]
            if idx >= 2:
                v.wait_ge(s_mm, idx - 1)
            for tb in (2 * c, 2 * c + 1):
                v.tensor_scalar(bdec[:, slot * 256 + (tb % 2) * 128:slot * 256 + (tb % 2) * 128 + 128],
                                BT[:, tb * 128:(tb + 1) * 128],
                                decayT[:, tb * 64 + h:tb * 64 + h + 1], None, op0=ALU.mult)
            dinc(v, s_bd)
            v.wait_ge(s_mm, idx + 1)
            srun_h = S_run[:, h * 64:(h + 1) * 64]
            if c == 0:
                v.tensor_copy(srun_h, pS)
            else:
                v.wait_ge(s_st4, idx + 1)
                sl4 = (idx % 4) * 64
                v.tensor_tensor(srun_h, ctx.t["stmp4"][:, sl4:sl4 + 64], pS,
                                op=ALU.add)
            dinc(v, s_rec)
        return None


class P4xExchange:
    def gpsimd(self, ctx, g):
        sem = ctx.sem("p4x_dma")
        sem2 = ctx.sem("p4x_dma2")
        cc = ctx.sem("p4x_cc")
        g.dma_start(out=ctx.t["stout"].ap(), in_=ctx.t["S_run"][:, :]).then_inc(sem, 16)
        g.wait_ge(sem, 16)
        g.collective_compute(
            "AllGather", ALU.bypass,
            replica_groups=PAIRS,
            ins=[ctx.t["stout"].ap().opt()],
            outs=[ctx.t["stgath"].ap().opt()],
        ).then_inc(cc, 1)
        g.wait_ge(cc, 1)
        g.dma_start(out=ctx.t["S_run"][:, :], in_=ctx.t["stgath"].ap()[0:128, :]).then_inc(sem2, 16)
        g.wait_ge(sem2, 16)
        g.sem_inc(ctx.sem("p4x_ld"), 1)
        return None

    def vector(self, ctx, v):
        S_run = ctx.t["S_run"]
        role = ctx.t["roleflag"]
        bnc = ctx.t["roleb"]
        v.wait_ge(ctx.sem("p4x_ld"), 1)
        for half in range(2):
            v.tensor_copy(bnc[:, 0:2048], S_run[:, half * 2048:(half + 1) * 2048])
            v.drain()
            v.tensor_scalar(S_run[:, half * 2048:(half + 1) * 2048], bnc[:, 0:2048],
                            role[:, 0:1], None, op0=ALU.mult)
            v.drain()
        return None


class P4bSSD:

    def tensor(self, ctx, t):
        ident32 = ctx.t["ident_f32"]
        identb = ctx.t["ident_bf"]
        acsT = ctx.t["acsT"]
        xdtT = ctx.t["xdtT"]
        MT = ctx.t["MT"]
        bdec = ctx.t["bdec"]
        S_run = ctx.t["S_run"]
        sel4 = ctx.t["sel4"]
        s_gm = ctx.sem("p4b_gm")
        s_gtp = ctx.sem("p4b_gtp")
        s_rtp = ctx.sem("p4b_rtp")
        s_rowv = ctx.sem("p4b_rowv")
        s_hl = ctx.sem("p4b_hl")
        s_acsbtp = ctx.sem("p4b_acsbtp")
        s_acsb = ctx.sem("p4b_acsb")
        s_mtg = ctx.sem("p4b_mtg")
        s_mtv = ctx.sem("p4b_mtv")
        s_ymm = ctx.sem("p4b_ymm")
        s_yev = ctx.sem("p4b_yev")
        s_smm = ctx.sem("p4b_smm")
        s_rec = ctx.sem("p4b_rec")

        def ybl(k):
            c2, h2 = k // NH, k % NH
            sl = k % 2
            t.wait_ge(s_mtg, k + 1)
            t.wait_ge(s_mtv, k + 1)
            if k >= 2:
                t.wait_ge(s_yev, k - 1)
            pY = ctx.t["pY0"] if sl == 0 else ctx.t["pY1"]
            t.matmul(pY, xdtT[:, (2 * c2) * DIN + h2 * 64:(2 * c2) * DIN + h2 * 64 + 64],
                     MT[:, sl * 512:sl * 512 + 256], start=True, stop=False)
            t.matmul(pY, xdtT[:, (2 * c2 + 1) * DIN + h2 * 64:(2 * c2 + 1) * DIN + h2 * 64 + 64],
                     MT[:, sl * 512 + 256:sl * 512 + 512], start=False, stop=False)
            t.matmul(pY, S_run[:, h2 * 64:(h2 + 1) * 64],
                     ctx.t["csdo"][:, sl * 256:sl * 256 + 256], start=False, stop=False)
            t.matmul(pY[0:64, 0:128], ctx.t["xdtmp"][:, sl * 128:sl * 128 + 64],
                     identb[:, :], start=False, stop=False)
            t.matmul(pY[0:64, 128:256], ctx.t["xdtmp"][:, sl * 128 + 64:sl * 128 + 128],
                     identb[:, :], start=False, stop=True).then_inc(s_ymm, 1)
            if k >= 2:
                t.wait_ge(s_rec, k - 1)
            pS = ctx.t["pS0"] if sl == 0 else ctx.t["pS1"]
            for tb in (2 * c2, 2 * c2 + 1):
                mm = t.matmul(pS, bdec[:, sl * 256 + (tb % 2) * 128:sl * 256 + (tb % 2) * 128 + 128],
                              xdtT[:, tb * DIN + h2 * 64:tb * DIN + h2 * 64 + 64],
                              start=(tb % 2 == 0), stop=(tb % 2 == 1))
            mm.then_inc(s_smm, 1)

        for i in range(NCH * NH):
            c, h = i // NH, i % NH
            slot = i % 2
            p = i // 2
            if h == 0:
                if c > 0:
                    t.wait_ge(s_rowv, c * 32)
                for st in range(2):
                    pG = ctx.t["pG0"] if st == 0 else ctx.t["pG1"]
                    t.matmul(pG, ctx.t["Bfeat"][:, c * CH + st * 128:c * CH + st * 128 + 128],
                             ctx.t["Cfeat"][:, c * CH:(c + 1) * CH],
                             start=True, stop=True).then_inc(s_gtp, 1)
            if h % 2 == 0:
                t.wait_ge(s_gm, 2 * (c + 1))
                if p >= 2:
                    t.wait_ge(s_rowv, p - 1)
                pR = ctx.t["pRT0"] if p % 2 == 0 else ctx.t["pRT1"]
                t.matmul(pR, acsT[:, c * 128 + 2 * h:c * 128 + 2 * h + 4], ident32[:, :],
                         start=True, stop=True).then_inc(s_rtp, 1)
            t.wait_ge(s_hl, p + 1)
            if i >= 2:
                t.wait_ge(s_acsb, i - 1)
            pB_ = ctx.t["pAcsB0"] if slot == 0 else ctx.t["pAcsB1"]
            hiB = ctx.t["hiB0"] if p % 2 == 0 else ctx.t["hiB1"]
            loB = ctx.t["loB0"] if p % 2 == 0 else ctx.t["loB1"]
            hp2 = h % 2
            r0, r1 = 2 * hp2, 2 * hp2 + 1
            t.matmul(pB_[:, 0:128], sel4[:, r0 * 128:r0 * 128 + 128], hiB,
                     start=True, stop=False)
            t.matmul(pB_[:, 0:128], sel4[:, r0 * 128:r0 * 128 + 128], loB,
                     start=False, stop=True)
            t.matmul(pB_[:, 128:256], sel4[:, r1 * 128:r1 * 128 + 128], hiB,
                     start=True, stop=False)
            t.matmul(pB_[:, 128:256], sel4[:, r1 * 128:r1 * 128 + 128], loB,
                     start=False, stop=True).then_inc(s_acsbtp, 1)
            if i >= 2:
                ybl(i - 2)
        ybl(NCH * NH - 2)
        ybl(NCH * NH - 1)
        return None

    def gpsimd(self, ctx, g):
        MT = ctx.t["MT"]
        expLmin = ctx.t["expLmin"]
        csdo = ctx.t["csdo"]
        sdo = ctx.t["sdo_bf"]
        Cfeat = ctx.t["Cfeat"]
        s_mtv = ctx.sem("p4b_mtv")
        s_gm = ctx.sem("p4b_gm")
        s_mtg = ctx.sem("p4b_mtg")
        for j in range(NCH * NH):
            sl = j % 2
            c2 = j // NH
            g.wait_ge(s_mtv, j + 1)
            g.wait_ge(s_gm, 2 * (c2 + 1))
            Gmb = ctx.t["Gm"] if c2 % 2 == 0 else ctx.t["GmB"]
            g.tensor_tensor(MT[:, sl * 512:sl * 512 + 256],
                            expLmin[:, sl * 512:sl * 512 + 256],
                            Gmb[:, 0:256], op=ALU.mult)
            g.tensor_tensor(MT[:, sl * 512 + 256:sl * 512 + 512],
                            expLmin[:, sl * 512 + 256:sl * 512 + 512],
                            Gmb[:, 256:512], op=ALU.mult)
            g.tensor_tensor(csdo[:, sl * 256:sl * 256 + 256],
                            Cfeat[:, c2 * CH:(c2 + 1) * CH],
                            sdo[:, (j % 4) * 256:(j % 4) * 256 + 256],
                            op=ALU.mult).then_inc(s_mtg, 1)
        return None

    def scalar(self, ctx, s):
        acsT = ctx.t["acsT"]
        expL = ctx.t["expL"]
        sdo = ctx.t["sdo_bf"]
        S_run = ctx.t["S_run"]
        expAlB = ctx.t["expAlB"]
        y = ctx.t["y_feat"]
        s_acsbtp = ctx.sem("p4b_acsbtp")
        s_mtg = ctx.sem("p4b_mtg")
        s_expl = ctx.sem("p4b_expl")
        s_acsb = ctx.sem("p4b_acsb")
        s_stm = ctx.sem("p4b_stm")
        s_rec = ctx.sem("p4b_rec")
        s_ymm = ctx.sem("p4b_ymm")
        s_yev = ctx.sem("p4b_yev")

        def yev(k):
            c2, h2 = k // NH, k % NH
            sl = k % 2
            s.wait_ge(s_ymm, k + 1)
            pY = ctx.t["pY0"] if sl == 0 else ctx.t["pY1"]
            r0 = (h2 % 2) * 64
            s.copy(y[r0:r0 + 64, (h2 // 2) * T + c2 * CH:(h2 // 2) * T + (c2 + 1) * CH], pY)
            s.drain()
            s.sem_inc(s_yev, 1)

        for i in range(NCH * NH):
            c, h = i // NH, i % NH
            slot = i % 2
            pB_ = ctx.t["pAcsB0"] if slot == 0 else ctx.t["pAcsB1"]
            s.wait_ge(s_acsbtp, i + 1)
            if i >= 2:
                s.wait_ge(ctx.sem("p4b_mtv"), i - 1)
            if i >= 4:
                s.wait_ge(s_mtg, i - 3)
            s.activation(expL[:, slot * 512:slot * 512 + 256], pB_, AF.Exp,
                         scale=-1.0, bias=acsT[:, c * 128 + 2 * h:c * 128 + 2 * h + 1])
            s.activation(expL[:, slot * 512 + 256:slot * 512 + 512], pB_, AF.Exp,
                         scale=-1.0, bias=acsT[:, c * 128 + 2 * h + 1:c * 128 + 2 * h + 2])
            s.activation(sdo[:, (i % 4) * 256:(i % 4) * 256 + 256], pB_, AF.Exp,
                         scale=-1.0)
            if c < NCH - 1:
                if i >= 4:
                    s.wait_ge(s_rec, i - 3)
                sl4 = (i % 4) * 64
                s.activation(ctx.t["stmp4"][:, sl4:sl4 + 64],
                             S_run[:, h * 64:(h + 1) * 64], AF.Copy,
                             scale=expAlB[:, c * 64 + h:c * 64 + h + 1])
            s.drain()
            s.sem_inc(s_expl, 1)
            s.sem_inc(s_acsb, 1)
            s.sem_inc(s_stm, 1)
            if i >= 2:
                yev(i - 2)
        yev(NCH * NH - 2)
        yev(NCH * NH - 1)
        return None

    def vector(self, ctx, v):
        sel4 = ctx.t["sel4"]
        S_run = ctx.t["S_run"]
        BT = ctx.t["BT"]
        decayT = ctx.t["decayT"]
        bdec = ctx.t["bdec"]
        xdtT = ctx.t["xdtT"]
        s_gtp = ctx.sem("p4b_gtp")
        s_gm = ctx.sem("p4b_gm")
        s_rtp = ctx.sem("p4b_rtp")
        s_rowv = ctx.sem("p4b_rowv")
        s_hl = ctx.sem("p4b_hl")
        s_acsbtp = ctx.sem("p4b_acsbtp")
        s_mtg = ctx.sem("p4b_mtg")
        s_mtv = ctx.sem("p4b_mtv")
        s_smm = ctx.sem("p4b_smm")
        s_stm = ctx.sem("p4b_stm")
        s_rec = ctx.sem("p4b_rec")
        def rec(k):
            c2, h2 = k // NH, k % NH
            if c2 < NCH - 1:
                sl4 = (k % 4) * 64
                pS = ctx.t["pS0"] if k % 2 == 0 else ctx.t["pS1"]
                v.wait_ge(s_stm, k + 1)
                v.wait_ge(s_smm, k + 1)
                v.tensor_tensor(S_run[:, h2 * 64:(h2 + 1) * 64],
                                ctx.t["stmp4"][:, sl4:sl4 + 64], pS, op=ALU.add)
                v.drain()
            v.sem_inc(s_rec, 1)

        for c in range(NCH):
            Gmb = ctx.t["Gm"] if c % 2 == 0 else ctx.t["GmB"]
            if c >= 2:
                v.wait_ge(s_mtg, (c - 1) * NH)
            for st in range(2):
                pG = ctx.t["pG0"] if st == 0 else ctx.t["pG1"]
                v.wait_ge(s_gtp, 2 * c + st + 1)
                v.tensor_tensor(Gmb[:, st * CH:(st + 1) * CH], pG,
                                ctx.t["gmaskA" if st == 0 else "gmaskB"][:, :], op=ALU.mult)
            v.drain()
            v.sem_inc(s_gm, 2)
            for h in range(NH):
                i = c * NH + h
                slot = i % 2
                p = i // 2
                if h % 2 == 0:
                    v.wait_ge(s_rtp, p + 1)
                    if p >= 2:
                        v.wait_ge(s_acsbtp, 2 * p - 2)
                    pR = ctx.t["pRT0"] if p % 2 == 0 else ctx.t["pRT1"]
                    rowF = ctx.t["rowF0"] if p % 2 == 0 else ctx.t["rowF1"]
                    hiB = ctx.t["hiB0"] if p % 2 == 0 else ctx.t["hiB1"]
                    loB = ctx.t["loB0"] if p % 2 == 0 else ctx.t["loB1"]
                    v.tensor_copy(rowF, pR)
                    v.tensor_copy(hiB, pR)
                    v.drain()
                    v.sem_inc(s_rowv, 1)
                    v.tensor_tensor(loB, rowF, hiB, op=ALU.subtract)
                    v.drain()
                    v.sem_inc(s_hl, 1)
                if i >= 2:
                    v.wait_ge(s_smm, i - 1)
                v.wait_ge(ctx.sem("p4b_expl"), i + 1)
                v.tensor_scalar(ctx.t["expLmin"][:, slot * 512:slot * 512 + 512],
                                ctx.t["expL"][:, slot * 512:slot * 512 + 512],
                                1.0, None, op0=ALU.min)
                for tb in range(2):
                    v.tensor_scalar(ctx.t["xdtmp"][:, slot * 128 + tb * 64:slot * 128 + tb * 64 + 64],
                                    xdtT[:, (2 * c + tb) * DIN + h * 64:(2 * c + tb) * DIN + h * 64 + 64],
                                    ctx.t["Dinvdt"][:, (2 * c + tb) * 64 + h:(2 * c + tb) * 64 + h + 1],
                                    None, op0=ALU.mult)
                for tb in (2 * c, 2 * c + 1):
                    v.tensor_scalar(bdec[:, slot * 256 + (tb % 2) * 128:slot * 256 + (tb % 2) * 128 + 128],
                                    BT[:, tb * 128:(tb + 1) * 128],
                                    decayT[:, tb * 64 + h:tb * 64 + h + 1], None, op0=ALU.mult)
                v.drain()
                v.sem_inc(s_mtv, 1)
                if i >= 2:
                    rec(i - 2)
        rec(NCH * NH - 2)
        rec(NCH * NH - 1)
        return None


P5_CHUNKS = [(512 * i, 512) for i in range(8)]


class P5Gate:

    def groups(self):
        idx = 0
        for ci, (e0, ew) in enumerate(P5_CHUNKS):
            for j in range(4):
                for tc in range(2):
                    yield idx, ci, ew, j, ci * 4 + j, tc
                    idx += 1

    def gpsimd(self, ctx, g):
        w = ctx.t["W_in"]
        s_mm = ctx.sem("p5_mm")
        for ci, (e0, ew) in enumerate(P5_CHUNKS):
            if ci >= 2:
                g.wait_ge(s_mm, 8 * (ci - 1))
            slab = ctx.t["slabA"] if ci % 2 == 0 else ctx.t["slabB"]
            s_wl = ctx.sem("p5_wlA" if ci % 2 == 0 else "p5_wlB")
            src = w.ap()[:, e0:e0 + ew].rearrange("(k p) e -> p k e", p=128)
            dst = slab[:, 0:16 * ew].rearrange("p (k e) -> p k e", k=16)
            g.dma_start(out=dst, in_=src).then_inc(s_wl, 16)
        g.wait_ge(s_mm, 64)
        wo = ctx.t["W_out_d"]
        qi = 0
        for th in range(2):
            for oh in range(2):
                for et in range(32):
                    if qi >= 2:
                        g.wait_ge(ctx.sem("p5_wou"), qi - 1)
                    slab = ctx.t["slabC"] if qi % 2 == 0 else ctx.t["slabD"]
                    s_wl = ctx.sem("p5_woA" if qi % 2 == 0 else "p5_woB")
                    src = wo.ap()[et * 128:(et + 1) * 128, oh * 1024:(oh + 1) * 1024]
                    g.dma_start(out=slab[:, 0:1024], in_=src).then_inc(s_wl, 16)
                    qi += 1
        return None

    def tensor(self, ctx, t):
        xn = ctx.t["xn"]
        s_silu = ctx.sem("p5_silu")
        s_mm = ctx.sem("p5_mm")
        s_sq = ctx.sem("p5_sq")
        last_ci = -1
        for idx, ci, ew, j, jglob, tc in self.groups():
            if ci != last_ci:
                t.wait_ge(ctx.sem("p5_wlA" if ci % 2 == 0 else "p5_wlB"),
                          16 * (ci // 2 + 1))
                last_ci = ci
            slab = ctx.t["slabA"] if ci % 2 == 0 else ctx.t["slabB"]
            p = ctx.t["pA"] if idx % 2 == 0 else ctx.t["pB"]
            if idx >= 2:
                t.wait_ge(s_silu, idx - 1)
            for k in range(16):
                mm = t.matmul(p, slab[:, k * ew + j * 128:k * ew + j * 128 + 128],
                              xn[:, k * TF + HALO + tc * 512:k * TF + HALO + tc * 512 + 512],
                              start=(k == 0), stop=(k == 15))
            mm.then_inc(s_mm, 1)
            if idx >= 2:
                self._sq_mm(ctx, t, idx - 2)
        self._sq_mm(ctx, t, 62)
        self._sq_mm(ctx, t, 63)
        t.wait_ge(ctx.sem("p5_rrow"), 1)
        for tc in range(2):
            t.matmul(ctx.t["bank0" if tc == 0 else "bank1"].ap()[:, 0:512],
                     ctx.t["ones_bf"][0:1, 0:128],
                     ctx.t["rrow_bf"][0:1, tc * 512:tc * 512 + 512],
                     start=True, stop=True).then_inc(ctx.sem("p5_rbtp"), 1)
        t.wait_ge(ctx.sem("p5_fin"), 32)
        banks = [ctx.t[f"bank{i}"] for i in range(8)]
        yfin = ctx.t["y_feat"]
        qi = 0
        for th in range(2):
            for oh in range(2):
                q = th * 2 + oh
                if q >= 1:
                    t.wait_ge(ctx.sem("p5_wev"), q)
                for et in range(32):
                    t.wait_ge(ctx.sem("p5_woA" if qi % 2 == 0 else "p5_woB"),
                              16 * (qi // 2 + 1))
                    slab = ctx.t["slabC"] if qi % 2 == 0 else ctx.t["slabD"]
                    for tt in range(4):
                        for oc in range(2):
                            mm = t.matmul(banks[tt * 2 + oc].ap(),
                                          yfin[:, et * T + th * 512 + tt * 128:et * T + th * 512 + tt * 128 + 128],
                                          slab[:, oc * 512:oc * 512 + 512],
                                          start=(et == 0), stop=(et == 31))
                    mm.then_inc(ctx.sem("p5_wou"), 1)
                    qi += 1
        return None

    def _sq_mm(self, ctx, t, idx):
        jglob, tc = idx // 2, idx % 2
        t.wait_ge(ctx.sem("p5_sq"), idx + 1)
        psq = ctx.t["bank2"].ap() if tc == 0 else ctx.t["bank3"].ap()
        t.matmul(psq[0:1, 0:512], ctx.t["ones_bf"][:, 0:1],
                 ctx.t["sqbuf"][:, (idx % 2) * 512:(idx % 2) * 512 + 512],
                 start=(jglob == 0), stop=(jglob == 31)).then_inc(ctx.sem("p5_sqmm"), 1)

    def scalar(self, ctx, s):
        s_mm = ctx.sem("p5_mm")
        for idx, ci, ew, j, jglob, tc in self.groups():
            p = ctx.t["pA"] if idx % 2 == 0 else ctx.t["pB"]
            s.wait_ge(s_mm, idx + 1)
            if idx >= 2:
                s.wait_ge(ctx.sem("p5_zt"), idx - 1)
            s.activation(ctx.t["ztmp"][:, (idx % 2) * 512:(idx % 2) * 512 + 512],
                         p, AF.Silu)
            dinc(s, ctx.sem("p5_silu"))
        s.wait_ge(ctx.sem("p5_sqmm"), 64)
        s.activation(ctx.t["rr_ln0"][0:1, 0:512], ctx.t["bank2"].ap()[0:1, 0:512], AF.Ln,
                     scale=1.0 / DIN, bias=ctx.constf_r(EPS, 1))
        s.activation(ctx.t["rr_ln1"][0:1, 0:512], ctx.t["bank3"].ap()[0:1, 0:512], AF.Ln,
                     scale=1.0 / DIN, bias=ctx.constf_r(EPS, 1))
        s.drain()
        s.activation(ctx.t["rrow_bf"][0:1, 0:512], ctx.t["rr_ln0"][0:1, 0:512],
                     AF.Exp, scale=-0.5)
        s.activation(ctx.t["rrow_bf"][0:1, 512:1024], ctx.t["rr_ln1"][0:1, 0:512],
                     AF.Exp, scale=-0.5)
        dinc(s, ctx.sem("p5_rrow"))
        return None

    def vector(self, ctx, v):
        y = ctx.t["y_feat"]
        for idx, ci, ew, j, jglob, tc in self.groups():
            v.wait_ge(ctx.sem("p5_silu"), idx + 1)
            ysl = y[:, jglob * T + tc * 512:jglob * T + tc * 512 + 512]
            zt = ctx.t["ztmp"][:, (idx % 2) * 512:(idx % 2) * 512 + 512]
            bounce = ctx.t["ztmp2"][:, (idx % 2) * 512:(idx % 2) * 512 + 512]
            v.tensor_tensor(bounce, ysl, zt, op=ALU.mult)
            dinc(v, ctx.sem("p5_zt"))
            if idx >= 2:
                v.wait_ge(ctx.sem("p5_sqmm"), idx - 1)
            v.tensor_tensor(ctx.t["sqbuf"][:, (idx % 2) * 512:(idx % 2) * 512 + 512],
                            bounce, bounce, op=ALU.mult)
            dinc(v, ctx.sem("p5_sq"))
            v.tensor_copy(ysl, bounce)
            dinc(v, ctx.sem("p5_yg"))
            v.drain()
        for tc in range(2):
            v.wait_ge(ctx.sem("p5_rbtp"), tc + 1)
            pb = ctx.t["bank0" if tc == 0 else "bank1"].ap()
            v.tensor_copy(ctx.t["rstd_b"][:, tc * 512:tc * 512 + 512], pb[:, 0:512])
        v.drain()
        for jglob in range(32):
            gs = y[:, jglob * T:(jglob + 1) * T]
            v.tensor_tensor(ctx.t["ztmp2"][:, 0:1024], gs, ctx.t["rstd_b"][:, 0:1024],
                            op=ALU.mult)
            v.drain()
            v.tensor_scalar(gs, ctx.t["ztmp2"][:, 0:1024],
                            ctx.t["norm_scale"][:, jglob:jglob + 1], None, op0=ALU.mult)
            dinc(v, ctx.sem("p5_fin"))
        banks = [ctx.t[f"bank{i}"] for i in range(8)]
        for q in range(4):
            v.wait_ge(ctx.sem("p5_wou"), 32 * (q + 1))
            for tt in range(4):
                k = q * 4 + tt
                sl = k % 2
                if k >= 2:
                    v.wait_ge(ctx.sem("p5_oA" if sl == 0 else "p5_oB"), 16 * (k // 2))
                for oc in range(2):
                    v.tensor_copy(ctx.t["stage2"][:, sl * 1024 + oc * 512:sl * 1024 + oc * 512 + 512],
                                  banks[tt * 2 + oc].ap())
                dinc(v, ctx.sem("p5_wst"))
            dinc(v, ctx.sem("p5_wev"))
        return None

    def sync(self, ctx, sy):
        ms = ctx.t["mamba_s"]
        for q in range(4):
            th, oh = q // 2, q % 2
            for tt in range(4):
                k = q * 4 + tt
                sl = k % 2
                sy.wait_ge(ctx.sem("p5_wst"), k + 1)
                dst = ms.ap()[th * 512 + tt * 128:th * 512 + tt * 128 + 128,
                              oh * 1024:(oh + 1) * 1024]
                sy.dma_start(out=dst, in_=ctx.t["stage2"][:, sl * 1024:sl * 1024 + 1024]) \
                    .then_inc(ctx.sem("p5_oA" if sl == 0 else "p5_oB"), 16)
        sy.wait_ge(ctx.sem("p5_oA"), 16 * 8)
        sy.wait_ge(ctx.sem("p5_oB"), 16 * 8)
        return None


class P6FFN0:

    def __init__(self, xn_name="xn", xn_off=HALO, out_name="ffn0_s",
                 wg="WgT_d", wu="WuT_d", wd="WdT_d", pfx="p6",
                 ne=FF0, gate_scale=None, up_scale=None, slab0=65536,
                 xw=None):
        self.xn_name, self.xn_off, self.out_name = xn_name, xn_off, out_name
        self.wg, self.wu, self.wd, self.pfx = wg, wu, wd, pfx
        self.ne = ne
        self.n_ch = ne // 256
        self.n_et = ne // 128
        self.gate_scale = gate_scale
        self.up_scale = up_scale
        self.slab0 = slab0
        self.xw = xw if xw is not None else (TF if xn_name == "xn" else T)

    def _slabs(self, ctx):
        a = ctx.t["arena"]
        s0 = self.slab0
        return ([a.ap()[:, s0 + i * 4096:s0 + (i + 1) * 4096] for i in range(2)],
                [a.ap()[:, s0 + 8192 + i * 4096:s0 + 8192 + (i + 1) * 4096] for i in range(2)],
                [a.ap()[:, s0 + i * 512:s0 + (i + 1) * 512] for i in range(2)])

    def gpsimd(self, ctx, g):
        x = self.pfx
        slG, slU, slD = self._slabs(ctx)
        for ci in range(self.n_ch):
            if ci >= 2:
                g.wait_ge(ctx.sem(x + "_mm"), 4 * (ci - 1))
            sg = ctx.sem(x + ("_wgA" if ci % 2 == 0 else "_wgB"))
            su = ctx.sem(x + ("_wuA" if ci % 2 == 0 else "_wuB"))
            for wname, sl, sem in ((self.wg, slG[ci % 2], sg), (self.wu, slU[ci % 2], su)):
                src = ctx.t[wname].ap()[:, ci * 256:(ci + 1) * 256] \
                    .rearrange("(k p) e -> p k e", p=128)
                dst = sl.rearrange("p (k e) -> p k e", k=16)
                g.dma_start(out=dst, in_=src).then_inc(sem, 16)
        g.wait_ge(ctx.sem(x + "_mm"), 4 * self.n_ch)
        for dq in range(4):
            for et in range(self.n_et):
                qd = dq * self.n_et + et
                if qd >= 2:
                    g.wait_ge(ctx.sem(x + "_du"), qd - 1)
                sem = ctx.sem(x + ("_wdA" if qd % 2 == 0 else "_wdB"))
                src = ctx.t[self.wd].ap()[et * 128:(et + 1) * 128,
                                          dq * 512:(dq + 1) * 512]
                g.dma_start(out=slD[qd % 2], in_=src).then_inc(sem, 16)
        return None

    def tensor(self, ctx, t):
        x = self.pfx
        xn = ctx.t[self.xn_name]
        xw, off = self.xw, self.xn_off
        banks = [ctx.t[f"bank{i}"] for i in range(8)]
        h1 = ctx.t["h1f"]
        slG, slU, slD = self._slabs(ctx)
        for gi in range(4 * self.n_ch):
            et, tc = gi // 2, gi % 2
            ci, j = et // 2, et % 2
            if tc == 0 and j == 0:
                t.wait_ge(ctx.sem(x + ("_wgA" if ci % 2 == 0 else "_wgB")),
                          16 * (ci // 2 + 1))
                t.wait_ge(ctx.sem(x + ("_wuA" if ci % 2 == 0 else "_wuB")),
                          16 * (ci // 2 + 1))
            pg = banks[gi % 2].ap()
            pu = banks[2 + gi % 2].ap()
            if gi >= 2:
                t.wait_ge(ctx.sem(x + "_h1"), gi - 1)
            for k in range(16):
                t.matmul(pg, slG[ci % 2][:, k * 256 + j * 128:k * 256 + j * 128 + 128],
                         xn[:, k * xw + off + tc * 512:k * xw + off + tc * 512 + 512],
                         start=(k == 0), stop=(k == 15))
                mm = t.matmul(pu, slU[ci % 2][:, k * 256 + j * 128:k * 256 + j * 128 + 128],
                              xn[:, k * xw + off + tc * 512:k * xw + off + tc * 512 + 512],
                              start=(k == 0), stop=(k == 15))
            mm.then_inc(ctx.sem(x + "_mm"), 1)
        t.wait_ge(ctx.sem(x + "_h1"), 4 * self.n_ch)
        for dq in range(4):
            if dq >= 1:
                t.wait_ge(ctx.sem(x + "_wst"), 8 * dq)
            for et in range(self.n_et):
                qd = dq * self.n_et + et
                t.wait_ge(ctx.sem(x + ("_wdA" if qd % 2 == 0 else "_wdB")),
                          16 * (qd // 2 + 1))
                for tt in range(8):
                    mm = t.matmul(banks[tt].ap(),
                                  h1[:, et * 1024 + tt * 128:et * 1024 + tt * 128 + 128],
                                  slD[qd % 2],
                                  start=(et == 0), stop=(et == self.n_et - 1))
                mm.then_inc(ctx.sem(x + "_du"), 1)
        return None

    def scalar(self, ctx, s):
        x = self.pfx
        banks = [ctx.t[f"bank{i}"] for i in range(8)]
        for gi in range(4 * self.n_ch):
            et = gi // 2
            s.wait_ge(ctx.sem(x + "_mm"), gi + 1)
            if gi >= 2:
                s.wait_ge(ctx.sem(x + "_zt"), gi - 1)
            if self.gate_scale:
                s.activation(ctx.t["ztmp"][:, (gi % 2) * 512:(gi % 2) * 512 + 512],
                             banks[gi % 2].ap(), AF.Silu,
                             scale=ctx.t[self.gate_scale][:, et:et + 1])
            else:
                s.activation(ctx.t["ztmp"][:, (gi % 2) * 512:(gi % 2) * 512 + 512],
                             banks[gi % 2].ap(), AF.Silu)
            dinc(s, ctx.sem(x + "_silu"))
        return None

    def vector(self, ctx, v):
        x = self.pfx
        banks = [ctx.t[f"bank{i}"] for i in range(8)]
        h1 = ctx.t["h1f"]
        for gi in range(4 * self.n_ch):
            et, tc = gi // 2, gi % 2
            v.wait_ge(ctx.sem(x + "_silu"), gi + 1)
            dst = h1[:, et * 1024 + tc * 512:et * 1024 + tc * 512 + 512]
            if self.up_scale:
                v.scalar_tensor_tensor(dst, banks[2 + gi % 2].ap(),
                                       ctx.t[self.up_scale][:, et:et + 1],
                                       ctx.t["ztmp"][:, (gi % 2) * 512:(gi % 2) * 512 + 512],
                                       op0=ALU.mult, op1=ALU.mult)
            else:
                v.tensor_tensor(dst, banks[2 + gi % 2].ap(),
                                ctx.t["ztmp"][:, (gi % 2) * 512:(gi % 2) * 512 + 512],
                                op=ALU.mult)
            v.drain()
            v.sem_inc(ctx.sem(x + "_zt"), 1)
            v.sem_inc(ctx.sem(x + "_h1"), 1)
        stage = ctx.t["dstage"]
        for dq in range(4):
            v.wait_ge(ctx.sem(x + "_du"), self.n_et * (dq + 1))
            for tt in range(8):
                k = dq * 8 + tt
                sl = k % 2
                if k >= 2:
                    v.wait_ge(ctx.sem(x + ("_oA" if sl == 0 else "_oB")), 16 * (k // 2))
                v.tensor_copy(stage[:, sl * 512:sl * 512 + 512], banks[tt].ap())
                dinc(v, ctx.sem(x + "_wst"))
        return None

    def sync(self, ctx, sy):
        x = self.pfx
        stage = ctx.t["dstage"]
        out = ctx.t[self.out_name]
        for dq in range(4):
            for tt in range(8):
                k = dq * 8 + tt
                sl = k % 2
                sy.wait_ge(ctx.sem(x + "_wst"), k + 1)
                dst = out.ap()[tt * 128:(tt + 1) * 128, dq * 512:(dq + 1) * 512]
                sy.dma_start(out=dst, in_=stage[:, sl * 512:sl * 512 + 512]) \
                    .then_inc(ctx.sem(x + ("_oA" if sl == 0 else "_oB")), 16)
        sy.wait_ge(ctx.sem(x + "_oA"), 16 * 16)
        sy.wait_ge(ctx.sem(x + "_oB"), 16 * 16)
        return None


class P7Combine:
    def sync(self, ctx, sy):
        u = ctx.t["u_dram"]
        for tt in range(8):
            sl = tt % 2
            if tt >= 2:
                sy.wait_ge(ctx.sem("p7_used"), tt - 1)
            sy.dma_start(out=ctx.t["uR"][:, sl * D:(sl + 1) * D],
                         in_=u.ap()[HALO + tt * 128:HALO + tt * 128 + 128, :]) \
                .then_inc(ctx.sem("p7_uA" if sl == 0 else "p7_uB"), 16)
            sy.dma_start(out=ctx.t["mR"][:, sl * D:(sl + 1) * D],
                         in_=ctx.t["mamba_s"].ap()[tt * 128:(tt + 1) * 128, :]) \
                .then_inc(ctx.sem("p7_mA" if sl == 0 else "p7_mB"), 16)
            sy.dma_start(out=ctx.t["fR"][:, sl * D:(sl + 1) * D],
                         in_=ctx.t["ffn0_s"].ap()[tt * 128:(tt + 1) * 128, :]) \
                .then_inc(ctx.sem("p7_fA" if sl == 0 else "p7_fB"), 16)
            sy.wait_ge(ctx.sem("p7_x1"), tt + 1)
            sy.dma_start(out=ctx.t["x1_out"].ap()[tt * 128:(tt + 1) * 128, :],
                         in_=ctx.t["x1R"][:, sl * D:(sl + 1) * D]) \
                .then_inc(ctx.sem("p7_oA" if sl == 0 else "p7_oB"), 16)
        sy.wait_ge(ctx.sem("p7_oA"), 16 * 4)
        sy.wait_ge(ctx.sem("p7_oB"), 16 * 4)
        return None

    def vector(self, ctx, v):
        for tt in range(8):
            sl = tt % 2
            v.wait_ge(ctx.sem("p7_uA" if sl == 0 else "p7_uB"), 16 * (tt // 2 + 1))
            v.wait_ge(ctx.sem("p7_mA" if sl == 0 else "p7_mB"), 16 * (tt // 2 + 1))
            v.wait_ge(ctx.sem("p7_fA" if sl == 0 else "p7_fB"), 16 * (tt // 2 + 1))
            if tt >= 2:
                v.wait_ge(ctx.sem("p7_oA" if sl == 0 else "p7_oB"), 16 * (tt // 2))
            x1 = ctx.t["x1R"][:, sl * D:(sl + 1) * D]
            x2 = ctx.t["x2R"][:, sl * D:(sl + 1) * D]
            v.tensor_tensor(x2, ctx.t["uR"][:, sl * D:(sl + 1) * D],
                            ctx.t["mR"][:, sl * D:(sl + 1) * D], op=ALU.add)
            v.drain()
            v.tensor_tensor(x1, x2, ctx.t["fR"][:, sl * D:(sl + 1) * D], op=ALU.add)
            dinc(v, ctx.sem("p7_x1"))
            dinc(v, ctx.sem("p7_used"))
        return None


NQH = 16
NKH = 4
TT2 = 2048


class Q1LN:

    def sync(self, ctx, sy):
        x1 = ctx.t["x1_in"]
        s_u = ctx.sem("q1_ld")
        src = x1.ap().rearrange("(tt p) d -> p tt d", p=128)
        dst = ctx.t["u_tok8"].rearrange("p (tt d) -> p tt d", tt=8)
        sy.dma_start(out=dst, in_=src).then_inc(s_u, 16)
        return None

    def vector(self, ctx, v):
        u_tok = ctx.t["u_tok8"]
        xn_tok = ctx.t["xn1_tok"]
        aggr = ctx.t["ln_aggr"]
        rows = ctx.t["ln_rows"]
        sq = ctx.t["ln_sq"]
        v.wait_ge(ctx.sem("q1_ld"), 16)
        for tt in range(8):
            ut = u_tok[:, tt * D:(tt + 1) * D]
            a = aggr[:, 2 * tt:2 * tt + 2]
            r = rows[:, 4 * tt:4 * tt + 4]
            v.reduce_sum(r[:, 2:3], ut, axis=AX.X)
            v.wait_ge(ctx.sem("q1_sq"), tt + 1)
            v.reduce_sum(r[:, 3:4], sq[:, :], axis=AX.X)
            dinc(v, ctx.sem("q1_sqfree"))
            v.drain()
            v.tensor_scalar(a[:, 0:1], r[:, 2:3], 1.0 / D, None, op0=ALU.mult)
            v.tensor_scalar(r[:, 3:4], r[:, 3:4], 1.0 / D, None, op0=ALU.mult)
            v.drain()
            v.tensor_tensor(a[:, 1:2], a[:, 0:1], a[:, 0:1], op=ALU.mult)
            v.drain()
            v.tensor_tensor(a[:, 1:2], r[:, 3:4], a[:, 1:2], op=ALU.subtract)
            dinc(v, ctx.sem("q1_var"))
        tmp2 = ctx.t["ln_tmp2b"]
        for tt in range(8):
            ut = u_tok[:, tt * D:(tt + 1) * D]
            a = aggr[:, 2 * tt:2 * tt + 2]
            r = rows[:, 4 * tt:4 * tt + 4]
            v.wait_ge(ctx.sem("q1_rstd"), tt + 1)
            v.tensor_scalar(tmp2[:, :], ut, a[:, 0:1], r[:, 0:1],
                            op0=ALU.subtract, op1=ALU.mult)
            v.drain()
            v.tensor_copy(xn_tok[:, tt * D:(tt + 1) * D], tmp2[:, :])
            dinc(v, ctx.sem("q1_xn"))
        return None

    def scalar(self, ctx, s):
        u_tok = ctx.t["u_tok8"]
        sq = ctx.t["ln_sq"]
        aggr = ctx.t["ln_aggr"]
        rows = ctx.t["ln_rows"]
        s.wait_ge(ctx.sem("q1_ld"), 16)
        for tt in range(8):
            if tt > 0:
                s.wait_ge(ctx.sem("q1_sqfree"), tt)
            s.activation(sq[:, :], u_tok[:, tt * D:(tt + 1) * D], AF.Square)
            dinc(s, ctx.sem("q1_sq"))
        for tt in range(8):
            a = aggr[:, 2 * tt:2 * tt + 2]
            r = rows[:, 4 * tt:4 * tt + 4]
            s.wait_ge(ctx.sem("q1_var"), tt + 1)
            s.activation(r[:, 1:2], a[:, 1:2], AF.Ln, bias=ctx.constf(EPS))
            s.drain()
            s.activation(r[:, 0:1], r[:, 1:2], AF.Exp, scale=-0.5)
            dinc(s, ctx.sem("q1_rstd"))
        return None

    def gpsimd(self, ctx, g):
        g.wait_ge(ctx.sem("q1_xn"), 8)
        src = ctx.t["xn1_tok"].rearrange("p (tt d) -> p tt d", tt=8)
        dst = ctx.t["xn1b"].ap().rearrange("(tt p) d -> p tt d", p=128)
        g.dma_start(out=dst, in_=src).then_inc(ctx.sem("q1_bd"), 16)
        g.wait_ge(ctx.sem("q1_bd"), 16)
        g.collective_compute("AllGather", ALU.bypass, replica_groups=PAIRS,
                             ins=[ctx.t["xn1b"].ap().opt()],
                             outs=[ctx.t["xn1g"].ap().opt()]).then_inc(ctx.sem("q1_cc"), 1)
        g.wait_ge(ctx.sem("q1_cc"), 1)
        return None


class Q2QKV:

    def sync(self, ctx, sy):
        xg = ctx.t["xn1g"]
        for k in range(16):
            sy.dma_start(out=ctx.t["xnF"][:, k * TT2:(k + 1) * TT2],
                         in_=xg.ap()[:, k * 128:(k + 1) * 128],
                         transpose=True).then_inc(ctx.sem("q2_xf"), 16)
        return None

    def gpsimd(self, ctx, g):
        s_w = ctx.sem("q2_w")
        offs = [0, 8192, 16384, 20480]
        for i, (wname, w) in enumerate([("WqT_d", 512), ("WqT2", 512),
                                        ("WkT_d", 256), ("WvT_d", 256)]):
            src_t = ctx.t["WqT_d"] if wname in ("WqT_d", "WqT2") else ctx.t[wname]
            off = 512 if wname == "WqT2" else 0
            src = src_t.ap()[:, off:off + w].rearrange("(k p) e -> p k e", p=128)
            dst = ctx.t["qkvslab"][:, offs[i]:offs[i] + 16 * w] \
                .rearrange("p (k e) -> p k e", k=16)
            g.dma_start(out=dst, in_=src).then_inc(s_w, 16)
        return None

    def tensor(self, ctx, t):
        xnF = ctx.t["xnF"]
        t.wait_ge(ctx.sem("q2_xf"), 16 * 16)
        t.wait_ge(ctx.sem("q2_w"), 16 * 4)
        mmidx = 0
        offs = [0, 8192, 16384, 20480]
        for ci, w in [(0, 512), (1, 512), (2, 256), (3, 256)]:
            for tt in range(16):
                p = ctx.t["pA"] if mmidx % 2 == 0 else ctx.t["pB"]
                if mmidx >= 2:
                    t.wait_ge(ctx.sem("q2_ev"), mmidx - 1)
                for k in range(16):
                    mm = t.matmul(p[:, 0:w],
                                  xnF[:, k * TT2 + tt * 128:k * TT2 + tt * 128 + 128],
                                  ctx.t["qkvslab"][:, offs[ci] + k * w:offs[ci] + k * w + w],
                                  start=(k == 0), stop=(k == 15))
                mm.then_inc(ctx.sem("q2_mm"), 1)
                mmidx += 1
        t.wait_ge(ctx.sem("q2_rope"), 48)
        kb = 0
        for srcn, dstn, ntiles in (("Qtok", "Qf", 8), ("Ktok", "Kf", 2)):
            src = ctx.t[srcn]
            nw = ntiles * 128
            for tt in range(16):
                for j in range(ntiles):
                    pt = ctx.t["pt0"] if kb % 2 == 0 else ctx.t["pt1"]
                    if kb >= 2:
                        t.wait_ge(ctx.sem("q2_tev"), kb - 1)
                    t.transpose(pt[:, :], src[:, tt * nw + j * 128:tt * nw + j * 128 + 128],
                                ctx.t["ident_bf"][:, :]).then_inc(ctx.sem("q2_tp"), 1)
                    kb += 1
        return None

    def scalar(self, ctx, s):
        return None

    def vector(self, ctx, v):
        ce8, co8, se8, so8 = (ctx.t["ce8"], ctx.t["co8"], ctx.t["se8"], ctx.t["so8"])
        rt1, rt2 = ctx.t["ropetmp"][:, 0:256], ctx.t["ropetmp"][:, 256:512]
        for mmidx in range(64):
            p = ctx.t["pA"] if mmidx % 2 == 0 else ctx.t["pB"]
            v.wait_ge(ctx.sem("q2_mm"), mmidx + 1)
            if mmidx < 32:
                ci, tt = mmidx // 16, mmidx % 16
                w = 512
                dst = ctx.t["Qtok"][:, tt * 1024 + ci * 512:tt * 1024 + ci * 512 + 512]
            elif mmidx < 48:
                tt = mmidx % 16
                w = 256
                dst = ctx.t["Ktok"][:, tt * 256:tt * 256 + 256]
            else:
                tt = mmidx % 16
                w = 256
                dst = None
            if dst is not None:
                nh = w // 64
                tb = ce8[:, tt * 256:tt * 256 + 32 * nh]
                tbo = co8[:, tt * 256:tt * 256 + 32 * nh]
                tbs = se8[:, tt * 256:tt * 256 + 32 * nh]
                tbso = so8[:, tt * 256:tt * 256 + 32 * nh]
                pe = p[:, 0:w - 1:2]
                po = p[:, 1:w:2]
                v.tensor_tensor(rt1[:, 0:w // 2], pe, tb, op=ALU.mult)
                v.tensor_tensor(rt2[:, 0:w // 2], po, tbs, op=ALU.mult)
                v.drain()
                v.tensor_tensor(dst[:, 0:w - 1:2], rt1[:, 0:w // 2], rt2[:, 0:w // 2],
                                op=ALU.subtract)
                v.drain()
                v.tensor_tensor(rt1[:, 0:w // 2], po, tbo, op=ALU.mult)
                v.tensor_tensor(rt2[:, 0:w // 2], pe, tbso, op=ALU.mult)
                v.drain()
                v.tensor_tensor(dst[:, 1:w:2], rt1[:, 0:w // 2], rt2[:, 0:w // 2],
                                op=ALU.add)
                dinc(v, ctx.sem("q2_rope"))
            else:
                for h in range(4):
                    v.tensor_copy(ctx.t["Vtok"][:, tt * 260 + h * 65:tt * 260 + h * 65 + 64],
                                  p[:, h * 64:h * 64 + 64])
                dinc(v, ctx.sem("q2_rope"))
            dinc(v, ctx.sem("q2_ev"))
        for tt in range(16):
            for h in range(4):
                v.memset(ctx.t["Vtok"][:, tt * 260 + h * 65 + 64:tt * 260 + h * 65 + 65], 1.0)
        dinc(v, ctx.sem("q2_vones"))
        kb = 0
        for srcn, dstn, ntiles in (("Qtok", "Qf", 8), ("Ktok", "Kf", 2)):
            for tt in range(16):
                for j in range(ntiles):
                    pt = ctx.t["pt0"] if kb % 2 == 0 else ctx.t["pt1"]
                    v.wait_ge(ctx.sem("q2_tp"), kb + 1)
                    dst = ctx.t[dstn]
                    v.tensor_copy(dst[0:64, (2 * j) * TT2 + tt * 128:(2 * j) * TT2 + tt * 128 + 128],
                                  pt[0:64, :])
                    v.tensor_copy(dst[0:64, (2 * j + 1) * TT2 + tt * 128:(2 * j + 1) * TT2 + tt * 128 + 128],
                                  pt[64:128, :])
                    dinc(v, ctx.sem("q2_tev"))
                    kb += 1
        return None


class Q3Attn:

    def groups(self):
        out = []
        n = 0
        for qh in range(NQH):
            for qc in range(4):
                out.append((qh, qc, 4 * qc + 4, n))
                n += 4 * qc + 4
        return out

    def tensor(self, ctx, t):
        Qf, Kf, Vt = ctx.t["Qf"], ctx.t["Kf"], ctx.t["Vtok"]
        ones = ctx.t["ones_bf"]
        expS = ctx.t["expS"]
        for gi, (qh, qc, n_kt, n0) in enumerate(self.groups()):
            kvh = qh // 4
            qsl = Qf[0:64, qh * TT2 + qc * 512:qh * TT2 + qc * 512 + 512]
            for kt in range(n_kt):
                n = n0 + kt
                ps = ctx.t["pA"] if n % 2 == 0 else ctx.t["pB"]
                if n >= 2:
                    t.wait_ge(ctx.sem("q3_sev"), n - 1)
                ksl = Kf[0:64, kvh * TT2 + kt * 128:kvh * TT2 + kt * 128 + 128]
                t.matmul(ps, ksl, qsl, start=True, stop=True).then_inc(ctx.sem("q3_smm"), 1)
            pav = (ctx.t["bank4"] if gi % 2 == 0 else ctx.t["bank5"]).ap()[0:65, 0:512]
            if gi >= 2:
                t.wait_ge(ctx.sem("q3_aev"), gi - 1)
            ndiag = 0
            for kt in range(n_kt):
                t.wait_ge(ctx.sem("q3_sev"), n0 + kt + 1)
                if kt >= 4 * qc:
                    t.wait_ge(ctx.sem("q3_msk"), gi * 4 + (kt - 4 * qc) + 1)
                mm = t.matmul(pav, Vt[:, kt * 260 + (qh // 4) * 65:kt * 260 + (qh // 4) * 65 + 65],
                              expS[:, kt * 512:kt * 512 + 512],
                              start=(kt == 0), stop=(kt == n_kt - 1))
            mm.then_inc(ctx.sem("q3_avmm"), 1)
            t.wait_ge(ctx.sem("q3_den"), gi + 1)
            prb = (ctx.t["bank6"] if gi % 2 == 0 else ctx.t["bank7"]).ap()[0:64, 0:512]
            if gi >= 2:
                t.wait_ge(ctx.sem("q3_rbe"), gi - 1)
            t.matmul(prb, ones[0:1, 0:64], ctx.t["rrow1"][0:1, (gi % 2) * 512:(gi % 2) * 512 + 512],
                     start=True, stop=True).then_inc(ctx.sem("q3_rb"), 1)
        return None

    def scalar(self, ctx, s):
        expS = ctx.t["expS"]
        nbias = ctx.constf(-30.0)
        for gi, (qh, qc, n_kt, n0) in enumerate(self.groups()):
            for kt in range(0, 4 * qc, 2):
                for k2 in (kt, kt + 1):
                    n = n0 + k2
                    ps = ctx.t["pA"] if n % 2 == 0 else ctx.t["pB"]
                    if gi >= 1:
                        s.wait_ge(ctx.sem("q3_avmm"), gi)
                    s.wait_ge(ctx.sem("q3_smm"), n + 1)
                    s.activation(expS[:, k2 * 512:k2 * 512 + 512], ps, AF.Exp,
                                 scale=0.125, bias=nbias)
                s.drain()
                s.sem_inc(ctx.sem("q3_sev"), 1)
                s.sem_inc(ctx.sem("q3_sev"), 1)
            for kt in range(4 * qc, n_kt):
                n = n0 + kt
                ps = ctx.t["pA"] if n % 2 == 0 else ctx.t["pB"]
                if gi >= 1:
                    s.wait_ge(ctx.sem("q3_avmm"), gi)
                s.wait_ge(ctx.sem("q3_smm"), n + 1)
                m = kt - 4 * qc
                if gi * 4 + m >= 2:
                    s.wait_ge(ctx.sem("q3_dtf"), gi * 4 + m - 1)
                s.activation(ctx.t["dtmp"][:, (m % 2) * 512:(m % 2) * 512 + 512],
                             ps, AF.Exp, scale=0.125, bias=nbias)
                dinc(s, ctx.sem("q3_dex"))
            if gi >= 2:
                s.wait_ge(ctx.sem("q3_rb"), gi - 1)
            s.wait_ge(ctx.sem("q3_avmm"), gi + 1)
            pav = (ctx.t["bank4"] if gi % 2 == 0 else ctx.t["bank5"]).ap()
            s.activation(ctx.t["drow"][0:1, 0:512], pav[64:65, 0:512], AF.Ln)
            s.drain()
            s.activation(ctx.t["rrow1"][0:1, (gi % 2) * 512:(gi % 2) * 512 + 512],
                         ctx.t["drow"][0:1, 0:512], AF.Exp, scale=-1.0)
            dinc(s, ctx.sem("q3_den"))
        return None

    def vector(self, ctx, v):
        expS = ctx.t["expS"]
        attnO = ctx.t["attnO"]
        for gi, (qh, qc, n_kt, n0) in enumerate(self.groups()):
            for m in range(4):
                kt = 4 * qc + m
                v.wait_ge(ctx.sem("q3_dex"), gi * 4 + m + 1)
                v.tensor_tensor(expS[:, kt * 512:kt * 512 + 512],
                                ctx.t["dtmp"][:, (m % 2) * 512:(m % 2) * 512 + 512],
                                ctx.t["dmask"][:, m * 512:m * 512 + 512], op=ALU.mult)
                dinc(v, ctx.sem("q3_dtf"))
                dinc(v, ctx.sem("q3_sev"))
                dinc(v, ctx.sem("q3_msk"))
            v.wait_ge(ctx.sem("q3_rb"), gi + 1)
            prb = (ctx.t["bank6"] if gi % 2 == 0 else ctx.t["bank7"]).ap()[0:64, 0:512]
            v.tensor_copy(ctx.t["rBs"][0:64, 0:512], prb)
            dinc(v, ctx.sem("q3_rbe"))
            pav = (ctx.t["bank4"] if gi % 2 == 0 else ctx.t["bank5"]).ap()[0:64, 0:512]
            v.tensor_tensor(attnO[(qh % 2) * 64:(qh % 2) * 64 + 64,
                                  (qh // 2) * TT2 + qc * 512:(qh // 2) * TT2 + qc * 512 + 512],
                            pav, ctx.t["rBs"][0:64, 0:512], op=ALU.mult)
            dinc(v, ctx.sem("q3_aev"))
        return None


class Q4Exchange:
    def gpsimd(self, ctx, g):
        src = ctx.t["attnO"].rearrange("p (j t) -> p j t", j=8)
        dst = ctx.t["aOb"].ap().rearrange("(j p) t -> p j t", p=128)
        g.dma_start(out=dst, in_=src).then_inc(ctx.sem("q4_d"), 16)
        g.wait_ge(ctx.sem("q4_d"), 16)
        g.collective_compute("AllGather", ALU.bypass, replica_groups=PAIRS,
                             ins=[ctx.t["aOb"].ap().opt()],
                             outs=[ctx.t["aOg"].ap().opt()]).then_inc(ctx.sem("q4_cc"), 1)
        g.wait_ge(ctx.sem("q4_cc"), 1)
        for arm in g.switch_core_id(n=128):
            t0 = 0 if int(arm) % 2 == 0 else 1024
            src2 = ctx.t["aOg"].ap()[:, t0:t0 + T].rearrange("(j p) t -> p j t", p=128)
            dst2 = ctx.t["attnG"].rearrange("p (j t) -> p j t", j=16)
            g.dma_start(out=dst2, in_=src2).then_inc(ctx.sem("q4_d2"), 16)
        g.wait_ge(ctx.sem("q4_d2"), 16)
        return None


class Q5Wo:

    def sync(self, ctx, sy):
        for arm in sy.switch_core_id(n=128):
            t0 = 0 if int(arm) % 2 == 0 else 1024
            for k in range(16):
                sy.dma_start(out=ctx.t["xn1own"][:, k * T:(k + 1) * T],
                             in_=ctx.t["xn1g"].ap()[t0:t0 + T, k * 128:(k + 1) * 128],
                             transpose=True).then_inc(ctx.sem("q5_xo"), 16)
        sy.wait_ge(ctx.sem("q5_xo"), 16 * 16)
        return None

    def gpsimd(self, ctx, g):
        wo = ctx.t["WoT_d"]
        qi = 0
        for th in range(2):
            for oh in range(2):
                for et in range(16):
                    if qi >= 2:
                        g.wait_ge(ctx.sem("q5_wu"), qi - 1)
                    sl = ctx.t["woslabA"] if qi % 2 == 0 else ctx.t["woslabB"]
                    sem = ctx.sem("q5_wA" if qi % 2 == 0 else "q5_wB")
                    src = wo.ap()[et * 128:(et + 1) * 128, oh * 1024:(oh + 1) * 1024]
                    g.dma_start(out=sl[:, 0:1024], in_=src).then_inc(sem, 16)
                    qi += 1
        return None

    def tensor(self, ctx, t):
        banks = [ctx.t[f"bank{i}"] for i in range(8)]
        aG = ctx.t["attnG"]
        qi = 0
        for th in range(2):
            for oh in range(2):
                q = th * 2 + oh
                if q >= 1:
                    t.wait_ge(ctx.sem("q5_wev"), q)
                for et in range(16):
                    t.wait_ge(ctx.sem("q5_wA" if qi % 2 == 0 else "q5_wB"),
                              16 * (qi // 2 + 1))
                    sl = ctx.t["woslabA"] if qi % 2 == 0 else ctx.t["woslabB"]
                    for tt in range(4):
                        for oc in range(2):
                            mm = t.matmul(banks[tt * 2 + oc].ap(),
                                          aG[:, et * T + th * 512 + tt * 128:et * T + th * 512 + tt * 128 + 128],
                                          sl[:, oc * 512:oc * 512 + 512],
                                          start=(et == 0), stop=(et == 15))
                    mm.then_inc(ctx.sem("q5_wu"), 1)
                    qi += 1
        return None

    def vector(self, ctx, v):
        banks = [ctx.t[f"bank{i}"] for i in range(8)]
        for q in range(4):
            v.wait_ge(ctx.sem("q5_wu"), 16 * (q + 1))
            for tt in range(4):
                k = q * 4 + tt
                sl = k % 2
                if k >= 2:
                    v.wait_ge(ctx.sem("q5_oA" if sl == 0 else "q5_oB"), 16 * (k // 2))
                for oc in range(2):
                    v.tensor_copy(ctx.t["stage2"][:, sl * 1024 + oc * 512:sl * 1024 + oc * 512 + 512],
                                  banks[tt * 2 + oc].ap())
                dinc(v, ctx.sem("q5_wst"))
            dinc(v, ctx.sem("q5_wev"))
        return None

    def scalar(self, ctx, s):
        out = ctx.t["attn_s"]
        for q in range(4):
            th, oh = q // 2, q % 2
            for tt in range(4):
                k = q * 4 + tt
                sl = k % 2
                s.wait_ge(ctx.sem("q5_wst"), k + 1)
                dst = out.ap()[th * 512 + tt * 128:th * 512 + tt * 128 + 128,
                               oh * 1024:(oh + 1) * 1024]
                s.dma_start(out=dst, in_=ctx.t["stage2"][:, sl * 1024:sl * 1024 + 1024]) \
                    .then_inc(ctx.sem("q5_oA" if sl == 0 else "q5_oB"), 16)
        s.wait_ge(ctx.sem("q5_oA"), 16 * 8)
        s.wait_ge(ctx.sem("q5_oB"), 16 * 8)
        return None


class Q7Final:

    def tensor(self, ctx, t):
        for c in range(4):
            pb = (ctx.t["bank0"] if c % 2 == 0 else ctx.t["bank1"]).ap()
            if c >= 2:
                t.wait_ge(ctx.sem("q7_dse"), c - 1)
            t.matmul(pb, ctx.t["ones_bf"][0:1, 0:128],
                     ctx.t["dsrow"][0:1, c * 512:(c + 1) * 512],
                     start=True, stop=True).then_inc(ctx.sem("q7_dsb"), 1)
        return None

    def sync(self, ctx, sy):
        x1 = ctx.t["x1_in"]
        for tt in range(8):
            sl = tt % 2
            if tt >= 2:
                sy.wait_ge(ctx.sem("q7_used"), tt - 1)
            sy.dma_start(out=ctx.t["uR"][:, sl * D:(sl + 1) * D],
                         in_=x1.ap()[tt * 128:(tt + 1) * 128, :]) \
                .then_inc(ctx.sem("q7_uA" if sl == 0 else "q7_uB"), 16)
            sy.dma_start(out=ctx.t["mR"][:, sl * D:(sl + 1) * D],
                         in_=ctx.t["attn_s"].ap()[tt * 128:(tt + 1) * 128, :]) \
                .then_inc(ctx.sem("q7_mA" if sl == 0 else "q7_mB"), 16)
            sy.dma_start(out=ctx.t["fR"][:, sl * D:(sl + 1) * D],
                         in_=ctx.t["ffn1_s"].ap()[tt * 128:(tt + 1) * 128, :]) \
                .then_inc(ctx.sem("q7_fA" if sl == 0 else "q7_fB"), 16)
            sy.wait_ge(ctx.sem("q7_x1"), tt + 1)
            sy.dma_start(out=ctx.t["out_d"].ap()[tt * 128:(tt + 1) * 128, :],
                         in_=ctx.t["x1R"][:, sl * D:(sl + 1) * D]) \
                .then_inc(ctx.sem("q7_oA" if sl == 0 else "q7_oB"), 16)
        sy.wait_ge(ctx.sem("q7_oA"), 16 * 4)
        sy.wait_ge(ctx.sem("q7_oB"), 16 * 4)
        return None

    def vector(self, ctx, v):
        for c in range(4):
            v.wait_ge(ctx.sem("q7_dsb"), c + 1)
            pb = (ctx.t["bank0"] if c % 2 == 0 else ctx.t["bank1"]).ap()
            v.tensor_copy(ctx.t["dsB"][:, c * 512:(c + 1) * 512], pb[:, 0:512])
            dinc(v, ctx.sem("q7_dse"))
        for tt in range(8):
            sl = tt % 2
            v.wait_ge(ctx.sem("q7_uA" if sl == 0 else "q7_uB"), 16 * (tt // 2 + 1))
            v.wait_ge(ctx.sem("q7_mA" if sl == 0 else "q7_mB"), 16 * (tt // 2 + 1))
            v.wait_ge(ctx.sem("q7_fA" if sl == 0 else "q7_fB"), 16 * (tt // 2 + 1))
            if tt >= 2:
                v.wait_ge(ctx.sem("q7_oA" if sl == 0 else "q7_oB"), 16 * (tt // 2))
            x1 = ctx.t["x1R"][:, sl * D:(sl + 1) * D]
            x2 = ctx.t["x2R"][:, sl * D:(sl + 1) * D]
            v.tensor_tensor(x1, ctx.t["fR"][:, sl * D:(sl + 1) * D],
                            ctx.t["dsB"][:, 0:2048], op=ALU.mult)
            v.drain()
            v.tensor_tensor(x2, x1, ctx.t["mR"][:, sl * D:(sl + 1) * D], op=ALU.add)
            v.drain()
            v.tensor_tensor(x1, x2, ctx.t["uR"][:, sl * D:(sl + 1) * D], op=ALU.add)
            dinc(v, ctx.sem("q7_x1"))
            dinc(v, ctx.sem("q7_used"))
        return None


class Q0Consts:

    def gpsimd(self, ctx, g):
        ident = ctx.t["ident_bf"]
        g.memset(ident[:, :], 0.0)
        g.drain()
        g.affine_select(out=ident[:, :], in_=ident[:, :],
                        compare_op=ALU.not_equal, fill=1.0,
                        base=0, pattern=[[-1, 128]], channel_multiplier=1)
        for m in range(4):
            dm = ctx.t["dmask"][:, m * 512:(m + 1) * 512]
            g.memset(dm, 1.0)
            g.drain()
            g.affine_select(out=dm, in_=dm, compare_op=ALU.is_ge, fill=0.0,
                            base=-128 * m, pattern=[[1, 512]], channel_multiplier=-1)
            g.drain()
        return None

    def vector(self, ctx, v):
        last = v.memset(ctx.t["ones_bf"][:, :], 1.0)
        for val, idx in ctx.consts.items():
            last = v.memset(ctx.cpool[:, idx:idx + 1], val)
        return last

    def sync(self, ctx, sy):
        sem = ctx.sem("q0_ld")
        n = 0
        for sb_name, dram_name in [("ce8", "ce8_d"), ("co8", "co8_d"),
                                   ("se8", "se8_d"), ("so8", "so8_d"),
                                   ("gs1", "gs1_d"), ("us1", "us1_d"),
                                   ("dsrow", "ds_d")]:
            sy.dma_start(out=as_ap(ctx.t[sb_name])[:, :],
                         in_=ctx.t[dram_name].ap()).then_inc(sem, 16)
            n += 1
        sy.wait_ge(sem, 16 * n)
        return None


def alloc_l1(ctx, nc):
    arena = ctx.sb("arenaB", [128, 82048], BF16)
    ar32 = arena.bitcast(F32)
    ctx.t["u_tok8"] = ar32.ap()[:, 0:16384]
    ctx.t["xn1_tok"] = arena.ap()[:, 32768:32768 + 16384]
    ctx.t["ln_sq"] = ar32.ap()[:, 24576:24576 + 2048]
    ctx.t["ln_tmp2b"] = ar32.ap()[:, 26624:26624 + 2048]
    ctx.t["ln_aggr"] = ar32.ap()[:, 28672:28672 + 16]
    ctx.t["ln_rows"] = ar32.ap()[:, 28688:28688 + 32]
    ctx.t["xnF"] = arena.ap()[:, 0:32768]
    ctx.t["qkvslab"] = arena.ap()[:, 57408:57408 + 24576]
    ctx.t["Qtok"] = arena.ap()[:, 32768:32768 + 16384]
    ctx.t["Ktok"] = arena.ap()[:, 49152:49152 + 4096]
    ctx.t["Vtok"] = arena.ap()[:, 53248:53248 + 4160]
    pass
    ctx.t["Qf"] = arena.ap()[:, 0:32768]
    ctx.t["expS"] = arena.ap()[:, 32768:32768 + 8192]
    ctx.t["Kf"] = arena.ap()[:, 57408:57408 + 8192]
    ctx.t["attnO"] = arena.ap()[:, 65600:65600 + 16384]
    ctx.t["dtmp"] = arena.ap()[:, 40960:40960 + 1024]
    ctx.t["rBs"] = arena.ap()[:, 41984:41984 + 512]
    ctx.t["rrow1"] = arena.ap()[:, 42496:42496 + 1024]
    ctx.t["drow"] = ar32.ap()[:, 21760:21760 + 512]
    ctx.t["attnG"] = arena.ap()[:, 0:16384]
    ctx.t["woslabA"] = arena.ap()[:, 16384:16384 + 1024]
    ctx.t["woslabB"] = arena.ap()[:, 17408:17408 + 1024]
    ctx.t["stage2"] = ar32.ap()[:, 11264:11264 + 2048]
    ctx.t["xn1own"] = arena.ap()[:, 45056:45056 + 16384]
    ctx.t["h1f"] = arena.ap()[:, 0:45056]
    ctx.t["ztmp"] = arena.ap()[:, 77824:77824 + 1024]
    ctx.t["uR"] = ar32.ap()[:, 0:4096]
    ctx.t["mR"] = ar32.ap()[:, 4096:8192]
    ctx.t["fR"] = ar32.ap()[:, 8192:12288]
    ctx.t["x1R"] = ar32.ap()[:, 12288:16384]
    ctx.t["x2R"] = ar32.ap()[:, 16384:20480]
    ctx.t["dsB"] = ar32.ap()[:, 20480:22528]
    ctx.sb("ident_bf", [128, 128], BF16)
    ctx.sb("ones_bf", [128, 128], BF16)
    ctx.sb("ropetmp", [128, 512], BF16)
    ctx.sb("dmask", [128, 4 * 512], BF16)
    ctx.sb("ce8", [128, 16 * 256], BF16)
    ctx.sb("co8", [128, 16 * 256], BF16)
    ctx.sb("se8", [128, 16 * 256], BF16)
    ctx.sb("so8", [128, 16 * 256], BF16)
    ctx.sb("gs1", [128, 44], F32)
    ctx.sb("us1", [128, 44], F32)
    ctx.sb("dsrow", [128, 2048], BF16)
    banks = [ctx.ps(f"bank{i}", [128, 512], F32) for i in range(8)]
    ctx.t["pA"] = banks[0].ap()
    ctx.t["pB"] = banks[1].ap()
    ctx.t["pt0"] = banks[6].bitcast(BF16).ap()[:, 0:128]
    ctx.t["pt1"] = banks[7].bitcast(BF16).ap()[:, 0:128]
    ctx.dram("x1_in", [T, D], F32, kind="ExternalInput")
    ctx.dram("WqT_d", [D, NQH * 64], F32, kind="ExternalInput")
    ctx.dram("WkT_d", [D, NKH * 64], F32, kind="ExternalInput")
    ctx.dram("WvT_d", [D, NKH * 64], F32, kind="ExternalInput")
    ctx.dram("WoT_d", [D, D], F32, kind="ExternalInput")
    ctx.dram("Wg1T_d", [D, FF1], F32, kind="ExternalInput")
    ctx.dram("Wu1T_d", [D, FF1], F32, kind="ExternalInput")
    ctx.dram("Wd1T_d", [FF1, D], F32, kind="ExternalInput")
    ctx.dram("ce8_d", [128, 16 * 256], BF16, kind="ExternalInput")
    ctx.dram("co8_d", [128, 16 * 256], BF16, kind="ExternalInput")
    ctx.dram("se8_d", [128, 16 * 256], BF16, kind="ExternalInput")
    ctx.dram("so8_d", [128, 16 * 256], BF16, kind="ExternalInput")
    ctx.dram("gs1_d", [128, 44], F32, kind="ExternalInput")
    ctx.dram("us1_d", [128, 44], F32, kind="ExternalInput")
    ctx.dram("ds_d", [128, 2048], BF16, kind="ExternalInput")
    ctx.dram("xn1b", [T, D], BF16)
    ctx.dram("xn1g", [TT2, D], BF16)
    ctx.dram("aOb", [T, TT2], BF16)
    ctx.dram("aOg", [TT2, TT2], BF16)
    ctx.dram("attn_s", [T, D], F32)
    ctx.dram("ffn1_s", [T, D], F32)
    ctx.dram("out_d", [T, D], F32, kind="ExternalOutput")


def build_l1():
    nc = bass.Bass("TRN2", target_bir_lowering=False)
    stack = contextlib.ExitStack()
    with stack:
        ctx = Ctx(nc, stack)
        alloc_l1(ctx, nc)
        ctx.constf(EPS)
        ctx.constf(-30.0)
        ffn1 = P6FFN0(xn_name="xn1own", xn_off=0, out_name="ffn1_s",
                      wg="Wg1T_d", wu="Wu1T_d", wd="Wd1T_d", pfx="q6",
                      ne=FF1, gate_scale="gs1", up_scale="us1", slab0=61440)
        emit_phases(nc, ctx, [Q0Consts(), Q1LN(), Q2QKV(), Q3Attn(),
                              Q4Exchange(), Q5Wo(), ffn1, Q7Final()])
    return nc


def alloc_l0(ctx, nc):
    arena = ctx.sb("arena", [128, 82048], BF16)
    ar32 = arena.bitcast(F32)
    ctx.t["u_tok"] = ar32.ap()[:, 0:18432]
    ctx.t["u_bf"] = arena.ap()[:, 36896:36896 + 18432]
    ctx.t["ln_tmp"] = ar32.ap()[:, 27664:27664 + 2048]
    ctx.t["ln_tmp2"] = ar32.ap()[:, 29712:29712 + 2048]
    ctx.t["xBC"] = arena.ap()[:, 0:35 * TF]
    ctx.t["slabA"] = arena.ap()[:, 36896:36896 + 8192]
    ctx.t["slabB"] = arena.ap()[:, 45088:45088 + 8192]
    ctx.t["xdtT"] = arena.ap()[:, 36896:36896 + 32768]
    ctx.t["S_run"] = arena.ap()[:, 32800:32800 + 4096]
    ctx.t["y_feat"] = arena.ap()[:, 0:32768]
    ctx.t["h1f"] = arena.ap()[:, 0:65536]
    ctx.t["conv_tmp"] = arena.ap()[:, 32896:32896 + 2048]
    ctx.t["slabC"] = arena.ap()[:, 36896:36896 + 1024]
    ctx.t["slabD"] = arena.ap()[:, 37920:37920 + 1024]
    ctx.t["stage2"] = ar32.ap()[:, 16400:16400 + 2048]
    ctx.t["slabC2"] = arena.ap()[:, 53280:53280 + 8192]
    ctx.t["slabD2"] = arena.ap()[:, 61472:61472 + 8192]
    ctx.t["uR"] = ar32.ap()[:, 0:4096]
    ctx.t["mR"] = ar32.ap()[:, 4096:8192]
    ctx.t["fR"] = ar32.ap()[:, 8192:12288]
    ctx.t["x1R"] = ar32.ap()[:, 12288:16384]
    ctx.t["x2R"] = ar32.ap()[:, 16384:20480]
    ctx.sb("norm_scale", [128, 32], F32)
    ctx.t["tmp_bf"] = arena.ap()[:, 34952:34952 + 1024]
    ctx.sb("ident_bf", [128, 128], BF16)
    ctx.sb("ident_f32", [128, 128], F32)
    ctx.sb("ones_bf", [128, 128], BF16)
    ctx.sb("gmaskA", [128, CH], BF16)
    ctx.sb("gmaskB", [128, CH], BF16)
    xn_h = ctx.sb("xn", [128, 16 * TF], BF16)
    ctx.t["dstage"] = xn_h.bitcast(F32).ap()[:, 0:1024]
    ctx.t["ln_aggr"] = ar32.ap()[:, 31760:31760 + 18]
    ctx.t["ln_rows"] = ar32.ap()[:, 31800:31800 + 36]
    dtbuf = ctx.sb("dtbuf", [128, T], F32)
    ctx.t["decay_bf"] = dtbuf.bitcast(BF16).ap()[:, 0:1024]
    ctx.t["sdo_bf"] = dtbuf.bitcast(BF16).ap()[:, 0:1024]
    ctx.t["csdo"] = dtbuf.bitcast(BF16).ap()[:, 1024:1536]
    ctx.t["xdtmp"] = dtbuf.bitcast(BF16).ap()[:, 1536:1792]
    ctx.t["stmp4"] = dtbuf.bitcast(BF16).ap()[:, 1792:2048]
    ctx.t["stmpA"] = dtbuf.bitcast(BF16).ap()[:, 1792:1856]
    ctx.t["stmpB"] = dtbuf.bitcast(BF16).ap()[:, 1856:1920]
    acsA_h = ctx.sb("acsA", [128, T], F32)
    ctx.t["expL"] = acsA_h.bitcast(BF16).ap()[:, 0:1024]
    ctx.t["MT"] = acsA_h.bitcast(BF16).ap()[:, 1024:2048]
    ctx.t["roleb"] = acsA_h.bitcast(BF16).ap()[:, 0:2048]
    acsB = ctx.sb("acsB", [128, T], F32)
    ctx.t["invdt"] = acsB.ap()[:, 0:512]
    ctx.t["convfG"] = acsB.ap()[:, 0:1024]
    ctx.t["expLmin"] = acsB.bitcast(BF16).ap()[:, 0:2048]
    ctx.t["rrow_bf"] = acsB.bitcast(BF16).ap()[:, 1024:2048]
    ctx.t["convfV"] = ar32.ap()[:, 39328:39328 + 1024]
    ctx.t["sel4"] = arena.ap()[0:4, 78656:79168]
    ctx.t["rowF0"] = ar32.ap()[0:4, 39584:39712]
    ctx.t["rowF1"] = ar32.ap()[0:4, 39712:39840]
    ctx.t["hiB0"] = arena.ap()[0:4, 79680:79808]
    ctx.t["hiB1"] = arena.ap()[0:4, 79808:79936]
    ctx.t["loB0"] = arena.ap()[0:4, 79936:80064]
    ctx.t["loB1"] = arena.ap()[0:4, 80064:80192]
    ctx.t["dtT"] = ar32.ap()[:, 40352:40352 + 512]
    ctx.t["rr_ln0"] = ctx.t["dtT"]
    ctx.t["GmB"] = arena.ap()[:, 80704:81728]
    ctx.t["Dinvdt"] = ar32.ap()[:, 38176:38176 + 512]
    ctx.t["rr_ln1"] = ctx.t["Dinvdt"]
    ctx.t["acsT"] = ar32.ap()[:, 37152:37152 + 512]
    ctx.t["decayT"] = ar32.ap()[:, 37664:37664 + 512]
    ctx.t["rstd_b"] = arena.ap()[:, 75328:75328 + 1024]
    ctx.t["alast_rows"] = ar32.ap()[0:1, 38944:38944 + 256]
    ctx.t["alast_erow"] = arena.ap()[0:1, 78400:78400 + 256]
    ctx.sb("expA", [128, 1], F32)
    ctx.t["Bfeat"] = arena.ap()[:, 70720:70720 + 1024]
    ctx.t["Cfeat"] = arena.ap()[:, 71744:71744 + 1024]
    ctx.t["BT"] = arena.ap()[:, 72768:72768 + 1024]
    ctx.t["bdec"] = arena.ap()[:, 73792:73792 + 512]
    ctx.t["expAlB"] = ar32.ap()[:, 38688:38688 + 256]
    ctx.sb("roleflag", [128, 1], F32)
    ctx.t["Gm"] = arena.ap()[:, 69696:69696 + 1024]
    ctx.t["sqbuf"] = ctx.t["Gm"]
    ctx.t["ztmp"] = acsA_h.bitcast(BF16).ap()[:, 0:1024]
    ctx.t["ztmp2"] = acsA_h.bitcast(BF16).ap()[:, 1024:2048]
    ctx.sb("convw", [128, 4 * 34], F32)
    ctx.sb("convb", [128, 34], F32)
    ctx.sb("dt_bias", [128, 1], F32)
    ctx.sb("A_log", [128, 1], F32)
    ctx.sb("D_tile", [128, 64], F32)
    banks = [ctx.ps(f"bank{i}", [128, 512], F32) for i in range(8)]
    ctx.t["pA"] = banks[0].ap()
    ctx.t["pB"] = banks[1].ap()
    ctx.t["pt0"] = banks[6].bitcast(BF16).ap()[:, 0:128]
    ctx.t["pt1"] = banks[7].bitcast(BF16).ap()[:, 0:128]
    ctx.t["pf0"] = banks[4].ap()[:, 0:64]
    ctx.t["pf1"] = banks[5].ap()[:, 0:64]
    ctx.t["pAl"] = banks[2].ap()[0:1, 0:64]
    ctx.t["pAcsB0"] = banks[0].ap()[:, 0:256]
    ctx.t["pAcsB1"] = banks[1].ap()[:, 0:256]
    ctx.t["pRT0"] = banks[2].ap()[0:4, 0:128]
    ctx.t["pRT1"] = banks[3].ap()[0:4, 0:128]
    ctx.t["pG0"] = banks[2].ap()[:, 0:256]
    ctx.t["pG1"] = banks[3].ap()[:, 0:256]
    ctx.t["pY0"] = banks[4].ap()[0:64, 0:256]
    ctx.t["pY1"] = banks[5].ap()[0:64, 0:256]
    ctx.t["pS0"] = banks[6].ap()[:, 0:64]
    ctx.t["pS1"] = banks[7].ap()[:, 0:64]
    ctx.dram("u_dram", [TP, D], F32, kind="ExternalInput")
    ctx.dram("W_in", [D, EIN], F32, kind="ExternalInput")
    ctx.dram("convw_d", [128, 4 * 34], F32, kind="ExternalInput")
    ctx.dram("convb_d", [128, 34], F32, kind="ExternalInput")
    ctx.dram("dtb_d", [64, 1], F32, kind="ExternalInput")
    ctx.dram("alog_d", [64, 1], F32, kind="ExternalInput")
    ctx.dram("D_d", [128, 64], F32, kind="ExternalInput")
    ctx.dram("role_d", [128, 1], F32, kind="ExternalInput")
    ctx.dram("W_out_d", [DIN, D], F32, kind="ExternalInput")
    ctx.dram("ns_d", [128, 32], F32, kind="ExternalInput")
    ctx.dram("mamba_s", [T, D], F32)
    ctx.dram("ffn0_s", [T, D], F32)
    ctx.dram("WgT_d", [D, FF0], F32, kind="ExternalInput")
    ctx.dram("WuT_d", [D, FF0], F32, kind="ExternalInput")
    ctx.dram("WdT_d", [FF0, D], F32, kind="ExternalInput")
    ctx.dram("x1_out", [T, D], F32, kind="ExternalOutput")
    ctx.dram("stout", [128, DIN], BF16)
    ctx.dram("stgath", [256, DIN], BF16)



P0_LOADS = [("convw", "convw_d"), ("convb", "convb_d"), ("D_tile", "D_d"), ("roleflag", "role_d"), ("norm_scale", "ns_d")]



class P0Small:

    def sync(self, ctx, sy):
        sem = ctx.sem("p0b_ld")
        sy.dma_start(out=ctx.t["dt_bias"][0:64, 0:1], in_=ctx.t["dtb_d"].ap()).then_inc(sem, 16)
        sy.dma_start(out=ctx.t["A_log"][0:64, 0:1], in_=ctx.t["alog_d"].ap()).then_inc(sem, 16)
        sy.wait_ge(sem, 32)
        return None




def build_l0():
    nc = bass.Bass("TRN2", target_bir_lowering=False)
    stack = contextlib.ExitStack()
    with stack:
        ctx = Ctx(nc, stack)
        alloc_l0(ctx, nc)
        ctx.constf(EPS)
        ctx.constf(1.0)
        emit_phases(nc, ctx, [P0Consts(P0_LOADS), P0Small(), P1LN(), P2InProjA(),
                              P3Conv(), P4aState(), P4xExchange(), P4bSSD(),
                              P5Gate(), P6FFN0(), P7Combine()])
    return nc


def _rope_tables():
    inv = 1.0 / 10000.0 ** (np.arange(0, 64, 2) / 64.0)
    t = np.arange(2048.0)
    emb = np.concatenate([np.outer(t, inv), np.outer(t, inv)], axis=1)
    return np.cos(emb), np.sin(emb)


def _rope_inputs():
    cos, sin = _rope_tables()
    def tile8(tab):
        out = np.zeros((128, 16 * 256), dtype=BF)
        for tt in range(16):
            out[:, tt * 256:(tt + 1) * 256] = np.tile(tab[tt * 128:(tt + 1) * 128], (1, 8))
        return out
    return (tile8(cos[:, 0::2]), tile8(cos[:, 1::2]),
            tile8(sin[:, 0::2]), tile8(sin[:, 1::2]))


_NC_CACHE = {}


def kernel(L0_attn_u, L0_attn_W_in, L0_attn_conv_w, L0_attn_conv_b, L0_attn_dt_bias,
           L0_attn_A_log, L0_attn_D, L0_attn_norm_scale, L0_attn_W_out,
           L0_ffn_w_gate, L0_ffn_w_up, L0_ffn_w_down,
           L1_attn_Wq, L1_attn_Wk, L1_attn_Wv, L1_attn_Wo,
           L1_ffn_gate_w, L1_ffn_up_w, L1_ffn_down_w,
           L1_ffn_gate_s, L1_ffn_up_s, L1_ffn_down_s):
    u = np.asarray(L0_attn_u, dtype=np.float32)
    convw_l = np.zeros((128, 136), dtype=np.float32)
    convb_l = np.zeros((128, 34), dtype=np.float32)
    cw = np.asarray(L0_attn_conv_w, dtype=np.float32)
    cb = np.asarray(L0_attn_conv_b, dtype=np.float32)
    for j in range(34):
        for i in range(4):
            convw_l[:, 4 * j + i] = cw[i, 128 * j:128 * j + 128]
        convb_l[:, j] = cb[128 * j:128 * j + 128]
    D_t = np.tile(np.asarray(L0_attn_D, np.float32)[None, :], (128, 1))
    ns_l = np.ascontiguousarray(np.asarray(L0_attn_norm_scale, np.float32).reshape(32, 128).T)
    WgT = np.ascontiguousarray(np.asarray(L0_ffn_w_gate, np.float32).T)
    WuT = np.ascontiguousarray(np.asarray(L0_ffn_w_up, np.float32).T)
    WdT = np.ascontiguousarray(np.asarray(L0_ffn_w_down, np.float32).T)
    if "fused" not in _NC_CACHE:
        _NC_CACHE["fused"] = build_fused()
    ce8, co8, se8, so8 = _rope_inputs()
    Wq = np.asarray(L1_attn_Wq, np.float32)
    Wk = np.asarray(L1_attn_Wk, np.float32)
    Wv = np.asarray(L1_attn_Wv, np.float32)
    WoT = np.ascontiguousarray(np.asarray(L1_attn_Wo, np.float32).T)
    Wg1T = np.ascontiguousarray(np.asarray(L1_ffn_gate_w, np.float32).T)
    Wu1T = np.ascontiguousarray(np.asarray(L1_ffn_up_w, np.float32).T)
    Wd1T = np.ascontiguousarray(np.asarray(L1_ffn_down_w, np.float32).T)
    gs_l = np.ascontiguousarray(np.asarray(L1_ffn_gate_s, np.float32).reshape(44, 128).T)
    us_l = np.ascontiguousarray(np.asarray(L1_ffn_up_s, np.float32).reshape(44, 128).T)
    ds_t = np.tile(np.asarray(L1_ffn_down_s, np.float32)[None, :], (128, 1)).astype(BF)
    in_maps = []
    for c in range(8):
        half = c % 2
        hsl = slice(0, 1024) if half == 0 else slice(1024, 2048)
        ksl = slice(0, 256) if half == 0 else slice(256, 512)
        role = np.full((128, 1), float(half), dtype=np.float32)
        in_maps.append({
            "u_dram": prep_u_core(u, c),
            "W_in": np.asarray(L0_attn_W_in, np.float32),
            "convw_d": convw_l, "convb_d": convb_l,
            "dtb_d": np.asarray(L0_attn_dt_bias, np.float32)[:, None],
            "alog_d": np.asarray(L0_attn_A_log, np.float32)[:, None],
            "D_d": D_t, "role_d": role,
            "W_out_d": np.asarray(L0_attn_W_out, np.float32), "ns_d": ns_l,
            "WgT_d": WgT, "WuT_d": WuT, "WdT_d": WdT,
            "WqT_d": np.ascontiguousarray(Wq[hsl].T),
            "WkT_d": np.ascontiguousarray(Wk[ksl].T),
            "WvT_d": np.ascontiguousarray(Wv[ksl].T),
            "WoT_d": WoT,
            "Wg1T_d": Wg1T, "Wu1T_d": Wu1T, "Wd1T_d": Wd1T,
            "ce8_d": ce8, "co8_d": co8, "se8_d": se8, "so8_d": so8,
            "gs1_d": gs_l, "us1_d": us_l, "ds_d": ds_t,
        })
    res = run_bass_kernel_spmd(_NC_CACHE["fused"], in_maps, core_ids=list(range(8)))
    out = np.zeros((4, 2048, 2048), dtype=np.float32)
    for c in range(8):
        b, half = c // 2, c % 2
        out[b, half * 1024:half * 1024 + 1024] = np.asarray(res.results[c]["out_d"])
    return out

def alloc_l1_fused(ctx, nc):
    arena = ctx.t["arena"]
    ar32 = arena.bitcast(F32)
    b = {}
    b["u_tok8"] = ar32.ap()[:, 0:16384]
    b["xn1_tok"] = arena.ap()[:, 32768:32768 + 16384]
    b["ln_sq"] = ar32.ap()[:, 24576:24576 + 2048]
    b["ln_tmp2b"] = ar32.ap()[:, 26624:26624 + 2048]
    b["ln_aggr"] = ar32.ap()[:, 28672:28672 + 16]
    b["ln_rows"] = ar32.ap()[:, 28688:28688 + 32]
    b["xnF"] = arena.ap()[:, 0:32768]
    b["qkvslab"] = arena.ap()[:, 57408:57408 + 24576]
    b["Qtok"] = arena.ap()[:, 32768:32768 + 16384]
    b["Ktok"] = arena.ap()[:, 49152:49152 + 4096]
    b["Vtok"] = arena.ap()[:, 53248:53248 + 4160]
    b["Qf"] = arena.ap()[:, 0:32768]
    b["expS"] = arena.ap()[:, 32768:32768 + 8192]
    b["Kf"] = arena.ap()[:, 57408:57408 + 8192]
    b["attnO"] = arena.ap()[:, 65600:65600 + 16384]
    b["dtmp"] = arena.ap()[:, 40960:40960 + 1024]
    b["rBs"] = arena.ap()[:, 41984:41984 + 512]
    b["rrow1"] = arena.ap()[:, 42496:42496 + 1024]
    b["drow"] = ar32.ap()[:, 21760:21760 + 512]
    b["attnG"] = arena.ap()[:, 0:16384]
    b["woslabA"] = arena.ap()[:, 16384:16384 + 1024]
    b["woslabB"] = arena.ap()[:, 17408:17408 + 1024]
    b["stage2"] = ar32.ap()[:, 11264:11264 + 2048]
    b["xn1own"] = arena.ap()[:, 45056:45056 + 16384]
    b["h1f"] = arena.ap()[:, 0:45056]
    b["ztmp"] = arena.ap()[:, 77824:77824 + 1024]
    b["uR"] = ar32.ap()[:, 0:4096]
    b["mR"] = ar32.ap()[:, 4096:8192]
    b["fR"] = ar32.ap()[:, 8192:12288]
    b["x1R"] = ar32.ap()[:, 12288:16384]
    b["x2R"] = ar32.ap()[:, 16384:20480]
    b["dsB"] = ar32.ap()[:, 20480:22528]
    xn = ctx.t["xn"]
    b["ce8"] = xn.ap()[:, 0:4096]
    b["co8"] = xn.ap()[:, 4096:8192]
    b["se8"] = xn.ap()[:, 8192:12288]
    b["so8"] = xn.ap()[:, 12288:16384]
    b["dmask"] = ctx.t["acsA"].bitcast(BF16).ap()[:, 0:2048]
    b["ropetmp"] = ctx.t["acsB"].bitcast(BF16).ap()[:, 0:512]
    b["gs1"] = ctx.t["acsB"].ap()[:, 256:300]
    b["us1"] = ctx.t["acsB"].ap()[:, 300:344]
    b["dsrow"] = ctx.t["dtbuf"].bitcast(BF16).ap()[:, 0:2048]
    ctx.dram("x1_mid", [T, D], F32)
    b["x1_in"] = ctx.t["x1_mid"]
    ctx.t["x1_out"] = ctx.t["x1_mid"]
    ctx.dram("WqT_d", [D, NQH * 64], F32, kind="ExternalInput")
    ctx.dram("WkT_d", [D, NKH * 64], F32, kind="ExternalInput")
    ctx.dram("WvT_d", [D, NKH * 64], F32, kind="ExternalInput")
    ctx.dram("WoT_d", [D, D], F32, kind="ExternalInput")
    ctx.dram("Wg1T_d", [D, FF1], F32, kind="ExternalInput")
    ctx.dram("Wu1T_d", [D, FF1], F32, kind="ExternalInput")
    ctx.dram("Wd1T_d", [FF1, D], F32, kind="ExternalInput")
    ctx.dram("ce8_d", [128, 16 * 256], BF16, kind="ExternalInput")
    ctx.dram("co8_d", [128, 16 * 256], BF16, kind="ExternalInput")
    ctx.dram("se8_d", [128, 16 * 256], BF16, kind="ExternalInput")
    ctx.dram("so8_d", [128, 16 * 256], BF16, kind="ExternalInput")
    ctx.dram("gs1_d", [128, 44], F32, kind="ExternalInput")
    ctx.dram("us1_d", [128, 44], F32, kind="ExternalInput")
    ctx.dram("ds_d", [128, 2048], BF16, kind="ExternalInput")
    ctx.dram("xn1b", [T, D], BF16)
    ctx.dram("xn1g", [TT2, D], BF16)
    ctx.dram("aOb", [T, TT2], BF16)
    ctx.dram("aOg", [TT2, TT2], BF16)
    ctx.dram("attn_s", [T, D], F32)
    ctx.dram("ffn1_s", [T, D], F32)
    ctx.dram("out_d", [T, D], F32, kind="ExternalOutput")
    return b


def build_fused():
    nc = bass.Bass("TRN2", target_bir_lowering=False)
    stack = contextlib.ExitStack()
    with stack:
        ctx = Ctx(nc, stack)
        alloc_l0(ctx, nc)
        rebind = alloc_l1_fused(ctx, nc)
        ctx.constf(EPS)
        ctx.constf(1.0)
        ctx.constf(-30.0)
        ffn1 = P6FFN0(xn_name="xn1own", xn_off=0, out_name="ffn1_s",
                      wg="Wg1T_d", wu="Wu1T_d", wd="Wd1T_d", pfx="q6",
                      ne=FF1, gate_scale="gs1", up_scale="us1", slab0=61440)
        phases = [P0Consts(P0_LOADS), P0Small(), P1LN(), P2InProjA(), P3Conv(),
                  P4aState(), P4xExchange(), P4bSSD(), P5Gate(), P6FFN0(),
                  P7Combine(), rebind, Q0Consts(), Q1LN(), Q2QKV(), Q3Attn(),
                  Q4Exchange(), Q5Wo(), ffn1, Q7Final()]
        emit_phases(nc, ctx, phases)
    return nc
